# version 09
# baseline (speedup 1.0000x reference)
import numpy as np
from contextlib import ExitStack

import concourse.bass as bass
import concourse.mybir as mybir
from concourse.bass_utils import run_bass_kernel_spmd

F32 = mybir.dt.float32
BF16 = mybir.dt.bfloat16
AF = mybir.ActivationFunctionType
ALU = mybir.AluOpType
AX = mybir.AxisListType

B, L, D = 4, 4096, 1024
NH, DK = 16, 64
NQ = 2048
NT = 16
NJ = 2047
SCALE = 0.125

IMG_W = 4348
IMG_FL = 128
IMG_RS = IMG_W + 2 * IMG_FL
PW2 = 4349
BU_W = 4224
AUF_W = 4222
PB_L, PB_R = 3072, 1024
PB_W = PB_L + L + PB_R

AR_W = 43008
KA_, KB_ = 0, 4096
QA_, QB_ = 8192, 10240
VV_ = 12288
WW_ = 16416
QR_ = 19488
BU_ = 21792
PBUF_ = 26016
AUF_ = 34208
PT_ = 38430
APT_ = 40478
ATR_ = 42526
XT_, SQ_, XN32_, XNTM_, XF_ = 0, 2048, 4096, 6144, 8192
OACC_, WO_, XTE_ = 0, 32768, 40960
FXT_, FSQ_, FXN_ = 32768, 34816, 36864
GP_, UP_ = 32768, 36864

ENGS = ("pe", "dve", "act", "sy", "gp")

DBG = False

(MNEG, RSA, RSB, RSUM, RINV, CSLL, CSLH, CSRL, CSRH, RS0, RS2, WSUM,
 MEAN, VAR, RSTD, SSQ, T1, T2, T3, T4, QK0, QK2, EPS) = range(23)
SMEAN0 = 23
SRSTD0 = 55


class Sched:
    def __init__(self):
        self.plan = {k: [] for k in ENGS}
        self.cnt = {}
        self.sem = {}

    def add_sem(self, name, handle):
        self.sem[name] = handle
        self.cnt[name] = 0

    def op(self, eng, fn, waits=(), inc=None, n=1):
        val = None
        if inc is not None:
            self.cnt[inc] += n
            val = self.cnt[inc]
        wl = tuple((s, v) for (s, v) in waits if v is not None and v > 0)
        sems = self.sem

        def run(e, fn=fn, wl=wl, inc=inc, n=n):
            seen = {}
            for s, v in wl:
                if seen.get(s, -1) < v:
                    seen[s] = v
            for s, v in seen.items():
                e.wait_ge(sems[s], v)
            ins = fn(e)
            if inc is not None:
                ins.then_inc(sems[inc], n)

        self.plan[eng].append(run)
        return val


def build_program():
    nc = bass.Bass("TRN2", target_bir_lowering=False)

    ap_in = {}
    def din(name, shape, dt=F32):
        ap_in[name] = nc.dram_tensor(name, shape, dt, kind="ExternalInput").ap()
        return ap_in[name]

    din("xb", [L, D]); din("xq", [NQ, D])
    din("wqT", [D, D]); din("wkT", [D, D]); din("wvT", [D, D]); din("woT", [D, D])
    din("bq", [1, D]); din("bk", [1, D]); din("bv", [1, D]); din("bo", [1, D])
    din("rkT", [DK, NJ]); din("rv", [NJ, DK])
    din("gwT", [D, 8]); din("ebias", [1, 8])
    din("gpT", [8, D, 512]); din("upT", [8, D, 512]); din("dpT", [8, 512, D])
    din("identf", [128, 128])
    din("idlo", [128, 128]); din("idhi", [128, 128])
    din("masklo", [128, 128]); din("maskup", [128, 128])
    din("fmask64", [128, 64]); din("hmask16", [128, 16])
    din("steps", [2 * NT, L])
    din("mcol", [128, 2])

    out_d = nc.dram_tensor("out", [NQ, D], F32, kind="ExternalOutput").ap()
    qrimg = nc.dram_tensor("qrimg", [2 * 128 * IMG_RS], BF16, kind="Internal")
    pimg = nc.dram_tensor("pimg", [2 * 128 * PW2], BF16, kind="Internal")
    dbg = {}
    if DBG:
        dbg["dxn"] = nc.dram_tensor("dxn", [128, 8 * L], BF16,
                                    kind="ExternalOutput").ap()
        dbg["dcw"] = nc.dram_tensor("dcw", [128, 128], F32,
                                    kind="ExternalOutput").ap()
        dbg["datn"] = nc.dram_tensor("datn", [128, 8 * NQ], BF16,
                                     kind="ExternalOutput").ap()
        dbg["dscr"] = nc.dram_tensor("dscr", [128, NT * 4096], BF16,
                                     kind="ExternalOutput").ap()
        dbg["dpb"] = nc.dram_tensor("dpb", [128, NT * 4096], BF16,
                                    kind="ExternalOutput").ap()
        dbg["dstats"] = nc.dram_tensor("dstats", [128, NT * 32], F32,
                                       kind="ExternalOutput").ap()
        dbg["drelv"] = nc.dram_tensor("drelv", [128, NT * 129], F32,
                                      kind="ExternalOutput").ap()
        dbg["dauf"] = nc.dram_tensor("dauf", [128, NT * AUF_W], BF16,
                                     kind="ExternalOutput").ap()
        dbg["datr"] = nc.dram_tensor("datr", [128, NT * 64], BF16,
                                     kind="ExternalOutput").ap()
        dbg["dgt"] = nc.dram_tensor("dgt", [128, NT * 80], F32,
                                    kind="ExternalOutput").ap()

    es = ExitStack()
    with es:
        S = Sched()
        for s in ("s_pe", "s_dve", "s_act", "s_sy", "s_gp"):
            S.add_sem(s, es.enter_context(nc.semaphore(s)))

        t_ = {}
        def sb(name, shape, dt):
            t_[name] = es.enter_context(nc.sbuf_tensor(name, shape, dt))
            return t_[name]

        sb("arena", [128, AR_W], BF16)
        sb("xn_fm", [128, 8 * L], BF16)
        sb("attn_fm", [128, 8 * NQ], BF16)
        sb("stats", [128, 96], F32)
        sb("cw", [128, 128], F32)
        sb("identb", [128, 128], BF16)
        sb("identf_sb", [128, 128], F32)
        sb("idlo_sb", [128, 128], BF16)
        sb("idhi_sb", [128, 128], BF16)
        sb("masklo_sb", [128, 128], BF16)
        sb("maskup_sb", [128, 128], BF16)
        sb("fmask_sb", [128, 64], F32)
        sb("hmask_sb", [128, 16], F32)
        sb("mcol_sb", [128, 2], F32)
        sb("rkT_sb", [64, 2048], BF16)
        sb("rv_sb", [128, 16 * 65], BF16)
        sb("gw_sb", [128, 64], F32)
        sb("eb_sb", [1, 8], F32)
        sb("ebb_sb", [128, 8], F32)
        sb("rv0b", [128, 64], F32)
        sb("rv2b", [128, 64], F32)
        sb("bias_sb", [1, 4 * D], BF16)
        sb("ones_row", [1, 512], BF16)
        sb("onesf_row", [1, 128], F32)
        sb("qkT_row", [2, 128], BF16)
        sb("relv_f", [128, 65], F32)
        sb("add1_f", [128, 64], F32)
        sb("gt8", [128, 80], F32)
        t_["psc"] = es.enter_context(nc.psum_tensor("psc", [128, 4096], F32))

        with nc.Block() as block:
            build_plan(S, t_, ap_in, out_d, qrimg, pimg, dbg)
            for eng, attr in (("sy", "sync"), ("gp", "gpsimd"), ("pe", "tensor"),
                              ("dve", "vector"), ("act", "scalar")):
                lst = S.plan[eng]
                def runner(e, lst=lst, eng=eng):
                    for i, f in enumerate(lst):
                        try:
                            f(e)
                        except Exception:
                            print(f"FAILING OP: eng={eng} index={i}/{len(lst)}")
                            raise
                getattr(block, attr)(runner)
    return nc


def build_plan(S, t_, ap_in, out_d, qrimg, pimg, dbg):
    ar = t_["arena"]; xn_fm = t_["xn_fm"]; attn_fm = t_["attn_fm"]
    stats = t_["stats"]; cw_sb = t_["cw"]
    identb = t_["identb"]; identf_sb = t_["identf_sb"]
    idlo_sb = t_["idlo_sb"]; idhi_sb = t_["idhi_sb"]
    masklo_sb = t_["masklo_sb"]; maskup_sb = t_["maskup_sb"]
    fmask_sb = t_["fmask_sb"]; hmask_sb = t_["hmask_sb"]; mcol_sb = t_["mcol_sb"]
    rkT_sb = t_["rkT_sb"]; rv_sb = t_["rv_sb"]; gw_sb = t_["gw_sb"]
    eb_sb = t_["eb_sb"]; ebb_sb = t_["ebb_sb"]; rv0b = t_["rv0b"]; rv2b = t_["rv2b"]
    bias_sb = t_["bias_sb"]; ones_row = t_["ones_row"]; onesf_row = t_["onesf_row"]
    qkT_row = t_["qkT_row"]
    relv_f = t_["relv_f"]; add1_f = t_["add1_f"]
    gt8 = t_["gt8"]; psc = t_["psc"]

    def st(i):
        return stats[:, i:i + 1]

    def arf(c0, ncols):
        return ar[:, c0:c0 + 2 * ncols].bitcast(F32)

    def dram_flat(th, off, rowstride, p, width):
        return bass.AP(tensor=th, offset=off, ap=[[rowstride, p], [1, width]])

    m_lo = mcol_sb[:, 0:1]
    m_hi = mcol_sb[:, 1:2]

    def gpdma(fn, waits=()):
        return S.op("gp", fn, waits=waits, inc="s_gp", n=16)

    def sydma(fn, waits=()):
        return S.op("sy", fn, waits=waits, inc="s_sy", n=16)

    def sydma_split(fn2, waits=()):
        def g(e, fn2=fn2):
            fn2(e, 0).then_inc(S.sem["s_sy"], 16)
            return fn2(e, 64)
        S.cnt["s_sy"] += 16
        return S.op("sy", g, waits=waits, inc="s_sy", n=16)

    ld0 = gpdma(lambda e: e.dma_start(out=identb[:, :], in_=ap_in["identf"][:, :]))
    gpdma(lambda e: e.dma_start(out=identf_sb[:, :], in_=ap_in["identf"][:, :]))
    gpdma(lambda e: e.dma_start(out=idlo_sb[:, :], in_=ap_in["idlo"][:, :]))
    gpdma(lambda e: e.dma_start(out=idhi_sb[:, :], in_=ap_in["idhi"][:, :]))
    gpdma(lambda e: e.dma_start(out=masklo_sb[:, :], in_=ap_in["masklo"][:, :]))
    gpdma(lambda e: e.dma_start(out=maskup_sb[:, :], in_=ap_in["maskup"][:, :]))
    gpdma(lambda e: e.dma_start(out=fmask_sb[:, :], in_=ap_in["fmask64"][:, :]))
    gpdma(lambda e: e.dma_start(out=hmask_sb[:, :], in_=ap_in["hmask16"][:, :]))
    gpdma(lambda e: e.dma_start(out=mcol_sb[:, :], in_=ap_in["mcol"][:, :]))
    gpdma(lambda e: e.dma_start(out=rkT_sb[0:64, 0:NJ], in_=ap_in["rkT"][:, :]))
    gpdma(lambda e: e.dma_start(out=gw_sb[:, :].rearrange("p (c e) -> p c e", c=8),
                                in_=ap_in["gwT"].rearrange("(c p) e -> p c e", p=128)))
    gpdma(lambda e: e.dma_start(out=eb_sb[:, :], in_=ap_in["ebias"][:, :]))
    for i, nm in enumerate(("bq", "bk", "bv", "bo")):
        gpdma(lambda e, i=i, nm=nm: e.dma_start(
            out=bias_sb[0:1, i * D:(i + 1) * D], in_=ap_in[nm][:, :]))
    mrv = S.op("dve", lambda e: e.memset(rv_sb[:, :], 0.0), inc="s_dve")
    rv_last = None
    for k in range(16):
        r0, r1 = k * 128, min(NJ, (k + 1) * 128)
        rv_last = gpdma(lambda e, k=k, r0=r0, r1=r1: e.dma_start(
            out=rv_sb[0:r1 - r0, k * 65:k * 65 + 64], in_=ap_in["rv"][r0:r1, :]),
            waits=(("s_dve", mrv),))
    ld_consts = rv_last

    S.op("dve", lambda e: e.memset(ones_row[:, :], 1.0), inc="s_dve")
    S.op("dve", lambda e: e.memset(onesf_row[:, :], 1.0), inc="s_dve")
    ev_eps = S.op("dve", lambda e: (
        e.memset(st(EPS), 1e-5),
        e.memset(st(MNEG), 0.0))[-1], inc="s_dve")
    rv_ones = S.op("dve", lambda e: e.memset(
        rv_sb[:, :].rearrange("p (k c) -> p k c", c=65)[:, :, 64:65], 1.0),
        waits=(("s_gp", rv_last),), inc="s_dve")
    S.op("dve", lambda e: e.memset(
        ar[:, VV_:VV_ + 4128].rearrange("p (k c) -> p k c", c=129)[:, :, 64:65], 1.0),
        inc="s_dve")
    S.op("dve", lambda e: e.memset(ar[:, PBUF_:PBUF_ + PB_L], 0.0), inc="s_dve")
    S.op("dve", lambda e: e.memset(ar[:, PBUF_ + PB_L + L:PBUF_ + PB_W], 0.0),
         inc="s_dve")
    mzf = S.op("dve", lambda e: e.memset(ar[:, BU_:BU_ + 128], 0.0), inc="s_dve")
    fl_ev = None
    for slot in range(2):
        for off in (0, IMG_FL + IMG_W):
            fl_ev = sydma(lambda e, slot=slot, off=off: e.dma_start(
                out=dram_flat(qrimg, slot * 128 * IMG_RS + off, IMG_RS, 128, IMG_FL),
                in_=ar[:, BU_:BU_ + 128]),
                waits=(("s_dve", mzf),))

    pe_eb = S.op("pe", lambda e: e.matmul(psc[:, 0:8], onesf_row[0:1, :],
                                          eb_sb[0:1, :], start=True, stop=True),
                 waits=(("s_gp", ld_consts),), inc="s_pe")
    pe_rv0 = S.op("pe", lambda e: e.matmul(psc[:, 512:576], ones_row[0:1, 0:128],
                                           rv_sb[0:1, 0:64], start=True, stop=True),
                  waits=(("s_dve", rv_ones),), inc="s_pe")
    cp2046 = sydma(lambda e: e.dma_start(out=qkT_row[0:1, 0:64],
                                         in_=rv_sb[126:127, 15 * 65:15 * 65 + 64]),
                   waits=(("s_dve", rv_ones),))
    pe_rv2 = S.op("pe", lambda e: e.matmul(psc[:, 1024:1088], ones_row[0:1, 0:128],
                                           qkT_row[0:1, 0:64], start=True, stop=True),
                  waits=(("s_sy", cp2046),), inc="s_pe")
    ev_ebb = S.op("dve", lambda e: (
        e.tensor_copy(ebb_sb[:, :], psc[:, 0:8]),
        e.tensor_copy(rv0b[:, :], psc[:, 512:576]),
        e.tensor_copy(rv2b[:, :], psc[:, 1024:1088]))[-1],
        waits=(("s_pe", pe_rv2), ("s_pe", pe_rv0), ("s_pe", pe_eb)), inc="s_dve")

    x_t = arf(XT_, 1024)
    sq_t = arf(SQ_, 1024)
    xn_tm = ar[:, XNTM_:XNTM_ + 1024]
    xb = ap_in["xb"]; xq = ap_in["xq"]

    xv8 = xn_fm[:, :].rearrange("p (c n) -> p c n", c=8)
    prev_tr, prev_ev = 0, 0
    for t in range(32):
        lx = sydma(lambda e, t=t: e.dma_start(out=x_t[:, :],
                                              in_=xb[t * 128:(t + 1) * 128, :]),
                   waits=(("s_pe", prev_tr),))
        l1 = S.op("dve", lambda e: (
            e.reduce_sum(st(T1), x_t[:, :], AX.X),
            e.tensor_tensor(sq_t[:, :], x_t[:, :], x_t[:, :], ALU.mult))[-1],
            waits=(("s_sy", lx),), inc="s_dve")
        l2 = S.op("dve", lambda e, t=t: (
            e.reduce_sum(st(SSQ), sq_t[:, :], AX.X),
            e.tensor_scalar_mul(st(SMEAN0 + t), st(T1), 1.0 / 1024.0))[-1],
            waits=(("s_dve", l1),), inc="s_dve")
        l3 = S.op("dve", lambda e, t=t: (
            e.tensor_scalar_mul(st(VAR), st(SSQ), 1.0 / 1024.0),
            e.tensor_tensor(st(T2), st(SMEAN0 + t), st(SMEAN0 + t), ALU.mult))[-1],
            waits=(("s_dve", l2),), inc="s_dve")
        lnc = S.op("dve", lambda e: e.tensor_sub(st(VAR), st(VAR), st(T2)),
                   waits=(("s_dve", l3),), inc="s_dve")
        S.op("act", lambda e: e.activation(st(T1), st(VAR), AF.Sqrt, bias=st(EPS)),
             waits=(("s_dve", lnc),))
        rstd = S.op("act", lambda e: e.activation(sq_t[:, 0:512], x_t[:, 0:512],
                                                  AF.Square), inc="s_act")
        rc1 = S.op("dve", lambda e, t=t: e.reciprocal(st(SRSTD0 + t), st(T1)),
                   waits=(("s_act", rstd), ("s_pe", prev_tr)), inc="s_dve")
        rcp = S.op("dve", lambda e, t=t: e.tensor_scalar(
            xn_tm[:, :], x_t[:, :], st(SMEAN0 + t), st(SRSTD0 + t),
            ALU.subtract, ALU.mult),
            waits=(("s_dve", rc1),), inc="s_dve")
        trp = S.op("pe", lambda e: [
            e.matmul(psc[:, c * 128:(c + 1) * 128], xn_tm[:, c * 128:(c + 1) * 128],
                     identb[:, :], start=True, stop=True) for c in range(8)][-1],
            waits=(("s_dve", rcp), ("s_gp", ld0), ("s_dve", ev_ebb),
                   ("s_dve", prev_ev)), inc="s_pe")
        prev_ev = S.op("dve", lambda e, t=t: e.tensor_copy(
            xv8[:, :, t * 128:(t + 1) * 128],
            psc[:, 0:1024].rearrange("p (c n) -> p c n", c=8)),
            waits=(("s_pe", trp),), inc="s_dve")
        prev_tr = trp
    ev_xn_done = prev_ev
    ev_b_pe = prev_tr
    dbg_ev = {}
    if DBG:
        dbg_ev["dxn"] = sydma(lambda e: e.dma_start(out=dbg["dxn"], in_=xn_fm[:, :]),
                              waits=(("s_dve", ev_xn_done),))

    xf32 = arf(XF_, 1024)
    xn32 = arf(XN32_, 1024)
    sC = gt8[:, 0:8]; cC = gt8[:, 8:16]; G4 = gt8[:, 16:20]
    cnt4 = gt8[:, 20:24]; gm4 = gt8[:, 24:28]
    A16 = gt8[:, 32:48]; C16 = gt8[:, 48:64]
    sm8 = gt8[:, 64:72]; tmp8 = gt8[:, 72:80]

    prev_c_pe = ev_b_pe
    ev_cw = 0
    prev_sig = 0
    prev_dgt = 0
    for t in range(NT):
        lx = sydma(lambda e, t=t: e.dma_start(out=x_t[:, :],
                                              in_=xq[t * 128:(t + 1) * 128, :]),
                   waits=(("s_pe", prev_c_pe), ("s_dve", ev_xn_done)))
        xsel = S.op("dve", lambda e, t=t: (
            e.tensor_scalar_mul(st(T1), st(SMEAN0 + 2 * t), m_lo),
            e.tensor_scalar_mul(st(T2), st(SRSTD0 + 2 * t), m_lo))[-1],
            waits=(("s_sy", lx), ("s_pe", prev_c_pe)), inc="s_dve")
        xsel = S.op("dve", lambda e, t=t: (
            e.scalar_tensor_tensor(st(T1), st(SMEAN0 + 2 * t + 1), m_hi, st(T1),
                                   ALU.mult, ALU.add),
            e.scalar_tensor_tensor(st(T2), st(SRSTD0 + 2 * t + 1), m_hi, st(T2),
                                   ALU.mult, ALU.add))[-1],
            waits=(("s_dve", xsel),), inc="s_dve")
        xnf = S.op("dve", lambda e: e.tensor_scalar(
            xn32[:, :], x_t[:, :], st(T1), st(T2), ALU.subtract, ALU.mult),
            waits=(("s_dve", xsel),), inc="s_dve")
        trp = S.op("pe", lambda e: [
            e.transpose(psc[:, c * 128:(c + 1) * 128], xn32[:, c * 128:(c + 1) * 128],
                        identf_sb[:, :]) for c in range(8)][-1],
            waits=(("s_dve", xnf),), inc="s_pe")
        evf = S.op("dve", lambda e: e.tensor_copy(xf32[:, :], psc[:, 0:1024]),
                   waits=(("s_pe", trp),), inc="s_dve")
        lg = S.op("pe", lambda e: [
            e.matmul(psc[:, 1024:1032], xf32[:, c * 128:(c + 1) * 128],
                     gw_sb[:, c * 8:(c + 1) * 8], start=(c == 0), stop=(c == 7))
            for c in range(8)][-1],
            waits=(("s_dve", evf), ("s_act", prev_sig)), inc="s_pe")
        S.op("act", lambda e: e.activation(sC, psc[:, 1024:1032], AF.Sigmoid),
             waits=(("s_pe", lg), ("s_dve", ev_cw), ("s_sy", prev_dgt)))
        sig = S.op("act", lambda e: e.activation(xf32[:, 0:512], xf32[:, 512:1024],
                                                 AF.Square), inc="s_act")
        prev_sig = sig
        cprev = S.op("dve", lambda e: e.tensor_add(cC, sC, ebb_sb[:, :]),
                     waits=(("s_act", sig), ("s_dve", ev_ebb)), inc="s_dve")
        cprev = S.op("dve", lambda e: e.tensor_add(G4, cC[:, 0:8:2], cC[:, 1:8:2]),
                     waits=(("s_dve", cprev),), inc="s_dve")
        cprev = S.op("dve", lambda e: (
            e.tensor_tensor(A16.rearrange("p (a b) -> p a b", a=4),
                            G4.unsqueeze(2).broadcast_to((128, 4, 4)),
                            G4.unsqueeze(1).broadcast_to((128, 4, 4)),
                            ALU.is_lt),
            e.tensor_tensor(C16.rearrange("p (a b) -> p a b", a=4),
                            G4.unsqueeze(2).broadcast_to((128, 4, 4)),
                            G4.unsqueeze(1).broadcast_to((128, 4, 4)),
                            ALU.is_equal))[-1],
            waits=(("s_dve", cprev),), inc="s_dve")
        cprev = S.op("dve", lambda e: e.tensor_tensor(C16, C16, hmask_sb[:, :],
                                                      ALU.mult),
                     waits=(("s_dve", cprev),), inc="s_dve")
        cprev = S.op("dve", lambda e: e.tensor_add(A16, A16, C16),
                     waits=(("s_dve", cprev),), inc="s_dve")
        cprev = S.op("dve", lambda e: e.tensor_reduce(
            cnt4, A16.rearrange("p (a b) -> p a b", a=4), AX.X, ALU.add),
            waits=(("s_dve", cprev),), inc="s_dve")
        cprev = S.op("dve", lambda e: e.tensor_single_scalar(gm4, cnt4, 2.0,
                                                             ALU.is_lt),
                     waits=(("s_dve", cprev),), inc="s_dve")
        cprev = S.op("dve", lambda e: e.tensor_copy(
            sm8.rearrange("p (g j) -> p g j", g=4),
            gm4.unsqueeze(2).broadcast_to((128, 4, 2))),
            waits=(("s_dve", cprev),), inc="s_dve")
        cprev = S.op("dve", lambda e: e.tensor_tensor(tmp8, cC, sm8, ALU.mult),
                     waits=(("s_dve", cprev),), inc="s_dve")
        cprev = S.op("dve", lambda e: (
            e.tensor_tensor(relv_f[:, 0:64].rearrange("p (a b) -> p a b", a=8),
                            tmp8.unsqueeze(2).broadcast_to((128, 8, 8)),
                            tmp8.unsqueeze(1).broadcast_to((128, 8, 8)),
                            ALU.is_lt),
            e.tensor_tensor(add1_f[:, 0:64].rearrange("p (a b) -> p a b", a=8),
                            tmp8.unsqueeze(2).broadcast_to((128, 8, 8)),
                            tmp8.unsqueeze(1).broadcast_to((128, 8, 8)),
                            ALU.is_equal))[-1],
            waits=(("s_dve", cprev),), inc="s_dve")
        cprev = S.op("dve", lambda e: e.tensor_tensor(
            add1_f[:, 0:64], add1_f[:, 0:64], fmask_sb[:, :], ALU.mult),
            waits=(("s_dve", cprev),), inc="s_dve")
        cprev = S.op("dve", lambda e: e.tensor_add(
            relv_f[:, 0:64], relv_f[:, 0:64], add1_f[:, 0:64]),
            waits=(("s_dve", cprev),), inc="s_dve")
        cprev = S.op("dve", lambda e: e.tensor_reduce(
            tmp8, relv_f[:, 0:64].rearrange("p (a b) -> p a b", a=8), AX.X, ALU.add),
            waits=(("s_dve", cprev),), inc="s_dve")
        cprev = S.op("dve", lambda e: e.tensor_single_scalar(tmp8, tmp8, 2.0,
                                                             ALU.is_lt),
                     waits=(("s_dve", cprev),), inc="s_dve")
        cprev = S.op("dve", lambda e: e.tensor_tensor(tmp8, tmp8, sC, ALU.mult),
                     waits=(("s_dve", cprev),), inc="s_dve")
        cprev = S.op("dve", lambda e: e.reduce_sum(st(WSUM), tmp8, AX.X),
                     waits=(("s_dve", cprev),), inc="s_dve")
        cprev = S.op("dve", lambda e: e.tensor_scalar_add(st(T1), st(WSUM), 1e-20),
                     waits=(("s_dve", cprev),), inc="s_dve")
        cprev = S.op("dve", lambda e: e.reciprocal(st(T2), st(T1)),
                     waits=(("s_dve", cprev),), inc="s_dve")
        ev_cw = S.op("dve", lambda e, t=t: e.tensor_scalar_mul(
            cw_sb[:, t * 8:(t + 1) * 8], tmp8, st(T2)),
            waits=(("s_dve", cprev),), inc="s_dve")
        if DBG:
            prev_dgt = sydma(lambda e, t=t: e.dma_start(
                out=dbg["dgt"][:, t * 80:(t + 1) * 80], in_=gt8[:, :]),
                waits=(("s_dve", ev_cw),))
        prev_c_pe = lg
    ev_c_dve = ev_cw
    if DBG:
        sydma(lambda e: e.dma_start(out=dbg["dcw"], in_=cw_sb[:, :]),
              waits=(("s_dve", ev_c_dve),))

    K_aug = [ar[:, KA_:KA_ + 4096], ar[:, KB_:KB_ + 4096]]
    Q_aug = [ar[:, QA_:QA_ + 2048], ar[:, QB_:QB_ + 2048]]
    v_pair = ar[:, VV_:VV_ + 4128]
    ww_raw = ar[:, WW_:WW_ + 1024]
    ww_lo = ar[:, WW_ + 1024:WW_ + 2048]
    ww_hi = ar[:, WW_ + 2048:WW_ + 3072]
    qr_s = ar[:, QR_:QR_ + IMG_W]
    bu = ar[:, BU_:BU_ + BU_W]
    p_b = ar[:, PBUF_:PBUF_ + PB_W]
    au = ar[:, AUF_:AUF_ + AUF_W]
    pt_s = ar[:, PT_:PT_ + 2048]
    apt_s = ar[:, APT_:APT_ + 2048]
    atr_s = ar[:, ATR_:ATR_ + 64]

    def geo(t):
        w0 = max(0, 256 * t - 2942)
        ce = min(BU_W, 256 * t + 1279)
        roff = w0 + 3197 - 256 * t
        wlo2 = max(0, 256 * t - 3070)
        whi2_a = max(0, 256 * t - 3070)
        whi2_b = max(0, 256 * t - 2942)
        return w0, ce, roff, wlo2, whi2_a, whi2_b

    ev = {
        "exp": S.cnt["s_act"],
        "epi_dve": S.cnt["s_dve"],
        "qk_pe": S.cnt["s_pe"],
        "band_pe": S.cnt["s_pe"],
        "aread_sy": [0, 0],
        "pbuf_sy": 0, "pbuf_dve": 0, "pbuf_pe": 0,
        "ev_v": 0,
        "bu_ready": 0,
    }
    gtile = [0]

    def emit_tile(hp, hi, t, Ka, Qa):
        g = gtile[0]; gtile[0] += 1
        slot = g % 2
        w0, ce, roff, wlo2, whi2_a, whi2_b = geo(t)

        qr_mm = S.op("pe", lambda e: [
            e.matmul(psc[:, n0:n1], Qa[0:64, t * 128:(t + 1) * 128],
                     rkT_sb[0:64, n0:n1], start=True, stop=True)
            for (n0, n1) in ((0, 512), (512, 1024), (1024, 1536), (1536, 2047))][-1],
            waits=(("s_act", ev["exp"]), ("s_dve", ev["ev_v"]), ("s_gp", ld0),
                   ("s_sy", fl_ev)),
            inc="s_pe")
        qr_ev = S.op("dve", lambda e: (
            e.tensor_copy(stats[:, QK0:QK0 + 1], psc[:, 0:1]),
            e.tensor_copy(stats[:, QK2:QK2 + 1], psc[:, 2046:2047]),
            e.tensor_copy(qr_s[:, 127:127 + 512], psc[:, 0:512]),
            e.tensor_copy(qr_s[:, 639:639 + 512], psc[:, 512:1024]),
            e.tensor_copy(qr_s[:, 1151:1151 + 512], psc[:, 1024:1536]),
            e.tensor_copy(qr_s[:, 1663:1663 + 511], psc[:, 1536:2047]))[-1],
            waits=(("s_pe", qr_mm),), inc="s_dve")
        pads = S.op("dve", lambda e: (
            e.tensor_scalar(qr_s[:, 0:127], identb[:, 0:127], 0.0,
                            stats[:, QK0:QK0 + 1], ALU.mult, ALU.add),
            e.tensor_scalar(qr_s[:, 2174:2301], identb[:, 0:127], 0.0,
                            stats[:, QK2:QK2 + 1], ALU.mult, ALU.add))[-1],
            waits=(("s_dve", qr_ev),), inc="s_dve")
        qkT_mm = S.op("pe", lambda e: e.matmul(
            psc[0:2, 2304:2432], stats[:, QK0:QK0 + 2], identf_sb[:, :],
            start=True, stop=True),
            waits=(("s_dve", qr_ev), ("s_dve", ev.get("qkT_ev", 0))), inc="s_pe")
        qkT_ev = S.op("dve", lambda e: e.tensor_copy(qkT_row[0:2, :],
                                                     psc[0:2, 2304:2432]),
                      waits=(("s_pe", qkT_mm), ("s_sy", ev.get("qrow", 0))),
                      inc="s_dve")
        ev["qkT_ev"] = qkT_ev
        qrow_w = sydma(lambda e, t=t: e.dma_start(
            out=Qa[64:66, t * 128:(t + 1) * 128], in_=qkT_row[0:2, :]),
            waits=(("s_dve", qkT_ev),))
        ev["qrow"] = qrow_w
        step_ev = gpdma(lambda e, t=t: e.dma_start(
            out=Ka[64:66, :], in_=ap_in["steps"][2 * t:2 * t + 2, :]),
            waits=(("s_pe", ev["qk_pe"]),))
        img_w = sydma(lambda e, slot=slot: e.dma_start(
            out=dram_flat(qrimg, slot * 128 * IMG_RS + IMG_FL, IMG_RS, 128, 2174),
            in_=qr_s[:, 0:2174]),
            waits=(("s_dve", qr_ev), ("s_dve", pads)))
        img_w = sydma(lambda e, slot=slot: e.dma_start(
            out=dram_flat(qrimg, slot * 128 * IMG_RS + IMG_FL + 2174, IMG_RS,
                          128, 2174),
            in_=qr_s[:, 127:2301]),
            waits=(("s_dve", qr_ev), ("s_dve", pads)))
        if t == 0:
            bz = S.op("dve", lambda e: e.memset(bu[:, :], 0.0),
                      waits=(("s_pe", ev["band_pe"]),), inc="s_dve")
        else:
            pw0_ = geo(t - 1)[0]
            if w0 > pw0_:
                bz = S.op("dve", lambda e, pw0_=pw0_, w0=w0: e.memset(
                    bu[:, pw0_:w0], 0.0),
                    waits=(("s_pe", ev["band_pe"]),), inc="s_dve")
            else:
                bz = ev["bu_ready"]
        band_r = sydma(lambda e, slot=slot, w0=w0, ce=ce, roff=roff: e.dma_start(
            out=bu[:, w0:ce],
            in_=dram_flat(qrimg, slot * 128 * IMG_RS + roff, IMG_RS - 1,
                          128, ce - w0)),
            waits=(("s_sy", img_w), ("s_pe", ev["band_pe"]), ("s_dve", bz)))
        ev["bu_ready"] = bz
        exp_ps = []
        for PS in (0, 1):
            lo_banks = [b for b in range(4)
                        if max(128 + 512 * (4 * PS + b), w0)
                        < min(128 + 512 * (4 * PS + b) + 512, ce)]
            hi_banks = [b for b in range(4)
                        if max(512 * (4 * PS + b), w0)
                        < min(512 * (4 * PS + b) + 512, ce)]
            def qk(e, lo_banks=lo_banks, hi_banks=hi_banks, t=t, PS=PS):
                last = None
                for b in range(4):
                    gb = 4 * PS + b
                    nadd = (b in lo_banks) + (b in hi_banks)
                    last = e.matmul(psc[:, b * 512:(b + 1) * 512],
                                    Qa[0:66, t * 128:(t + 1) * 128],
                                    Ka[0:66, gb * 512:(gb + 1) * 512],
                                    start=True, stop=(nadd == 0))
                    if b in lo_banks:
                        nadd -= 1
                        last = e.matmul(psc[:, b * 512:(b + 1) * 512],
                                        idlo_sb[:, :],
                                        bu[:, 128 + 512 * gb:640 + 512 * gb],
                                        start=False, stop=(nadd == 0))
                    if b in hi_banks:
                        last = e.matmul(psc[:, b * 512:(b + 1) * 512],
                                        idhi_sb[:, :],
                                        bu[:, 512 * gb:512 * gb + 512],
                                        start=False, stop=True)
                return last
            qk_mm = S.op("pe", qk, waits=(
                ("s_sy", band_r), ("s_sy", qrow_w), ("s_gp", step_ev),
                ("s_act", exp_ps[-1] if exp_ps else ev["exp"])), inc="s_pe")
            dscr_ev = None
            if DBG and hp == 0 and hi == 0:
                dcp = S.op("dve", lambda e: e.tensor_copy(
                    ar[:, PT_:PT_ + 4096].bitcast(F32)[:, 0:2048], psc[:, 0:2048]),
                    waits=(("s_pe", qk_mm),), inc="s_dve")
                dscr_ev = sydma(lambda e, t=t, PS=PS: e.dma_start(
                    out=dbg["dscr"][:, t * 4096 + 2048 * PS:
                                    t * 4096 + 2048 * PS + 2048],
                    in_=ar[:, PT_:PT_ + 4096].bitcast(F32)[:, 0:2048]),
                    waits=(("s_dve", dcp),))
            expp = S.op("act", lambda e, PS=PS: e.activation(
                p_b[:, PB_L + 2048 * PS:PB_L + 2048 * PS + 2048], psc[:, 0:2048],
                AF.Exp, bias=st(MNEG)),
                waits=(("s_pe", qk_mm), ("s_sy", ev["pbuf_sy"]),
                       ("s_dve", ev["pbuf_dve"]), ("s_pe", ev["pbuf_pe"]),
                       ("s_sy", dscr_ev)),
                inc="s_act")
            exp_ps.append(expp)
            ev["band_pe"] = qk_mm
            ev["qk_pe"] = qk_mm
        exp_a, expv = exp_ps
        ev["exp"] = expv
        if DBG and hp == 0 and hi == 0:
            ev["pbuf_sy"] = sydma(lambda e, t=t: e.dma_start(
                out=dbg["dpb"][:, t * 4096:(t + 1) * 4096],
                in_=p_b[:, PB_L:PB_L + 4096]),
                waits=(("s_act", expv),))
        pw0 = PBUF_ + PB_L + 256 * t - 3070
        pimg_w = sydma(lambda e, slot=slot, pw0=pw0: e.dma_start(
            out=dram_flat(pimg, slot * 128 * PW2, PW2, 128, PW2),
            in_=ar[:, pw0:pw0 + PW2]),
            waits=(("s_act", expv), ("s_sy", ev["aread_sy"][slot])))
        auf = sydma(lambda e, slot=slot: e.dma_start(
            out=au[:, :], in_=dram_flat(pimg, slot * 128 * PW2, PW2 + 1,
                                        128, AUF_W)),
            waits=(("s_sy", pimg_w),))
        ev["aread_sy"][slot] = auf
        ev["pbuf_sy"] = pimg_w
        def rs0fn(e, wlo2=wlo2, whi2_a=whi2_a, whi2_b=whi2_b):
            if wlo2 > 0:
                last = e.reduce_sum(st(RS0), p_b[:, PB_L:PB_L + wlo2], AX.X)
            else:
                last = e.memset(st(RS0), 0.0)
            if whi2_b > whi2_a:
                last = e.reduce_sum(st(T3), p_b[:, PB_L + whi2_a:PB_L + whi2_b],
                                    AX.X)
            return last
        rs0e = S.op("dve", rs0fn, waits=(("s_act", exp_a),), inc="s_dve")
        if whi2_b > whi2_a:
            rs0e = S.op("dve", lambda e: e.scalar_tensor_tensor(
                st(RS0), st(T3), m_hi, st(RS0), ALU.mult, ALU.add),
                waits=(("s_dve", rs0e),), inc="s_dve")
        axl0 = PBUF_ + PB_L + 256 * t - 3070
        axr0 = PBUF_ + PB_L + 256 * t + 1023
        cs1 = S.op("dve", lambda e, axl0=axl0, axr0=axr0: (
            e.tensor_tensor(apt_s[:, 0:128], ar[:, axl0:axl0 + 128],
                            masklo_sb[:, :], ALU.mult),
            e.tensor_tensor(apt_s[:, 128:256], ar[:, axl0 + 128:axl0 + 256],
                            masklo_sb[:, :], ALU.mult),
            e.tensor_tensor(apt_s[:, 256:384], ar[:, axr0:axr0 + 128],
                            maskup_sb[:, :], ALU.mult),
            e.tensor_tensor(apt_s[:, 384:512], ar[:, axr0 + 128:axr0 + 256],
                            maskup_sb[:, :], ALU.mult))[-1],
            waits=(("s_sy", auf), ("s_act", expv)), inc="s_dve")
        cs2 = S.op("dve", lambda e: (
            e.reduce_sum(st(CSLL), apt_s[:, 0:128], AX.X),
            e.reduce_sum(st(CSLH), apt_s[:, 128:256], AX.X),
            e.reduce_sum(st(CSRL), apt_s[:, 256:384], AX.X),
            e.reduce_sum(st(CSRH), apt_s[:, 384:512], AX.X))[-1],
            waits=(("s_dve", cs1),), inc="s_dve")
        cs3 = S.op("dve", lambda e: (
            e.tensor_tensor(st(T1), st(CSLL), m_lo, ALU.mult),
            e.tensor_tensor(st(T2), st(CSLH), m_hi, ALU.mult),
            e.tensor_tensor(st(T3), st(CSRL), m_lo, ALU.mult),
            e.tensor_tensor(st(T4), st(CSRH), m_hi, ALU.mult))[-1],
            waits=(("s_dve", cs2),), inc="s_dve")
        csum = S.op("dve", lambda e: (
            e.tensor_add(au[:, 0:1], au[:, 0:1], st(T1)),
            e.tensor_add(au[:, 128:129], au[:, 128:129], st(T2)),
            e.tensor_add(au[:, 4093:4094], au[:, 4093:4094], st(T3)),
            e.tensor_add(au[:, 4221:4222], au[:, 4221:4222], st(T4)))[-1],
            waits=(("s_dve", cs3),), inc="s_dve")
        ev["pbuf_dve"] = csum
        if DBG and hp == 0 and hi == 0:
            ev["aread_sy"][slot] = sydma(lambda e, t=t: e.dma_start(
                out=dbg["dauf"][:, t * AUF_W:(t + 1) * AUF_W], in_=au[:, :]),
                waits=(("s_dve", csum),))
        pt_ev = []
        av_mm = None
        for gq in range(8):
            bank_off = 2560 if gq % 2 == 0 else 3072
            ptm = S.op("pe", lambda e, gq=gq, bank_off=bank_off: [
                e.matmul(psc[:, bank_off + j * 128:bank_off + (j + 1) * 128],
                         p_b[:, PB_L + (gq * 4 + j) * 128:PB_L + (gq * 4 + j + 1) * 128],
                         identb[:, :], start=True, stop=True) for j in range(4)][-1],
                waits=(("s_act", exp_a if gq < 4 else expv),
                       ("s_dve", pt_ev[gq - 2] if gq >= 2 else ev.get("pte_last", 0))),
                inc="s_pe")
            pte = S.op("dve", lambda e, gq=gq, bank_off=bank_off: e.tensor_copy(
                pt_s[:, (gq % 4) * 512:(gq % 4 + 1) * 512],
                psc[:, bank_off:bank_off + 512]),
                waits=(("s_pe", ptm),), inc="s_dve")
            pt_ev.append(pte)
            if gq in (3, 7):
                k0 = 0 if gq == 3 else 16
                av_mm = S.op("pe", lambda e, hi=hi, k0=k0: [
                    e.matmul(psc[:, 2048:2113],
                             pt_s[:, (k - k0) * 128:(k - k0 + 1) * 128],
                             v_pair[:, k * 129 + hi * 64:k * 129 + hi * 64 + 65],
                             start=(k == 0), stop=(k == 31))
                    for k in range(k0, k0 + 16)][-1],
                    waits=(("s_dve", pte), ("s_dve", ev["ev_v"])), inc="s_pe")
        av0 = 2048 + (1 if hi == 1 else 0)
        rs_col = 2112 if hi == 0 else 2048
        ev["pbuf_pe"] = av_mm
        apt_ev = []
        for gq in range(4):
            def atf(e, gq=gq, t=t):
                last = None
                for j in range(4):
                    k = gq * 4 + j
                    w = 128 if k < 15 else 127
                    keep_wrap = (k + 2) * 128 > 3070 - 256 * t
                    reg = psc[0:w, 3584 + j * 128:3584 + (j + 1) * 128]
                    last = e.matmul(reg, au[:, 2047 + k * 128:2047 + k * 128 + w],
                                    idlo_sb[:, :], start=True, stop=False)
                    last = e.matmul(reg, au[:, 2175 + k * 128:2175 + k * 128 + w],
                                    idhi_sb[:, :], start=False,
                                    stop=not keep_wrap)
                    if keep_wrap:
                        last = e.matmul(reg, au[:, k * 128:k * 128 + w],
                                        idlo_sb[:, :], start=False, stop=False)
                        last = e.matmul(reg, au[:, 128 + k * 128:128 + k * 128 + w],
                                        idhi_sb[:, :], start=False, stop=True)
                return last
            atm = S.op("pe", atf,
                       waits=(("s_dve", csum),
                              ("s_dve", apt_ev[gq - 1] if gq >= 1
                               else ev.get("ate_last", 0))),
                       inc="s_pe")
            ate = S.op("dve", lambda e, gq=gq: e.tensor_copy(
                apt_s[:, gq * 512:(gq + 1) * 512], psc[:, 3584:3584 + 512]),
                waits=(("s_pe", atm),), inc="s_dve")
            apt_ev.append(ate)
        rv_mm = S.op("pe", lambda e: [
            e.matmul(psc[:, 2113:2178],
                     apt_s[0:(128 if k < 15 else 127), k * 128:(k + 1) * 128],
                     rv_sb[0:(128 if k < 15 else 127), k * 65:(k + 1) * 65],
                     start=(k == 0), stop=(k == 15)) for k in range(16)][-1],
            waits=(("s_dve", apt_ev[-1]),), inc="s_pe")
        epiA = S.op("dve", lambda e, av0=av0, rs_col=rs_col: (
            e.tensor_copy(st(RSUM), psc[:, rs_col:rs_col + 1]),
            e.tensor_copy(relv_f[:, :], psc[:, 2113:2178]),
            e.tensor_copy(add1_f[:, :], psc[:, av0:av0 + 64]))[-1],
            waits=(("s_pe", rv_mm), ("s_pe", av_mm), ("s_dve", rs0e),
                   ("s_dve", ev_ebb)), inc="s_dve")
        epiB = S.op("dve", lambda e: (
            e.reciprocal(st(RINV), st(RSUM)),
            e.tensor_add(st(T3), relv_f[:, 64:65], st(RS0)))[-1],
            waits=(("s_dve", epiA),), inc="s_dve")
        epiC = S.op("dve", lambda e: (
            e.tensor_sub(st(RS2), st(RSUM), st(T3)),
            e.tensor_add(add1_f[:, :], add1_f[:, :], relv_f[:, 0:64]))[-1],
            waits=(("s_dve", epiB),), inc="s_dve")
        epiD = S.op("dve", lambda e: e.scalar_tensor_tensor(
            add1_f[:, :], rv0b[:, :], st(RS0), add1_f[:, :], ALU.mult, ALU.add),
            waits=(("s_dve", epiC),), inc="s_dve")
        epiE = S.op("dve", lambda e: e.scalar_tensor_tensor(
            add1_f[:, :], rv2b[:, :], st(RS2), add1_f[:, :], ALU.mult, ALU.add),
            waits=(("s_dve", epiD),), inc="s_dve")
        epi1 = S.op("dve", lambda e: e.tensor_scalar_mul(atr_s[:, :], add1_f[:, :],
                                                         st(RINV)),
                    waits=(("s_dve", epiE),), inc="s_dve")
        atr_mm = S.op("pe", lambda e: e.matmul(
            psc[0:64, 2432:2560], atr_s[:, :], identb[:, :], start=True, stop=True),
            waits=(("s_dve", epi1),), inc="s_pe")
        dep_ev = None
        if DBG and hp == 0 and hi == 0:
            sydma(lambda e, t=t: e.dma_start(
                out=dbg["dstats"][:, t * 32:t * 32 + 32], in_=stats[:, 0:32]),
                waits=(("s_dve", epi1),))
            sydma(lambda e, t=t: e.dma_start(
                out=dbg["drelv"][:, t * 129:t * 129 + 65], in_=relv_f[:, :]),
                waits=(("s_dve", epi1),))
            sydma(lambda e, t=t: e.dma_start(
                out=dbg["drelv"][:, t * 129 + 65:t * 129 + 129], in_=add1_f[:, :]),
                waits=(("s_dve", epi1),))
            dep_ev = sydma(lambda e, t=t: e.dma_start(
                out=dbg["datr"][:, t * 64:(t + 1) * 64], in_=atr_s[:, :]),
                waits=(("s_dve", epi1),))
        epi2 = S.op("dve", lambda e, t=t, hp=hp, hi=hi: e.tensor_copy(
            attn_fm[64 * hi:64 * hi + 64,
                    hp * 2048 + t * 128:hp * 2048 + (t + 1) * 128],
            psc[0:64, 2432:2560]),
            waits=(("s_pe", atr_mm), ("s_sy", dep_ev)), inc="s_dve")
        ev["epi_dve"] = epi2
        ev["pte_last"] = pt_ev[-1]
        ev["ate_last"] = apt_ev[-1]

    def emit_pair(hp):
        c0 = hp * 128
        ww3 = ww_raw.rearrange("p (c n) -> p c n", c=8)
        wq_in = ap_in["wqT"][:, c0:c0 + 128].rearrange("(c p) n -> p c n", p=128)
        wk_in = ap_in["wkT"][:, c0:c0 + 128].rearrange("(c p) n -> p c n", p=128)
        wv_in = ap_in["wvT"][:, c0:c0 + 128].rearrange("(c p) n -> p c n", p=128)

        wq_ev = gpdma(lambda e: e.dma_start(out=ww3, in_=wq_in),
                      waits=(("s_pe", ev["qk_pe"]),))
        wqm = S.op("dve", lambda e: (
            e.tensor_scalar_mul(ww_lo[:, :], ww_raw[:, :], m_lo),
            e.tensor_scalar_mul(ww_hi[:, :], ww_raw[:, :], m_hi))[-1],
            waits=(("s_gp", wq_ev), ("s_pe", ev["qk_pe"]), ("s_dve", ev_ebb)),
            inc="s_dve")

        def qproj(e):
            last = None
            for T in range(NT):
                reg = psc[:, T * 128:(T + 1) * 128]
                for c in range(8):
                    last = e.matmul(reg, ww_lo[:, c * 128:(c + 1) * 128],
                                    xn_fm[:, c * 4096 + 256 * T:c * 4096 + 256 * T + 128],
                                    start=(c == 0), stop=False)
                    last = e.matmul(reg, ww_hi[:, c * 128:(c + 1) * 128],
                                    xn_fm[:, c * 4096 + 256 * T + 128:c * 4096 + 256 * T + 256],
                                    start=False, stop=False)
                last = e.matmul(reg, bias_sb[0:1, c0:c0 + 128],
                                ones_row[0:1, 0:128], start=False, stop=True)
            return last
        qp = S.op("pe", qproj, waits=(("s_dve", wqm), ("s_dve", ev_xn_done),
                                      ("s_act", ev["exp"]), ("s_dve", ev["epi_dve"])),
                  inc="s_pe")
        qe = S.op("dve", lambda e: (
            e.tensor_copy(Q_aug[0][0:64, :], psc[0:64, 0:2048]),
            e.tensor_copy(Q_aug[1][0:64, :], psc[64:128, 0:2048]))[-1],
            waits=(("s_pe", qp),), inc="s_dve")

        wk_ev = gpdma(lambda e: e.dma_start(out=ww3, in_=wk_in),
                      waits=(("s_pe", qp),))
        def kproj(e):
            last = None
            for b in range(8):
                for c in range(8):
                    last = e.matmul(psc[:, b * 512:(b + 1) * 512],
                                    ww_raw[:, c * 128:(c + 1) * 128],
                                    xn_fm[:, c * 4096 + b * 512:c * 4096 + (b + 1) * 512],
                                    start=(c == 0), stop=False)
                last = e.matmul(psc[:, b * 512:(b + 1) * 512],
                                bias_sb[0:1, D + c0:D + c0 + 128],
                                ones_row[0:1, 0:512],
                                start=False, stop=True)
            return last
        kp = S.op("pe", kproj, waits=(("s_gp", wk_ev), ("s_dve", qe)), inc="s_pe")
        ke = S.op("dve", lambda e: (
            e.tensor_scalar_mul(K_aug[0][0:64, :], psc[0:64, :], SCALE),
            e.tensor_scalar_mul(K_aug[1][0:64, :], psc[64:128, :], SCALE))[-1],
            waits=(("s_pe", kp),), inc="s_dve")

        wv_ev = gpdma(lambda e: e.dma_start(out=ww3, in_=wv_in),
                      waits=(("s_pe", kp),))
        last_ve = ke
        for rc4 in range(8):
            def vgrp(e, rc4=rc4):
                last = None
                for j in range(4):
                    rcx = rc4 * 4 + j
                    for c in range(8):
                        last = e.matmul(psc[:, j * 512:j * 512 + 128],
                                        xn_fm[:, c * 4096 + rcx * 128:c * 4096 + (rcx + 1) * 128],
                                        ww_raw[:, c * 128:(c + 1) * 128],
                                        start=(c == 0), stop=False)
                    last = e.matmul(psc[:, j * 512:j * 512 + 128],
                                    ones_row[0:1, 0:128],
                                    bias_sb[0:1, 2 * D + c0:2 * D + c0 + 128],
                                    start=False, stop=True)
                return last
            vp = S.op("pe", vgrp, waits=(("s_gp", wv_ev), ("s_dve", last_ve)),
                      inc="s_pe")
            last_ve = S.op("dve", lambda e, rc4=rc4: (
                e.tensor_copy(
                    v_pair[:, rc4 * 516:rc4 * 516 + 516].rearrange(
                        "p (j a) -> p j a", j=4)[:, :, 0:64],
                    psc[:, 0:2048].rearrange("p (j a) -> p j a", j=4)[:, :, 0:64]),
                e.tensor_copy(
                    v_pair[:, rc4 * 516:rc4 * 516 + 516].rearrange(
                        "p (j a) -> p j a", j=4)[:, :, 65:129],
                    psc[:, 0:2048].rearrange("p (j a) -> p j a", j=4)[:, :, 64:128]))[-1],
                waits=(("s_pe", vp),), inc="s_dve")
        ev["ev_v"] = last_ve
        ev["qk_pe"] = S.cnt["s_pe"]

        for hi in (0, 1):
            for t in range(NT):
                emit_tile(hp, hi, t, K_aug[hi], Q_aug[hi])

    for hp in range(8):
        emit_pair(hp)

    ev_attn_pe = S.cnt["s_pe"]
    ev_attn_dve = S.cnt["s_dve"]
    if DBG:
        dbg_ev["datn"] = sydma(lambda e: e.dma_start(out=dbg["datn"],
                                                     in_=attn_fm[:, :]),
                               waits=(("s_dve", ev_attn_dve),))

    oacc = arf(OACC_, 16384)
    wo_s = ar[:, WO_:WO_ + 8192]
    x_te = arf(XTE_, 1024)
    wo_in = ap_in["woT"].rearrange("(c p) n -> p c n", p=128)
    wo_ev = gpdma(lambda e: e.dma_start(
        out=wo_s.rearrange("p (c n) -> p c n", c=8), in_=wo_in),
        waits=(("s_pe", ev_attn_pe), ("s_dve", ev_attn_dve)))
    prev_ev2 = ev_attn_dve
    prev_pe2 = ev_attn_pe
    for t in range(NT):
        lx = sydma(lambda e, t=t: e.dma_start(out=x_te[:, :],
                                              in_=xq[t * 128:(t + 1) * 128, :]),
                   waits=(("s_dve", prev_ev2),))
        def oproj(e, t=t):
            last = None
            for b in range(2):
                for c in range(8):
                    last = e.matmul(psc[:, b * 512:(b + 1) * 512],
                                    attn_fm[:, c * 2048 + t * 128:c * 2048 + (t + 1) * 128],
                                    wo_s[:, c * 1024 + b * 512:c * 1024 + (b + 1) * 512],
                                    start=(c == 0), stop=False)
                last = e.matmul(psc[:, b * 512:(b + 1) * 512],
                                ones_row[0:1, 0:128],
                                bias_sb[0:1, 3 * D + b * 512:3 * D + (b + 1) * 512],
                                start=False, stop=True)
            return last
        om = S.op("pe", oproj, waits=(("s_gp", wo_ev), ("s_dve", prev_ev2)),
                  inc="s_pe")
        prev_ev2 = S.op("dve", lambda e, t=t: e.tensor_add(
            oacc[:, t * 1024:(t + 1) * 1024], psc[:, 0:1024], x_te[:, :]),
            waits=(("s_pe", om), ("s_sy", lx)), inc="s_dve")
        prev_pe2 = om

    fx_t = arf(FXT_, 1024)
    fxn_tm = ar[:, FXN_:FXN_ + 1024]
    xnq8 = xn_fm[:, 0:16384].rearrange("p (c n) -> p c n", c=8)
    prev_f_pe, prev_f_ev = prev_pe2, prev_ev2
    for t in range(NT):
        lx = sydma(lambda e, t=t: e.dma_start(out=fx_t[:, :],
                                              in_=xq[t * 128:(t + 1) * 128, :]),
                   waits=(("s_pe", prev_f_pe), ("s_dve", prev_f_ev)))
        fsel = S.op("dve", lambda e, t=t: (
            e.tensor_scalar_mul(st(T1), st(SMEAN0 + 2 * t), m_lo),
            e.tensor_scalar_mul(st(T2), st(SRSTD0 + 2 * t), m_lo))[-1],
            waits=(("s_sy", lx), ("s_pe", prev_f_pe)), inc="s_dve")
        fsel = S.op("dve", lambda e, t=t: (
            e.scalar_tensor_tensor(st(T1), st(SMEAN0 + 2 * t + 1), m_hi, st(T1),
                                   ALU.mult, ALU.add),
            e.scalar_tensor_tensor(st(T2), st(SRSTD0 + 2 * t + 1), m_hi, st(T2),
                                   ALU.mult, ALU.add))[-1],
            waits=(("s_dve", fsel),), inc="s_dve")
        rcp = S.op("dve", lambda e: e.tensor_scalar(
            fxn_tm[:, :], fx_t[:, :], st(T1), st(T2), ALU.subtract, ALU.mult),
            waits=(("s_dve", fsel),), inc="s_dve")
        trp = S.op("pe", lambda e: [
            e.matmul(psc[:, c * 128:(c + 1) * 128], fxn_tm[:, c * 128:(c + 1) * 128],
                     identb[:, :], start=True, stop=True) for c in range(8)][-1],
            waits=(("s_dve", rcp), ("s_dve", prev_f_ev),
                   ("s_sy", dbg_ev.get("dxn"))), inc="s_pe")
        prev_f_ev = S.op("dve", lambda e, t=t: e.tensor_copy(
            xnq8[:, :, t * 128:(t + 1) * 128],
            psc[:, 0:1024].rearrange("p (c n) -> p c n", c=8)),
            waits=(("s_pe", trp),), inc="s_dve")
        prev_f_pe = trp

    gpw = ar[:, GP_:GP_ + 4096]
    upw = ar[:, UP_:UP_ + 4096]
    dpw = attn_fm[:, 12288:16384]
    h_fm = attn_fm[:, 0:8192]
    silu_f = attn_fm[:, 8192:12288].bitcast(F32)
    last_comb = prev_f_ev
    last_pe = prev_f_pe
    for ex in range(8):
        gp_in = ap_in["gpT"][ex].rearrange("(c p) n -> p c n", p=128)
        up_in = ap_in["upT"][ex].rearrange("(c p) n -> p c n", p=128)
        dp_in = ap_in["dpT"][ex].rearrange("(c p) n -> p c n", p=128)
        wg = gpdma(lambda e, gp_in=gp_in: e.dma_start(
            out=gpw.rearrange("p (c n) -> p c n", c=8), in_=gp_in),
            waits=(("s_pe", last_pe),))
        wu = gpdma(lambda e, up_in=up_in: e.dma_start(
            out=upw.rearrange("p (c n) -> p c n", c=8), in_=up_in))
        wd = gpdma(lambda e, dp_in=dp_in: e.dma_start(
            out=dpw.rearrange("p (c n) -> p c n", c=4), in_=dp_in),
            waits=(("s_pe", last_pe), ("s_dve", last_comb),
                   ("s_sy", dbg_ev.get("datn"))))
        hev = last_comb
        for ib in range(4):
            def gu(e, ib=ib):
                last = None
                for half, wten in ((0, gpw), (1, upw)):
                    for b in range(4):
                        for c in range(8):
                            last = e.matmul(
                                psc[:, half * 2048 + b * 512:half * 2048 + (b + 1) * 512],
                                wten[:, c * 512 + ib * 128:c * 512 + ib * 128 + 128],
                                xn_fm[:, c * 2048 + b * 512:c * 2048 + (b + 1) * 512],
                                start=(c == 0), stop=(c == 7))
                return last
            gum = S.op("pe", gu, waits=(("s_gp", wu), ("s_dve", hev),
                                        ("s_sy", dbg_ev.get("datn"))), inc="s_pe")
            sil = S.op("act", lambda e: (
                [e.activation(silu_f[:, j * 512:(j + 1) * 512],
                              psc[:, j * 512:(j + 1) * 512], AF.Silu)
                 for j in range(4)],
                e.activation(psc[:, 0:256], silu_f[:, 0:256], AF.Square))[-1],
                waits=(("s_pe", gum),), inc="s_act")
            hev = S.op("dve", lambda e, ib=ib: e.tensor_mul(
                h_fm[:, ib * 2048:(ib + 1) * 2048], silu_f[:, 0:2048],
                psc[:, 2048:4096]),
                waits=(("s_act", sil),), inc="s_dve")
        for t in range(NT):
            def down(e, t=t):
                last = None
                for b in range(2):
                    for ic in range(4):
                        last = e.matmul(psc[:, b * 512:(b + 1) * 512],
                                        h_fm[:, ic * 2048 + t * 128:ic * 2048 + (t + 1) * 128],
                                        dpw[:, ic * 1024 + b * 512:ic * 1024 + (b + 1) * 512],
                                        start=(ic == 0), stop=(ic == 3))
                return last
            dm = S.op("pe", down, waits=(("s_gp", wd), ("s_dve", hev),
                                         ("s_dve", last_comb)), inc="s_pe")
            last_comb = S.op("dve", lambda e, t=t, ex=ex: e.scalar_tensor_tensor(
                oacc[:, t * 1024:(t + 1) * 1024], psc[:, 0:1024],
                cw_sb[:, t * 8 + ex:t * 8 + ex + 1],
                oacc[:, t * 1024:(t + 1) * 1024],
                ALU.mult, ALU.add),
                waits=(("s_pe", dm), ("s_dve", ev_c_dve)), inc="s_dve")
            last_pe = dm

    fin = sydma(lambda e: e.dma_start(
        out=out_d.rearrange("(t p) d -> p t d", p=128),
        in_=oacc.rearrange("p (t d) -> p t d", t=16)),
        waits=(("s_dve", last_comb),))
    S.plan["sy"].append(lambda e, fin=fin: e.wait_ge(S.sem["s_sy"], fin))


_NC_CACHE = {}


def _steps(par):
    s = np.zeros((2 * NT, L), dtype=np.float32)
    q = np.arange(L)
    for t in range(NT):
        l0 = 256 * t + 128 * par
        c_lo2 = max(0, 256 * t - 2942) - 128 * (1 - par)
        c_hi = l0 + 1151
        s[2 * t] = (q < c_lo2).astype(np.float32)
        s[2 * t + 1] = (q >= c_hi).astype(np.float32)
    return s


def kernel(L0_attn_x, L0_attn_Wq, L0_attn_bq, L0_attn_Wk, L0_attn_bk, L0_attn_Wv,
           L0_attn_bv, L0_attn_Wo, L0_attn_bo, L0_attn_relative_k, L0_attn_relative_v,
           L0_ffn_gate_weight, L0_ffn_e_score_correction_bias, L0_ffn_gate_proj,
           L0_ffn_up_proj, L0_ffn_down_proj):
    x = np.ascontiguousarray(np.asarray(L0_attn_x, dtype=np.float32))
    C = np.ascontiguousarray

    fm = np.zeros((8, 8), np.float32)
    for e_ in range(8):
        fm[e_, :e_] = 1.0
    hm = np.zeros((4, 4), np.float32)
    for g_ in range(4):
        hm[g_, :g_] = 1.0
    eye = np.eye(128, dtype=np.float32)

    shared = {
        "wqT": C(np.asarray(L0_attn_Wq, np.float32).T),
        "wkT": C(np.asarray(L0_attn_Wk, np.float32).T),
        "wvT": C(np.asarray(L0_attn_Wv, np.float32).T),
        "woT": C(np.asarray(L0_attn_Wo, np.float32).T),
        "bq": C(np.asarray(L0_attn_bq, np.float32).reshape(1, D)),
        "bk": C(np.asarray(L0_attn_bk, np.float32).reshape(1, D)),
        "bv": C(np.asarray(L0_attn_bv, np.float32).reshape(1, D)),
        "bo": C(np.asarray(L0_attn_bo, np.float32).reshape(1, D)),
        "rkT": C(np.asarray(L0_attn_relative_k, np.float32).T),
        "rv": C(np.asarray(L0_attn_relative_v, np.float32)),
        "gwT": C(np.asarray(L0_ffn_gate_weight, np.float32).T),
        "ebias": C(np.asarray(L0_ffn_e_score_correction_bias,
                              np.float32).reshape(1, 8)),
        "gpT": C(np.asarray(L0_ffn_gate_proj, np.float32).transpose(0, 2, 1)),
        "upT": C(np.asarray(L0_ffn_up_proj, np.float32).transpose(0, 2, 1)),
        "dpT": C(np.asarray(L0_ffn_down_proj, np.float32).transpose(0, 2, 1)),
        "identf": eye,
        "masklo": np.tril(np.ones((128, 128), np.float32), -1),
        "maskup": np.triu(np.ones((128, 128), np.float32), 1),
        "fmask64": np.tile(fm.reshape(1, 64), (128, 1)),
        "hmask16": np.tile(hm.reshape(1, 16), (128, 1)),
    }

    if "nc" not in _NC_CACHE:
        _NC_CACHE["nc"] = build_program()
    nc = _NC_CACHE["nc"]

    in_maps = []
    for c in range(8):
        b = c // 2
        par = c % 2
        m = dict(shared)
        m["xb"] = C(x[b])
        m["xq"] = C(x[b].reshape(32, 128, D)[par::2].reshape(NQ, D))
        m["steps"] = _steps(par)
        m["mcol"] = np.tile(np.array([[1.0 - par, float(par)]], np.float32),
                            (128, 1))
        m["idlo"] = C(eye * (1.0 - par))
        m["idhi"] = C(eye * par)
        in_maps.append(m)

    res = run_bass_kernel_spmd(nc, in_maps, core_ids=list(range(8)))
    out = np.empty((B, L, D), dtype=np.float32)
    for c in range(8):
        b = c // 2
        par = c % 2
        out[b].reshape(32, 128, D)[par::2] = res.results[c]["out"].reshape(NT, 128, D)
    return out


if __name__ == "__main__":
    build_program()
    print("program built ok")


# version 10
# speedup vs baseline: 1.0173x; 1.0173x over previous
import numpy as np
from contextlib import ExitStack

import concourse.bass as bass
import concourse.mybir as mybir
from concourse.bass_utils import run_bass_kernel_spmd

F32 = mybir.dt.float32
BF16 = mybir.dt.bfloat16
AF = mybir.ActivationFunctionType
ALU = mybir.AluOpType
AX = mybir.AxisListType

B, L, D = 4, 4096, 1024
NH, DK = 16, 64
NQ = 2048
NT = 16
NJ = 2047
SCALE = 0.125

IMG_W = 4348
IMG_FL = 128
IMG_RS = IMG_W + 2 * IMG_FL
PW2 = 4349
BU_W = 4224
AUF_W = 4222
PB_L, PB_R = 3072, 1024
PB_W = PB_L + L + PB_R

AR_W = 43008
KA_, KB_ = 0, 4096
QA_, QB_ = 8192, 10240
VV_ = 12288
WW_ = 16416
QR_ = 19488
BU_ = 21792
PBUF_ = 26016
AUF_ = 34208
PT_ = 38430
APT_ = 40478
ATR_ = 42526
XT_, SQ_, XN32_, XNTM_, XF_ = 0, 2048, 4096, 6144, 8192
OACC_, WO_, XTE_ = 0, 32768, 40960
FXT_, FSQ_, FXN_ = 32768, 34816, 36864
GP_, UP_ = 32768, 36864

ENGS = ("pe", "dve", "act", "sy", "gp")

DBG = False

(MNEG, RSA, RSB, RSUM, RINV, CSLL, CSLH, CSRL, CSRH, RS0, RS2, WSUM,
 MEAN, VAR, RSTD, SSQ, T1, T2, T3, T4, QK0, QK2, EPS) = range(23)
SMEAN0 = 23
SRSTD0 = 55


class Sched:
    def __init__(self):
        self.plan = {k: [] for k in ENGS}
        self.cnt = {}
        self.sem = {}

    def add_sem(self, name, handle):
        self.sem[name] = handle
        self.cnt[name] = 0

    def op(self, eng, fn, waits=(), inc=None, n=1):
        val = None
        if inc is not None:
            self.cnt[inc] += n
            val = self.cnt[inc]
        wl = tuple((s, v) for (s, v) in waits if v is not None and v > 0)
        sems = self.sem

        def run(e, fn=fn, wl=wl, inc=inc, n=n):
            seen = {}
            for s, v in wl:
                if seen.get(s, -1) < v:
                    seen[s] = v
            for s, v in seen.items():
                e.wait_ge(sems[s], v)
            ins = fn(e)
            if inc is not None:
                ins.then_inc(sems[inc], n)

        self.plan[eng].append(run)
        return val


def build_program():
    nc = bass.Bass("TRN2", target_bir_lowering=False)

    ap_in = {}
    def din(name, shape, dt=F32):
        ap_in[name] = nc.dram_tensor(name, shape, dt, kind="ExternalInput").ap()
        return ap_in[name]

    din("xb", [L, D]); din("xq", [NQ, D])
    din("wqT", [D, D]); din("wkT", [D, D]); din("wvT", [D, D]); din("woT", [D, D])
    din("bq", [1, D]); din("bk", [1, D]); din("bv", [1, D]); din("bo", [1, D])
    din("rkT", [DK, NJ]); din("rv", [NJ, DK])
    din("gwT", [D, 8]); din("ebias", [1, 8])
    din("gpT", [8, D, 512]); din("upT", [8, D, 512]); din("dpT", [8, 512, D])
    din("identf", [128, 128])
    din("idlo", [128, 128]); din("idhi", [128, 128])
    din("masklo", [128, 128]); din("maskup", [128, 128])
    din("fmask64", [128, 64]); din("hmask16", [128, 16])
    din("steps", [2 * NT, L])
    din("mcol", [128, 2])

    out_d = nc.dram_tensor("out", [NQ, D], F32, kind="ExternalOutput").ap()
    qrimg = nc.dram_tensor("qrimg", [2 * 128 * IMG_RS], BF16, kind="Internal")
    pimg = nc.dram_tensor("pimg", [2 * 128 * PW2], BF16, kind="Internal")
    dbg = {}
    if DBG:
        dbg["dxn"] = nc.dram_tensor("dxn", [128, 8 * L], BF16,
                                    kind="ExternalOutput").ap()
        dbg["dcw"] = nc.dram_tensor("dcw", [128, 128], F32,
                                    kind="ExternalOutput").ap()
        dbg["datn"] = nc.dram_tensor("datn", [128, 8 * NQ], BF16,
                                     kind="ExternalOutput").ap()
        dbg["dscr"] = nc.dram_tensor("dscr", [128, NT * 4096], BF16,
                                     kind="ExternalOutput").ap()
        dbg["dpb"] = nc.dram_tensor("dpb", [128, NT * 4096], BF16,
                                    kind="ExternalOutput").ap()
        dbg["dstats"] = nc.dram_tensor("dstats", [128, NT * 32], F32,
                                       kind="ExternalOutput").ap()
        dbg["drelv"] = nc.dram_tensor("drelv", [128, NT * 129], F32,
                                      kind="ExternalOutput").ap()
        dbg["dauf"] = nc.dram_tensor("dauf", [128, NT * AUF_W], BF16,
                                     kind="ExternalOutput").ap()
        dbg["datr"] = nc.dram_tensor("datr", [128, NT * 64], BF16,
                                     kind="ExternalOutput").ap()
        dbg["dgt"] = nc.dram_tensor("dgt", [128, NT * 80], F32,
                                    kind="ExternalOutput").ap()

    es = ExitStack()
    with es:
        S = Sched()
        for s in ("s_pe", "s_dve", "s_act", "s_sy", "s_gp"):
            S.add_sem(s, es.enter_context(nc.semaphore(s)))

        t_ = {}
        def sb(name, shape, dt):
            t_[name] = es.enter_context(nc.sbuf_tensor(name, shape, dt))
            return t_[name]

        sb("arena", [128, AR_W], BF16)
        sb("xn_fm", [128, 8 * L], BF16)
        sb("attn_fm", [128, 8 * NQ], BF16)
        sb("stats", [128, 96], F32)
        sb("cw", [128, 128], F32)
        sb("identb", [128, 128], BF16)
        sb("identf_sb", [128, 128], F32)
        sb("idlo_sb", [128, 128], BF16)
        sb("idhi_sb", [128, 128], BF16)
        sb("masklo_sb", [128, 128], BF16)
        sb("maskup_sb", [128, 128], BF16)
        sb("fmask_sb", [128, 64], F32)
        sb("hmask_sb", [128, 16], F32)
        sb("mcol_sb", [128, 2], F32)
        sb("rkT_sb", [64, 2048], BF16)
        sb("rv_sb", [128, 16 * 65], BF16)
        sb("gw_sb", [128, 64], F32)
        sb("eb_sb", [1, 8], F32)
        sb("ebb_sb", [128, 8], F32)
        sb("rv0b", [128, 64], F32)
        sb("rv2b", [128, 64], F32)
        sb("bias_sb", [1, 4 * D], BF16)
        sb("ones_row", [1, 512], BF16)
        sb("onesf_row", [1, 128], F32)
        sb("qkT_row", [2, 128], BF16)
        sb("relv_f", [128, 65], F32)
        sb("add1_f", [128, 64], F32)
        sb("gt8", [128, 80], F32)
        t_["psc"] = es.enter_context(nc.psum_tensor("psc", [128, 4096], F32))

        with nc.Block() as block:
            build_plan(S, t_, ap_in, out_d, qrimg, pimg, dbg)
            for eng, attr in (("sy", "sync"), ("gp", "gpsimd"), ("pe", "tensor"),
                              ("dve", "vector"), ("act", "scalar")):
                lst = S.plan[eng]
                def runner(e, lst=lst, eng=eng):
                    for i, f in enumerate(lst):
                        try:
                            f(e)
                        except Exception:
                            print(f"FAILING OP: eng={eng} index={i}/{len(lst)}")
                            raise
                getattr(block, attr)(runner)
    return nc


def build_plan(S, t_, ap_in, out_d, qrimg, pimg, dbg):
    ar = t_["arena"]; xn_fm = t_["xn_fm"]; attn_fm = t_["attn_fm"]
    stats = t_["stats"]; cw_sb = t_["cw"]
    identb = t_["identb"]; identf_sb = t_["identf_sb"]
    idlo_sb = t_["idlo_sb"]; idhi_sb = t_["idhi_sb"]
    masklo_sb = t_["masklo_sb"]; maskup_sb = t_["maskup_sb"]
    fmask_sb = t_["fmask_sb"]; hmask_sb = t_["hmask_sb"]; mcol_sb = t_["mcol_sb"]
    rkT_sb = t_["rkT_sb"]; rv_sb = t_["rv_sb"]; gw_sb = t_["gw_sb"]
    eb_sb = t_["eb_sb"]; ebb_sb = t_["ebb_sb"]; rv0b = t_["rv0b"]; rv2b = t_["rv2b"]
    bias_sb = t_["bias_sb"]; ones_row = t_["ones_row"]; onesf_row = t_["onesf_row"]
    qkT_row = t_["qkT_row"]
    relv_f = t_["relv_f"]; add1_f = t_["add1_f"]
    gt8 = t_["gt8"]; psc = t_["psc"]

    def st(i):
        return stats[:, i:i + 1]

    def arf(c0, ncols):
        return ar[:, c0:c0 + 2 * ncols].bitcast(F32)

    def dram_flat(th, off, rowstride, p, width):
        return bass.AP(tensor=th, offset=off, ap=[[rowstride, p], [1, width]])

    m_lo = mcol_sb[:, 0:1]
    m_hi = mcol_sb[:, 1:2]

    def gpdma(fn, waits=()):
        return S.op("gp", fn, waits=waits, inc="s_gp", n=16)

    def sydma(fn, waits=()):
        return S.op("sy", fn, waits=waits, inc="s_sy", n=16)

    def sydma_split(fn2, waits=()):
        def g(e, fn2=fn2):
            fn2(e, 0).then_inc(S.sem["s_sy"], 16)
            return fn2(e, 64)
        S.cnt["s_sy"] += 16
        return S.op("sy", g, waits=waits, inc="s_sy", n=16)

    ld0 = gpdma(lambda e: e.dma_start(out=identb[:, :], in_=ap_in["identf"][:, :]))
    gpdma(lambda e: e.dma_start(out=identf_sb[:, :], in_=ap_in["identf"][:, :]))
    gpdma(lambda e: e.dma_start(out=idlo_sb[:, :], in_=ap_in["idlo"][:, :]))
    gpdma(lambda e: e.dma_start(out=idhi_sb[:, :], in_=ap_in["idhi"][:, :]))
    gpdma(lambda e: e.dma_start(out=masklo_sb[:, :], in_=ap_in["masklo"][:, :]))
    gpdma(lambda e: e.dma_start(out=maskup_sb[:, :], in_=ap_in["maskup"][:, :]))
    gpdma(lambda e: e.dma_start(out=fmask_sb[:, :], in_=ap_in["fmask64"][:, :]))
    gpdma(lambda e: e.dma_start(out=hmask_sb[:, :], in_=ap_in["hmask16"][:, :]))
    gpdma(lambda e: e.dma_start(out=mcol_sb[:, :], in_=ap_in["mcol"][:, :]))
    gpdma(lambda e: e.dma_start(out=rkT_sb[0:64, 0:NJ], in_=ap_in["rkT"][:, :]))
    gpdma(lambda e: e.dma_start(out=gw_sb[:, :].rearrange("p (c e) -> p c e", c=8),
                                in_=ap_in["gwT"].rearrange("(c p) e -> p c e", p=128)))
    gpdma(lambda e: e.dma_start(out=eb_sb[:, :], in_=ap_in["ebias"][:, :]))
    for i, nm in enumerate(("bq", "bk", "bv", "bo")):
        gpdma(lambda e, i=i, nm=nm: e.dma_start(
            out=bias_sb[0:1, i * D:(i + 1) * D], in_=ap_in[nm][:, :]))
    mrv = S.op("dve", lambda e: e.memset(rv_sb[:, :], 0.0), inc="s_dve")
    rv_last = None
    for k in range(16):
        r0, r1 = k * 128, min(NJ, (k + 1) * 128)
        rv_last = gpdma(lambda e, k=k, r0=r0, r1=r1: e.dma_start(
            out=rv_sb[0:r1 - r0, k * 65:k * 65 + 64], in_=ap_in["rv"][r0:r1, :]),
            waits=(("s_dve", mrv),))
    ld_consts = rv_last

    S.op("dve", lambda e: e.memset(ones_row[:, :], 1.0), inc="s_dve")
    S.op("dve", lambda e: e.memset(onesf_row[:, :], 1.0), inc="s_dve")
    ev_eps = S.op("dve", lambda e: (
        e.memset(st(EPS), 1e-5),
        e.memset(st(MNEG), 0.0))[-1], inc="s_dve")
    rv_ones = S.op("dve", lambda e: e.memset(
        rv_sb[:, :].rearrange("p (k c) -> p k c", c=65)[:, :, 64:65], 1.0),
        waits=(("s_gp", rv_last),), inc="s_dve")
    S.op("dve", lambda e: e.memset(
        ar[:, VV_:VV_ + 4128].rearrange("p (k c) -> p k c", c=129)[:, :, 64:65], 1.0),
        inc="s_dve")
    S.op("dve", lambda e: e.memset(ar[:, PBUF_:PBUF_ + PB_L], 0.0), inc="s_dve")
    S.op("dve", lambda e: e.memset(ar[:, PBUF_ + PB_L + L:PBUF_ + PB_W], 0.0),
         inc="s_dve")
    mzf = S.op("dve", lambda e: e.memset(ar[:, BU_:BU_ + 128], 0.0), inc="s_dve")
    fl_ev = None
    for slot in range(2):
        for off in (0, IMG_FL + IMG_W):
            fl_ev = sydma(lambda e, slot=slot, off=off: e.dma_start(
                out=dram_flat(qrimg, slot * 128 * IMG_RS + off, IMG_RS, 128, IMG_FL),
                in_=ar[:, BU_:BU_ + 128]),
                waits=(("s_dve", mzf),))

    pe_eb = S.op("pe", lambda e: e.matmul(psc[:, 0:8], onesf_row[0:1, :],
                                          eb_sb[0:1, :], start=True, stop=True),
                 waits=(("s_gp", ld_consts),), inc="s_pe")
    pe_rv0 = S.op("pe", lambda e: e.matmul(psc[:, 512:576], ones_row[0:1, 0:128],
                                           rv_sb[0:1, 0:64], start=True, stop=True),
                  waits=(("s_dve", rv_ones),), inc="s_pe")
    cp2046 = sydma(lambda e: e.dma_start(out=qkT_row[0:1, 0:64],
                                         in_=rv_sb[126:127, 15 * 65:15 * 65 + 64]),
                   waits=(("s_dve", rv_ones),))
    pe_rv2 = S.op("pe", lambda e: e.matmul(psc[:, 1024:1088], ones_row[0:1, 0:128],
                                           qkT_row[0:1, 0:64], start=True, stop=True),
                  waits=(("s_sy", cp2046),), inc="s_pe")
    ev_ebb = S.op("dve", lambda e: (
        e.tensor_copy(ebb_sb[:, :], psc[:, 0:8]),
        e.tensor_copy(rv0b[:, :], psc[:, 512:576]),
        e.tensor_copy(rv2b[:, :], psc[:, 1024:1088]))[-1],
        waits=(("s_pe", pe_rv2), ("s_pe", pe_rv0), ("s_pe", pe_eb)), inc="s_dve")

    x_t = arf(XT_, 1024)
    sq_t = arf(SQ_, 1024)
    xn_tm = ar[:, XNTM_:XNTM_ + 1024]
    xb = ap_in["xb"]; xq = ap_in["xq"]

    xv8 = xn_fm[:, :].rearrange("p (c n) -> p c n", c=8)
    prev_tr, prev_ev = 0, 0
    for t in range(32):
        lx = sydma(lambda e, t=t: e.dma_start(out=x_t[:, :],
                                              in_=xb[t * 128:(t + 1) * 128, :]),
                   waits=(("s_pe", prev_tr),))
        l1 = S.op("dve", lambda e: (
            e.reduce_sum(st(T1), x_t[:, :], AX.X),
            e.tensor_tensor(sq_t[:, :], x_t[:, :], x_t[:, :], ALU.mult))[-1],
            waits=(("s_sy", lx),), inc="s_dve")
        l2 = S.op("dve", lambda e, t=t: (
            e.reduce_sum(st(SSQ), sq_t[:, :], AX.X),
            e.tensor_scalar_mul(st(SMEAN0 + t), st(T1), 1.0 / 1024.0))[-1],
            waits=(("s_dve", l1),), inc="s_dve")
        l3 = S.op("dve", lambda e, t=t: (
            e.tensor_scalar_mul(st(VAR), st(SSQ), 1.0 / 1024.0),
            e.tensor_tensor(st(T2), st(SMEAN0 + t), st(SMEAN0 + t), ALU.mult))[-1],
            waits=(("s_dve", l2),), inc="s_dve")
        lnc = S.op("dve", lambda e: e.tensor_sub(st(VAR), st(VAR), st(T2)),
                   waits=(("s_dve", l3),), inc="s_dve")
        S.op("act", lambda e: e.activation(st(T1), st(VAR), AF.Sqrt, bias=st(EPS)),
             waits=(("s_dve", lnc),))
        rstd = S.op("act", lambda e: e.activation(sq_t[:, 0:512], x_t[:, 0:512],
                                                  AF.Square), inc="s_act")
        rc1 = S.op("dve", lambda e, t=t: e.reciprocal(st(SRSTD0 + t), st(T1)),
                   waits=(("s_act", rstd), ("s_pe", prev_tr)), inc="s_dve")
        rcp = S.op("dve", lambda e, t=t: e.tensor_scalar(
            xn_tm[:, :], x_t[:, :], st(SMEAN0 + t), st(SRSTD0 + t),
            ALU.subtract, ALU.mult),
            waits=(("s_dve", rc1),), inc="s_dve")
        trp = S.op("pe", lambda e: [
            e.matmul(psc[:, c * 128:(c + 1) * 128], xn_tm[:, c * 128:(c + 1) * 128],
                     identb[:, :], start=True, stop=True) for c in range(8)][-1],
            waits=(("s_dve", rcp), ("s_gp", ld0), ("s_dve", ev_ebb),
                   ("s_dve", prev_ev)), inc="s_pe")
        prev_ev = S.op("dve", lambda e, t=t: e.tensor_copy(
            xv8[:, :, t * 128:(t + 1) * 128],
            psc[:, 0:1024].rearrange("p (c n) -> p c n", c=8)),
            waits=(("s_pe", trp),), inc="s_dve")
        prev_tr = trp
    ev_xn_done = prev_ev
    ev_b_pe = prev_tr
    dbg_ev = {}
    if DBG:
        dbg_ev["dxn"] = sydma(lambda e: e.dma_start(out=dbg["dxn"], in_=xn_fm[:, :]),
                              waits=(("s_dve", ev_xn_done),))

    xf32 = arf(XF_, 1024)
    xn32 = arf(XN32_, 1024)
    sC = gt8[:, 0:8]; cC = gt8[:, 8:16]; G4 = gt8[:, 16:20]
    cnt4 = gt8[:, 20:24]; gm4 = gt8[:, 24:28]
    A16 = gt8[:, 32:48]; C16 = gt8[:, 48:64]
    sm8 = gt8[:, 64:72]; tmp8 = gt8[:, 72:80]

    prev_c_pe = ev_b_pe
    ev_cw = 0
    prev_sig = 0
    prev_dgt = 0
    for t in range(NT):
        lx = sydma(lambda e, t=t: e.dma_start(out=x_t[:, :],
                                              in_=xq[t * 128:(t + 1) * 128, :]),
                   waits=(("s_pe", prev_c_pe), ("s_dve", ev_xn_done)))
        xsel = S.op("dve", lambda e, t=t: (
            e.tensor_scalar_mul(st(T1), st(SMEAN0 + 2 * t), m_lo),
            e.tensor_scalar_mul(st(T2), st(SRSTD0 + 2 * t), m_lo))[-1],
            waits=(("s_sy", lx), ("s_pe", prev_c_pe)), inc="s_dve")
        xsel = S.op("dve", lambda e, t=t: (
            e.scalar_tensor_tensor(st(T1), st(SMEAN0 + 2 * t + 1), m_hi, st(T1),
                                   ALU.mult, ALU.add),
            e.scalar_tensor_tensor(st(T2), st(SRSTD0 + 2 * t + 1), m_hi, st(T2),
                                   ALU.mult, ALU.add))[-1],
            waits=(("s_dve", xsel),), inc="s_dve")
        xnf = S.op("dve", lambda e: e.tensor_scalar(
            xn32[:, :], x_t[:, :], st(T1), st(T2), ALU.subtract, ALU.mult),
            waits=(("s_dve", xsel),), inc="s_dve")
        trp = S.op("pe", lambda e: [
            e.transpose(psc[:, c * 128:(c + 1) * 128], xn32[:, c * 128:(c + 1) * 128],
                        identf_sb[:, :]) for c in range(8)][-1],
            waits=(("s_dve", xnf),), inc="s_pe")
        evf = S.op("dve", lambda e: e.tensor_copy(xf32[:, :], psc[:, 0:1024]),
                   waits=(("s_pe", trp),), inc="s_dve")
        lg = S.op("pe", lambda e: [
            e.matmul(psc[:, 1024:1032], xf32[:, c * 128:(c + 1) * 128],
                     gw_sb[:, c * 8:(c + 1) * 8], start=(c == 0), stop=(c == 7))
            for c in range(8)][-1],
            waits=(("s_dve", evf), ("s_act", prev_sig)), inc="s_pe")
        S.op("act", lambda e: e.activation(sC, psc[:, 1024:1032], AF.Sigmoid),
             waits=(("s_pe", lg), ("s_dve", ev_cw), ("s_sy", prev_dgt)))
        sig = S.op("act", lambda e: e.activation(xf32[:, 0:512], xf32[:, 512:1024],
                                                 AF.Square), inc="s_act")
        prev_sig = sig
        cprev = S.op("dve", lambda e: e.tensor_add(cC, sC, ebb_sb[:, :]),
                     waits=(("s_act", sig), ("s_dve", ev_ebb)), inc="s_dve")
        cprev = S.op("dve", lambda e: e.tensor_add(G4, cC[:, 0:8:2], cC[:, 1:8:2]),
                     waits=(("s_dve", cprev),), inc="s_dve")
        cprev = S.op("dve", lambda e: (
            e.tensor_tensor(A16.rearrange("p (a b) -> p a b", a=4),
                            G4.unsqueeze(2).broadcast_to((128, 4, 4)),
                            G4.unsqueeze(1).broadcast_to((128, 4, 4)),
                            ALU.is_lt),
            e.tensor_tensor(C16.rearrange("p (a b) -> p a b", a=4),
                            G4.unsqueeze(2).broadcast_to((128, 4, 4)),
                            G4.unsqueeze(1).broadcast_to((128, 4, 4)),
                            ALU.is_equal))[-1],
            waits=(("s_dve", cprev),), inc="s_dve")
        cprev = S.op("dve", lambda e: e.tensor_tensor(C16, C16, hmask_sb[:, :],
                                                      ALU.mult),
                     waits=(("s_dve", cprev),), inc="s_dve")
        cprev = S.op("dve", lambda e: e.tensor_add(A16, A16, C16),
                     waits=(("s_dve", cprev),), inc="s_dve")
        cprev = S.op("dve", lambda e: e.tensor_reduce(
            cnt4, A16.rearrange("p (a b) -> p a b", a=4), AX.X, ALU.add),
            waits=(("s_dve", cprev),), inc="s_dve")
        cprev = S.op("dve", lambda e: e.tensor_single_scalar(gm4, cnt4, 2.0,
                                                             ALU.is_lt),
                     waits=(("s_dve", cprev),), inc="s_dve")
        cprev = S.op("dve", lambda e: e.tensor_copy(
            sm8.rearrange("p (g j) -> p g j", g=4),
            gm4.unsqueeze(2).broadcast_to((128, 4, 2))),
            waits=(("s_dve", cprev),), inc="s_dve")
        cprev = S.op("dve", lambda e: e.tensor_tensor(tmp8, cC, sm8, ALU.mult),
                     waits=(("s_dve", cprev),), inc="s_dve")
        cprev = S.op("dve", lambda e: (
            e.tensor_tensor(relv_f[:, 0:64].rearrange("p (a b) -> p a b", a=8),
                            tmp8.unsqueeze(2).broadcast_to((128, 8, 8)),
                            tmp8.unsqueeze(1).broadcast_to((128, 8, 8)),
                            ALU.is_lt),
            e.tensor_tensor(add1_f[:, 0:64].rearrange("p (a b) -> p a b", a=8),
                            tmp8.unsqueeze(2).broadcast_to((128, 8, 8)),
                            tmp8.unsqueeze(1).broadcast_to((128, 8, 8)),
                            ALU.is_equal))[-1],
            waits=(("s_dve", cprev),), inc="s_dve")
        cprev = S.op("dve", lambda e: e.tensor_tensor(
            add1_f[:, 0:64], add1_f[:, 0:64], fmask_sb[:, :], ALU.mult),
            waits=(("s_dve", cprev),), inc="s_dve")
        cprev = S.op("dve", lambda e: e.tensor_add(
            relv_f[:, 0:64], relv_f[:, 0:64], add1_f[:, 0:64]),
            waits=(("s_dve", cprev),), inc="s_dve")
        cprev = S.op("dve", lambda e: e.tensor_reduce(
            tmp8, relv_f[:, 0:64].rearrange("p (a b) -> p a b", a=8), AX.X, ALU.add),
            waits=(("s_dve", cprev),), inc="s_dve")
        cprev = S.op("dve", lambda e: e.tensor_single_scalar(tmp8, tmp8, 2.0,
                                                             ALU.is_lt),
                     waits=(("s_dve", cprev),), inc="s_dve")
        cprev = S.op("dve", lambda e: e.tensor_tensor(tmp8, tmp8, sC, ALU.mult),
                     waits=(("s_dve", cprev),), inc="s_dve")
        cprev = S.op("dve", lambda e: e.reduce_sum(st(WSUM), tmp8, AX.X),
                     waits=(("s_dve", cprev),), inc="s_dve")
        cprev = S.op("dve", lambda e: e.tensor_scalar_add(st(T1), st(WSUM), 1e-20),
                     waits=(("s_dve", cprev),), inc="s_dve")
        cprev = S.op("dve", lambda e: e.reciprocal(st(T2), st(T1)),
                     waits=(("s_dve", cprev),), inc="s_dve")
        ev_cw = S.op("dve", lambda e, t=t: e.tensor_scalar_mul(
            cw_sb[:, t * 8:(t + 1) * 8], tmp8, st(T2)),
            waits=(("s_dve", cprev),), inc="s_dve")
        if DBG:
            prev_dgt = sydma(lambda e, t=t: e.dma_start(
                out=dbg["dgt"][:, t * 80:(t + 1) * 80], in_=gt8[:, :]),
                waits=(("s_dve", ev_cw),))
        prev_c_pe = lg
    ev_c_dve = ev_cw
    if DBG:
        sydma(lambda e: e.dma_start(out=dbg["dcw"], in_=cw_sb[:, :]),
              waits=(("s_dve", ev_c_dve),))

    K_aug = [ar[:, KA_:KA_ + 4096], ar[:, KB_:KB_ + 4096]]
    Q_aug = [ar[:, QA_:QA_ + 2048], ar[:, QB_:QB_ + 2048]]
    v_pair = ar[:, VV_:VV_ + 4128]
    ww_raw = ar[:, WW_:WW_ + 1024]
    ww_lo = ar[:, WW_ + 1024:WW_ + 2048]
    ww_hi = ar[:, WW_ + 2048:WW_ + 3072]
    qr_s = ar[:, QR_:QR_ + IMG_W]
    bu = ar[:, BU_:BU_ + BU_W]
    p_b = ar[:, PBUF_:PBUF_ + PB_W]
    au = ar[:, AUF_:AUF_ + AUF_W]
    pt_s = ar[:, PT_:PT_ + 2048]
    apt_s = ar[:, APT_:APT_ + 2048]
    atr_s = ar[:, ATR_:ATR_ + 64]

    def geo(t):
        w0 = max(0, 256 * t - 2942)
        ce = min(BU_W, 256 * t + 1279)
        roff = w0 + 3197 - 256 * t
        wlo2 = max(0, 256 * t - 3070)
        whi2_a = max(0, 256 * t - 3070)
        whi2_b = max(0, 256 * t - 2942)
        return w0, ce, roff, wlo2, whi2_a, whi2_b

    ev = {
        "exp": S.cnt["s_act"],
        "epi_dve": S.cnt["s_dve"],
        "qk_pe": S.cnt["s_pe"],
        "band_pe": S.cnt["s_pe"],
        "aread_sy": [0, 0],
        "pbuf_sy": 0, "pbuf_dve": 0, "pbuf_pe": 0,
        "ev_v": 0,
        "bu_ready": 0,
    }
    gtile = [0]
    pend = [None]

    def emit_front1(hp, hi, t, Ka, Qa):
        g = gtile[0]; gtile[0] += 1
        slot = g % 2
        w0, ce, roff, wlo2, whi2_a, whi2_b = geo(t)

        qr_mm = S.op("pe", lambda e: [
            e.matmul(psc[:, n0:n1], Qa[0:64, t * 128:(t + 1) * 128],
                     rkT_sb[0:64, n0:n1], start=True, stop=True)
            for (n0, n1) in ((0, 512), (512, 1024), (1024, 1536), (1536, 2047))][-1],
            waits=(("s_act", ev["exp"]), ("s_dve", ev["ev_v"]), ("s_gp", ld0),
                   ("s_sy", fl_ev)),
            inc="s_pe")
        qr_ev = S.op("dve", lambda e: (
            e.tensor_copy(stats[:, QK0:QK0 + 1], psc[:, 0:1]),
            e.tensor_copy(stats[:, QK2:QK2 + 1], psc[:, 2046:2047]),
            e.tensor_copy(qr_s[:, 127:127 + 512], psc[:, 0:512]),
            e.tensor_copy(qr_s[:, 639:639 + 512], psc[:, 512:1024]),
            e.tensor_copy(qr_s[:, 1151:1151 + 512], psc[:, 1024:1536]),
            e.tensor_copy(qr_s[:, 1663:1663 + 511], psc[:, 1536:2047]))[-1],
            waits=(("s_pe", qr_mm),), inc="s_dve")
        pads = S.op("dve", lambda e: (
            e.tensor_scalar(qr_s[:, 0:127], identb[:, 0:127], 0.0,
                            stats[:, QK0:QK0 + 1], ALU.mult, ALU.add),
            e.tensor_scalar(qr_s[:, 2174:2301], identb[:, 0:127], 0.0,
                            stats[:, QK2:QK2 + 1], ALU.mult, ALU.add))[-1],
            waits=(("s_dve", qr_ev),), inc="s_dve")
        qkT_mm = S.op("pe", lambda e: e.matmul(
            psc[0:2, 2304:2432], stats[:, QK0:QK0 + 2], identf_sb[:, :],
            start=True, stop=True),
            waits=(("s_dve", qr_ev), ("s_dve", ev.get("qkT_ev", 0))), inc="s_pe")
        qkT_ev = S.op("dve", lambda e: e.tensor_copy(qkT_row[0:2, :],
                                                     psc[0:2, 2304:2432]),
                      waits=(("s_pe", qkT_mm), ("s_sy", ev.get("qrow", 0))),
                      inc="s_dve")
        ev["qkT_ev"] = qkT_ev
        qrow_w = sydma(lambda e, t=t: e.dma_start(
            out=Qa[64:66, t * 128:(t + 1) * 128], in_=qkT_row[0:2, :]),
            waits=(("s_dve", qkT_ev),))
        ev["qrow"] = qrow_w
        step_ev = gpdma(lambda e, t=t: e.dma_start(
            out=Ka[64:66, :], in_=ap_in["steps"][2 * t:2 * t + 2, :]),
            waits=(("s_pe", ev["qk_pe"]),))
        img_w = sydma(lambda e, slot=slot: e.dma_start(
            out=dram_flat(qrimg, slot * 128 * IMG_RS + IMG_FL, IMG_RS, 128, 2174),
            in_=qr_s[:, 0:2174]),
            waits=(("s_dve", qr_ev), ("s_dve", pads)))
        img_w = sydma(lambda e, slot=slot: e.dma_start(
            out=dram_flat(qrimg, slot * 128 * IMG_RS + IMG_FL + 2174, IMG_RS,
                          128, 2174),
            in_=qr_s[:, 127:2301]),
            waits=(("s_dve", qr_ev), ("s_dve", pads)))
        if t == 0:
            bz = S.op("dve", lambda e: e.memset(bu[:, :], 0.0),
                      waits=(("s_pe", ev["band_pe"]),), inc="s_dve")
        else:
            pw0_ = geo(t - 1)[0]
            if w0 > pw0_:
                bz = S.op("dve", lambda e, pw0_=pw0_, w0=w0: e.memset(
                    bu[:, pw0_:w0], 0.0),
                    waits=(("s_pe", ev["band_pe"]),), inc="s_dve")
            else:
                bz = ev["bu_ready"]
        band_r = sydma(lambda e, slot=slot, w0=w0, ce=ce, roff=roff: e.dma_start(
            out=bu[:, w0:ce],
            in_=dram_flat(qrimg, slot * 128 * IMG_RS + roff, IMG_RS - 1,
                          128, ce - w0)),
            waits=(("s_sy", img_w), ("s_pe", ev["band_pe"]), ("s_dve", bz)))
        ev["bu_ready"] = bz
        return dict(hp=hp, hi=hi, t=t, Ka=Ka, Qa=Qa, slot=slot, w0=w0, ce=ce,
                    wlo2=wlo2, whi2_a=whi2_a, whi2_b=whi2_b, band_r=band_r,
                    qrow_w=qrow_w, step_ev=step_ev)

    def emit_front2(fs):
        hp, hi, t, Ka, Qa = fs["hp"], fs["hi"], fs["t"], fs["Ka"], fs["Qa"]
        w0, ce = fs["w0"], fs["ce"]
        band_r, qrow_w, step_ev = fs["band_r"], fs["qrow_w"], fs["step_ev"]
        exp_ps = []
        for PS in (0, 1):
            lo_banks = [b for b in range(4)
                        if max(128 + 512 * (4 * PS + b), w0)
                        < min(128 + 512 * (4 * PS + b) + 512, ce)]
            hi_banks = [b for b in range(4)
                        if max(512 * (4 * PS + b), w0)
                        < min(512 * (4 * PS + b) + 512, ce)]
            def qk(e, lo_banks=lo_banks, hi_banks=hi_banks, t=t, PS=PS):
                last = None
                for b in range(4):
                    gb = 4 * PS + b
                    nadd = (b in lo_banks) + (b in hi_banks)
                    last = e.matmul(psc[:, b * 512:(b + 1) * 512],
                                    Qa[0:66, t * 128:(t + 1) * 128],
                                    Ka[0:66, gb * 512:(gb + 1) * 512],
                                    start=True, stop=(nadd == 0))
                    if b in lo_banks:
                        nadd -= 1
                        last = e.matmul(psc[:, b * 512:(b + 1) * 512],
                                        idlo_sb[:, :],
                                        bu[:, 128 + 512 * gb:640 + 512 * gb],
                                        start=False, stop=(nadd == 0))
                    if b in hi_banks:
                        last = e.matmul(psc[:, b * 512:(b + 1) * 512],
                                        idhi_sb[:, :],
                                        bu[:, 512 * gb:512 * gb + 512],
                                        start=False, stop=True)
                return last
            qk_mm = S.op("pe", qk, waits=(
                ("s_sy", band_r), ("s_sy", qrow_w), ("s_gp", step_ev),
                ("s_act", exp_ps[-1] if exp_ps else ev["exp"])), inc="s_pe")
            dscr_ev = None
            if DBG and hp == 0 and hi == 0:
                dcp = S.op("dve", lambda e: e.tensor_copy(
                    ar[:, PT_:PT_ + 4096].bitcast(F32)[:, 0:2048], psc[:, 0:2048]),
                    waits=(("s_pe", qk_mm),), inc="s_dve")
                dscr_ev = sydma(lambda e, t=t, PS=PS: e.dma_start(
                    out=dbg["dscr"][:, t * 4096 + 2048 * PS:
                                    t * 4096 + 2048 * PS + 2048],
                    in_=ar[:, PT_:PT_ + 4096].bitcast(F32)[:, 0:2048]),
                    waits=(("s_dve", dcp),))
            expp = S.op("act", lambda e, PS=PS: e.activation(
                p_b[:, PB_L + 2048 * PS:PB_L + 2048 * PS + 2048], psc[:, 0:2048],
                AF.Exp, bias=st(MNEG)),
                waits=(("s_pe", qk_mm), ("s_sy", ev["pbuf_sy"]),
                       ("s_dve", ev["pbuf_dve"]), ("s_pe", ev["pbuf_pe"]),
                       ("s_sy", dscr_ev)),
                inc="s_act")
            exp_ps.append(expp)
            ev["band_pe"] = qk_mm
            ev["qk_pe"] = qk_mm
        exp_a, expv = exp_ps
        ev["exp"] = expv
        if DBG and hp == 0 and hi == 0:
            ev["pbuf_sy"] = sydma(lambda e, t=t: e.dma_start(
                out=dbg["dpb"][:, t * 4096:(t + 1) * 4096],
                in_=p_b[:, PB_L:PB_L + 4096]),
                waits=(("s_act", expv),))
        fs["exp_a"] = exp_a
        fs["expv"] = expv

    def emit_back(fs):
        hp, hi, t, Ka, Qa = fs["hp"], fs["hi"], fs["t"], fs["Ka"], fs["Qa"]
        slot, w0, ce = fs["slot"], fs["w0"], fs["ce"]
        wlo2, whi2_a, whi2_b = fs["wlo2"], fs["whi2_a"], fs["whi2_b"]
        exp_a, expv = fs["exp_a"], fs["expv"]
        pw0 = PBUF_ + PB_L + 256 * t - 3070
        pimg_w = sydma(lambda e, slot=slot, pw0=pw0: e.dma_start(
            out=dram_flat(pimg, slot * 128 * PW2, PW2, 128, PW2),
            in_=ar[:, pw0:pw0 + PW2]),
            waits=(("s_act", expv), ("s_sy", ev["aread_sy"][slot])))
        auf = sydma(lambda e, slot=slot: e.dma_start(
            out=au[:, :], in_=dram_flat(pimg, slot * 128 * PW2, PW2 + 1,
                                        128, AUF_W)),
            waits=(("s_sy", pimg_w),))
        ev["aread_sy"][slot] = auf
        ev["pbuf_sy"] = pimg_w
        def rs0fn(e, wlo2=wlo2, whi2_a=whi2_a, whi2_b=whi2_b):
            if wlo2 > 0:
                last = e.reduce_sum(st(RS0), p_b[:, PB_L:PB_L + wlo2], AX.X)
            else:
                last = e.memset(st(RS0), 0.0)
            if whi2_b > whi2_a:
                last = e.reduce_sum(st(T3), p_b[:, PB_L + whi2_a:PB_L + whi2_b],
                                    AX.X)
            return last
        rs0e = S.op("dve", rs0fn, waits=(("s_act", exp_a),), inc="s_dve")
        if whi2_b > whi2_a:
            rs0e = S.op("dve", lambda e: e.scalar_tensor_tensor(
                st(RS0), st(T3), m_hi, st(RS0), ALU.mult, ALU.add),
                waits=(("s_dve", rs0e),), inc="s_dve")
        axl0 = PBUF_ + PB_L + 256 * t - 3070
        axr0 = PBUF_ + PB_L + 256 * t + 1023
        cs1 = S.op("dve", lambda e, axl0=axl0, axr0=axr0: (
            e.tensor_tensor(apt_s[:, 0:128], ar[:, axl0:axl0 + 128],
                            masklo_sb[:, :], ALU.mult),
            e.tensor_tensor(apt_s[:, 128:256], ar[:, axl0 + 128:axl0 + 256],
                            masklo_sb[:, :], ALU.mult),
            e.tensor_tensor(apt_s[:, 256:384], ar[:, axr0:axr0 + 128],
                            maskup_sb[:, :], ALU.mult),
            e.tensor_tensor(apt_s[:, 384:512], ar[:, axr0 + 128:axr0 + 256],
                            maskup_sb[:, :], ALU.mult))[-1],
            waits=(("s_sy", auf), ("s_act", expv)), inc="s_dve")
        cs2 = S.op("dve", lambda e: (
            e.reduce_sum(st(CSLL), apt_s[:, 0:128], AX.X),
            e.reduce_sum(st(CSLH), apt_s[:, 128:256], AX.X),
            e.reduce_sum(st(CSRL), apt_s[:, 256:384], AX.X),
            e.reduce_sum(st(CSRH), apt_s[:, 384:512], AX.X))[-1],
            waits=(("s_dve", cs1),), inc="s_dve")
        cs3 = S.op("dve", lambda e: (
            e.tensor_tensor(st(T1), st(CSLL), m_lo, ALU.mult),
            e.tensor_tensor(st(T2), st(CSLH), m_hi, ALU.mult),
            e.tensor_tensor(st(T3), st(CSRL), m_lo, ALU.mult),
            e.tensor_tensor(st(T4), st(CSRH), m_hi, ALU.mult))[-1],
            waits=(("s_dve", cs2),), inc="s_dve")
        csum = S.op("dve", lambda e: (
            e.tensor_add(au[:, 0:1], au[:, 0:1], st(T1)),
            e.tensor_add(au[:, 128:129], au[:, 128:129], st(T2)),
            e.tensor_add(au[:, 4093:4094], au[:, 4093:4094], st(T3)),
            e.tensor_add(au[:, 4221:4222], au[:, 4221:4222], st(T4)))[-1],
            waits=(("s_dve", cs3),), inc="s_dve")
        ev["pbuf_dve"] = csum
        if DBG and hp == 0 and hi == 0:
            ev["aread_sy"][slot] = sydma(lambda e, t=t: e.dma_start(
                out=dbg["dauf"][:, t * AUF_W:(t + 1) * AUF_W], in_=au[:, :]),
                waits=(("s_dve", csum),))
        pt_ev = []
        av_mm = None
        for gq in range(8):
            bank_off = 2560 if gq % 2 == 0 else 3072
            ptm = S.op("pe", lambda e, gq=gq, bank_off=bank_off: [
                e.matmul(psc[:, bank_off + j * 128:bank_off + (j + 1) * 128],
                         p_b[:, PB_L + (gq * 4 + j) * 128:PB_L + (gq * 4 + j + 1) * 128],
                         identb[:, :], start=True, stop=True) for j in range(4)][-1],
                waits=(("s_act", exp_a if gq < 4 else expv),
                       ("s_dve", pt_ev[gq - 2] if gq >= 2 else ev.get("pte_last", 0))),
                inc="s_pe")
            pte = S.op("dve", lambda e, gq=gq, bank_off=bank_off: e.tensor_copy(
                pt_s[:, (gq % 4) * 512:(gq % 4 + 1) * 512],
                psc[:, bank_off:bank_off + 512]),
                waits=(("s_pe", ptm),), inc="s_dve")
            pt_ev.append(pte)
            if gq in (3, 7):
                k0 = 0 if gq == 3 else 16
                av_mm = S.op("pe", lambda e, hi=hi, k0=k0: [
                    e.matmul(psc[:, 2048:2113],
                             pt_s[:, (k - k0) * 128:(k - k0 + 1) * 128],
                             v_pair[:, k * 129 + hi * 64:k * 129 + hi * 64 + 65],
                             start=(k == 0), stop=(k == 31))
                    for k in range(k0, k0 + 16)][-1],
                    waits=(("s_dve", pte), ("s_dve", ev["ev_v"])), inc="s_pe")
        av0 = 2048 + (1 if hi == 1 else 0)
        rs_col = 2112 if hi == 0 else 2048
        ev["pbuf_pe"] = av_mm
        apt_ev = []
        for gq in range(4):
            def atf(e, gq=gq, t=t):
                last = None
                for j in range(4):
                    k = gq * 4 + j
                    w = 128 if k < 15 else 127
                    keep_wrap = (k + 2) * 128 > 3070 - 256 * t
                    reg = psc[0:w, 3584 + j * 128:3584 + (j + 1) * 128]
                    last = e.matmul(reg, au[:, 2047 + k * 128:2047 + k * 128 + w],
                                    idlo_sb[:, :], start=True, stop=False)
                    last = e.matmul(reg, au[:, 2175 + k * 128:2175 + k * 128 + w],
                                    idhi_sb[:, :], start=False,
                                    stop=not keep_wrap)
                    if keep_wrap:
                        last = e.matmul(reg, au[:, k * 128:k * 128 + w],
                                        idlo_sb[:, :], start=False, stop=False)
                        last = e.matmul(reg, au[:, 128 + k * 128:128 + k * 128 + w],
                                        idhi_sb[:, :], start=False, stop=True)
                return last
            atm = S.op("pe", atf,
                       waits=(("s_dve", csum),
                              ("s_dve", apt_ev[gq - 1] if gq >= 1
                               else ev.get("ate_last", 0))),
                       inc="s_pe")
            ate = S.op("dve", lambda e, gq=gq: e.tensor_copy(
                apt_s[:, gq * 512:(gq + 1) * 512], psc[:, 3584:3584 + 512]),
                waits=(("s_pe", atm),), inc="s_dve")
            apt_ev.append(ate)
        rv_mm = S.op("pe", lambda e: [
            e.matmul(psc[:, 2113:2178],
                     apt_s[0:(128 if k < 15 else 127), k * 128:(k + 1) * 128],
                     rv_sb[0:(128 if k < 15 else 127), k * 65:(k + 1) * 65],
                     start=(k == 0), stop=(k == 15)) for k in range(16)][-1],
            waits=(("s_dve", apt_ev[-1]),), inc="s_pe")
        epiA = S.op("dve", lambda e, av0=av0, rs_col=rs_col: (
            e.tensor_copy(st(RSUM), psc[:, rs_col:rs_col + 1]),
            e.tensor_copy(relv_f[:, :], psc[:, 2113:2178]),
            e.tensor_copy(add1_f[:, :], psc[:, av0:av0 + 64]))[-1],
            waits=(("s_pe", rv_mm), ("s_pe", av_mm), ("s_dve", rs0e),
                   ("s_dve", ev_ebb)), inc="s_dve")
        epiB = S.op("dve", lambda e: (
            e.reciprocal(st(RINV), st(RSUM)),
            e.tensor_add(st(T3), relv_f[:, 64:65], st(RS0)))[-1],
            waits=(("s_dve", epiA),), inc="s_dve")
        epiC = S.op("dve", lambda e: (
            e.tensor_sub(st(RS2), st(RSUM), st(T3)),
            e.tensor_add(add1_f[:, :], add1_f[:, :], relv_f[:, 0:64]))[-1],
            waits=(("s_dve", epiB),), inc="s_dve")
        epiD = S.op("dve", lambda e: e.scalar_tensor_tensor(
            add1_f[:, :], rv0b[:, :], st(RS0), add1_f[:, :], ALU.mult, ALU.add),
            waits=(("s_dve", epiC),), inc="s_dve")
        epiE = S.op("dve", lambda e: e.scalar_tensor_tensor(
            add1_f[:, :], rv2b[:, :], st(RS2), add1_f[:, :], ALU.mult, ALU.add),
            waits=(("s_dve", epiD),), inc="s_dve")
        epi1 = S.op("dve", lambda e: e.tensor_scalar_mul(atr_s[:, :], add1_f[:, :],
                                                         st(RINV)),
                    waits=(("s_dve", epiE),), inc="s_dve")
        atr_mm = S.op("pe", lambda e: e.matmul(
            psc[0:64, 2432:2560], atr_s[:, :], identb[:, :], start=True, stop=True),
            waits=(("s_dve", epi1),), inc="s_pe")
        dep_ev = None
        if DBG and hp == 0 and hi == 0:
            sydma(lambda e, t=t: e.dma_start(
                out=dbg["dstats"][:, t * 32:t * 32 + 32], in_=stats[:, 0:32]),
                waits=(("s_dve", epi1),))
            sydma(lambda e, t=t: e.dma_start(
                out=dbg["drelv"][:, t * 129:t * 129 + 65], in_=relv_f[:, :]),
                waits=(("s_dve", epi1),))
            sydma(lambda e, t=t: e.dma_start(
                out=dbg["drelv"][:, t * 129 + 65:t * 129 + 129], in_=add1_f[:, :]),
                waits=(("s_dve", epi1),))
            dep_ev = sydma(lambda e, t=t: e.dma_start(
                out=dbg["datr"][:, t * 64:(t + 1) * 64], in_=atr_s[:, :]),
                waits=(("s_dve", epi1),))
        epi2 = S.op("dve", lambda e, t=t, hp=hp, hi=hi: e.tensor_copy(
            attn_fm[64 * hi:64 * hi + 64,
                    hp * 2048 + t * 128:hp * 2048 + (t + 1) * 128],
            psc[0:64, 2432:2560]),
            waits=(("s_pe", atr_mm), ("s_sy", dep_ev)), inc="s_dve")
        ev["epi_dve"] = epi2
        ev["pte_last"] = pt_ev[-1]
        ev["ate_last"] = apt_ev[-1]

    def emit_pair(hp):
        if pend[0] is not None:
            emit_back(pend[0])
            pend[0] = None
        c0 = hp * 128
        ww3 = ww_raw.rearrange("p (c n) -> p c n", c=8)
        wq_in = ap_in["wqT"][:, c0:c0 + 128].rearrange("(c p) n -> p c n", p=128)
        wk_in = ap_in["wkT"][:, c0:c0 + 128].rearrange("(c p) n -> p c n", p=128)
        wv_in = ap_in["wvT"][:, c0:c0 + 128].rearrange("(c p) n -> p c n", p=128)

        wq_ev = gpdma(lambda e: e.dma_start(out=ww3, in_=wq_in),
                      waits=(("s_pe", ev["qk_pe"]),))
        wqm = S.op("dve", lambda e: (
            e.tensor_scalar_mul(ww_lo[:, :], ww_raw[:, :], m_lo),
            e.tensor_scalar_mul(ww_hi[:, :], ww_raw[:, :], m_hi))[-1],
            waits=(("s_gp", wq_ev), ("s_pe", ev["qk_pe"]), ("s_dve", ev_ebb)),
            inc="s_dve")

        def qproj(e):
            last = None
            for T in range(NT):
                reg = psc[:, T * 128:(T + 1) * 128]
                for c in range(8):
                    last = e.matmul(reg, ww_lo[:, c * 128:(c + 1) * 128],
                                    xn_fm[:, c * 4096 + 256 * T:c * 4096 + 256 * T + 128],
                                    start=(c == 0), stop=False)
                    last = e.matmul(reg, ww_hi[:, c * 128:(c + 1) * 128],
                                    xn_fm[:, c * 4096 + 256 * T + 128:c * 4096 + 256 * T + 256],
                                    start=False, stop=False)
                last = e.matmul(reg, bias_sb[0:1, c0:c0 + 128],
                                ones_row[0:1, 0:128], start=False, stop=True)
            return last
        qp = S.op("pe", qproj, waits=(("s_dve", wqm), ("s_dve", ev_xn_done),
                                      ("s_act", ev["exp"]), ("s_dve", ev["epi_dve"])),
                  inc="s_pe")
        qe = S.op("dve", lambda e: (
            e.tensor_copy(Q_aug[0][0:64, :], psc[0:64, 0:2048]),
            e.tensor_copy(Q_aug[1][0:64, :], psc[64:128, 0:2048]))[-1],
            waits=(("s_pe", qp),), inc="s_dve")

        wk_ev = gpdma(lambda e: e.dma_start(out=ww3, in_=wk_in),
                      waits=(("s_pe", qp),))
        def kproj(e):
            last = None
            for b in range(8):
                for c in range(8):
                    last = e.matmul(psc[:, b * 512:(b + 1) * 512],
                                    ww_raw[:, c * 128:(c + 1) * 128],
                                    xn_fm[:, c * 4096 + b * 512:c * 4096 + (b + 1) * 512],
                                    start=(c == 0), stop=False)
                last = e.matmul(psc[:, b * 512:(b + 1) * 512],
                                bias_sb[0:1, D + c0:D + c0 + 128],
                                ones_row[0:1, 0:512],
                                start=False, stop=True)
            return last
        kp = S.op("pe", kproj, waits=(("s_gp", wk_ev), ("s_dve", qe)), inc="s_pe")
        ke = S.op("dve", lambda e: (
            e.tensor_scalar_mul(K_aug[0][0:64, :], psc[0:64, :], SCALE),
            e.tensor_scalar_mul(K_aug[1][0:64, :], psc[64:128, :], SCALE))[-1],
            waits=(("s_pe", kp),), inc="s_dve")

        wv_ev = gpdma(lambda e: e.dma_start(out=ww3, in_=wv_in),
                      waits=(("s_pe", kp),))
        last_ve = ke
        for rc4 in range(8):
            def vgrp(e, rc4=rc4):
                last = None
                for j in range(4):
                    rcx = rc4 * 4 + j
                    for c in range(8):
                        last = e.matmul(psc[:, j * 512:j * 512 + 128],
                                        xn_fm[:, c * 4096 + rcx * 128:c * 4096 + (rcx + 1) * 128],
                                        ww_raw[:, c * 128:(c + 1) * 128],
                                        start=(c == 0), stop=False)
                    last = e.matmul(psc[:, j * 512:j * 512 + 128],
                                    ones_row[0:1, 0:128],
                                    bias_sb[0:1, 2 * D + c0:2 * D + c0 + 128],
                                    start=False, stop=True)
                return last
            vp = S.op("pe", vgrp, waits=(("s_gp", wv_ev), ("s_dve", last_ve)),
                      inc="s_pe")
            last_ve = S.op("dve", lambda e, rc4=rc4: (
                e.tensor_copy(
                    v_pair[:, rc4 * 516:rc4 * 516 + 516].rearrange(
                        "p (j a) -> p j a", j=4)[:, :, 0:64],
                    psc[:, 0:2048].rearrange("p (j a) -> p j a", j=4)[:, :, 0:64]),
                e.tensor_copy(
                    v_pair[:, rc4 * 516:rc4 * 516 + 516].rearrange(
                        "p (j a) -> p j a", j=4)[:, :, 65:129],
                    psc[:, 0:2048].rearrange("p (j a) -> p j a", j=4)[:, :, 64:128]))[-1],
                waits=(("s_pe", vp),), inc="s_dve")
        ev["ev_v"] = last_ve
        ev["qk_pe"] = S.cnt["s_pe"]

        for hi in (0, 1):
            for t in range(NT):
                fs = emit_front1(hp, hi, t, K_aug[hi], Q_aug[hi])
                if pend[0] is not None:
                    emit_back(pend[0])
                emit_front2(fs)
                pend[0] = fs

    for hp in range(8):
        emit_pair(hp)
    emit_back(pend[0])
    pend[0] = None

    ev_attn_pe = S.cnt["s_pe"]
    ev_attn_dve = S.cnt["s_dve"]
    if DBG:
        dbg_ev["datn"] = sydma(lambda e: e.dma_start(out=dbg["datn"],
                                                     in_=attn_fm[:, :]),
                               waits=(("s_dve", ev_attn_dve),))

    oacc = arf(OACC_, 16384)
    wo_s = ar[:, WO_:WO_ + 8192]
    x_te = arf(XTE_, 1024)
    wo_in = ap_in["woT"].rearrange("(c p) n -> p c n", p=128)
    wo_ev = gpdma(lambda e: e.dma_start(
        out=wo_s.rearrange("p (c n) -> p c n", c=8), in_=wo_in),
        waits=(("s_pe", ev_attn_pe), ("s_dve", ev_attn_dve)))
    prev_ev2 = ev_attn_dve
    prev_pe2 = ev_attn_pe
    for t in range(NT):
        lx = sydma(lambda e, t=t: e.dma_start(out=x_te[:, :],
                                              in_=xq[t * 128:(t + 1) * 128, :]),
                   waits=(("s_dve", prev_ev2),))
        def oproj(e, t=t):
            last = None
            for b in range(2):
                for c in range(8):
                    last = e.matmul(psc[:, b * 512:(b + 1) * 512],
                                    attn_fm[:, c * 2048 + t * 128:c * 2048 + (t + 1) * 128],
                                    wo_s[:, c * 1024 + b * 512:c * 1024 + (b + 1) * 512],
                                    start=(c == 0), stop=False)
                last = e.matmul(psc[:, b * 512:(b + 1) * 512],
                                ones_row[0:1, 0:128],
                                bias_sb[0:1, 3 * D + b * 512:3 * D + (b + 1) * 512],
                                start=False, stop=True)
            return last
        om = S.op("pe", oproj, waits=(("s_gp", wo_ev), ("s_dve", prev_ev2)),
                  inc="s_pe")
        prev_ev2 = S.op("dve", lambda e, t=t: e.tensor_add(
            oacc[:, t * 1024:(t + 1) * 1024], psc[:, 0:1024], x_te[:, :]),
            waits=(("s_pe", om), ("s_sy", lx)), inc="s_dve")
        prev_pe2 = om

    fx_t = arf(FXT_, 1024)
    fxn_tm = ar[:, FXN_:FXN_ + 1024]
    xnq8 = xn_fm[:, 0:16384].rearrange("p (c n) -> p c n", c=8)
    prev_f_pe, prev_f_ev = prev_pe2, prev_ev2
    for t in range(NT):
        lx = sydma(lambda e, t=t: e.dma_start(out=fx_t[:, :],
                                              in_=xq[t * 128:(t + 1) * 128, :]),
                   waits=(("s_pe", prev_f_pe), ("s_dve", prev_f_ev)))
        fsel = S.op("dve", lambda e, t=t: (
            e.tensor_scalar_mul(st(T1), st(SMEAN0 + 2 * t), m_lo),
            e.tensor_scalar_mul(st(T2), st(SRSTD0 + 2 * t), m_lo))[-1],
            waits=(("s_sy", lx), ("s_pe", prev_f_pe)), inc="s_dve")
        fsel = S.op("dve", lambda e, t=t: (
            e.scalar_tensor_tensor(st(T1), st(SMEAN0 + 2 * t + 1), m_hi, st(T1),
                                   ALU.mult, ALU.add),
            e.scalar_tensor_tensor(st(T2), st(SRSTD0 + 2 * t + 1), m_hi, st(T2),
                                   ALU.mult, ALU.add))[-1],
            waits=(("s_dve", fsel),), inc="s_dve")
        rcp = S.op("dve", lambda e: e.tensor_scalar(
            fxn_tm[:, :], fx_t[:, :], st(T1), st(T2), ALU.subtract, ALU.mult),
            waits=(("s_dve", fsel),), inc="s_dve")
        trp = S.op("pe", lambda e: [
            e.matmul(psc[:, c * 128:(c + 1) * 128], fxn_tm[:, c * 128:(c + 1) * 128],
                     identb[:, :], start=True, stop=True) for c in range(8)][-1],
            waits=(("s_dve", rcp), ("s_dve", prev_f_ev),
                   ("s_sy", dbg_ev.get("dxn"))), inc="s_pe")
        prev_f_ev = S.op("dve", lambda e, t=t: e.tensor_copy(
            xnq8[:, :, t * 128:(t + 1) * 128],
            psc[:, 0:1024].rearrange("p (c n) -> p c n", c=8)),
            waits=(("s_pe", trp),), inc="s_dve")
        prev_f_pe = trp

    gpw = ar[:, GP_:GP_ + 4096]
    upw = ar[:, UP_:UP_ + 4096]
    dpw = attn_fm[:, 12288:16384]
    h_fm = attn_fm[:, 0:8192]
    silu_f = attn_fm[:, 8192:12288].bitcast(F32)
    last_comb = prev_f_ev
    last_pe = prev_f_pe
    for ex in range(8):
        gp_in = ap_in["gpT"][ex].rearrange("(c p) n -> p c n", p=128)
        up_in = ap_in["upT"][ex].rearrange("(c p) n -> p c n", p=128)
        dp_in = ap_in["dpT"][ex].rearrange("(c p) n -> p c n", p=128)
        wg = gpdma(lambda e, gp_in=gp_in: e.dma_start(
            out=gpw.rearrange("p (c n) -> p c n", c=8), in_=gp_in),
            waits=(("s_pe", last_pe),))
        wu = gpdma(lambda e, up_in=up_in: e.dma_start(
            out=upw.rearrange("p (c n) -> p c n", c=8), in_=up_in))
        wd = gpdma(lambda e, dp_in=dp_in: e.dma_start(
            out=dpw.rearrange("p (c n) -> p c n", c=4), in_=dp_in),
            waits=(("s_pe", last_pe), ("s_dve", last_comb),
                   ("s_sy", dbg_ev.get("datn"))))
        hev = last_comb
        for ib in range(4):
            def gu(e, ib=ib):
                last = None
                for half, wten in ((0, gpw), (1, upw)):
                    for b in range(4):
                        for c in range(8):
                            last = e.matmul(
                                psc[:, half * 2048 + b * 512:half * 2048 + (b + 1) * 512],
                                wten[:, c * 512 + ib * 128:c * 512 + ib * 128 + 128],
                                xn_fm[:, c * 2048 + b * 512:c * 2048 + (b + 1) * 512],
                                start=(c == 0), stop=(c == 7))
                return last
            gum = S.op("pe", gu, waits=(("s_gp", wu), ("s_dve", hev),
                                        ("s_sy", dbg_ev.get("datn"))), inc="s_pe")
            sil = S.op("act", lambda e: (
                [e.activation(silu_f[:, j * 512:(j + 1) * 512],
                              psc[:, j * 512:(j + 1) * 512], AF.Silu)
                 for j in range(4)],
                e.activation(psc[:, 0:256], silu_f[:, 0:256], AF.Square))[-1],
                waits=(("s_pe", gum),), inc="s_act")
            hev = S.op("dve", lambda e, ib=ib: e.tensor_mul(
                h_fm[:, ib * 2048:(ib + 1) * 2048], silu_f[:, 0:2048],
                psc[:, 2048:4096]),
                waits=(("s_act", sil),), inc="s_dve")
        for t in range(NT):
            def down(e, t=t):
                last = None
                for b in range(2):
                    for ic in range(4):
                        last = e.matmul(psc[:, b * 512:(b + 1) * 512],
                                        h_fm[:, ic * 2048 + t * 128:ic * 2048 + (t + 1) * 128],
                                        dpw[:, ic * 1024 + b * 512:ic * 1024 + (b + 1) * 512],
                                        start=(ic == 0), stop=(ic == 3))
                return last
            dm = S.op("pe", down, waits=(("s_gp", wd), ("s_dve", hev),
                                         ("s_dve", last_comb)), inc="s_pe")
            last_comb = S.op("dve", lambda e, t=t, ex=ex: e.scalar_tensor_tensor(
                oacc[:, t * 1024:(t + 1) * 1024], psc[:, 0:1024],
                cw_sb[:, t * 8 + ex:t * 8 + ex + 1],
                oacc[:, t * 1024:(t + 1) * 1024],
                ALU.mult, ALU.add),
                waits=(("s_pe", dm), ("s_dve", ev_c_dve)), inc="s_dve")
            last_pe = dm

    fin = sydma(lambda e: e.dma_start(
        out=out_d.rearrange("(t p) d -> p t d", p=128),
        in_=oacc.rearrange("p (t d) -> p t d", t=16)),
        waits=(("s_dve", last_comb),))
    S.plan["sy"].append(lambda e, fin=fin: e.wait_ge(S.sem["s_sy"], fin))


_NC_CACHE = {}


def _steps(par):
    s = np.zeros((2 * NT, L), dtype=np.float32)
    q = np.arange(L)
    for t in range(NT):
        l0 = 256 * t + 128 * par
        c_lo2 = max(0, 256 * t - 2942) - 128 * (1 - par)
        c_hi = l0 + 1151
        s[2 * t] = (q < c_lo2).astype(np.float32)
        s[2 * t + 1] = (q >= c_hi).astype(np.float32)
    return s


def kernel(L0_attn_x, L0_attn_Wq, L0_attn_bq, L0_attn_Wk, L0_attn_bk, L0_attn_Wv,
           L0_attn_bv, L0_attn_Wo, L0_attn_bo, L0_attn_relative_k, L0_attn_relative_v,
           L0_ffn_gate_weight, L0_ffn_e_score_correction_bias, L0_ffn_gate_proj,
           L0_ffn_up_proj, L0_ffn_down_proj):
    x = np.ascontiguousarray(np.asarray(L0_attn_x, dtype=np.float32))
    C = np.ascontiguousarray

    fm = np.zeros((8, 8), np.float32)
    for e_ in range(8):
        fm[e_, :e_] = 1.0
    hm = np.zeros((4, 4), np.float32)
    for g_ in range(4):
        hm[g_, :g_] = 1.0
    eye = np.eye(128, dtype=np.float32)

    shared = {
        "wqT": C(np.asarray(L0_attn_Wq, np.float32).T),
        "wkT": C(np.asarray(L0_attn_Wk, np.float32).T),
        "wvT": C(np.asarray(L0_attn_Wv, np.float32).T),
        "woT": C(np.asarray(L0_attn_Wo, np.float32).T),
        "bq": C(np.asarray(L0_attn_bq, np.float32).reshape(1, D)),
        "bk": C(np.asarray(L0_attn_bk, np.float32).reshape(1, D)),
        "bv": C(np.asarray(L0_attn_bv, np.float32).reshape(1, D)),
        "bo": C(np.asarray(L0_attn_bo, np.float32).reshape(1, D)),
        "rkT": C(np.asarray(L0_attn_relative_k, np.float32).T),
        "rv": C(np.asarray(L0_attn_relative_v, np.float32)),
        "gwT": C(np.asarray(L0_ffn_gate_weight, np.float32).T),
        "ebias": C(np.asarray(L0_ffn_e_score_correction_bias,
                              np.float32).reshape(1, 8)),
        "gpT": C(np.asarray(L0_ffn_gate_proj, np.float32).transpose(0, 2, 1)),
        "upT": C(np.asarray(L0_ffn_up_proj, np.float32).transpose(0, 2, 1)),
        "dpT": C(np.asarray(L0_ffn_down_proj, np.float32).transpose(0, 2, 1)),
        "identf": eye,
        "masklo": np.tril(np.ones((128, 128), np.float32), -1),
        "maskup": np.triu(np.ones((128, 128), np.float32), 1),
        "fmask64": np.tile(fm.reshape(1, 64), (128, 1)),
        "hmask16": np.tile(hm.reshape(1, 16), (128, 1)),
    }

    if "nc" not in _NC_CACHE:
        _NC_CACHE["nc"] = build_program()
    nc = _NC_CACHE["nc"]

    in_maps = []
    for c in range(8):
        b = c // 2
        par = c % 2
        m = dict(shared)
        m["xb"] = C(x[b])
        m["xq"] = C(x[b].reshape(32, 128, D)[par::2].reshape(NQ, D))
        m["steps"] = _steps(par)
        m["mcol"] = np.tile(np.array([[1.0 - par, float(par)]], np.float32),
                            (128, 1))
        m["idlo"] = C(eye * (1.0 - par))
        m["idhi"] = C(eye * par)
        in_maps.append(m)

    res = run_bass_kernel_spmd(nc, in_maps, core_ids=list(range(8)))
    out = np.empty((B, L, D), dtype=np.float32)
    for c in range(8):
        b = c // 2
        par = c % 2
        out[b].reshape(32, 128, D)[par::2] = res.results[c]["out"].reshape(NT, 128, D)
    return out


if __name__ == "__main__":
    build_program()
    print("program built ok")


# version 11
# speedup vs baseline: 1.4387x; 1.4141x over previous
import numpy as np
from contextlib import ExitStack

import concourse.bass as bass
import concourse.mybir as mybir
from concourse.bass_utils import run_bass_kernel_spmd

F32 = mybir.dt.float32
BF16 = mybir.dt.bfloat16
AF = mybir.ActivationFunctionType
ALU = mybir.AluOpType
AX = mybir.AxisListType

B, L, D = 4, 4096, 1024
NH, DK = 16, 64
NQ = 2048
NT = 16
NJ = 2047
SCALE = 0.125

IMG_W = 4348
IMG_FL = 128
IMG_RS = IMG_W + 2 * IMG_FL
PW2 = 4349
BU_W = 4224
AUF_W = 4222
PB_L, PB_R = 3072, 1024
PB_W = PB_L + L + PB_R

AR_W = 43008
KA_, KB_ = 0, 4096
QA_, QB_ = 8192, 10240
VV_ = 12288
WW_ = 16416
QR_ = 19488
BU_ = 21792
PBUF_ = 26016
AUF_ = 34208
PT_ = 38430
APT_ = 40478
ATR_ = 42526
XT_, SQ_, XN32_, XNTM_, XF_ = 0, 2048, 4096, 6144, 8192
OACC_, WO_, XTE_ = 0, 32768, 40960
FXT_, FSQ_, FXN_ = 32768, 34816, 36864
GP_, UP_ = 32768, 36864

ENGS = ("pe", "dve", "act", "sy", "gp")

DBG = False

(MNEG, RSA, RSB, RSUM, RINV, CSLL, CSLH, CSRL, CSRH, RS0, RS2, WSUM,
 MEAN, VAR, RSTD, SSQ, T1, T2, T3, T4, QK0, QK2, EPS) = range(23)
SMEAN0 = 23
SRSTD0 = 55


class Sched:
    def __init__(self):
        self.plan = {k: [] for k in ENGS}
        self.cnt = {}
        self.sem = {}

    def add_sem(self, name, handle):
        self.sem[name] = handle
        self.cnt[name] = 0

    def op(self, eng, fn, waits=(), inc=None, n=1):
        val = None
        if inc is not None:
            self.cnt[inc] += n
            val = self.cnt[inc]
        wl = tuple((s, v) for (s, v) in waits if v is not None and v > 0)
        sems = self.sem

        def run(e, fn=fn, wl=wl, inc=inc, n=n):
            seen = {}
            for s, v in wl:
                if seen.get(s, -1) < v:
                    seen[s] = v
            for s, v in seen.items():
                e.wait_ge(sems[s], v)
            ins = fn(e)
            if inc is not None:
                ins.then_inc(sems[inc], n)

        self.plan[eng].append(run)
        return val


def build_program():
    nc = bass.Bass("TRN2", target_bir_lowering=False)

    ap_in = {}
    def din(name, shape, dt=F32):
        ap_in[name] = nc.dram_tensor(name, shape, dt, kind="ExternalInput").ap()
        return ap_in[name]

    din("xb", [L, D]); din("xq", [NQ, D])
    din("wqT", [D, D]); din("wkT", [D, D]); din("wvT", [D, D]); din("woT", [D, D])
    din("bq", [1, D]); din("bk", [1, D]); din("bv", [1, D]); din("bo", [1, D])
    din("rkT", [DK, NJ]); din("rv", [NJ, DK])
    din("gwT", [D, 8]); din("ebias", [1, 8])
    din("gpT", [8, D, 512]); din("upT", [8, D, 512]); din("dpT", [8, 512, D])
    din("identf", [128, 128])
    din("idlo", [128, 128]); din("idhi", [128, 128])
    din("masklo", [128, 128]); din("maskup", [128, 128])
    din("fmask64", [128, 64]); din("hmask16", [128, 16])
    din("steps", [2 * NT, L])
    din("mcol", [128, 2])

    out_d = nc.dram_tensor("out", [NQ, D], F32, kind="ExternalOutput").ap()
    qrimg = nc.dram_tensor("qrimg", [2 * 128 * IMG_RS], BF16, kind="Internal")
    pimg = nc.dram_tensor("pimg", [2 * 128 * PW2], BF16, kind="Internal")
    dbg = {}
    if DBG:
        dbg["dxn"] = nc.dram_tensor("dxn", [128, 8 * L], BF16,
                                    kind="ExternalOutput").ap()
        dbg["dcw"] = nc.dram_tensor("dcw", [128, 128], F32,
                                    kind="ExternalOutput").ap()
        dbg["datn"] = nc.dram_tensor("datn", [128, 8 * NQ], BF16,
                                     kind="ExternalOutput").ap()
        dbg["dscr"] = nc.dram_tensor("dscr", [128, NT * 4096], BF16,
                                     kind="ExternalOutput").ap()
        dbg["dpb"] = nc.dram_tensor("dpb", [128, NT * 4096], BF16,
                                    kind="ExternalOutput").ap()
        dbg["dstats"] = nc.dram_tensor("dstats", [128, NT * 32], F32,
                                       kind="ExternalOutput").ap()
        dbg["drelv"] = nc.dram_tensor("drelv", [128, NT * 129], F32,
                                      kind="ExternalOutput").ap()
        dbg["dauf"] = nc.dram_tensor("dauf", [128, NT * AUF_W], BF16,
                                     kind="ExternalOutput").ap()
        dbg["datr"] = nc.dram_tensor("datr", [128, NT * 64], BF16,
                                     kind="ExternalOutput").ap()
        dbg["dgt"] = nc.dram_tensor("dgt", [128, NT * 80], F32,
                                    kind="ExternalOutput").ap()

    es = ExitStack()
    with es:
        S = Sched()
        for s in ("s_pe", "s_dve", "s_act", "s_sy", "s_gp"):
            S.add_sem(s, es.enter_context(nc.semaphore(s)))

        t_ = {}
        def sb(name, shape, dt):
            t_[name] = es.enter_context(nc.sbuf_tensor(name, shape, dt))
            return t_[name]

        sb("arena", [128, AR_W], BF16)
        sb("xn_fm", [128, 8 * L], BF16)
        sb("attn_fm", [128, 8 * NQ], BF16)
        sb("stats", [128, 96], F32)
        sb("cw", [128, 128], F32)
        sb("identb", [128, 128], BF16)
        sb("identf_sb", [128, 128], F32)
        sb("idlo_sb", [128, 128], BF16)
        sb("idhi_sb", [128, 128], BF16)
        sb("masklo_sb", [128, 128], BF16)
        sb("maskup_sb", [128, 128], BF16)
        sb("fmask_sb", [128, 64], F32)
        sb("hmask_sb", [128, 16], F32)
        sb("mcol_sb", [128, 2], F32)
        sb("rkT_sb", [64, 2048], BF16)
        sb("rv_sb", [128, 16 * 65], BF16)
        sb("gw_sb", [128, 64], F32)
        sb("eb_sb", [1, 8], F32)
        sb("ebb_sb", [128, 8], F32)
        sb("rv0b", [128, 64], F32)
        sb("rv2b", [128, 64], F32)
        sb("bias_sb", [1, 4 * D], BF16)
        sb("ones_row", [1, 512], BF16)
        sb("onesf_row", [1, 128], F32)
        sb("qkT_row", [2, 128], BF16)
        sb("relv_f", [128, 65], F32)
        sb("add1_f", [128, 64], F32)
        sb("gt8", [128, 80], F32)
        t_["psc"] = es.enter_context(nc.psum_tensor("psc", [128, 4096], F32))

        with nc.Block() as block:
            build_plan(S, t_, ap_in, out_d, qrimg, pimg, dbg)
            for eng, attr in (("sy", "sync"), ("gp", "gpsimd"), ("pe", "tensor"),
                              ("dve", "vector"), ("act", "scalar")):
                lst = S.plan[eng]
                def runner(e, lst=lst, eng=eng):
                    for i, f in enumerate(lst):
                        try:
                            f(e)
                        except Exception:
                            print(f"FAILING OP: eng={eng} index={i}/{len(lst)}")
                            raise
                getattr(block, attr)(runner)
    return nc


def build_plan(S, t_, ap_in, out_d, qrimg, pimg, dbg):
    ar = t_["arena"]; xn_fm = t_["xn_fm"]; attn_fm = t_["attn_fm"]
    stats = t_["stats"]; cw_sb = t_["cw"]
    identb = t_["identb"]; identf_sb = t_["identf_sb"]
    idlo_sb = t_["idlo_sb"]; idhi_sb = t_["idhi_sb"]
    masklo_sb = t_["masklo_sb"]; maskup_sb = t_["maskup_sb"]
    fmask_sb = t_["fmask_sb"]; hmask_sb = t_["hmask_sb"]; mcol_sb = t_["mcol_sb"]
    rkT_sb = t_["rkT_sb"]; rv_sb = t_["rv_sb"]; gw_sb = t_["gw_sb"]
    eb_sb = t_["eb_sb"]; ebb_sb = t_["ebb_sb"]; rv0b = t_["rv0b"]; rv2b = t_["rv2b"]
    bias_sb = t_["bias_sb"]; ones_row = t_["ones_row"]; onesf_row = t_["onesf_row"]
    qkT_row = t_["qkT_row"]
    relv_f = t_["relv_f"]; add1_f = t_["add1_f"]
    gt8 = t_["gt8"]; psc = t_["psc"]

    def st(i):
        return stats[:, i:i + 1]

    def arf(c0, ncols):
        return ar[:, c0:c0 + 2 * ncols].bitcast(F32)

    def dram_flat(th, off, rowstride, p, width):
        return bass.AP(tensor=th, offset=off, ap=[[rowstride, p], [1, width]])

    m_lo = mcol_sb[:, 0:1]
    m_hi = mcol_sb[:, 1:2]

    def gpdma(fn, waits=()):
        return S.op("gp", fn, waits=waits, inc="s_gp", n=16)

    def sydma(fn, waits=()):
        return S.op("sy", fn, waits=waits, inc="s_sy", n=16)

    def sydma_split(fn2, waits=()):
        def g(e, fn2=fn2):
            fn2(e, 0).then_inc(S.sem["s_sy"], 16)
            return fn2(e, 64)
        S.cnt["s_sy"] += 16
        return S.op("sy", g, waits=waits, inc="s_sy", n=16)

    ld0 = gpdma(lambda e: e.dma_start(out=identb[:, :], in_=ap_in["identf"][:, :]))
    gpdma(lambda e: e.dma_start(out=identf_sb[:, :], in_=ap_in["identf"][:, :]))
    gpdma(lambda e: e.dma_start(out=idlo_sb[:, :], in_=ap_in["idlo"][:, :]))
    gpdma(lambda e: e.dma_start(out=idhi_sb[:, :], in_=ap_in["idhi"][:, :]))
    gpdma(lambda e: e.dma_start(out=masklo_sb[:, :], in_=ap_in["masklo"][:, :]))
    gpdma(lambda e: e.dma_start(out=maskup_sb[:, :], in_=ap_in["maskup"][:, :]))
    gpdma(lambda e: e.dma_start(out=fmask_sb[:, :], in_=ap_in["fmask64"][:, :]))
    gpdma(lambda e: e.dma_start(out=hmask_sb[:, :], in_=ap_in["hmask16"][:, :]))
    gpdma(lambda e: e.dma_start(out=mcol_sb[:, :], in_=ap_in["mcol"][:, :]))
    gpdma(lambda e: e.dma_start(out=rkT_sb[0:64, 0:NJ], in_=ap_in["rkT"][:, :]))
    gpdma(lambda e: e.dma_start(out=gw_sb[:, :].rearrange("p (c e) -> p c e", c=8),
                                in_=ap_in["gwT"].rearrange("(c p) e -> p c e", p=128)))
    gpdma(lambda e: e.dma_start(out=eb_sb[:, :], in_=ap_in["ebias"][:, :]))
    for i, nm in enumerate(("bq", "bk", "bv", "bo")):
        gpdma(lambda e, i=i, nm=nm: e.dma_start(
            out=bias_sb[0:1, i * D:(i + 1) * D], in_=ap_in[nm][:, :]))
    mrv = S.op("dve", lambda e: e.memset(rv_sb[:, :], 0.0), inc="s_dve")
    rv_last = None
    for k in range(16):
        r0, r1 = k * 128, min(NJ, (k + 1) * 128)
        rv_last = gpdma(lambda e, k=k, r0=r0, r1=r1: e.dma_start(
            out=rv_sb[0:r1 - r0, k * 65:k * 65 + 64], in_=ap_in["rv"][r0:r1, :]),
            waits=(("s_dve", mrv),))
    ld_consts = rv_last

    S.op("dve", lambda e: e.memset(ones_row[:, :], 1.0), inc="s_dve")
    S.op("dve", lambda e: e.memset(onesf_row[:, :], 1.0), inc="s_dve")
    ev_eps = S.op("dve", lambda e: (
        e.memset(st(EPS), 1e-5),
        e.memset(st(MNEG), 0.0))[-1], inc="s_dve")
    rv_ones = S.op("dve", lambda e: e.memset(
        rv_sb[:, :].rearrange("p (k c) -> p k c", c=65)[:, :, 64:65], 1.0),
        waits=(("s_gp", rv_last),), inc="s_dve")
    S.op("dve", lambda e: e.memset(
        ar[:, VV_:VV_ + 4128].rearrange("p (k c) -> p k c", c=129)[:, :, 64:65], 1.0),
        inc="s_dve")
    S.op("dve", lambda e: e.memset(ar[:, PBUF_:PBUF_ + PB_L], 0.0), inc="s_dve")
    S.op("dve", lambda e: e.memset(ar[:, PBUF_ + PB_L + L:PBUF_ + PB_W], 0.0),
         inc="s_dve")
    mzf = S.op("dve", lambda e: e.memset(ar[:, BU_:BU_ + 128], 0.0), inc="s_dve")
    fl_ev = None
    for slot in range(2):
        for off in (0, IMG_FL + IMG_W):
            fl_ev = sydma(lambda e, slot=slot, off=off: e.dma_start(
                out=dram_flat(qrimg, slot * 128 * IMG_RS + off, IMG_RS, 128, IMG_FL),
                in_=ar[:, BU_:BU_ + 128]),
                waits=(("s_dve", mzf),))

    pe_eb = S.op("pe", lambda e: e.matmul(psc[:, 0:8], onesf_row[0:1, :],
                                          eb_sb[0:1, :], start=True, stop=True),
                 waits=(("s_gp", ld_consts),), inc="s_pe")
    pe_rv0 = S.op("pe", lambda e: e.matmul(psc[:, 512:576], ones_row[0:1, 0:128],
                                           rv_sb[0:1, 0:64], start=True, stop=True),
                  waits=(("s_dve", rv_ones),), inc="s_pe")
    cp2046 = sydma(lambda e: e.dma_start(out=qkT_row[0:1, 0:64],
                                         in_=rv_sb[126:127, 15 * 65:15 * 65 + 64]),
                   waits=(("s_dve", rv_ones),))
    pe_rv2 = S.op("pe", lambda e: e.matmul(psc[:, 1024:1088], ones_row[0:1, 0:128],
                                           qkT_row[0:1, 0:64], start=True, stop=True),
                  waits=(("s_sy", cp2046),), inc="s_pe")
    ev_ebb = S.op("dve", lambda e: (
        e.tensor_copy(ebb_sb[:, :], psc[:, 0:8]),
        e.tensor_copy(rv0b[:, :], psc[:, 512:576]),
        e.tensor_copy(rv2b[:, :], psc[:, 1024:1088]))[-1],
        waits=(("s_pe", pe_rv2), ("s_pe", pe_rv0), ("s_pe", pe_eb)), inc="s_dve")

    x_t = arf(XT_, 1024)
    sq_t = arf(SQ_, 1024)
    xn_tm = ar[:, XNTM_:XNTM_ + 1024]
    xb = ap_in["xb"]; xq = ap_in["xq"]

    xv8 = xn_fm[:, :].rearrange("p (c n) -> p c n", c=8)
    prev_tr, prev_ev = 0, 0
    for t in range(32):
        lx = sydma(lambda e, t=t: e.dma_start(out=x_t[:, :],
                                              in_=xb[t * 128:(t + 1) * 128, :]),
                   waits=(("s_pe", prev_tr),))
        l1 = S.op("dve", lambda e: (
            e.reduce_sum(st(T1), x_t[:, :], AX.X),
            e.tensor_tensor(sq_t[:, :], x_t[:, :], x_t[:, :], ALU.mult))[-1],
            waits=(("s_sy", lx),), inc="s_dve")
        l2 = S.op("dve", lambda e, t=t: (
            e.reduce_sum(st(SSQ), sq_t[:, :], AX.X),
            e.tensor_scalar_mul(st(SMEAN0 + t), st(T1), 1.0 / 1024.0))[-1],
            waits=(("s_dve", l1),), inc="s_dve")
        l3 = S.op("dve", lambda e, t=t: (
            e.tensor_scalar_mul(st(VAR), st(SSQ), 1.0 / 1024.0),
            e.tensor_tensor(st(T2), st(SMEAN0 + t), st(SMEAN0 + t), ALU.mult))[-1],
            waits=(("s_dve", l2),), inc="s_dve")
        lnc = S.op("dve", lambda e: e.tensor_sub(st(VAR), st(VAR), st(T2)),
                   waits=(("s_dve", l3),), inc="s_dve")
        S.op("act", lambda e: e.activation(st(T1), st(VAR), AF.Sqrt, bias=st(EPS)),
             waits=(("s_dve", lnc),))
        rstd = S.op("act", lambda e: e.activation(sq_t[:, 0:512], x_t[:, 0:512],
                                                  AF.Square), inc="s_act")
        rc1 = S.op("dve", lambda e, t=t: e.reciprocal(st(SRSTD0 + t), st(T1)),
                   waits=(("s_act", rstd), ("s_pe", prev_tr)), inc="s_dve")
        rcp = S.op("dve", lambda e, t=t: e.tensor_scalar(
            xn_tm[:, :], x_t[:, :], st(SMEAN0 + t), st(SRSTD0 + t),
            ALU.subtract, ALU.mult),
            waits=(("s_dve", rc1),), inc="s_dve")
        trp = S.op("pe", lambda e: [
            e.matmul(psc[:, c * 128:(c + 1) * 128], xn_tm[:, c * 128:(c + 1) * 128],
                     identb[:, :], start=True, stop=True) for c in range(8)][-1],
            waits=(("s_dve", rcp), ("s_gp", ld0), ("s_dve", ev_ebb),
                   ("s_dve", prev_ev)), inc="s_pe")
        prev_ev = S.op("dve", lambda e, t=t: e.tensor_copy(
            xv8[:, :, t * 128:(t + 1) * 128],
            psc[:, 0:1024].rearrange("p (c n) -> p c n", c=8)),
            waits=(("s_pe", trp),), inc="s_dve")
        prev_tr = trp
    ev_xn_done = prev_ev
    ev_b_pe = prev_tr
    dbg_ev = {}
    if DBG:
        dbg_ev["dxn"] = sydma(lambda e: e.dma_start(out=dbg["dxn"], in_=xn_fm[:, :]),
                              waits=(("s_dve", ev_xn_done),))

    xf32 = arf(XF_, 1024)
    xn32 = arf(XN32_, 1024)
    sC = gt8[:, 0:8]; cC = gt8[:, 8:16]; G4 = gt8[:, 16:20]
    cnt4 = gt8[:, 20:24]; gm4 = gt8[:, 24:28]
    A16 = gt8[:, 32:48]; C16 = gt8[:, 48:64]
    sm8 = gt8[:, 64:72]; tmp8 = gt8[:, 72:80]

    prev_c_pe = ev_b_pe
    ev_cw = 0
    prev_sig = 0
    prev_dgt = 0
    for t in range(NT):
        lx = sydma(lambda e, t=t: e.dma_start(out=x_t[:, :],
                                              in_=xq[t * 128:(t + 1) * 128, :]),
                   waits=(("s_pe", prev_c_pe), ("s_dve", ev_xn_done)))
        xsel = S.op("dve", lambda e, t=t: (
            e.tensor_scalar_mul(st(T1), st(SMEAN0 + 2 * t), m_lo),
            e.tensor_scalar_mul(st(T2), st(SRSTD0 + 2 * t), m_lo))[-1],
            waits=(("s_sy", lx), ("s_pe", prev_c_pe)), inc="s_dve")
        xsel = S.op("dve", lambda e, t=t: (
            e.scalar_tensor_tensor(st(T1), st(SMEAN0 + 2 * t + 1), m_hi, st(T1),
                                   ALU.mult, ALU.add),
            e.scalar_tensor_tensor(st(T2), st(SRSTD0 + 2 * t + 1), m_hi, st(T2),
                                   ALU.mult, ALU.add))[-1],
            waits=(("s_dve", xsel),), inc="s_dve")
        xnf = S.op("dve", lambda e: e.tensor_scalar(
            xn32[:, :], x_t[:, :], st(T1), st(T2), ALU.subtract, ALU.mult),
            waits=(("s_dve", xsel),), inc="s_dve")
        trp = S.op("pe", lambda e: [
            e.transpose(psc[:, c * 128:(c + 1) * 128], xn32[:, c * 128:(c + 1) * 128],
                        identf_sb[:, :]) for c in range(8)][-1],
            waits=(("s_dve", xnf),), inc="s_pe")
        evf = S.op("dve", lambda e: e.tensor_copy(xf32[:, :], psc[:, 0:1024]),
                   waits=(("s_pe", trp),), inc="s_dve")
        lg = S.op("pe", lambda e: [
            e.matmul(psc[:, 1024:1032], xf32[:, c * 128:(c + 1) * 128],
                     gw_sb[:, c * 8:(c + 1) * 8], start=(c == 0), stop=(c == 7))
            for c in range(8)][-1],
            waits=(("s_dve", evf), ("s_act", prev_sig)), inc="s_pe")
        S.op("act", lambda e: e.activation(sC, psc[:, 1024:1032], AF.Sigmoid),
             waits=(("s_pe", lg), ("s_dve", ev_cw), ("s_sy", prev_dgt)))
        sig = S.op("act", lambda e: e.activation(xf32[:, 0:512], xf32[:, 512:1024],
                                                 AF.Square), inc="s_act")
        prev_sig = sig
        cprev = S.op("dve", lambda e: e.tensor_add(cC, sC, ebb_sb[:, :]),
                     waits=(("s_act", sig), ("s_dve", ev_ebb)), inc="s_dve")
        cprev = S.op("dve", lambda e: e.tensor_add(G4, cC[:, 0:8:2], cC[:, 1:8:2]),
                     waits=(("s_dve", cprev),), inc="s_dve")
        cprev = S.op("dve", lambda e: (
            e.tensor_tensor(A16.rearrange("p (a b) -> p a b", a=4),
                            G4.unsqueeze(2).broadcast_to((128, 4, 4)),
                            G4.unsqueeze(1).broadcast_to((128, 4, 4)),
                            ALU.is_lt),
            e.tensor_tensor(C16.rearrange("p (a b) -> p a b", a=4),
                            G4.unsqueeze(2).broadcast_to((128, 4, 4)),
                            G4.unsqueeze(1).broadcast_to((128, 4, 4)),
                            ALU.is_equal))[-1],
            waits=(("s_dve", cprev),), inc="s_dve")
        cprev = S.op("dve", lambda e: e.tensor_tensor(C16, C16, hmask_sb[:, :],
                                                      ALU.mult),
                     waits=(("s_dve", cprev),), inc="s_dve")
        cprev = S.op("dve", lambda e: e.tensor_add(A16, A16, C16),
                     waits=(("s_dve", cprev),), inc="s_dve")
        cprev = S.op("dve", lambda e: e.tensor_reduce(
            cnt4, A16.rearrange("p (a b) -> p a b", a=4), AX.X, ALU.add),
            waits=(("s_dve", cprev),), inc="s_dve")
        cprev = S.op("dve", lambda e: e.tensor_single_scalar(gm4, cnt4, 2.0,
                                                             ALU.is_lt),
                     waits=(("s_dve", cprev),), inc="s_dve")
        cprev = S.op("dve", lambda e: e.tensor_copy(
            sm8.rearrange("p (g j) -> p g j", g=4),
            gm4.unsqueeze(2).broadcast_to((128, 4, 2))),
            waits=(("s_dve", cprev),), inc="s_dve")
        cprev = S.op("dve", lambda e: e.tensor_tensor(tmp8, cC, sm8, ALU.mult),
                     waits=(("s_dve", cprev),), inc="s_dve")
        cprev = S.op("dve", lambda e: (
            e.tensor_tensor(relv_f[:, 0:64].rearrange("p (a b) -> p a b", a=8),
                            tmp8.unsqueeze(2).broadcast_to((128, 8, 8)),
                            tmp8.unsqueeze(1).broadcast_to((128, 8, 8)),
                            ALU.is_lt),
            e.tensor_tensor(add1_f[:, 0:64].rearrange("p (a b) -> p a b", a=8),
                            tmp8.unsqueeze(2).broadcast_to((128, 8, 8)),
                            tmp8.unsqueeze(1).broadcast_to((128, 8, 8)),
                            ALU.is_equal))[-1],
            waits=(("s_dve", cprev),), inc="s_dve")
        cprev = S.op("dve", lambda e: e.tensor_tensor(
            add1_f[:, 0:64], add1_f[:, 0:64], fmask_sb[:, :], ALU.mult),
            waits=(("s_dve", cprev),), inc="s_dve")
        cprev = S.op("dve", lambda e: e.tensor_add(
            relv_f[:, 0:64], relv_f[:, 0:64], add1_f[:, 0:64]),
            waits=(("s_dve", cprev),), inc="s_dve")
        cprev = S.op("dve", lambda e: e.tensor_reduce(
            tmp8, relv_f[:, 0:64].rearrange("p (a b) -> p a b", a=8), AX.X, ALU.add),
            waits=(("s_dve", cprev),), inc="s_dve")
        cprev = S.op("dve", lambda e: e.tensor_single_scalar(tmp8, tmp8, 2.0,
                                                             ALU.is_lt),
                     waits=(("s_dve", cprev),), inc="s_dve")
        cprev = S.op("dve", lambda e: e.tensor_tensor(tmp8, tmp8, sC, ALU.mult),
                     waits=(("s_dve", cprev),), inc="s_dve")
        cprev = S.op("dve", lambda e: e.reduce_sum(st(WSUM), tmp8, AX.X),
                     waits=(("s_dve", cprev),), inc="s_dve")
        cprev = S.op("dve", lambda e: e.tensor_scalar_add(st(T1), st(WSUM), 1e-20),
                     waits=(("s_dve", cprev),), inc="s_dve")
        cprev = S.op("dve", lambda e: e.reciprocal(st(T2), st(T1)),
                     waits=(("s_dve", cprev),), inc="s_dve")
        ev_cw = S.op("dve", lambda e, t=t: e.tensor_scalar_mul(
            cw_sb[:, t * 8:(t + 1) * 8], tmp8, st(T2)),
            waits=(("s_dve", cprev),), inc="s_dve")
        if DBG:
            prev_dgt = sydma(lambda e, t=t: e.dma_start(
                out=dbg["dgt"][:, t * 80:(t + 1) * 80], in_=gt8[:, :]),
                waits=(("s_dve", ev_cw),))
        prev_c_pe = lg
    ev_c_dve = ev_cw
    if DBG:
        sydma(lambda e: e.dma_start(out=dbg["dcw"], in_=cw_sb[:, :]),
              waits=(("s_dve", ev_c_dve),))

    K_aug = [ar[:, KA_:KA_ + 4096], ar[:, KB_:KB_ + 4096]]
    Q_aug = [ar[:, QA_:QA_ + 2048], ar[:, QB_:QB_ + 2048]]
    v_pair = ar[:, VV_:VV_ + 4128]
    ww_raw = ar[:, WW_:WW_ + 1024]
    ww_lo = ar[:, WW_ + 1024:WW_ + 2048]
    ww_hi = ar[:, WW_ + 2048:WW_ + 3072]
    qr_s = ar[:, QR_:QR_ + IMG_W]
    bu = ar[:, BU_:BU_ + BU_W]
    p_b = ar[:, PBUF_:PBUF_ + PB_W]
    au = ar[:, AUF_:AUF_ + AUF_W]
    pt_s = ar[:, PT_:PT_ + 2048]
    apt_s = ar[:, APT_:APT_ + 2048]
    atr_s = ar[:, ATR_:ATR_ + 64]

    def geo(t):
        w0 = max(0, 256 * t - 2942)
        ce = min(BU_W, 256 * t + 1279)
        roff = w0 + 3197 - 256 * t
        wlo2 = max(0, 256 * t - 3070)
        whi2_a = max(0, 256 * t - 3070)
        whi2_b = max(0, 256 * t - 2942)
        return w0, ce, roff, wlo2, whi2_a, whi2_b

    ev = {
        "exp": S.cnt["s_act"],
        "epi_dve": S.cnt["s_dve"],
        "qk_pe": S.cnt["s_pe"],
        "band_pe": S.cnt["s_pe"],
        "aread_sy": [0, 0],
        "aread_gp": [0, 0],
        "pbuf_sy": 0, "pbuf_gp": 0, "pbuf_dve": 0, "pbuf_pe": 0,
        "ev_v": 0,
        "bu_ready": 0,
    }
    gtile = [0]
    pend = [None]

    def emit_front1(hp, hi, t, Ka, Qa):
        g = gtile[0]; gtile[0] += 1
        slot = g % 2
        w0, ce, roff, wlo2, whi2_a, whi2_b = geo(t)

        qr_mm = S.op("pe", lambda e: [
            e.matmul(psc[:, n0:n1], Qa[0:64, t * 128:(t + 1) * 128],
                     rkT_sb[0:64, n0:n1], start=True, stop=True)
            for (n0, n1) in ((0, 512), (512, 1024), (1024, 1536), (1536, 2047))][-1],
            waits=(("s_act", ev["exp"]), ("s_dve", ev["ev_v"]), ("s_gp", ld0),
                   ("s_sy", fl_ev)),
            inc="s_pe")
        qr_ev = S.op("dve", lambda e: (
            e.tensor_copy(stats[:, QK0:QK0 + 1], psc[:, 0:1]),
            e.tensor_copy(stats[:, QK2:QK2 + 1], psc[:, 2046:2047]),
            e.tensor_copy(qr_s[:, 127:127 + 512], psc[:, 0:512]),
            e.tensor_copy(qr_s[:, 639:639 + 512], psc[:, 512:1024]),
            e.tensor_copy(qr_s[:, 1151:1151 + 512], psc[:, 1024:1536]),
            e.tensor_copy(qr_s[:, 1663:1663 + 511], psc[:, 1536:2047]))[-1],
            waits=(("s_pe", qr_mm),), inc="s_dve")
        pads = S.op("dve", lambda e: (
            e.tensor_scalar(qr_s[:, 0:127], identb[:, 0:127], 0.0,
                            stats[:, QK0:QK0 + 1], ALU.mult, ALU.add),
            e.tensor_scalar(qr_s[:, 2174:2301], identb[:, 0:127], 0.0,
                            stats[:, QK2:QK2 + 1], ALU.mult, ALU.add))[-1],
            waits=(("s_dve", qr_ev),), inc="s_dve")
        qkT_mm = S.op("pe", lambda e: e.matmul(
            psc[0:2, 2304:2432], stats[:, QK0:QK0 + 2], identf_sb[:, :],
            start=True, stop=True),
            waits=(("s_dve", qr_ev), ("s_dve", ev.get("qkT_ev", 0))), inc="s_pe")
        qkT_ev = S.op("dve", lambda e: e.tensor_copy(qkT_row[0:2, :],
                                                     psc[0:2, 2304:2432]),
                      waits=(("s_pe", qkT_mm), ("s_sy", ev.get("qrow", 0))),
                      inc="s_dve")
        ev["qkT_ev"] = qkT_ev
        qrow_w = sydma(lambda e, t=t: e.dma_start(
            out=Qa[64:66, t * 128:(t + 1) * 128], in_=qkT_row[0:2, :]),
            waits=(("s_dve", qkT_ev),))
        ev["qrow"] = qrow_w
        step_ev = gpdma(lambda e, t=t: e.dma_start(
            out=Ka[64:66, :], in_=ap_in["steps"][2 * t:2 * t + 2, :]),
            waits=(("s_pe", ev["qk_pe"]),))
        img_w = sydma(lambda e, slot=slot: e.dma_start(
            out=dram_flat(qrimg, slot * 128 * IMG_RS + IMG_FL, IMG_RS, 128, 2174),
            in_=qr_s[:, 0:2174]),
            waits=(("s_dve", qr_ev), ("s_dve", pads)))
        img_w = sydma(lambda e, slot=slot: e.dma_start(
            out=dram_flat(qrimg, slot * 128 * IMG_RS + IMG_FL + 2174, IMG_RS,
                          128, 2174),
            in_=qr_s[:, 127:2301]),
            waits=(("s_dve", qr_ev), ("s_dve", pads)))
        if t == 0:
            bz = S.op("dve", lambda e: e.memset(bu[:, :], 0.0),
                      waits=(("s_pe", ev["band_pe"]),), inc="s_dve")
        else:
            pw0_ = geo(t - 1)[0]
            if w0 > pw0_:
                bz = S.op("dve", lambda e, pw0_=pw0_, w0=w0: e.memset(
                    bu[:, pw0_:w0], 0.0),
                    waits=(("s_pe", ev["band_pe"]),), inc="s_dve")
            else:
                bz = ev["bu_ready"]
        band_r = sydma(lambda e, slot=slot, w0=w0, ce=ce, roff=roff: e.dma_start(
            out=bu[:, w0:ce],
            in_=dram_flat(qrimg, slot * 128 * IMG_RS + roff, IMG_RS - 1,
                          128, ce - w0)),
            waits=(("s_sy", img_w), ("s_pe", ev["band_pe"]), ("s_dve", bz)))
        ev["bu_ready"] = bz
        return dict(hp=hp, hi=hi, t=t, Ka=Ka, Qa=Qa, slot=slot, w0=w0, ce=ce,
                    wlo2=wlo2, whi2_a=whi2_a, whi2_b=whi2_b, band_r=band_r,
                    qrow_w=qrow_w, step_ev=step_ev)

    def emit_front2(fs):
        hp, hi, t, Ka, Qa = fs["hp"], fs["hi"], fs["t"], fs["Ka"], fs["Qa"]
        w0, ce = fs["w0"], fs["ce"]
        band_r, qrow_w, step_ev = fs["band_r"], fs["qrow_w"], fs["step_ev"]
        exp_ps = []
        for PS in (0, 1):
            lo_banks = [b for b in range(4)
                        if max(128 + 512 * (4 * PS + b), w0)
                        < min(128 + 512 * (4 * PS + b) + 512, ce)]
            hi_banks = [b for b in range(4)
                        if max(512 * (4 * PS + b), w0)
                        < min(512 * (4 * PS + b) + 512, ce)]
            def qk(e, lo_banks=lo_banks, hi_banks=hi_banks, t=t, PS=PS):
                last = None
                for b in range(4):
                    gb = 4 * PS + b
                    nadd = (b in lo_banks) + (b in hi_banks)
                    last = e.matmul(psc[:, b * 512:(b + 1) * 512],
                                    Qa[0:66, t * 128:(t + 1) * 128],
                                    Ka[0:66, gb * 512:(gb + 1) * 512],
                                    start=True, stop=(nadd == 0))
                    if b in lo_banks:
                        nadd -= 1
                        last = e.matmul(psc[:, b * 512:(b + 1) * 512],
                                        idlo_sb[:, :],
                                        bu[:, 128 + 512 * gb:640 + 512 * gb],
                                        start=False, stop=(nadd == 0))
                    if b in hi_banks:
                        last = e.matmul(psc[:, b * 512:(b + 1) * 512],
                                        idhi_sb[:, :],
                                        bu[:, 512 * gb:512 * gb + 512],
                                        start=False, stop=True)
                return last
            qk_mm = S.op("pe", qk, waits=(
                ("s_sy", band_r), ("s_sy", qrow_w), ("s_gp", step_ev),
                ("s_act", exp_ps[-1] if exp_ps else ev["exp"])), inc="s_pe")
            dscr_ev = None
            if DBG and hp == 0 and hi == 0:
                dcp = S.op("dve", lambda e: e.tensor_copy(
                    ar[:, PT_:PT_ + 4096].bitcast(F32)[:, 0:2048], psc[:, 0:2048]),
                    waits=(("s_pe", qk_mm),), inc="s_dve")
                dscr_ev = sydma(lambda e, t=t, PS=PS: e.dma_start(
                    out=dbg["dscr"][:, t * 4096 + 2048 * PS:
                                    t * 4096 + 2048 * PS + 2048],
                    in_=ar[:, PT_:PT_ + 4096].bitcast(F32)[:, 0:2048]),
                    waits=(("s_dve", dcp),))
            expp = S.op("act", lambda e, PS=PS: e.activation(
                p_b[:, PB_L + 2048 * PS:PB_L + 2048 * PS + 2048], psc[:, 0:2048],
                AF.Exp, bias=st(MNEG)),
                waits=(("s_pe", qk_mm), ("s_sy", ev["pbuf_sy"]),
                       ("s_gp", ev["pbuf_gp"]),
                       ("s_dve", ev["pbuf_dve"]), ("s_pe", ev["pbuf_pe"]),
                       ("s_sy", dscr_ev)),
                inc="s_act")
            exp_ps.append(expp)
            ev["band_pe"] = qk_mm
            ev["qk_pe"] = qk_mm
        exp_a, expv = exp_ps
        ev["exp"] = expv
        if DBG and hp == 0 and hi == 0:
            ev["pbuf_sy"] = sydma(lambda e, t=t: e.dma_start(
                out=dbg["dpb"][:, t * 4096:(t + 1) * 4096],
                in_=p_b[:, PB_L:PB_L + 4096]),
                waits=(("s_act", expv),))
        fs["exp_a"] = exp_a
        fs["expv"] = expv

    def emit_back1(fs):
        hp, hi, t, Ka, Qa = fs["hp"], fs["hi"], fs["t"], fs["Ka"], fs["Qa"]
        slot, w0, ce = fs["slot"], fs["w0"], fs["ce"]
        wlo2, whi2_a, whi2_b = fs["wlo2"], fs["whi2_a"], fs["whi2_b"]
        exp_a, expv = fs["exp_a"], fs["expv"]
        pw0 = PBUF_ + PB_L + 256 * t - 3070
        pimg_w = gpdma(lambda e, slot=slot, pw0=pw0: e.dma_start(
            out=dram_flat(pimg, slot * 128 * PW2, PW2, 128, PW2),
            in_=ar[:, pw0:pw0 + PW2]),
            waits=(("s_act", expv), ("s_gp", ev["aread_gp"][slot]),
                   ("s_sy", ev["aread_sy"][slot])))
        auf = gpdma(lambda e, slot=slot: e.dma_start(
            out=au[:, :], in_=dram_flat(pimg, slot * 128 * PW2, PW2 + 1,
                                        128, AUF_W)),
            waits=(("s_gp", pimg_w), ("s_pe", ev.get("atf_pe", 0))))
        ev["aread_gp"][slot] = auf
        ev["pbuf_sy"] = 0
        ev["pbuf_gp"] = pimg_w
        def rs0fn(e, wlo2=wlo2, whi2_a=whi2_a, whi2_b=whi2_b):
            if wlo2 > 0:
                last = e.reduce_sum(st(RS0), p_b[:, PB_L:PB_L + wlo2], AX.X)
            else:
                last = e.memset(st(RS0), 0.0)
            if whi2_b > whi2_a:
                last = e.reduce_sum(st(T3), p_b[:, PB_L + whi2_a:PB_L + whi2_b],
                                    AX.X)
            return last
        rs0e = S.op("dve", rs0fn, waits=(("s_act", exp_a),), inc="s_dve")
        if whi2_b > whi2_a:
            rs0e = S.op("dve", lambda e: e.scalar_tensor_tensor(
                st(RS0), st(T3), m_hi, st(RS0), ALU.mult, ALU.add),
                waits=(("s_dve", rs0e),), inc="s_dve")
        axl0 = PBUF_ + PB_L + 256 * t - 3070
        axr0 = PBUF_ + PB_L + 256 * t + 1023
        cs1 = S.op("dve", lambda e, axl0=axl0, axr0=axr0: (
            e.tensor_tensor(apt_s[:, 0:128], ar[:, axl0:axl0 + 128],
                            masklo_sb[:, :], ALU.mult),
            e.tensor_tensor(apt_s[:, 128:256], ar[:, axl0 + 128:axl0 + 256],
                            masklo_sb[:, :], ALU.mult),
            e.tensor_tensor(apt_s[:, 256:384], ar[:, axr0:axr0 + 128],
                            maskup_sb[:, :], ALU.mult),
            e.tensor_tensor(apt_s[:, 384:512], ar[:, axr0 + 128:axr0 + 256],
                            maskup_sb[:, :], ALU.mult))[-1],
            waits=(("s_gp", auf), ("s_act", expv),
                   ("s_pe", ev.get("rv_pe", 0))), inc="s_dve")
        cs2 = S.op("dve", lambda e: (
            e.reduce_sum(st(CSLL), apt_s[:, 0:128], AX.X),
            e.reduce_sum(st(CSLH), apt_s[:, 128:256], AX.X),
            e.reduce_sum(st(CSRL), apt_s[:, 256:384], AX.X),
            e.reduce_sum(st(CSRH), apt_s[:, 384:512], AX.X))[-1],
            waits=(("s_dve", cs1),), inc="s_dve")
        cs3 = S.op("dve", lambda e: (
            e.tensor_tensor(st(T1), st(CSLL), m_lo, ALU.mult),
            e.tensor_tensor(st(T2), st(CSLH), m_hi, ALU.mult),
            e.tensor_tensor(st(T3), st(CSRL), m_lo, ALU.mult),
            e.tensor_tensor(st(T4), st(CSRH), m_hi, ALU.mult))[-1],
            waits=(("s_dve", cs2),), inc="s_dve")
        csum = S.op("dve", lambda e: (
            e.tensor_add(au[:, 0:1], au[:, 0:1], st(T1)),
            e.tensor_add(au[:, 128:129], au[:, 128:129], st(T2)),
            e.tensor_add(au[:, 4093:4094], au[:, 4093:4094], st(T3)),
            e.tensor_add(au[:, 4221:4222], au[:, 4221:4222], st(T4)))[-1],
            waits=(("s_dve", cs3),), inc="s_dve")
        ev["pbuf_dve"] = csum
        if DBG and hp == 0 and hi == 0:
            ev["aread_sy"][slot] = sydma(lambda e, t=t: e.dma_start(
                out=dbg["dauf"][:, t * AUF_W:(t + 1) * AUF_W], in_=au[:, :]),
                waits=(("s_dve", csum),))
        pt_ev = []
        av_mm = None
        for gq in range(8):
            bank_off = 2560 if gq % 2 == 0 else 3072
            ptm = S.op("pe", lambda e, gq=gq, bank_off=bank_off: [
                e.matmul(psc[:, bank_off + j * 128:bank_off + (j + 1) * 128],
                         p_b[:, PB_L + (gq * 4 + j) * 128:PB_L + (gq * 4 + j + 1) * 128],
                         identb[:, :], start=True, stop=True) for j in range(4)][-1],
                waits=(("s_act", exp_a if gq < 4 else expv),
                       ("s_dve", pt_ev[gq - 2] if gq >= 2 else ev.get("pte_last", 0))),
                inc="s_pe")
            pte = S.op("dve", lambda e, gq=gq, bank_off=bank_off: e.tensor_copy(
                pt_s[:, (gq % 4) * 512:(gq % 4 + 1) * 512],
                psc[:, bank_off:bank_off + 512]),
                waits=(("s_pe", ptm),), inc="s_dve")
            pt_ev.append(pte)
            if gq in (3, 7):
                k0 = 0 if gq == 3 else 16
                av_mm = S.op("pe", lambda e, hi=hi, k0=k0: [
                    e.matmul(psc[:, 2048:2113],
                             pt_s[:, (k - k0) * 128:(k - k0 + 1) * 128],
                             v_pair[:, k * 129 + hi * 64:k * 129 + hi * 64 + 65],
                             start=(k == 0), stop=(k == 31))
                    for k in range(k0, k0 + 16)][-1],
                    waits=(("s_dve", pte), ("s_dve", ev["ev_v"])), inc="s_pe")
        ev["pbuf_pe"] = av_mm
        ev["pte_last"] = pt_ev[-1]
        fs["csum"] = csum
        fs["rs0e"] = rs0e
        fs["av_mm"] = av_mm

    def emit_back2(fs):
        hp, hi, t = fs["hp"], fs["hi"], fs["t"]
        csum, rs0e, av_mm = fs["csum"], fs["rs0e"], fs["av_mm"]
        av0 = 2048 + (1 if hi == 1 else 0)
        rs_col = 2112 if hi == 0 else 2048
        apt_ev = []
        for gq in range(4):
            def atf(e, gq=gq, t=t):
                last = None
                for j in range(4):
                    k = gq * 4 + j
                    w = 128 if k < 15 else 127
                    keep_wrap = (k + 2) * 128 > 3070 - 256 * t
                    reg = psc[0:w, 3584 + j * 128:3584 + (j + 1) * 128]
                    last = e.matmul(reg, au[:, 2047 + k * 128:2047 + k * 128 + w],
                                    idlo_sb[:, :], start=True, stop=False)
                    last = e.matmul(reg, au[:, 2175 + k * 128:2175 + k * 128 + w],
                                    idhi_sb[:, :], start=False,
                                    stop=not keep_wrap)
                    if keep_wrap:
                        last = e.matmul(reg, au[:, k * 128:k * 128 + w],
                                        idlo_sb[:, :], start=False, stop=False)
                        last = e.matmul(reg, au[:, 128 + k * 128:128 + k * 128 + w],
                                        idhi_sb[:, :], start=False, stop=True)
                return last
            atm = S.op("pe", atf,
                       waits=(("s_dve", csum),
                              ("s_dve", apt_ev[gq - 1] if gq >= 1
                               else ev.get("ate_last", 0))),
                       inc="s_pe")
            ate = S.op("dve", lambda e, gq=gq: e.tensor_copy(
                apt_s[:, gq * 512:(gq + 1) * 512], psc[:, 3584:3584 + 512]),
                waits=(("s_pe", atm),), inc="s_dve")
            apt_ev.append(ate)
        rv_mm = S.op("pe", lambda e: [
            e.matmul(psc[:, 2113:2178],
                     apt_s[0:(128 if k < 15 else 127), k * 128:(k + 1) * 128],
                     rv_sb[0:(128 if k < 15 else 127), k * 65:(k + 1) * 65],
                     start=(k == 0), stop=(k == 15)) for k in range(16)][-1],
            waits=(("s_dve", apt_ev[-1]),), inc="s_pe")
        epiA = S.op("dve", lambda e, av0=av0, rs_col=rs_col: (
            e.tensor_copy(st(RSUM), psc[:, rs_col:rs_col + 1]),
            e.tensor_copy(relv_f[:, :], psc[:, 2113:2178]),
            e.tensor_copy(add1_f[:, :], psc[:, av0:av0 + 64]))[-1],
            waits=(("s_pe", rv_mm), ("s_pe", av_mm), ("s_dve", rs0e),
                   ("s_dve", ev_ebb)), inc="s_dve")
        epiB = S.op("dve", lambda e: (
            e.reciprocal(st(RINV), st(RSUM)),
            e.tensor_add(st(T3), relv_f[:, 64:65], st(RS0)))[-1],
            waits=(("s_dve", epiA),), inc="s_dve")
        epiC = S.op("dve", lambda e: (
            e.tensor_sub(st(RS2), st(RSUM), st(T3)),
            e.tensor_add(add1_f[:, :], add1_f[:, :], relv_f[:, 0:64]))[-1],
            waits=(("s_dve", epiB),), inc="s_dve")
        epiD = S.op("dve", lambda e: e.scalar_tensor_tensor(
            add1_f[:, :], rv0b[:, :], st(RS0), add1_f[:, :], ALU.mult, ALU.add),
            waits=(("s_dve", epiC),), inc="s_dve")
        epiE = S.op("dve", lambda e: e.scalar_tensor_tensor(
            add1_f[:, :], rv2b[:, :], st(RS2), add1_f[:, :], ALU.mult, ALU.add),
            waits=(("s_dve", epiD),), inc="s_dve")
        epi1 = S.op("dve", lambda e: e.tensor_scalar_mul(atr_s[:, :], add1_f[:, :],
                                                         st(RINV)),
                    waits=(("s_dve", epiE),), inc="s_dve")
        atr_mm = S.op("pe", lambda e: e.matmul(
            psc[0:64, 2432:2560], atr_s[:, :], identb[:, :], start=True, stop=True),
            waits=(("s_dve", epi1),), inc="s_pe")
        dep_ev = None
        if DBG and hp == 0 and hi == 0:
            sydma(lambda e, t=t: e.dma_start(
                out=dbg["dstats"][:, t * 32:t * 32 + 32], in_=stats[:, 0:32]),
                waits=(("s_dve", epi1),))
            sydma(lambda e, t=t: e.dma_start(
                out=dbg["drelv"][:, t * 129:t * 129 + 65], in_=relv_f[:, :]),
                waits=(("s_dve", epi1),))
            sydma(lambda e, t=t: e.dma_start(
                out=dbg["drelv"][:, t * 129 + 65:t * 129 + 129], in_=add1_f[:, :]),
                waits=(("s_dve", epi1),))
            dep_ev = sydma(lambda e, t=t: e.dma_start(
                out=dbg["datr"][:, t * 64:(t + 1) * 64], in_=atr_s[:, :]),
                waits=(("s_dve", epi1),))
        epi2 = S.op("dve", lambda e, t=t, hp=hp, hi=hi: e.tensor_copy(
            attn_fm[64 * hi:64 * hi + 64,
                    hp * 2048 + t * 128:hp * 2048 + (t + 1) * 128],
            psc[0:64, 2432:2560]),
            waits=(("s_pe", atr_mm), ("s_sy", dep_ev)), inc="s_dve")
        ev["epi_dve"] = epi2
        ev["ate_last"] = apt_ev[-1]
        ev["rv_pe"] = rv_mm
        ev["atf_pe"] = rv_mm

    def emit_pair(hp):
        if pend[0] is not None:
            emit_back1(pend[0])
            emit_back2(pend[0])
            pend[0] = None
        c0 = hp * 128
        ww3 = ww_raw.rearrange("p (c n) -> p c n", c=8)
        wq_in = ap_in["wqT"][:, c0:c0 + 128].rearrange("(c p) n -> p c n", p=128)
        wk_in = ap_in["wkT"][:, c0:c0 + 128].rearrange("(c p) n -> p c n", p=128)
        wv_in = ap_in["wvT"][:, c0:c0 + 128].rearrange("(c p) n -> p c n", p=128)

        wq_ev = gpdma(lambda e: e.dma_start(out=ww3, in_=wq_in),
                      waits=(("s_pe", ev["qk_pe"]),))
        wqm = S.op("dve", lambda e: (
            e.tensor_scalar_mul(ww_lo[:, :], ww_raw[:, :], m_lo),
            e.tensor_scalar_mul(ww_hi[:, :], ww_raw[:, :], m_hi))[-1],
            waits=(("s_gp", wq_ev), ("s_pe", ev["qk_pe"]), ("s_dve", ev_ebb)),
            inc="s_dve")

        def qproj(e):
            last = None
            for T in range(NT):
                reg = psc[:, T * 128:(T + 1) * 128]
                for c in range(8):
                    last = e.matmul(reg, ww_lo[:, c * 128:(c + 1) * 128],
                                    xn_fm[:, c * 4096 + 256 * T:c * 4096 + 256 * T + 128],
                                    start=(c == 0), stop=False)
                    last = e.matmul(reg, ww_hi[:, c * 128:(c + 1) * 128],
                                    xn_fm[:, c * 4096 + 256 * T + 128:c * 4096 + 256 * T + 256],
                                    start=False, stop=False)
                last = e.matmul(reg, bias_sb[0:1, c0:c0 + 128],
                                ones_row[0:1, 0:128], start=False, stop=True)
            return last
        qp = S.op("pe", qproj, waits=(("s_dve", wqm), ("s_dve", ev_xn_done),
                                      ("s_act", ev["exp"]), ("s_dve", ev["epi_dve"])),
                  inc="s_pe")
        qe = S.op("dve", lambda e: (
            e.tensor_copy(Q_aug[0][0:64, :], psc[0:64, 0:2048]),
            e.tensor_copy(Q_aug[1][0:64, :], psc[64:128, 0:2048]))[-1],
            waits=(("s_pe", qp),), inc="s_dve")

        wk_ev = gpdma(lambda e: e.dma_start(out=ww3, in_=wk_in),
                      waits=(("s_pe", qp),))
        def kproj(e):
            last = None
            for b in range(8):
                for c in range(8):
                    last = e.matmul(psc[:, b * 512:(b + 1) * 512],
                                    ww_raw[:, c * 128:(c + 1) * 128],
                                    xn_fm[:, c * 4096 + b * 512:c * 4096 + (b + 1) * 512],
                                    start=(c == 0), stop=False)
                last = e.matmul(psc[:, b * 512:(b + 1) * 512],
                                bias_sb[0:1, D + c0:D + c0 + 128],
                                ones_row[0:1, 0:512],
                                start=False, stop=True)
            return last
        kp = S.op("pe", kproj, waits=(("s_gp", wk_ev), ("s_dve", qe)), inc="s_pe")
        ke = S.op("dve", lambda e: (
            e.tensor_scalar_mul(K_aug[0][0:64, :], psc[0:64, :], SCALE),
            e.tensor_scalar_mul(K_aug[1][0:64, :], psc[64:128, :], SCALE))[-1],
            waits=(("s_pe", kp),), inc="s_dve")

        wv_ev = gpdma(lambda e: e.dma_start(out=ww3, in_=wv_in),
                      waits=(("s_pe", kp),))
        last_ve = ke
        for rc4 in range(8):
            def vgrp(e, rc4=rc4):
                last = None
                for j in range(4):
                    rcx = rc4 * 4 + j
                    for c in range(8):
                        last = e.matmul(psc[:, j * 512:j * 512 + 128],
                                        xn_fm[:, c * 4096 + rcx * 128:c * 4096 + (rcx + 1) * 128],
                                        ww_raw[:, c * 128:(c + 1) * 128],
                                        start=(c == 0), stop=False)
                    last = e.matmul(psc[:, j * 512:j * 512 + 128],
                                    ones_row[0:1, 0:128],
                                    bias_sb[0:1, 2 * D + c0:2 * D + c0 + 128],
                                    start=False, stop=True)
                return last
            vp = S.op("pe", vgrp, waits=(("s_gp", wv_ev), ("s_dve", last_ve)),
                      inc="s_pe")
            last_ve = S.op("dve", lambda e, rc4=rc4: (
                e.tensor_copy(
                    v_pair[:, rc4 * 516:rc4 * 516 + 516].rearrange(
                        "p (j a) -> p j a", j=4)[:, :, 0:64],
                    psc[:, 0:2048].rearrange("p (j a) -> p j a", j=4)[:, :, 0:64]),
                e.tensor_copy(
                    v_pair[:, rc4 * 516:rc4 * 516 + 516].rearrange(
                        "p (j a) -> p j a", j=4)[:, :, 65:129],
                    psc[:, 0:2048].rearrange("p (j a) -> p j a", j=4)[:, :, 64:128]))[-1],
                waits=(("s_pe", vp),), inc="s_dve")
        ev["ev_v"] = last_ve
        ev["qk_pe"] = S.cnt["s_pe"]

        for hi in (0, 1):
            for t in range(NT):
                fs = emit_front1(hp, hi, t, K_aug[hi], Q_aug[hi])
                if pend[0] is not None:
                    emit_back1(pend[0])
                emit_front2(fs)
                if pend[0] is not None:
                    emit_back2(pend[0])
                pend[0] = fs

    for hp in range(8):
        emit_pair(hp)
    emit_back1(pend[0])
    emit_back2(pend[0])
    pend[0] = None

    ev_attn_pe = S.cnt["s_pe"]
    ev_attn_dve = S.cnt["s_dve"]
    if DBG:
        dbg_ev["datn"] = sydma(lambda e: e.dma_start(out=dbg["datn"],
                                                     in_=attn_fm[:, :]),
                               waits=(("s_dve", ev_attn_dve),))

    oacc = arf(OACC_, 16384)
    wo_s = ar[:, WO_:WO_ + 8192]
    x_te = arf(XTE_, 1024)
    wo_in = ap_in["woT"].rearrange("(c p) n -> p c n", p=128)
    wo_ev = gpdma(lambda e: e.dma_start(
        out=wo_s.rearrange("p (c n) -> p c n", c=8), in_=wo_in),
        waits=(("s_pe", ev_attn_pe), ("s_dve", ev_attn_dve)))
    prev_ev2 = ev_attn_dve
    prev_pe2 = ev_attn_pe
    for t in range(NT):
        lx = sydma(lambda e, t=t: e.dma_start(out=x_te[:, :],
                                              in_=xq[t * 128:(t + 1) * 128, :]),
                   waits=(("s_dve", prev_ev2),))
        def oproj(e, t=t):
            last = None
            for b in range(2):
                for c in range(8):
                    last = e.matmul(psc[:, b * 512:(b + 1) * 512],
                                    attn_fm[:, c * 2048 + t * 128:c * 2048 + (t + 1) * 128],
                                    wo_s[:, c * 1024 + b * 512:c * 1024 + (b + 1) * 512],
                                    start=(c == 0), stop=False)
                last = e.matmul(psc[:, b * 512:(b + 1) * 512],
                                ones_row[0:1, 0:128],
                                bias_sb[0:1, 3 * D + b * 512:3 * D + (b + 1) * 512],
                                start=False, stop=True)
            return last
        om = S.op("pe", oproj, waits=(("s_gp", wo_ev), ("s_dve", prev_ev2)),
                  inc="s_pe")
        prev_ev2 = S.op("dve", lambda e, t=t: e.tensor_add(
            oacc[:, t * 1024:(t + 1) * 1024], psc[:, 0:1024], x_te[:, :]),
            waits=(("s_pe", om), ("s_sy", lx)), inc="s_dve")
        prev_pe2 = om

    fx_t = arf(FXT_, 1024)
    fxn_tm = ar[:, FXN_:FXN_ + 1024]
    xnq8 = xn_fm[:, 0:16384].rearrange("p (c n) -> p c n", c=8)
    prev_f_pe, prev_f_ev = prev_pe2, prev_ev2
    for t in range(NT):
        lx = sydma(lambda e, t=t: e.dma_start(out=fx_t[:, :],
                                              in_=xq[t * 128:(t + 1) * 128, :]),
                   waits=(("s_pe", prev_f_pe), ("s_dve", prev_f_ev)))
        fsel = S.op("dve", lambda e, t=t: (
            e.tensor_scalar_mul(st(T1), st(SMEAN0 + 2 * t), m_lo),
            e.tensor_scalar_mul(st(T2), st(SRSTD0 + 2 * t), m_lo))[-1],
            waits=(("s_sy", lx), ("s_pe", prev_f_pe)), inc="s_dve")
        fsel = S.op("dve", lambda e, t=t: (
            e.scalar_tensor_tensor(st(T1), st(SMEAN0 + 2 * t + 1), m_hi, st(T1),
                                   ALU.mult, ALU.add),
            e.scalar_tensor_tensor(st(T2), st(SRSTD0 + 2 * t + 1), m_hi, st(T2),
                                   ALU.mult, ALU.add))[-1],
            waits=(("s_dve", fsel),), inc="s_dve")
        rcp = S.op("dve", lambda e: e.tensor_scalar(
            fxn_tm[:, :], fx_t[:, :], st(T1), st(T2), ALU.subtract, ALU.mult),
            waits=(("s_dve", fsel),), inc="s_dve")
        trp = S.op("pe", lambda e: [
            e.matmul(psc[:, c * 128:(c + 1) * 128], fxn_tm[:, c * 128:(c + 1) * 128],
                     identb[:, :], start=True, stop=True) for c in range(8)][-1],
            waits=(("s_dve", rcp), ("s_dve", prev_f_ev),
                   ("s_sy", dbg_ev.get("dxn"))), inc="s_pe")
        prev_f_ev = S.op("dve", lambda e, t=t: e.tensor_copy(
            xnq8[:, :, t * 128:(t + 1) * 128],
            psc[:, 0:1024].rearrange("p (c n) -> p c n", c=8)),
            waits=(("s_pe", trp),), inc="s_dve")
        prev_f_pe = trp

    gpw = ar[:, GP_:GP_ + 4096]
    upw = ar[:, UP_:UP_ + 4096]
    dpw = attn_fm[:, 12288:16384]
    h_fm = attn_fm[:, 0:8192]
    silu_f = attn_fm[:, 8192:12288].bitcast(F32)
    last_comb = prev_f_ev
    last_pe = prev_f_pe
    for ex in range(8):
        gp_in = ap_in["gpT"][ex].rearrange("(c p) n -> p c n", p=128)
        up_in = ap_in["upT"][ex].rearrange("(c p) n -> p c n", p=128)
        dp_in = ap_in["dpT"][ex].rearrange("(c p) n -> p c n", p=128)
        wg = gpdma(lambda e, gp_in=gp_in: e.dma_start(
            out=gpw.rearrange("p (c n) -> p c n", c=8), in_=gp_in),
            waits=(("s_pe", last_pe),))
        wu = gpdma(lambda e, up_in=up_in: e.dma_start(
            out=upw.rearrange("p (c n) -> p c n", c=8), in_=up_in))
        wd = gpdma(lambda e, dp_in=dp_in: e.dma_start(
            out=dpw.rearrange("p (c n) -> p c n", c=4), in_=dp_in),
            waits=(("s_pe", last_pe), ("s_dve", last_comb),
                   ("s_sy", dbg_ev.get("datn"))))
        hev = last_comb
        for ib in range(4):
            def gu(e, ib=ib):
                last = None
                for half, wten in ((0, gpw), (1, upw)):
                    for b in range(4):
                        for c in range(8):
                            last = e.matmul(
                                psc[:, half * 2048 + b * 512:half * 2048 + (b + 1) * 512],
                                wten[:, c * 512 + ib * 128:c * 512 + ib * 128 + 128],
                                xn_fm[:, c * 2048 + b * 512:c * 2048 + (b + 1) * 512],
                                start=(c == 0), stop=(c == 7))
                return last
            gum = S.op("pe", gu, waits=(("s_gp", wu), ("s_dve", hev),
                                        ("s_sy", dbg_ev.get("datn"))), inc="s_pe")
            sil = S.op("act", lambda e: (
                [e.activation(silu_f[:, j * 512:(j + 1) * 512],
                              psc[:, j * 512:(j + 1) * 512], AF.Silu)
                 for j in range(4)],
                e.activation(psc[:, 0:256], silu_f[:, 0:256], AF.Square))[-1],
                waits=(("s_pe", gum),), inc="s_act")
            hev = S.op("dve", lambda e, ib=ib: e.tensor_mul(
                h_fm[:, ib * 2048:(ib + 1) * 2048], silu_f[:, 0:2048],
                psc[:, 2048:4096]),
                waits=(("s_act", sil),), inc="s_dve")
        for t in range(NT):
            def down(e, t=t):
                last = None
                for b in range(2):
                    for ic in range(4):
                        last = e.matmul(psc[:, b * 512:(b + 1) * 512],
                                        h_fm[:, ic * 2048 + t * 128:ic * 2048 + (t + 1) * 128],
                                        dpw[:, ic * 1024 + b * 512:ic * 1024 + (b + 1) * 512],
                                        start=(ic == 0), stop=(ic == 3))
                return last
            dm = S.op("pe", down, waits=(("s_gp", wd), ("s_dve", hev),
                                         ("s_dve", last_comb)), inc="s_pe")
            last_comb = S.op("dve", lambda e, t=t, ex=ex: e.scalar_tensor_tensor(
                oacc[:, t * 1024:(t + 1) * 1024], psc[:, 0:1024],
                cw_sb[:, t * 8 + ex:t * 8 + ex + 1],
                oacc[:, t * 1024:(t + 1) * 1024],
                ALU.mult, ALU.add),
                waits=(("s_pe", dm), ("s_dve", ev_c_dve)), inc="s_dve")
            last_pe = dm

    fin = sydma(lambda e: e.dma_start(
        out=out_d.rearrange("(t p) d -> p t d", p=128),
        in_=oacc.rearrange("p (t d) -> p t d", t=16)),
        waits=(("s_dve", last_comb),))
    S.plan["sy"].append(lambda e, fin=fin: e.wait_ge(S.sem["s_sy"], fin))


_NC_CACHE = {}


def _steps(par):
    s = np.zeros((2 * NT, L), dtype=np.float32)
    q = np.arange(L)
    for t in range(NT):
        l0 = 256 * t + 128 * par
        c_lo2 = max(0, 256 * t - 2942) - 128 * (1 - par)
        c_hi = l0 + 1151
        s[2 * t] = (q < c_lo2).astype(np.float32)
        s[2 * t + 1] = (q >= c_hi).astype(np.float32)
    return s


def kernel(L0_attn_x, L0_attn_Wq, L0_attn_bq, L0_attn_Wk, L0_attn_bk, L0_attn_Wv,
           L0_attn_bv, L0_attn_Wo, L0_attn_bo, L0_attn_relative_k, L0_attn_relative_v,
           L0_ffn_gate_weight, L0_ffn_e_score_correction_bias, L0_ffn_gate_proj,
           L0_ffn_up_proj, L0_ffn_down_proj):
    x = np.ascontiguousarray(np.asarray(L0_attn_x, dtype=np.float32))
    C = np.ascontiguousarray

    fm = np.zeros((8, 8), np.float32)
    for e_ in range(8):
        fm[e_, :e_] = 1.0
    hm = np.zeros((4, 4), np.float32)
    for g_ in range(4):
        hm[g_, :g_] = 1.0
    eye = np.eye(128, dtype=np.float32)

    shared = {
        "wqT": C(np.asarray(L0_attn_Wq, np.float32).T),
        "wkT": C(np.asarray(L0_attn_Wk, np.float32).T),
        "wvT": C(np.asarray(L0_attn_Wv, np.float32).T),
        "woT": C(np.asarray(L0_attn_Wo, np.float32).T),
        "bq": C(np.asarray(L0_attn_bq, np.float32).reshape(1, D)),
        "bk": C(np.asarray(L0_attn_bk, np.float32).reshape(1, D)),
        "bv": C(np.asarray(L0_attn_bv, np.float32).reshape(1, D)),
        "bo": C(np.asarray(L0_attn_bo, np.float32).reshape(1, D)),
        "rkT": C(np.asarray(L0_attn_relative_k, np.float32).T),
        "rv": C(np.asarray(L0_attn_relative_v, np.float32)),
        "gwT": C(np.asarray(L0_ffn_gate_weight, np.float32).T),
        "ebias": C(np.asarray(L0_ffn_e_score_correction_bias,
                              np.float32).reshape(1, 8)),
        "gpT": C(np.asarray(L0_ffn_gate_proj, np.float32).transpose(0, 2, 1)),
        "upT": C(np.asarray(L0_ffn_up_proj, np.float32).transpose(0, 2, 1)),
        "dpT": C(np.asarray(L0_ffn_down_proj, np.float32).transpose(0, 2, 1)),
        "identf": eye,
        "masklo": np.tril(np.ones((128, 128), np.float32), -1),
        "maskup": np.triu(np.ones((128, 128), np.float32), 1),
        "fmask64": np.tile(fm.reshape(1, 64), (128, 1)),
        "hmask16": np.tile(hm.reshape(1, 16), (128, 1)),
    }

    if "nc" not in _NC_CACHE:
        _NC_CACHE["nc"] = build_program()
    nc = _NC_CACHE["nc"]

    in_maps = []
    for c in range(8):
        b = c // 2
        par = c % 2
        m = dict(shared)
        m["xb"] = C(x[b])
        m["xq"] = C(x[b].reshape(32, 128, D)[par::2].reshape(NQ, D))
        m["steps"] = _steps(par)
        m["mcol"] = np.tile(np.array([[1.0 - par, float(par)]], np.float32),
                            (128, 1))
        m["idlo"] = C(eye * (1.0 - par))
        m["idhi"] = C(eye * par)
        in_maps.append(m)

    res = run_bass_kernel_spmd(nc, in_maps, core_ids=list(range(8)))
    out = np.empty((B, L, D), dtype=np.float32)
    for c in range(8):
        b = c // 2
        par = c % 2
        out[b].reshape(32, 128, D)[par::2] = res.results[c]["out"].reshape(NT, 128, D)
    return out


if __name__ == "__main__":
    build_program()
    print("program built ok")


# version 12
# speedup vs baseline: 1.4811x; 1.0295x over previous
import numpy as np
from contextlib import ExitStack

import concourse.bass as bass
import concourse.mybir as mybir
from concourse.bass_utils import run_bass_kernel_spmd

F32 = mybir.dt.float32
BF16 = mybir.dt.bfloat16
AF = mybir.ActivationFunctionType
ALU = mybir.AluOpType
AX = mybir.AxisListType

B, L, D = 4, 4096, 1024
NH, DK = 16, 64
NQ = 2048
NT = 16
NJ = 2047
SCALE = 0.125

IMG_W = 4348
IMG_FL = 128
IMG_RS = IMG_W + 2 * IMG_FL
PW2 = 4349
BU_W = 4224
AUF_W = 4222
PB_L, PB_R = 3072, 1024
PB_W = PB_L + L + PB_R

AR_W = 43008
KA_, KB_ = 0, 4096
QA_, QB_ = 8192, 10240
VV_ = 12288
WW_ = 16416
QR_ = 19488
BU_ = 21792
PBUF_ = 26016
AUF_ = 34208
PT_ = 38430
APT_ = 40478
ATR_ = 42526
XT_, SQ_, XN32_, XNTM_, XF_ = 0, 2048, 4096, 6144, 8192
OACC_, WO_, XTE_ = 0, 32768, 40960
FXT_, FSQ_, FXN_ = 32768, 34816, 36864
GP_, UP_ = 32768, 36864

ENGS = ("pe", "dve", "act", "sy", "gp")

DBG = False

(MNEG, RSA, RSB, RSUM, RINV, CSLL, CSLH, CSRL, CSRH, RS0, RS2, WSUM,
 MEAN, VAR, RSTD, SSQ, T1, T2, T3, T4, QK0, QK2, EPS) = range(23)
SMEAN0 = 23
SRSTD0 = 55


class Sched:
    def __init__(self):
        self.plan = {k: [] for k in ENGS}
        self.cnt = {}
        self.sem = {}

    def add_sem(self, name, handle):
        self.sem[name] = handle
        self.cnt[name] = 0

    def op(self, eng, fn, waits=(), inc=None, n=1):
        val = None
        if inc is not None:
            self.cnt[inc] += n
            val = self.cnt[inc]
        wl = tuple((s, v) for (s, v) in waits if v is not None and v > 0)
        sems = self.sem

        def run(e, fn=fn, wl=wl, inc=inc, n=n):
            seen = {}
            for s, v in wl:
                if seen.get(s, -1) < v:
                    seen[s] = v
            for s, v in seen.items():
                e.wait_ge(sems[s], v)
            ins = fn(e)
            if inc is not None:
                ins.then_inc(sems[inc], n)

        self.plan[eng].append(run)
        return val


def build_program():
    nc = bass.Bass("TRN2", target_bir_lowering=False)

    ap_in = {}
    def din(name, shape, dt=F32):
        ap_in[name] = nc.dram_tensor(name, shape, dt, kind="ExternalInput").ap()
        return ap_in[name]

    din("xb", [L, D]); din("xq", [NQ, D])
    din("wqT", [D, D]); din("wkT", [D, D]); din("wvT", [D, D]); din("woT", [D, D])
    din("bq", [1, D]); din("bk", [1, D]); din("bv", [1, D]); din("bo", [1, D])
    din("rkT", [DK, NJ]); din("rv", [NJ, DK])
    din("gwT", [D, 8]); din("ebias", [1, 8])
    din("gpT", [8, D, 512]); din("upT", [8, D, 512]); din("dpT", [8, 512, D])
    din("identf", [128, 128])
    din("idlo", [128, 128]); din("idhi", [128, 128])
    din("masklo", [128, 128]); din("maskup", [128, 128])
    din("fmask64", [128, 64]); din("hmask16", [128, 16])
    din("steps", [2 * NT, L])
    din("mcol", [128, 2])

    out_d = nc.dram_tensor("out", [NQ, D], F32, kind="ExternalOutput").ap()
    qrimg = nc.dram_tensor("qrimg", [2 * 128 * IMG_RS], BF16, kind="Internal")
    pimg = nc.dram_tensor("pimg", [2 * 128 * PW2], BF16, kind="Internal")
    dbg = {}
    if DBG:
        dbg["dxn"] = nc.dram_tensor("dxn", [128, 8 * L], BF16,
                                    kind="ExternalOutput").ap()
        dbg["dcw"] = nc.dram_tensor("dcw", [128, 128], F32,
                                    kind="ExternalOutput").ap()
        dbg["datn"] = nc.dram_tensor("datn", [128, 8 * NQ], BF16,
                                     kind="ExternalOutput").ap()
        dbg["dscr"] = nc.dram_tensor("dscr", [128, NT * 4096], BF16,
                                     kind="ExternalOutput").ap()
        dbg["dpb"] = nc.dram_tensor("dpb", [128, NT * 4096], BF16,
                                    kind="ExternalOutput").ap()
        dbg["dstats"] = nc.dram_tensor("dstats", [128, NT * 32], F32,
                                       kind="ExternalOutput").ap()
        dbg["drelv"] = nc.dram_tensor("drelv", [128, NT * 129], F32,
                                      kind="ExternalOutput").ap()
        dbg["dauf"] = nc.dram_tensor("dauf", [128, NT * AUF_W], BF16,
                                     kind="ExternalOutput").ap()
        dbg["datr"] = nc.dram_tensor("datr", [128, NT * 64], BF16,
                                     kind="ExternalOutput").ap()
        dbg["dgt"] = nc.dram_tensor("dgt", [128, NT * 80], F32,
                                    kind="ExternalOutput").ap()

    es = ExitStack()
    with es:
        S = Sched()
        for s in ("s_pe", "s_dve", "s_act", "s_sy", "s_gp"):
            S.add_sem(s, es.enter_context(nc.semaphore(s)))

        t_ = {}
        def sb(name, shape, dt):
            t_[name] = es.enter_context(nc.sbuf_tensor(name, shape, dt))
            return t_[name]

        sb("arena", [128, AR_W], BF16)
        sb("xn_fm", [128, 8 * L], BF16)
        sb("attn_fm", [128, 8 * NQ], BF16)
        sb("stats", [128, 96], F32)
        sb("cw", [128, 128], F32)
        sb("identb", [128, 128], BF16)
        sb("identf_sb", [128, 128], F32)
        sb("idlo_sb", [128, 128], BF16)
        sb("idhi_sb", [128, 128], BF16)
        sb("masklo_sb", [128, 128], BF16)
        sb("maskup_sb", [128, 128], BF16)
        sb("fmask_sb", [128, 64], F32)
        sb("hmask_sb", [128, 16], F32)
        sb("mcol_sb", [128, 2], F32)
        sb("rkT_sb", [64, 2048], BF16)
        sb("rv_sb", [128, 16 * 65], BF16)
        sb("gw_sb", [128, 64], F32)
        sb("eb_sb", [1, 8], F32)
        sb("ebb_sb", [128, 8], F32)
        sb("rv0b", [128, 64], F32)
        sb("rv2b", [128, 64], F32)
        sb("bias_sb", [1, 4 * D], BF16)
        sb("ones_row", [1, 512], BF16)
        sb("onesf_row", [1, 128], F32)
        sb("qkT_row", [2, 128], BF16)
        sb("relv_f", [128, 65], F32)
        sb("add1_f", [128, 64], F32)
        sb("gt8", [128, 80], F32)
        t_["psc"] = es.enter_context(nc.psum_tensor("psc", [128, 4096], F32))

        with nc.Block() as block:
            build_plan(S, t_, ap_in, out_d, qrimg, pimg, dbg)
            for eng, attr in (("sy", "sync"), ("gp", "gpsimd"), ("pe", "tensor"),
                              ("dve", "vector"), ("act", "scalar")):
                lst = S.plan[eng]
                def runner(e, lst=lst, eng=eng):
                    for i, f in enumerate(lst):
                        try:
                            f(e)
                        except Exception:
                            print(f"FAILING OP: eng={eng} index={i}/{len(lst)}")
                            raise
                getattr(block, attr)(runner)
    return nc


def build_plan(S, t_, ap_in, out_d, qrimg, pimg, dbg):
    ar = t_["arena"]; xn_fm = t_["xn_fm"]; attn_fm = t_["attn_fm"]
    stats = t_["stats"]; cw_sb = t_["cw"]
    identb = t_["identb"]; identf_sb = t_["identf_sb"]
    idlo_sb = t_["idlo_sb"]; idhi_sb = t_["idhi_sb"]
    masklo_sb = t_["masklo_sb"]; maskup_sb = t_["maskup_sb"]
    fmask_sb = t_["fmask_sb"]; hmask_sb = t_["hmask_sb"]; mcol_sb = t_["mcol_sb"]
    rkT_sb = t_["rkT_sb"]; rv_sb = t_["rv_sb"]; gw_sb = t_["gw_sb"]
    eb_sb = t_["eb_sb"]; ebb_sb = t_["ebb_sb"]; rv0b = t_["rv0b"]; rv2b = t_["rv2b"]
    bias_sb = t_["bias_sb"]; ones_row = t_["ones_row"]; onesf_row = t_["onesf_row"]
    qkT_row = t_["qkT_row"]
    relv_f = t_["relv_f"]; add1_f = t_["add1_f"]
    gt8 = t_["gt8"]; psc = t_["psc"]

    def st(i):
        return stats[:, i:i + 1]

    def arf(c0, ncols):
        return ar[:, c0:c0 + 2 * ncols].bitcast(F32)

    def dram_flat(th, off, rowstride, p, width):
        return bass.AP(tensor=th, offset=off, ap=[[rowstride, p], [1, width]])

    m_lo = mcol_sb[:, 0:1]
    m_hi = mcol_sb[:, 1:2]

    def gpdma(fn, waits=()):
        return S.op("gp", fn, waits=waits, inc="s_gp", n=16)

    def sydma(fn, waits=()):
        return S.op("sy", fn, waits=waits, inc="s_sy", n=16)

    def sydma_split(fn2, waits=()):
        def g(e, fn2=fn2):
            fn2(e, 0).then_inc(S.sem["s_sy"], 16)
            return fn2(e, 64)
        S.cnt["s_sy"] += 16
        return S.op("sy", g, waits=waits, inc="s_sy", n=16)

    ld0 = gpdma(lambda e: e.dma_start(out=identb[:, :], in_=ap_in["identf"][:, :]))
    gpdma(lambda e: e.dma_start(out=identf_sb[:, :], in_=ap_in["identf"][:, :]))
    gpdma(lambda e: e.dma_start(out=idlo_sb[:, :], in_=ap_in["idlo"][:, :]))
    gpdma(lambda e: e.dma_start(out=idhi_sb[:, :], in_=ap_in["idhi"][:, :]))
    gpdma(lambda e: e.dma_start(out=masklo_sb[:, :], in_=ap_in["masklo"][:, :]))
    gpdma(lambda e: e.dma_start(out=maskup_sb[:, :], in_=ap_in["maskup"][:, :]))
    gpdma(lambda e: e.dma_start(out=fmask_sb[:, :], in_=ap_in["fmask64"][:, :]))
    gpdma(lambda e: e.dma_start(out=hmask_sb[:, :], in_=ap_in["hmask16"][:, :]))
    gpdma(lambda e: e.dma_start(out=mcol_sb[:, :], in_=ap_in["mcol"][:, :]))
    gpdma(lambda e: e.dma_start(out=rkT_sb[0:64, 0:NJ], in_=ap_in["rkT"][:, :]))
    gpdma(lambda e: e.dma_start(out=gw_sb[:, :].rearrange("p (c e) -> p c e", c=8),
                                in_=ap_in["gwT"].rearrange("(c p) e -> p c e", p=128)))
    gpdma(lambda e: e.dma_start(out=eb_sb[:, :], in_=ap_in["ebias"][:, :]))
    for i, nm in enumerate(("bq", "bk", "bv", "bo")):
        gpdma(lambda e, i=i, nm=nm: e.dma_start(
            out=bias_sb[0:1, i * D:(i + 1) * D], in_=ap_in[nm][:, :]))
    mrv = S.op("dve", lambda e: e.memset(rv_sb[:, :], 0.0), inc="s_dve")
    rv_last = None
    for k in range(16):
        r0, r1 = k * 128, min(NJ, (k + 1) * 128)
        rv_last = gpdma(lambda e, k=k, r0=r0, r1=r1: e.dma_start(
            out=rv_sb[0:r1 - r0, k * 65:k * 65 + 64], in_=ap_in["rv"][r0:r1, :]),
            waits=(("s_dve", mrv),))
    ld_consts = rv_last

    S.op("dve", lambda e: e.memset(ones_row[:, :], 1.0), inc="s_dve")
    S.op("dve", lambda e: e.memset(onesf_row[:, :], 1.0), inc="s_dve")
    ev_eps = S.op("dve", lambda e: (
        e.memset(st(EPS), 1e-5),
        e.memset(st(MNEG), 0.0))[-1], inc="s_dve")
    rv_ones = S.op("dve", lambda e: e.memset(
        rv_sb[:, :].rearrange("p (k c) -> p k c", c=65)[:, :, 64:65], 1.0),
        waits=(("s_gp", rv_last),), inc="s_dve")
    S.op("dve", lambda e: e.memset(
        ar[:, VV_:VV_ + 4128].rearrange("p (k c) -> p k c", c=129)[:, :, 64:65], 1.0),
        inc="s_dve")
    S.op("dve", lambda e: e.memset(ar[:, PBUF_:PBUF_ + PB_L], 0.0), inc="s_dve")
    S.op("dve", lambda e: e.memset(ar[:, PBUF_ + PB_L + L:PBUF_ + PB_W], 0.0),
         inc="s_dve")
    mzf = S.op("dve", lambda e: e.memset(ar[:, BU_:BU_ + 128], 0.0), inc="s_dve")
    fl_ev = None
    for slot in range(2):
        for off in (0, IMG_FL + IMG_W):
            fl_ev = sydma(lambda e, slot=slot, off=off: e.dma_start(
                out=dram_flat(qrimg, slot * 128 * IMG_RS + off, IMG_RS, 128, IMG_FL),
                in_=ar[:, BU_:BU_ + 128]),
                waits=(("s_dve", mzf),))

    pe_eb = S.op("pe", lambda e: e.matmul(psc[:, 0:8], onesf_row[0:1, :],
                                          eb_sb[0:1, :], start=True, stop=True),
                 waits=(("s_gp", ld_consts),), inc="s_pe")
    pe_rv0 = S.op("pe", lambda e: e.matmul(psc[:, 512:576], ones_row[0:1, 0:128],
                                           rv_sb[0:1, 0:64], start=True, stop=True),
                  waits=(("s_dve", rv_ones),), inc="s_pe")
    cp2046 = sydma(lambda e: e.dma_start(out=qkT_row[0:1, 0:64],
                                         in_=rv_sb[126:127, 15 * 65:15 * 65 + 64]),
                   waits=(("s_dve", rv_ones),))
    pe_rv2 = S.op("pe", lambda e: e.matmul(psc[:, 1024:1088], ones_row[0:1, 0:128],
                                           qkT_row[0:1, 0:64], start=True, stop=True),
                  waits=(("s_sy", cp2046),), inc="s_pe")
    ev_ebb = S.op("dve", lambda e: (
        e.tensor_copy(ebb_sb[:, :], psc[:, 0:8]),
        e.tensor_copy(rv0b[:, :], psc[:, 512:576]),
        e.tensor_copy(rv2b[:, :], psc[:, 1024:1088]))[-1],
        waits=(("s_pe", pe_rv2), ("s_pe", pe_rv0), ("s_pe", pe_eb)), inc="s_dve")

    x_t = arf(XT_, 1024)
    sq_t = arf(SQ_, 1024)
    xn_tm = ar[:, XNTM_:XNTM_ + 1024]
    xb = ap_in["xb"]; xq = ap_in["xq"]

    xv8 = xn_fm[:, :].rearrange("p (c n) -> p c n", c=8)
    prev_tr, prev_ev = 0, 0
    for t in range(32):
        lx = sydma(lambda e, t=t: e.dma_start(out=x_t[:, :],
                                              in_=xb[t * 128:(t + 1) * 128, :]),
                   waits=(("s_pe", prev_tr),))
        l1 = S.op("dve", lambda e: (
            e.reduce_sum(st(T1), x_t[:, :], AX.X),
            e.tensor_tensor(sq_t[:, :], x_t[:, :], x_t[:, :], ALU.mult))[-1],
            waits=(("s_sy", lx),), inc="s_dve")
        l2 = S.op("dve", lambda e, t=t: (
            e.reduce_sum(st(SSQ), sq_t[:, :], AX.X),
            e.tensor_scalar_mul(st(SMEAN0 + t), st(T1), 1.0 / 1024.0))[-1],
            waits=(("s_dve", l1),), inc="s_dve")
        l3 = S.op("dve", lambda e, t=t: (
            e.tensor_scalar_mul(st(VAR), st(SSQ), 1.0 / 1024.0),
            e.tensor_tensor(st(T2), st(SMEAN0 + t), st(SMEAN0 + t), ALU.mult))[-1],
            waits=(("s_dve", l2),), inc="s_dve")
        lnc = S.op("dve", lambda e: e.tensor_sub(st(VAR), st(VAR), st(T2)),
                   waits=(("s_dve", l3),), inc="s_dve")
        S.op("act", lambda e: e.activation(st(T1), st(VAR), AF.Sqrt, bias=st(EPS)),
             waits=(("s_dve", lnc),))
        rstd = S.op("act", lambda e: e.activation(sq_t[:, 0:512], x_t[:, 0:512],
                                                  AF.Square), inc="s_act")
        rc1 = S.op("dve", lambda e, t=t: e.reciprocal(st(SRSTD0 + t), st(T1)),
                   waits=(("s_act", rstd), ("s_pe", prev_tr)), inc="s_dve")
        rcp = S.op("dve", lambda e, t=t: e.tensor_scalar(
            xn_tm[:, :], x_t[:, :], st(SMEAN0 + t), st(SRSTD0 + t),
            ALU.subtract, ALU.mult),
            waits=(("s_dve", rc1),), inc="s_dve")
        trp = S.op("pe", lambda e: [
            e.matmul(psc[:, c * 128:(c + 1) * 128], xn_tm[:, c * 128:(c + 1) * 128],
                     identb[:, :], start=True, stop=True) for c in range(8)][-1],
            waits=(("s_dve", rcp), ("s_gp", ld0), ("s_dve", ev_ebb),
                   ("s_dve", prev_ev)), inc="s_pe")
        prev_ev = S.op("dve", lambda e, t=t: e.tensor_copy(
            xv8[:, :, t * 128:(t + 1) * 128],
            psc[:, 0:1024].rearrange("p (c n) -> p c n", c=8)),
            waits=(("s_pe", trp),), inc="s_dve")
        prev_tr = trp
    ev_xn_done = prev_ev
    ev_b_pe = prev_tr
    dbg_ev = {}
    if DBG:
        dbg_ev["dxn"] = sydma(lambda e: e.dma_start(out=dbg["dxn"], in_=xn_fm[:, :]),
                              waits=(("s_dve", ev_xn_done),))

    xf32 = arf(XF_, 1024)
    xn32 = arf(XN32_, 1024)
    sC = gt8[:, 0:8]; cC = gt8[:, 8:16]; G4 = gt8[:, 16:20]
    cnt4 = gt8[:, 20:24]; gm4 = gt8[:, 24:28]
    A16 = gt8[:, 32:48]; C16 = gt8[:, 48:64]
    sm8 = gt8[:, 64:72]; tmp8 = gt8[:, 72:80]

    prev_c_pe = ev_b_pe
    ev_cw = 0
    prev_sig = 0
    prev_dgt = 0
    for t in range(NT):
        lx = sydma(lambda e, t=t: e.dma_start(out=x_t[:, :],
                                              in_=xq[t * 128:(t + 1) * 128, :]),
                   waits=(("s_pe", prev_c_pe), ("s_dve", ev_xn_done)))
        xsel = S.op("dve", lambda e, t=t: (
            e.tensor_scalar_mul(st(T1), st(SMEAN0 + 2 * t), m_lo),
            e.tensor_scalar_mul(st(T2), st(SRSTD0 + 2 * t), m_lo))[-1],
            waits=(("s_sy", lx), ("s_pe", prev_c_pe)), inc="s_dve")
        xsel = S.op("dve", lambda e, t=t: (
            e.scalar_tensor_tensor(st(T1), st(SMEAN0 + 2 * t + 1), m_hi, st(T1),
                                   ALU.mult, ALU.add),
            e.scalar_tensor_tensor(st(T2), st(SRSTD0 + 2 * t + 1), m_hi, st(T2),
                                   ALU.mult, ALU.add))[-1],
            waits=(("s_dve", xsel),), inc="s_dve")
        xnf = S.op("dve", lambda e: e.tensor_scalar(
            xn32[:, :], x_t[:, :], st(T1), st(T2), ALU.subtract, ALU.mult),
            waits=(("s_dve", xsel),), inc="s_dve")
        trp = S.op("pe", lambda e: [
            e.transpose(psc[:, c * 128:(c + 1) * 128], xn32[:, c * 128:(c + 1) * 128],
                        identf_sb[:, :]) for c in range(8)][-1],
            waits=(("s_dve", xnf),), inc="s_pe")
        evf = S.op("dve", lambda e: e.tensor_copy(xf32[:, :], psc[:, 0:1024]),
                   waits=(("s_pe", trp),), inc="s_dve")
        lg = S.op("pe", lambda e: [
            e.matmul(psc[:, 1024:1032], xf32[:, c * 128:(c + 1) * 128],
                     gw_sb[:, c * 8:(c + 1) * 8], start=(c == 0), stop=(c == 7))
            for c in range(8)][-1],
            waits=(("s_dve", evf), ("s_act", prev_sig)), inc="s_pe")
        S.op("act", lambda e: e.activation(sC, psc[:, 1024:1032], AF.Sigmoid),
             waits=(("s_pe", lg), ("s_dve", ev_cw), ("s_sy", prev_dgt)))
        sig = S.op("act", lambda e: e.activation(xf32[:, 0:512], xf32[:, 512:1024],
                                                 AF.Square), inc="s_act")
        prev_sig = sig
        cprev = S.op("dve", lambda e: e.tensor_add(cC, sC, ebb_sb[:, :]),
                     waits=(("s_act", sig), ("s_dve", ev_ebb)), inc="s_dve")
        cprev = S.op("dve", lambda e: e.tensor_add(G4, cC[:, 0:8:2], cC[:, 1:8:2]),
                     waits=(("s_dve", cprev),), inc="s_dve")
        cprev = S.op("dve", lambda e: (
            e.tensor_tensor(A16.rearrange("p (a b) -> p a b", a=4),
                            G4.unsqueeze(2).broadcast_to((128, 4, 4)),
                            G4.unsqueeze(1).broadcast_to((128, 4, 4)),
                            ALU.is_lt),
            e.tensor_tensor(C16.rearrange("p (a b) -> p a b", a=4),
                            G4.unsqueeze(2).broadcast_to((128, 4, 4)),
                            G4.unsqueeze(1).broadcast_to((128, 4, 4)),
                            ALU.is_equal))[-1],
            waits=(("s_dve", cprev),), inc="s_dve")
        cprev = S.op("dve", lambda e: e.tensor_tensor(C16, C16, hmask_sb[:, :],
                                                      ALU.mult),
                     waits=(("s_dve", cprev),), inc="s_dve")
        cprev = S.op("dve", lambda e: e.tensor_add(A16, A16, C16),
                     waits=(("s_dve", cprev),), inc="s_dve")
        cprev = S.op("dve", lambda e: e.tensor_reduce(
            cnt4, A16.rearrange("p (a b) -> p a b", a=4), AX.X, ALU.add),
            waits=(("s_dve", cprev),), inc="s_dve")
        cprev = S.op("dve", lambda e: e.tensor_single_scalar(gm4, cnt4, 2.0,
                                                             ALU.is_lt),
                     waits=(("s_dve", cprev),), inc="s_dve")
        cprev = S.op("dve", lambda e: e.tensor_copy(
            sm8.rearrange("p (g j) -> p g j", g=4),
            gm4.unsqueeze(2).broadcast_to((128, 4, 2))),
            waits=(("s_dve", cprev),), inc="s_dve")
        cprev = S.op("dve", lambda e: e.tensor_tensor(tmp8, cC, sm8, ALU.mult),
                     waits=(("s_dve", cprev),), inc="s_dve")
        cprev = S.op("dve", lambda e: (
            e.tensor_tensor(relv_f[:, 0:64].rearrange("p (a b) -> p a b", a=8),
                            tmp8.unsqueeze(2).broadcast_to((128, 8, 8)),
                            tmp8.unsqueeze(1).broadcast_to((128, 8, 8)),
                            ALU.is_lt),
            e.tensor_tensor(add1_f[:, 0:64].rearrange("p (a b) -> p a b", a=8),
                            tmp8.unsqueeze(2).broadcast_to((128, 8, 8)),
                            tmp8.unsqueeze(1).broadcast_to((128, 8, 8)),
                            ALU.is_equal))[-1],
            waits=(("s_dve", cprev),), inc="s_dve")
        cprev = S.op("dve", lambda e: e.tensor_tensor(
            add1_f[:, 0:64], add1_f[:, 0:64], fmask_sb[:, :], ALU.mult),
            waits=(("s_dve", cprev),), inc="s_dve")
        cprev = S.op("dve", lambda e: e.tensor_add(
            relv_f[:, 0:64], relv_f[:, 0:64], add1_f[:, 0:64]),
            waits=(("s_dve", cprev),), inc="s_dve")
        cprev = S.op("dve", lambda e: e.tensor_reduce(
            tmp8, relv_f[:, 0:64].rearrange("p (a b) -> p a b", a=8), AX.X, ALU.add),
            waits=(("s_dve", cprev),), inc="s_dve")
        cprev = S.op("dve", lambda e: e.tensor_single_scalar(tmp8, tmp8, 2.0,
                                                             ALU.is_lt),
                     waits=(("s_dve", cprev),), inc="s_dve")
        cprev = S.op("dve", lambda e: e.tensor_tensor(tmp8, tmp8, sC, ALU.mult),
                     waits=(("s_dve", cprev),), inc="s_dve")
        cprev = S.op("dve", lambda e: e.reduce_sum(st(WSUM), tmp8, AX.X),
                     waits=(("s_dve", cprev),), inc="s_dve")
        cprev = S.op("dve", lambda e: e.tensor_scalar_add(st(T1), st(WSUM), 1e-20),
                     waits=(("s_dve", cprev),), inc="s_dve")
        cprev = S.op("dve", lambda e: e.reciprocal(st(T2), st(T1)),
                     waits=(("s_dve", cprev),), inc="s_dve")
        ev_cw = S.op("dve", lambda e, t=t: e.tensor_scalar_mul(
            cw_sb[:, t * 8:(t + 1) * 8], tmp8, st(T2)),
            waits=(("s_dve", cprev),), inc="s_dve")
        if DBG:
            prev_dgt = sydma(lambda e, t=t: e.dma_start(
                out=dbg["dgt"][:, t * 80:(t + 1) * 80], in_=gt8[:, :]),
                waits=(("s_dve", ev_cw),))
        prev_c_pe = lg
    ev_c_dve = ev_cw
    if DBG:
        sydma(lambda e: e.dma_start(out=dbg["dcw"], in_=cw_sb[:, :]),
              waits=(("s_dve", ev_c_dve),))

    K_aug = [ar[:, KA_:KA_ + 4096], ar[:, KB_:KB_ + 4096]]
    Q_aug = [ar[:, QA_:QA_ + 2048], ar[:, QB_:QB_ + 2048]]
    v_pair = ar[:, VV_:VV_ + 4128]
    ww_raw = ar[:, WW_:WW_ + 1024]
    ww_lo = ar[:, WW_ + 1024:WW_ + 2048]
    ww_hi = ar[:, WW_ + 2048:WW_ + 3072]
    qr_s = ar[:, QR_:QR_ + IMG_W]
    bu = ar[:, BU_:BU_ + BU_W]
    p_b = ar[:, PBUF_:PBUF_ + PB_W]
    au = ar[:, AUF_:AUF_ + AUF_W]
    pt_s = ar[:, PT_:PT_ + 2048]
    apt_s = ar[:, APT_:APT_ + 2048]
    atr_s = ar[:, ATR_:ATR_ + 64]

    def geo(t):
        w0 = max(0, 256 * t - 2942)
        ce = min(BU_W, 256 * t + 1279)
        roff = w0 + 3197 - 256 * t
        wlo2 = max(0, 256 * t - 3070)
        whi2_a = max(0, 256 * t - 3070)
        whi2_b = max(0, 256 * t - 2942)
        return w0, ce, roff, wlo2, whi2_a, whi2_b

    ev = {
        "exp": S.cnt["s_act"],
        "epi_dve": S.cnt["s_dve"],
        "qk_pe": S.cnt["s_pe"],
        "band_pe": S.cnt["s_pe"],
        "aread_sy": [0, 0],
        "aread_gp": [0, 0],
        "pbuf_sy": 0, "pbuf_gp": 0, "pbuf_dve": 0, "pbuf_pe": 0,
        "ev_v": 0,
        "bu_ready": 0,
    }
    gtile = [0]
    pend = [None]

    def emit_front1(hp, hi, t, Ka, Qa):
        g = gtile[0]; gtile[0] += 1
        slot = g % 2
        w0, ce, roff, wlo2, whi2_a, whi2_b = geo(t)

        qr_mm = S.op("pe", lambda e: [
            e.matmul(psc[:, n0:n1], Qa[0:64, t * 128:(t + 1) * 128],
                     rkT_sb[0:64, n0:n1], start=True, stop=True)
            for (n0, n1) in ((0, 512), (512, 1024), (1024, 1536), (1536, 2047))][-1],
            waits=(("s_act", ev["exp"]), ("s_dve", ev["ev_v"]), ("s_gp", ld0),
                   ("s_sy", fl_ev)),
            inc="s_pe")
        qr_ev = S.op("dve", lambda e: (
            e.tensor_copy(stats[:, QK0:QK0 + 1], psc[:, 0:1]),
            e.tensor_copy(stats[:, QK2:QK2 + 1], psc[:, 2046:2047]),
            e.tensor_copy(qr_s[:, 127:127 + 512], psc[:, 0:512]),
            e.tensor_copy(qr_s[:, 639:639 + 512], psc[:, 512:1024]),
            e.tensor_copy(qr_s[:, 1151:1151 + 512], psc[:, 1024:1536]),
            e.tensor_copy(qr_s[:, 1663:1663 + 511], psc[:, 1536:2047]))[-1],
            waits=(("s_pe", qr_mm),), inc="s_dve")
        pads = S.op("dve", lambda e: (
            e.tensor_scalar(qr_s[:, 0:127], identb[:, 0:127], 0.0,
                            stats[:, QK0:QK0 + 1], ALU.mult, ALU.add),
            e.tensor_scalar(qr_s[:, 2174:2301], identb[:, 0:127], 0.0,
                            stats[:, QK2:QK2 + 1], ALU.mult, ALU.add))[-1],
            waits=(("s_dve", qr_ev),), inc="s_dve")
        qkT_mm = S.op("pe", lambda e: e.matmul(
            psc[0:2, 2304:2432], stats[:, QK0:QK0 + 2], identf_sb[:, :],
            start=True, stop=True),
            waits=(("s_dve", qr_ev), ("s_dve", ev.get("qkT_ev", 0))), inc="s_pe")
        qkT_ev = S.op("dve", lambda e: e.tensor_copy(qkT_row[0:2, :],
                                                     psc[0:2, 2304:2432]),
                      waits=(("s_pe", qkT_mm), ("s_sy", ev.get("qrow", 0))),
                      inc="s_dve")
        ev["qkT_ev"] = qkT_ev
        qrow_w = sydma(lambda e, t=t: e.dma_start(
            out=Qa[64:66, t * 128:(t + 1) * 128], in_=qkT_row[0:2, :]),
            waits=(("s_dve", qkT_ev),))
        ev["qrow"] = qrow_w
        step_ev = gpdma(lambda e, t=t: e.dma_start(
            out=Ka[64:66, :], in_=ap_in["steps"][2 * t:2 * t + 2, :]),
            waits=(("s_pe", ev["qk_pe"]),))
        img_w = sydma(lambda e, slot=slot: e.dma_start(
            out=dram_flat(qrimg, slot * 128 * IMG_RS + IMG_FL, IMG_RS, 128, 2174),
            in_=qr_s[:, 0:2174]),
            waits=(("s_dve", qr_ev), ("s_dve", pads)))
        img_w = sydma(lambda e, slot=slot: e.dma_start(
            out=dram_flat(qrimg, slot * 128 * IMG_RS + IMG_FL + 2174, IMG_RS,
                          128, 2174),
            in_=qr_s[:, 127:2301]),
            waits=(("s_dve", qr_ev), ("s_dve", pads)))
        if t == 0:
            bz = S.op("dve", lambda e: e.memset(bu[:, :], 0.0),
                      waits=(("s_pe", ev["band_pe"]),), inc="s_dve")
        else:
            pw0_ = geo(t - 1)[0]
            if w0 > pw0_:
                bz = S.op("dve", lambda e, pw0_=pw0_, w0=w0: e.memset(
                    bu[:, pw0_:w0], 0.0),
                    waits=(("s_pe", ev["band_pe"]),), inc="s_dve")
            else:
                bz = ev["bu_ready"]
        band_r = sydma(lambda e, slot=slot, w0=w0, ce=ce, roff=roff: e.dma_start(
            out=bu[:, w0:ce],
            in_=dram_flat(qrimg, slot * 128 * IMG_RS + roff, IMG_RS - 1,
                          128, ce - w0)),
            waits=(("s_sy", img_w), ("s_pe", ev["band_pe"]), ("s_dve", bz)))
        ev["bu_ready"] = bz
        return dict(hp=hp, hi=hi, t=t, Ka=Ka, Qa=Qa, slot=slot, w0=w0, ce=ce,
                    wlo2=wlo2, whi2_a=whi2_a, whi2_b=whi2_b, band_r=band_r,
                    qrow_w=qrow_w, step_ev=step_ev)

    def emit_front2(fs):
        hp, hi, t, Ka, Qa = fs["hp"], fs["hi"], fs["t"], fs["Ka"], fs["Qa"]
        w0, ce = fs["w0"], fs["ce"]
        band_r, qrow_w, step_ev = fs["band_r"], fs["qrow_w"], fs["step_ev"]
        exp_ps = []
        for PS in (0, 1):
            lo_banks = [b for b in range(4)
                        if max(128 + 512 * (4 * PS + b), w0)
                        < min(128 + 512 * (4 * PS + b) + 512, ce)]
            hi_banks = [b for b in range(4)
                        if max(512 * (4 * PS + b), w0)
                        < min(512 * (4 * PS + b) + 512, ce)]
            def qk(e, lo_banks=lo_banks, hi_banks=hi_banks, t=t, PS=PS):
                last = None
                for b in range(4):
                    gb = 4 * PS + b
                    nadd = (b in lo_banks) + (b in hi_banks)
                    last = e.matmul(psc[:, b * 512:(b + 1) * 512],
                                    Qa[0:66, t * 128:(t + 1) * 128],
                                    Ka[0:66, gb * 512:(gb + 1) * 512],
                                    start=True, stop=(nadd == 0))
                    if b in lo_banks:
                        nadd -= 1
                        last = e.matmul(psc[:, b * 512:(b + 1) * 512],
                                        idlo_sb[:, :],
                                        bu[:, 128 + 512 * gb:640 + 512 * gb],
                                        start=False, stop=(nadd == 0))
                    if b in hi_banks:
                        last = e.matmul(psc[:, b * 512:(b + 1) * 512],
                                        idhi_sb[:, :],
                                        bu[:, 512 * gb:512 * gb + 512],
                                        start=False, stop=True)
                return last
            qk_mm = S.op("pe", qk, waits=(
                ("s_sy", band_r), ("s_sy", qrow_w), ("s_gp", step_ev),
                ("s_act", exp_ps[-1] if exp_ps else ev["exp"])), inc="s_pe")
            dscr_ev = None
            if DBG and hp == 0 and hi == 0:
                dcp = S.op("dve", lambda e: e.tensor_copy(
                    ar[:, PT_:PT_ + 4096].bitcast(F32)[:, 0:2048], psc[:, 0:2048]),
                    waits=(("s_pe", qk_mm),), inc="s_dve")
                dscr_ev = sydma(lambda e, t=t, PS=PS: e.dma_start(
                    out=dbg["dscr"][:, t * 4096 + 2048 * PS:
                                    t * 4096 + 2048 * PS + 2048],
                    in_=ar[:, PT_:PT_ + 4096].bitcast(F32)[:, 0:2048]),
                    waits=(("s_dve", dcp),))
            expp = S.op("act", lambda e, PS=PS: e.activation(
                p_b[:, PB_L + 2048 * PS:PB_L + 2048 * PS + 2048], psc[:, 0:2048],
                AF.Exp, bias=st(MNEG)),
                waits=(("s_pe", qk_mm), ("s_sy", ev["pbuf_sy"]),
                       ("s_gp", ev["pbuf_gp"]),
                       ("s_dve", ev["pbuf_dve"]), ("s_pe", ev["pbuf_pe"]),
                       ("s_sy", dscr_ev)),
                inc="s_act")
            exp_ps.append(expp)
            ev["band_pe"] = qk_mm
            ev["qk_pe"] = qk_mm
        exp_a, expv = exp_ps
        ev["exp"] = expv
        if DBG and hp == 0 and hi == 0:
            ev["pbuf_sy"] = sydma(lambda e, t=t: e.dma_start(
                out=dbg["dpb"][:, t * 4096:(t + 1) * 4096],
                in_=p_b[:, PB_L:PB_L + 4096]),
                waits=(("s_act", expv),))
        fs["exp_a"] = exp_a
        fs["expv"] = expv

    def emit_back1(fs):
        hp, hi, t, Ka, Qa = fs["hp"], fs["hi"], fs["t"], fs["Ka"], fs["Qa"]
        slot, w0, ce = fs["slot"], fs["w0"], fs["ce"]
        wlo2, whi2_a, whi2_b = fs["wlo2"], fs["whi2_a"], fs["whi2_b"]
        exp_a, expv = fs["exp_a"], fs["expv"]
        pw0 = PBUF_ + PB_L + 256 * t - 3070
        pimg_w = gpdma(lambda e, slot=slot, pw0=pw0: e.dma_start(
            out=dram_flat(pimg, slot * 128 * PW2, PW2, 128, PW2),
            in_=ar[:, pw0:pw0 + PW2]),
            waits=(("s_act", expv), ("s_gp", ev["aread_gp"][slot]),
                   ("s_sy", ev["aread_sy"][slot])))
        auf = gpdma(lambda e, slot=slot: e.dma_start(
            out=au[:, :], in_=dram_flat(pimg, slot * 128 * PW2, PW2 + 1,
                                        128, AUF_W)),
            waits=(("s_gp", pimg_w), ("s_pe", ev.get("atf_pe", 0))))
        ev["aread_gp"][slot] = auf
        ev["pbuf_sy"] = 0
        ev["pbuf_gp"] = pimg_w
        def rs0fn(e, wlo2=wlo2, whi2_a=whi2_a, whi2_b=whi2_b):
            if wlo2 > 0:
                last = e.reduce_sum(st(RS0), p_b[:, PB_L:PB_L + wlo2], AX.X)
            else:
                last = e.memset(st(RS0), 0.0)
            if whi2_b > whi2_a:
                last = e.reduce_sum(st(T3), p_b[:, PB_L + whi2_a:PB_L + whi2_b],
                                    AX.X)
            return last
        rs0e = S.op("dve", rs0fn, waits=(("s_act", exp_a),), inc="s_dve")
        if whi2_b > whi2_a:
            rs0e = S.op("dve", lambda e: e.scalar_tensor_tensor(
                st(RS0), st(T3), m_hi, st(RS0), ALU.mult, ALU.add),
                waits=(("s_dve", rs0e),), inc="s_dve")
        axl0 = PBUF_ + PB_L + 256 * t - 3070
        axr0 = PBUF_ + PB_L + 256 * t + 1023
        cs1 = S.op("dve", lambda e, axl0=axl0, axr0=axr0: (
            e.tensor_tensor(apt_s[:, 0:128], ar[:, axl0:axl0 + 128],
                            masklo_sb[:, :], ALU.mult),
            e.tensor_tensor(apt_s[:, 128:256], ar[:, axl0 + 128:axl0 + 256],
                            masklo_sb[:, :], ALU.mult),
            e.tensor_tensor(apt_s[:, 256:384], ar[:, axr0:axr0 + 128],
                            maskup_sb[:, :], ALU.mult),
            e.tensor_tensor(apt_s[:, 384:512], ar[:, axr0 + 128:axr0 + 256],
                            maskup_sb[:, :], ALU.mult))[-1],
            waits=(("s_gp", auf), ("s_act", expv),
                   ("s_pe", ev.get("rv_pe", 0))), inc="s_dve")
        cs2 = S.op("dve", lambda e: (
            e.reduce_sum(st(CSLL), apt_s[:, 0:128], AX.X),
            e.reduce_sum(st(CSLH), apt_s[:, 128:256], AX.X),
            e.reduce_sum(st(CSRL), apt_s[:, 256:384], AX.X),
            e.reduce_sum(st(CSRH), apt_s[:, 384:512], AX.X))[-1],
            waits=(("s_dve", cs1),), inc="s_dve")
        cs3 = S.op("dve", lambda e: (
            e.tensor_tensor(st(T1), st(CSLL), m_lo, ALU.mult),
            e.tensor_tensor(st(T2), st(CSLH), m_hi, ALU.mult),
            e.tensor_tensor(st(T3), st(CSRL), m_lo, ALU.mult),
            e.tensor_tensor(st(T4), st(CSRH), m_hi, ALU.mult))[-1],
            waits=(("s_dve", cs2),), inc="s_dve")
        csum = S.op("dve", lambda e: (
            e.tensor_add(au[:, 0:1], au[:, 0:1], st(T1)),
            e.tensor_add(au[:, 128:129], au[:, 128:129], st(T2)),
            e.tensor_add(au[:, 4093:4094], au[:, 4093:4094], st(T3)),
            e.tensor_add(au[:, 4221:4222], au[:, 4221:4222], st(T4)))[-1],
            waits=(("s_dve", cs3),), inc="s_dve")
        ev["pbuf_dve"] = csum
        if DBG and hp == 0 and hi == 0:
            ev["aread_sy"][slot] = sydma(lambda e, t=t: e.dma_start(
                out=dbg["dauf"][:, t * AUF_W:(t + 1) * AUF_W], in_=au[:, :]),
                waits=(("s_dve", csum),))
        pt_ev = []
        av_mm = None
        for gq in range(8):
            bank_off = 2560 if gq % 2 == 0 else 3072
            ptm = S.op("pe", lambda e, gq=gq, bank_off=bank_off: [
                e.matmul(psc[:, bank_off + j * 128:bank_off + (j + 1) * 128],
                         p_b[:, PB_L + (gq * 4 + j) * 128:PB_L + (gq * 4 + j + 1) * 128],
                         identb[:, :], start=True, stop=True) for j in range(4)][-1],
                waits=(("s_act", exp_a if gq < 4 else expv),
                       ("s_dve", pt_ev[gq - 2] if gq >= 2
                        else ev.get("ate_last", 0))),
                inc="s_pe")
            pte = S.op("dve", lambda e, gq=gq, bank_off=bank_off: e.tensor_copy(
                pt_s[:, (gq % 4) * 512:(gq % 4 + 1) * 512],
                psc[:, bank_off:bank_off + 512]),
                waits=(("s_pe", ptm),), inc="s_dve")
            pt_ev.append(pte)
            if gq in (3, 7):
                k0 = 0 if gq == 3 else 16
                av_mm = S.op("pe", lambda e, hi=hi, k0=k0: [
                    e.matmul(psc[:, 2048:2113],
                             pt_s[:, (k - k0) * 128:(k - k0 + 1) * 128],
                             v_pair[:, k * 129 + hi * 64:k * 129 + hi * 64 + 65],
                             start=(k == 0), stop=(k == 31))
                    for k in range(k0, k0 + 16)][-1],
                    waits=(("s_dve", pte), ("s_dve", ev["ev_v"])), inc="s_pe")
        ev["pbuf_pe"] = av_mm
        ev["pte_last"] = pt_ev[-1]
        fs["csum"] = csum
        fs["rs0e"] = rs0e
        fs["av_mm"] = av_mm

    def emit_back2(fs):
        hp, hi, t = fs["hp"], fs["hi"], fs["t"]
        csum, rs0e, av_mm = fs["csum"], fs["rs0e"], fs["av_mm"]
        av0 = 2048 + (1 if hi == 1 else 0)
        rs_col = 2112 if hi == 0 else 2048
        apt_ev = []
        for gq, (k0g, nk) in enumerate(((0, 12), (12, 4))):
            def atf(e, k0g=k0g, nk=nk, t=t):
                last = None
                for j in range(nk):
                    k = k0g + j
                    w = 128 if k < 15 else 127
                    keep_wrap = (k + 2) * 128 > 3070 - 256 * t
                    reg = psc[0:w, 2560 + j * 128:2560 + (j + 1) * 128]
                    last = e.matmul(reg, au[:, 2047 + k * 128:2047 + k * 128 + w],
                                    idlo_sb[:, :], start=True, stop=False)
                    last = e.matmul(reg, au[:, 2175 + k * 128:2175 + k * 128 + w],
                                    idhi_sb[:, :], start=False,
                                    stop=not keep_wrap)
                    if keep_wrap:
                        last = e.matmul(reg, au[:, k * 128:k * 128 + w],
                                        idlo_sb[:, :], start=False, stop=False)
                        last = e.matmul(reg, au[:, 128 + k * 128:128 + k * 128 + w],
                                        idhi_sb[:, :], start=False, stop=True)
                return last
            atm = S.op("pe", atf,
                       waits=(("s_dve", csum),
                              ("s_dve", apt_ev[gq - 1] if gq >= 1
                               else ev.get("ate_last", 0))),
                       inc="s_pe")
            ate = S.op("dve", lambda e, k0g=k0g, nk=nk: e.tensor_copy(
                apt_s[:, k0g * 128:(k0g + nk) * 128],
                psc[:, 2560:2560 + nk * 128]),
                waits=(("s_pe", atm),), inc="s_dve")
            apt_ev.append(ate)
        rv_mm = S.op("pe", lambda e: [
            e.matmul(psc[:, 2113:2178],
                     apt_s[0:(128 if k < 15 else 127), k * 128:(k + 1) * 128],
                     rv_sb[0:(128 if k < 15 else 127), k * 65:(k + 1) * 65],
                     start=(k == 0), stop=(k == 15)) for k in range(16)][-1],
            waits=(("s_dve", apt_ev[-1]),), inc="s_pe")
        epiA = S.op("dve", lambda e, av0=av0, rs_col=rs_col: (
            e.tensor_copy(st(RSUM), psc[:, rs_col:rs_col + 1]),
            e.tensor_copy(relv_f[:, :], psc[:, 2113:2178]),
            e.tensor_copy(add1_f[:, :], psc[:, av0:av0 + 64]))[-1],
            waits=(("s_pe", rv_mm), ("s_pe", av_mm), ("s_dve", rs0e),
                   ("s_dve", ev_ebb)), inc="s_dve")
        epiB = S.op("dve", lambda e: (
            e.reciprocal(st(RINV), st(RSUM)),
            e.tensor_add(st(T3), relv_f[:, 64:65], st(RS0)))[-1],
            waits=(("s_dve", epiA),), inc="s_dve")
        epiC = S.op("dve", lambda e: (
            e.tensor_sub(st(RS2), st(RSUM), st(T3)),
            e.tensor_add(add1_f[:, :], add1_f[:, :], relv_f[:, 0:64]))[-1],
            waits=(("s_dve", epiB),), inc="s_dve")
        epiD = S.op("dve", lambda e: e.scalar_tensor_tensor(
            add1_f[:, :], rv0b[:, :], st(RS0), add1_f[:, :], ALU.mult, ALU.add),
            waits=(("s_dve", epiC),), inc="s_dve")
        epiE = S.op("dve", lambda e: e.scalar_tensor_tensor(
            add1_f[:, :], rv2b[:, :], st(RS2), add1_f[:, :], ALU.mult, ALU.add),
            waits=(("s_dve", epiD),), inc="s_dve")
        epi1 = S.op("dve", lambda e: e.tensor_scalar_mul(atr_s[:, :], add1_f[:, :],
                                                         st(RINV)),
                    waits=(("s_dve", epiE),), inc="s_dve")
        atr_mm = S.op("pe", lambda e: e.matmul(
            psc[0:64, 2432:2560], atr_s[:, :], identb[:, :], start=True, stop=True),
            waits=(("s_dve", epi1),), inc="s_pe")
        dep_ev = None
        if DBG and hp == 0 and hi == 0:
            sydma(lambda e, t=t: e.dma_start(
                out=dbg["dstats"][:, t * 32:t * 32 + 32], in_=stats[:, 0:32]),
                waits=(("s_dve", epi1),))
            sydma(lambda e, t=t: e.dma_start(
                out=dbg["drelv"][:, t * 129:t * 129 + 65], in_=relv_f[:, :]),
                waits=(("s_dve", epi1),))
            sydma(lambda e, t=t: e.dma_start(
                out=dbg["drelv"][:, t * 129 + 65:t * 129 + 129], in_=add1_f[:, :]),
                waits=(("s_dve", epi1),))
            dep_ev = sydma(lambda e, t=t: e.dma_start(
                out=dbg["datr"][:, t * 64:(t + 1) * 64], in_=atr_s[:, :]),
                waits=(("s_dve", epi1),))
        epi2 = S.op("dve", lambda e, t=t, hp=hp, hi=hi: e.tensor_copy(
            attn_fm[64 * hi:64 * hi + 64,
                    hp * 2048 + t * 128:hp * 2048 + (t + 1) * 128],
            psc[0:64, 2432:2560]),
            waits=(("s_pe", atr_mm), ("s_sy", dep_ev)), inc="s_dve")
        ev["epi_dve"] = epi2
        ev["ate_last"] = apt_ev[-1]
        ev["rv_pe"] = rv_mm
        ev["atf_pe"] = rv_mm

    def emit_pair(hp):
        if pend[0] is not None:
            emit_back1(pend[0])
            emit_back2(pend[0])
            pend[0] = None
        c0 = hp * 128
        ww3 = ww_raw.rearrange("p (c n) -> p c n", c=8)
        wq_in = ap_in["wqT"][:, c0:c0 + 128].rearrange("(c p) n -> p c n", p=128)
        wk_in = ap_in["wkT"][:, c0:c0 + 128].rearrange("(c p) n -> p c n", p=128)
        wv_in = ap_in["wvT"][:, c0:c0 + 128].rearrange("(c p) n -> p c n", p=128)

        wq_ev = gpdma(lambda e: e.dma_start(out=ww3, in_=wq_in),
                      waits=(("s_pe", ev["qk_pe"]),))
        wqm = S.op("dve", lambda e: (
            e.tensor_scalar_mul(ww_lo[:, :], ww_raw[:, :], m_lo),
            e.tensor_scalar_mul(ww_hi[:, :], ww_raw[:, :], m_hi))[-1],
            waits=(("s_gp", wq_ev), ("s_pe", ev["qk_pe"]), ("s_dve", ev_ebb)),
            inc="s_dve")

        def qproj(e):
            last = None
            for T in range(NT):
                reg = psc[:, T * 128:(T + 1) * 128]
                for c in range(8):
                    last = e.matmul(reg, ww_lo[:, c * 128:(c + 1) * 128],
                                    xn_fm[:, c * 4096 + 256 * T:c * 4096 + 256 * T + 128],
                                    start=(c == 0), stop=False)
                    last = e.matmul(reg, ww_hi[:, c * 128:(c + 1) * 128],
                                    xn_fm[:, c * 4096 + 256 * T + 128:c * 4096 + 256 * T + 256],
                                    start=False, stop=False)
                last = e.matmul(reg, bias_sb[0:1, c0:c0 + 128],
                                ones_row[0:1, 0:128], start=False, stop=True)
            return last
        qp = S.op("pe", qproj, waits=(("s_dve", wqm), ("s_dve", ev_xn_done),
                                      ("s_act", ev["exp"]), ("s_dve", ev["epi_dve"])),
                  inc="s_pe")
        qe = S.op("dve", lambda e: (
            e.tensor_copy(Q_aug[0][0:64, :], psc[0:64, 0:2048]),
            e.tensor_copy(Q_aug[1][0:64, :], psc[64:128, 0:2048]))[-1],
            waits=(("s_pe", qp),), inc="s_dve")

        wk_ev = gpdma(lambda e: e.dma_start(out=ww3, in_=wk_in),
                      waits=(("s_pe", qp),))
        def kproj(e):
            last = None
            for b in range(8):
                for c in range(8):
                    last = e.matmul(psc[:, b * 512:(b + 1) * 512],
                                    ww_raw[:, c * 128:(c + 1) * 128],
                                    xn_fm[:, c * 4096 + b * 512:c * 4096 + (b + 1) * 512],
                                    start=(c == 0), stop=False)
                last = e.matmul(psc[:, b * 512:(b + 1) * 512],
                                bias_sb[0:1, D + c0:D + c0 + 128],
                                ones_row[0:1, 0:512],
                                start=False, stop=True)
            return last
        kp = S.op("pe", kproj, waits=(("s_gp", wk_ev), ("s_dve", qe)), inc="s_pe")
        ke = S.op("dve", lambda e: (
            e.tensor_scalar_mul(K_aug[0][0:64, :], psc[0:64, :], SCALE),
            e.tensor_scalar_mul(K_aug[1][0:64, :], psc[64:128, :], SCALE))[-1],
            waits=(("s_pe", kp),), inc="s_dve")

        wv_ev = gpdma(lambda e: e.dma_start(out=ww3, in_=wv_in),
                      waits=(("s_pe", kp),))
        last_ve = ke
        for rc4 in range(8):
            def vgrp(e, rc4=rc4):
                last = None
                for j in range(4):
                    rcx = rc4 * 4 + j
                    for c in range(8):
                        last = e.matmul(psc[:, j * 512:j * 512 + 128],
                                        xn_fm[:, c * 4096 + rcx * 128:c * 4096 + (rcx + 1) * 128],
                                        ww_raw[:, c * 128:(c + 1) * 128],
                                        start=(c == 0), stop=False)
                    last = e.matmul(psc[:, j * 512:j * 512 + 128],
                                    ones_row[0:1, 0:128],
                                    bias_sb[0:1, 2 * D + c0:2 * D + c0 + 128],
                                    start=False, stop=True)
                return last
            vp = S.op("pe", vgrp, waits=(("s_gp", wv_ev), ("s_dve", last_ve)),
                      inc="s_pe")
            last_ve = S.op("dve", lambda e, rc4=rc4: (
                e.tensor_copy(
                    v_pair[:, rc4 * 516:rc4 * 516 + 516].rearrange(
                        "p (j a) -> p j a", j=4)[:, :, 0:64],
                    psc[:, 0:2048].rearrange("p (j a) -> p j a", j=4)[:, :, 0:64]),
                e.tensor_copy(
                    v_pair[:, rc4 * 516:rc4 * 516 + 516].rearrange(
                        "p (j a) -> p j a", j=4)[:, :, 65:129],
                    psc[:, 0:2048].rearrange("p (j a) -> p j a", j=4)[:, :, 64:128]))[-1],
                waits=(("s_pe", vp),), inc="s_dve")
        ev["ev_v"] = last_ve
        ev["qk_pe"] = S.cnt["s_pe"]

        for hi in (0, 1):
            for t in range(NT):
                fs = emit_front1(hp, hi, t, K_aug[hi], Q_aug[hi])
                if pend[0] is not None:
                    emit_back1(pend[0])
                emit_front2(fs)
                if pend[0] is not None:
                    emit_back2(pend[0])
                pend[0] = fs

    for hp in range(8):
        emit_pair(hp)
    emit_back1(pend[0])
    emit_back2(pend[0])
    pend[0] = None

    ev_attn_pe = S.cnt["s_pe"]
    ev_attn_dve = S.cnt["s_dve"]
    if DBG:
        dbg_ev["datn"] = sydma(lambda e: e.dma_start(out=dbg["datn"],
                                                     in_=attn_fm[:, :]),
                               waits=(("s_dve", ev_attn_dve),))

    oacc = arf(OACC_, 16384)
    wo_s = ar[:, WO_:WO_ + 8192]
    x_te = arf(XTE_, 1024)
    wo_in = ap_in["woT"].rearrange("(c p) n -> p c n", p=128)
    wo_ev = gpdma(lambda e: e.dma_start(
        out=wo_s.rearrange("p (c n) -> p c n", c=8), in_=wo_in),
        waits=(("s_pe", ev_attn_pe), ("s_dve", ev_attn_dve)))
    prev_ev2 = ev_attn_dve
    prev_pe2 = ev_attn_pe
    for t in range(NT):
        lx = sydma(lambda e, t=t: e.dma_start(out=x_te[:, :],
                                              in_=xq[t * 128:(t + 1) * 128, :]),
                   waits=(("s_dve", prev_ev2),))
        def oproj(e, t=t):
            last = None
            for b in range(2):
                for c in range(8):
                    last = e.matmul(psc[:, b * 512:(b + 1) * 512],
                                    attn_fm[:, c * 2048 + t * 128:c * 2048 + (t + 1) * 128],
                                    wo_s[:, c * 1024 + b * 512:c * 1024 + (b + 1) * 512],
                                    start=(c == 0), stop=False)
                last = e.matmul(psc[:, b * 512:(b + 1) * 512],
                                ones_row[0:1, 0:128],
                                bias_sb[0:1, 3 * D + b * 512:3 * D + (b + 1) * 512],
                                start=False, stop=True)
            return last
        om = S.op("pe", oproj, waits=(("s_gp", wo_ev), ("s_dve", prev_ev2)),
                  inc="s_pe")
        prev_ev2 = S.op("dve", lambda e, t=t: e.tensor_add(
            oacc[:, t * 1024:(t + 1) * 1024], psc[:, 0:1024], x_te[:, :]),
            waits=(("s_pe", om), ("s_sy", lx)), inc="s_dve")
        prev_pe2 = om

    fx_t = arf(FXT_, 1024)
    fxn_tm = ar[:, FXN_:FXN_ + 1024]
    xnq8 = xn_fm[:, 0:16384].rearrange("p (c n) -> p c n", c=8)
    prev_f_pe, prev_f_ev = prev_pe2, prev_ev2
    for t in range(NT):
        lx = sydma(lambda e, t=t: e.dma_start(out=fx_t[:, :],
                                              in_=xq[t * 128:(t + 1) * 128, :]),
                   waits=(("s_pe", prev_f_pe), ("s_dve", prev_f_ev)))
        fsel = S.op("dve", lambda e, t=t: (
            e.tensor_scalar_mul(st(T1), st(SMEAN0 + 2 * t), m_lo),
            e.tensor_scalar_mul(st(T2), st(SRSTD0 + 2 * t), m_lo))[-1],
            waits=(("s_sy", lx), ("s_pe", prev_f_pe)), inc="s_dve")
        fsel = S.op("dve", lambda e, t=t: (
            e.scalar_tensor_tensor(st(T1), st(SMEAN0 + 2 * t + 1), m_hi, st(T1),
                                   ALU.mult, ALU.add),
            e.scalar_tensor_tensor(st(T2), st(SRSTD0 + 2 * t + 1), m_hi, st(T2),
                                   ALU.mult, ALU.add))[-1],
            waits=(("s_dve", fsel),), inc="s_dve")
        rcp = S.op("dve", lambda e: e.tensor_scalar(
            fxn_tm[:, :], fx_t[:, :], st(T1), st(T2), ALU.subtract, ALU.mult),
            waits=(("s_dve", fsel),), inc="s_dve")
        trp = S.op("pe", lambda e: [
            e.matmul(psc[:, c * 128:(c + 1) * 128], fxn_tm[:, c * 128:(c + 1) * 128],
                     identb[:, :], start=True, stop=True) for c in range(8)][-1],
            waits=(("s_dve", rcp), ("s_dve", prev_f_ev),
                   ("s_sy", dbg_ev.get("dxn"))), inc="s_pe")
        prev_f_ev = S.op("dve", lambda e, t=t: e.tensor_copy(
            xnq8[:, :, t * 128:(t + 1) * 128],
            psc[:, 0:1024].rearrange("p (c n) -> p c n", c=8)),
            waits=(("s_pe", trp),), inc="s_dve")
        prev_f_pe = trp

    gpw = ar[:, GP_:GP_ + 4096]
    upw = ar[:, UP_:UP_ + 4096]
    dpw = attn_fm[:, 12288:16384]
    h_fm = attn_fm[:, 0:8192]
    silu_f = attn_fm[:, 8192:12288].bitcast(F32)
    last_comb = prev_f_ev
    last_pe = prev_f_pe
    for ex in range(8):
        gp_in = ap_in["gpT"][ex].rearrange("(c p) n -> p c n", p=128)
        up_in = ap_in["upT"][ex].rearrange("(c p) n -> p c n", p=128)
        dp_in = ap_in["dpT"][ex].rearrange("(c p) n -> p c n", p=128)
        wg = gpdma(lambda e, gp_in=gp_in: e.dma_start(
            out=gpw.rearrange("p (c n) -> p c n", c=8), in_=gp_in),
            waits=(("s_pe", last_pe),))
        wu = gpdma(lambda e, up_in=up_in: e.dma_start(
            out=upw.rearrange("p (c n) -> p c n", c=8), in_=up_in))
        wd = gpdma(lambda e, dp_in=dp_in: e.dma_start(
            out=dpw.rearrange("p (c n) -> p c n", c=4), in_=dp_in),
            waits=(("s_pe", last_pe), ("s_dve", last_comb),
                   ("s_sy", dbg_ev.get("datn"))))
        hev = last_comb
        for ib in range(4):
            def gu(e, ib=ib):
                last = None
                for half, wten in ((0, gpw), (1, upw)):
                    for b in range(4):
                        for c in range(8):
                            last = e.matmul(
                                psc[:, half * 2048 + b * 512:half * 2048 + (b + 1) * 512],
                                wten[:, c * 512 + ib * 128:c * 512 + ib * 128 + 128],
                                xn_fm[:, c * 2048 + b * 512:c * 2048 + (b + 1) * 512],
                                start=(c == 0), stop=(c == 7))
                return last
            gum = S.op("pe", gu, waits=(("s_gp", wu), ("s_dve", hev),
                                        ("s_sy", dbg_ev.get("datn"))), inc="s_pe")
            sil = S.op("act", lambda e: (
                [e.activation(silu_f[:, j * 512:(j + 1) * 512],
                              psc[:, j * 512:(j + 1) * 512], AF.Silu)
                 for j in range(4)],
                e.activation(psc[:, 0:256], silu_f[:, 0:256], AF.Square))[-1],
                waits=(("s_pe", gum),), inc="s_act")
            hev = S.op("dve", lambda e, ib=ib: e.tensor_mul(
                h_fm[:, ib * 2048:(ib + 1) * 2048], silu_f[:, 0:2048],
                psc[:, 2048:4096]),
                waits=(("s_act", sil),), inc="s_dve")
        for t in range(NT):
            def down(e, t=t):
                last = None
                for b in range(2):
                    for ic in range(4):
                        last = e.matmul(psc[:, b * 512:(b + 1) * 512],
                                        h_fm[:, ic * 2048 + t * 128:ic * 2048 + (t + 1) * 128],
                                        dpw[:, ic * 1024 + b * 512:ic * 1024 + (b + 1) * 512],
                                        start=(ic == 0), stop=(ic == 3))
                return last
            dm = S.op("pe", down, waits=(("s_gp", wd), ("s_dve", hev),
                                         ("s_dve", last_comb)), inc="s_pe")
            last_comb = S.op("dve", lambda e, t=t, ex=ex: e.scalar_tensor_tensor(
                oacc[:, t * 1024:(t + 1) * 1024], psc[:, 0:1024],
                cw_sb[:, t * 8 + ex:t * 8 + ex + 1],
                oacc[:, t * 1024:(t + 1) * 1024],
                ALU.mult, ALU.add),
                waits=(("s_pe", dm), ("s_dve", ev_c_dve)), inc="s_dve")
            last_pe = dm

    fin = sydma(lambda e: e.dma_start(
        out=out_d.rearrange("(t p) d -> p t d", p=128),
        in_=oacc.rearrange("p (t d) -> p t d", t=16)),
        waits=(("s_dve", last_comb),))
    S.plan["sy"].append(lambda e, fin=fin: e.wait_ge(S.sem["s_sy"], fin))


_NC_CACHE = {}


def _steps(par):
    s = np.zeros((2 * NT, L), dtype=np.float32)
    q = np.arange(L)
    for t in range(NT):
        l0 = 256 * t + 128 * par
        c_lo2 = max(0, 256 * t - 2942) - 128 * (1 - par)
        c_hi = l0 + 1151
        s[2 * t] = (q < c_lo2).astype(np.float32)
        s[2 * t + 1] = (q >= c_hi).astype(np.float32)
    return s


def kernel(L0_attn_x, L0_attn_Wq, L0_attn_bq, L0_attn_Wk, L0_attn_bk, L0_attn_Wv,
           L0_attn_bv, L0_attn_Wo, L0_attn_bo, L0_attn_relative_k, L0_attn_relative_v,
           L0_ffn_gate_weight, L0_ffn_e_score_correction_bias, L0_ffn_gate_proj,
           L0_ffn_up_proj, L0_ffn_down_proj):
    x = np.ascontiguousarray(np.asarray(L0_attn_x, dtype=np.float32))
    C = np.ascontiguousarray

    fm = np.zeros((8, 8), np.float32)
    for e_ in range(8):
        fm[e_, :e_] = 1.0
    hm = np.zeros((4, 4), np.float32)
    for g_ in range(4):
        hm[g_, :g_] = 1.0
    eye = np.eye(128, dtype=np.float32)

    shared = {
        "wqT": C(np.asarray(L0_attn_Wq, np.float32).T),
        "wkT": C(np.asarray(L0_attn_Wk, np.float32).T),
        "wvT": C(np.asarray(L0_attn_Wv, np.float32).T),
        "woT": C(np.asarray(L0_attn_Wo, np.float32).T),
        "bq": C(np.asarray(L0_attn_bq, np.float32).reshape(1, D)),
        "bk": C(np.asarray(L0_attn_bk, np.float32).reshape(1, D)),
        "bv": C(np.asarray(L0_attn_bv, np.float32).reshape(1, D)),
        "bo": C(np.asarray(L0_attn_bo, np.float32).reshape(1, D)),
        "rkT": C(np.asarray(L0_attn_relative_k, np.float32).T),
        "rv": C(np.asarray(L0_attn_relative_v, np.float32)),
        "gwT": C(np.asarray(L0_ffn_gate_weight, np.float32).T),
        "ebias": C(np.asarray(L0_ffn_e_score_correction_bias,
                              np.float32).reshape(1, 8)),
        "gpT": C(np.asarray(L0_ffn_gate_proj, np.float32).transpose(0, 2, 1)),
        "upT": C(np.asarray(L0_ffn_up_proj, np.float32).transpose(0, 2, 1)),
        "dpT": C(np.asarray(L0_ffn_down_proj, np.float32).transpose(0, 2, 1)),
        "identf": eye,
        "masklo": np.tril(np.ones((128, 128), np.float32), -1),
        "maskup": np.triu(np.ones((128, 128), np.float32), 1),
        "fmask64": np.tile(fm.reshape(1, 64), (128, 1)),
        "hmask16": np.tile(hm.reshape(1, 16), (128, 1)),
    }

    if "nc" not in _NC_CACHE:
        _NC_CACHE["nc"] = build_program()
    nc = _NC_CACHE["nc"]

    in_maps = []
    for c in range(8):
        b = c // 2
        par = c % 2
        m = dict(shared)
        m["xb"] = C(x[b])
        m["xq"] = C(x[b].reshape(32, 128, D)[par::2].reshape(NQ, D))
        m["steps"] = _steps(par)
        m["mcol"] = np.tile(np.array([[1.0 - par, float(par)]], np.float32),
                            (128, 1))
        m["idlo"] = C(eye * (1.0 - par))
        m["idhi"] = C(eye * par)
        in_maps.append(m)

    res = run_bass_kernel_spmd(nc, in_maps, core_ids=list(range(8)))
    out = np.empty((B, L, D), dtype=np.float32)
    for c in range(8):
        b = c // 2
        par = c % 2
        out[b].reshape(32, 128, D)[par::2] = res.results[c]["out"].reshape(NT, 128, D)
    return out


if __name__ == "__main__":
    build_program()
    print("program built ok")


# version 13
# speedup vs baseline: 1.6114x; 1.0879x over previous
import numpy as np
from contextlib import ExitStack

import concourse.bass as bass
import concourse.mybir as mybir
from concourse.bass_utils import run_bass_kernel_spmd

F32 = mybir.dt.float32
BF16 = mybir.dt.bfloat16
AF = mybir.ActivationFunctionType
ALU = mybir.AluOpType
AX = mybir.AxisListType

B, L, D = 4, 4096, 1024
NH, DK = 16, 64
NQ = 2048
NT = 16
NJ = 2047
SCALE = 0.125

IMG_W = 4348
IMG_FL = 128
IMG_RS = IMG_W + 2 * IMG_FL
PW2 = 4349
BU_W = 4224
AUF_W = 4222
PB_L, PB_R = 3072, 1024
PB_W = PB_L + L + PB_R

AR_W = 43008
KA_, KB_ = 0, 4096
QA_, QB_ = 8192, 10240
VV_ = 12288
WW_ = 16416
QR_ = 19488
BU_ = 21792
PBUF_ = 26016
AUF_ = 34208
PT_ = 38430
APT_ = 40478
ATR_ = 42526
XT_, SQ_, XN32_, XNTM_, XF_ = 0, 2048, 4096, 6144, 8192
OACC_, WO_, XTE_ = 0, 32768, 40960
FXT_, FSQ_, FXN_ = 32768, 34816, 36864
GP_, UP_ = 32768, 36864

ENGS = ("pe", "dve", "act", "sy", "gp")

DBG = False

(MNEG, RSA, RSB, RSUM, RINV, CSLL, CSLH, CSRL, CSRH, RS0, RS2, WSUM,
 MEAN, VAR, RSTD, SSQ, T1, T2, T3, T4, QK0, QK2, EPS) = range(23)
SMEAN0 = 23
SRSTD0 = 55


class Sched:
    def __init__(self):
        self.plan = {k: [] for k in ENGS}
        self.cnt = {}
        self.sem = {}

    def add_sem(self, name, handle):
        self.sem[name] = handle
        self.cnt[name] = 0

    def op(self, eng, fn, waits=(), inc=None, n=1):
        val = None
        if inc is not None:
            self.cnt[inc] += n
            val = self.cnt[inc]
        wl = tuple((s, v) for (s, v) in waits if v is not None and v > 0)
        sems = self.sem

        def run(e, fn=fn, wl=wl, inc=inc, n=n):
            seen = {}
            for s, v in wl:
                if seen.get(s, -1) < v:
                    seen[s] = v
            for s, v in seen.items():
                e.wait_ge(sems[s], v)
            ins = fn(e)
            if inc is not None:
                ins.then_inc(sems[inc], n)

        self.plan[eng].append(run)
        return val


def build_program():
    nc = bass.Bass("TRN2", target_bir_lowering=False)

    ap_in = {}
    def din(name, shape, dt=F32):
        ap_in[name] = nc.dram_tensor(name, shape, dt, kind="ExternalInput").ap()
        return ap_in[name]

    din("xb", [L, D]); din("xq", [NQ, D])
    din("wqT", [D, D]); din("wkT", [D, D]); din("wvT", [D, D]); din("woT", [D, D])
    din("bq", [1, D]); din("bk", [1, D]); din("bv", [1, D]); din("bo", [1, D])
    din("rkT", [DK, NJ]); din("rv", [NJ, DK])
    din("gwT", [D, 8]); din("ebias", [1, 8])
    din("gpT", [8, D, 512]); din("upT", [8, D, 512]); din("dpT", [8, 512, D])
    din("identf", [128, 128])
    din("idlo", [128, 128]); din("idhi", [128, 128])
    din("masklo", [128, 128]); din("maskup", [128, 128])
    din("fmask64", [128, 64]); din("hmask16", [128, 16])
    din("steps", [2 * NT, L])
    din("mcol", [128, 2])

    out_d = nc.dram_tensor("out", [NQ, D], F32, kind="ExternalOutput").ap()
    qrimg = nc.dram_tensor("qrimg", [2 * 128 * IMG_RS], BF16, kind="Internal")
    pimg = nc.dram_tensor("pimg", [2 * 128 * PW2], BF16, kind="Internal")
    dbg = {}
    if DBG:
        dbg["dxn"] = nc.dram_tensor("dxn", [128, 8 * L], BF16,
                                    kind="ExternalOutput").ap()
        dbg["dcw"] = nc.dram_tensor("dcw", [128, 128], F32,
                                    kind="ExternalOutput").ap()
        dbg["datn"] = nc.dram_tensor("datn", [128, 8 * NQ], BF16,
                                     kind="ExternalOutput").ap()
        dbg["dscr"] = nc.dram_tensor("dscr", [128, NT * 4096], BF16,
                                     kind="ExternalOutput").ap()
        dbg["dpb"] = nc.dram_tensor("dpb", [128, NT * 4096], BF16,
                                    kind="ExternalOutput").ap()
        dbg["dstats"] = nc.dram_tensor("dstats", [128, NT * 32], F32,
                                       kind="ExternalOutput").ap()
        dbg["drelv"] = nc.dram_tensor("drelv", [128, NT * 129], F32,
                                      kind="ExternalOutput").ap()
        dbg["dauf"] = nc.dram_tensor("dauf", [128, NT * AUF_W], BF16,
                                     kind="ExternalOutput").ap()
        dbg["datr"] = nc.dram_tensor("datr", [128, NT * 64], BF16,
                                     kind="ExternalOutput").ap()
        dbg["dgt"] = nc.dram_tensor("dgt", [128, NT * 80], F32,
                                    kind="ExternalOutput").ap()

    es = ExitStack()
    with es:
        S = Sched()
        for s in ("s_pe", "s_dve", "s_act", "s_sy", "s_gp"):
            S.add_sem(s, es.enter_context(nc.semaphore(s)))

        t_ = {}
        def sb(name, shape, dt):
            t_[name] = es.enter_context(nc.sbuf_tensor(name, shape, dt))
            return t_[name]

        sb("arena", [128, AR_W], BF16)
        sb("xn_fm", [128, 8 * L], BF16)
        sb("attn_fm", [128, 8 * NQ], BF16)
        sb("stats", [128, 96], F32)
        sb("cw", [128, 128], F32)
        sb("identb", [128, 128], BF16)
        sb("identf_sb", [128, 128], F32)
        sb("idlo_sb", [128, 128], BF16)
        sb("idhi_sb", [128, 128], BF16)
        sb("masklo_sb", [128, 128], BF16)
        sb("maskup_sb", [128, 128], BF16)
        sb("fmask_sb", [128, 64], F32)
        sb("hmask_sb", [128, 16], F32)
        sb("mcol_sb", [128, 2], F32)
        sb("rkT_sb", [64, 2048], BF16)
        sb("rv_sb", [128, 16 * 65], BF16)
        sb("gw_sb", [128, 64], F32)
        sb("eb_sb", [1, 8], F32)
        sb("ebb_sb", [128, 8], F32)
        sb("rv0b", [128, 64], F32)
        sb("rv2b", [128, 64], F32)
        sb("bias_sb", [1, 4 * D], BF16)
        sb("ones_row", [1, 512], BF16)
        sb("onesf_row", [1, 128], F32)
        sb("qkT_row", [2, 128], BF16)
        sb("relv_f", [128, 65], F32)
        sb("add1_f", [128, 64], F32)
        sb("gt8", [128, 80], F32)
        t_["psc"] = es.enter_context(nc.psum_tensor("psc", [128, 4096], F32))

        with nc.Block() as block:
            build_plan(S, t_, ap_in, out_d, qrimg, pimg, dbg)
            for eng, attr in (("sy", "sync"), ("gp", "gpsimd"), ("pe", "tensor"),
                              ("dve", "vector"), ("act", "scalar")):
                lst = S.plan[eng]
                def runner(e, lst=lst, eng=eng):
                    for i, f in enumerate(lst):
                        try:
                            f(e)
                        except Exception:
                            print(f"FAILING OP: eng={eng} index={i}/{len(lst)}")
                            raise
                getattr(block, attr)(runner)
    return nc


def build_plan(S, t_, ap_in, out_d, qrimg, pimg, dbg):
    ar = t_["arena"]; xn_fm = t_["xn_fm"]; attn_fm = t_["attn_fm"]
    stats = t_["stats"]; cw_sb = t_["cw"]
    identb = t_["identb"]; identf_sb = t_["identf_sb"]
    idlo_sb = t_["idlo_sb"]; idhi_sb = t_["idhi_sb"]
    masklo_sb = t_["masklo_sb"]; maskup_sb = t_["maskup_sb"]
    fmask_sb = t_["fmask_sb"]; hmask_sb = t_["hmask_sb"]; mcol_sb = t_["mcol_sb"]
    rkT_sb = t_["rkT_sb"]; rv_sb = t_["rv_sb"]; gw_sb = t_["gw_sb"]
    eb_sb = t_["eb_sb"]; ebb_sb = t_["ebb_sb"]; rv0b = t_["rv0b"]; rv2b = t_["rv2b"]
    bias_sb = t_["bias_sb"]; ones_row = t_["ones_row"]; onesf_row = t_["onesf_row"]
    qkT_row = t_["qkT_row"]
    relv_f = t_["relv_f"]; add1_f = t_["add1_f"]
    gt8 = t_["gt8"]; psc = t_["psc"]

    def st(i):
        return stats[:, i:i + 1]

    def arf(c0, ncols):
        return ar[:, c0:c0 + 2 * ncols].bitcast(F32)

    def dram_flat(th, off, rowstride, p, width):
        return bass.AP(tensor=th, offset=off, ap=[[rowstride, p], [1, width]])

    m_lo = mcol_sb[:, 0:1]
    m_hi = mcol_sb[:, 1:2]

    def gpdma(fn, waits=()):
        return S.op("gp", fn, waits=waits, inc="s_gp", n=16)

    def sydma(fn, waits=()):
        return S.op("sy", fn, waits=waits, inc="s_sy", n=16)

    def sydma_split(fn2, waits=()):
        def g(e, fn2=fn2):
            fn2(e, 0).then_inc(S.sem["s_sy"], 16)
            return fn2(e, 64)
        S.cnt["s_sy"] += 16
        return S.op("sy", g, waits=waits, inc="s_sy", n=16)

    ld0 = gpdma(lambda e: e.dma_start(out=identb[:, :], in_=ap_in["identf"][:, :]))
    gpdma(lambda e: e.dma_start(out=identf_sb[:, :], in_=ap_in["identf"][:, :]))
    gpdma(lambda e: e.dma_start(out=idlo_sb[:, :], in_=ap_in["idlo"][:, :]))
    gpdma(lambda e: e.dma_start(out=idhi_sb[:, :], in_=ap_in["idhi"][:, :]))
    gpdma(lambda e: e.dma_start(out=masklo_sb[:, :], in_=ap_in["masklo"][:, :]))
    gpdma(lambda e: e.dma_start(out=maskup_sb[:, :], in_=ap_in["maskup"][:, :]))
    gpdma(lambda e: e.dma_start(out=fmask_sb[:, :], in_=ap_in["fmask64"][:, :]))
    gpdma(lambda e: e.dma_start(out=hmask_sb[:, :], in_=ap_in["hmask16"][:, :]))
    gpdma(lambda e: e.dma_start(out=mcol_sb[:, :], in_=ap_in["mcol"][:, :]))
    gpdma(lambda e: e.dma_start(out=rkT_sb[0:64, 0:NJ], in_=ap_in["rkT"][:, :]))
    gpdma(lambda e: e.dma_start(out=gw_sb[:, :].rearrange("p (c e) -> p c e", c=8),
                                in_=ap_in["gwT"].rearrange("(c p) e -> p c e", p=128)))
    gpdma(lambda e: e.dma_start(out=eb_sb[:, :], in_=ap_in["ebias"][:, :]))
    for i, nm in enumerate(("bq", "bk", "bv", "bo")):
        gpdma(lambda e, i=i, nm=nm: e.dma_start(
            out=bias_sb[0:1, i * D:(i + 1) * D], in_=ap_in[nm][:, :]))
    mrv = S.op("dve", lambda e: e.memset(rv_sb[:, :], 0.0), inc="s_dve")
    rv_last = None
    for k in range(16):
        r0, r1 = k * 128, min(NJ, (k + 1) * 128)
        rv_last = gpdma(lambda e, k=k, r0=r0, r1=r1: e.dma_start(
            out=rv_sb[0:r1 - r0, k * 65:k * 65 + 64], in_=ap_in["rv"][r0:r1, :]),
            waits=(("s_dve", mrv),))
    ld_consts = rv_last

    S.op("dve", lambda e: e.memset(ones_row[:, :], 1.0), inc="s_dve")
    S.op("dve", lambda e: e.memset(onesf_row[:, :], 1.0), inc="s_dve")
    ev_eps = S.op("dve", lambda e: (
        e.memset(st(EPS), 1e-5),
        e.memset(st(MNEG), 0.0))[-1], inc="s_dve")
    rv_ones = S.op("dve", lambda e: e.memset(
        rv_sb[:, :].rearrange("p (k c) -> p k c", c=65)[:, :, 64:65], 1.0),
        waits=(("s_gp", rv_last),), inc="s_dve")
    S.op("dve", lambda e: e.memset(
        ar[:, VV_:VV_ + 4128].rearrange("p (k c) -> p k c", c=129)[:, :, 64:65], 1.0),
        inc="s_dve")
    S.op("dve", lambda e: e.memset(ar[:, PBUF_:PBUF_ + PB_L], 0.0), inc="s_dve")
    S.op("dve", lambda e: e.memset(ar[:, PBUF_ + PB_L + L:PBUF_ + PB_W], 0.0),
         inc="s_dve")
    mzf = S.op("dve", lambda e: e.memset(ar[:, BU_:BU_ + 128], 0.0), inc="s_dve")
    fl_ev = None
    for slot in range(2):
        for off in (0, IMG_FL + IMG_W):
            fl_ev = sydma(lambda e, slot=slot, off=off: e.dma_start(
                out=dram_flat(qrimg, slot * 128 * IMG_RS + off, IMG_RS, 128, IMG_FL),
                in_=ar[:, BU_:BU_ + 128]),
                waits=(("s_dve", mzf),))

    pe_eb = S.op("pe", lambda e: e.matmul(psc[:, 0:8], onesf_row[0:1, :],
                                          eb_sb[0:1, :], start=True, stop=True),
                 waits=(("s_gp", ld_consts),), inc="s_pe")
    pe_rv0 = S.op("pe", lambda e: e.matmul(psc[:, 512:576], ones_row[0:1, 0:128],
                                           rv_sb[0:1, 0:64], start=True, stop=True),
                  waits=(("s_dve", rv_ones),), inc="s_pe")
    cp2046 = sydma(lambda e: e.dma_start(out=qkT_row[0:1, 0:64],
                                         in_=rv_sb[126:127, 15 * 65:15 * 65 + 64]),
                   waits=(("s_dve", rv_ones),))
    pe_rv2 = S.op("pe", lambda e: e.matmul(psc[:, 1024:1088], ones_row[0:1, 0:128],
                                           qkT_row[0:1, 0:64], start=True, stop=True),
                  waits=(("s_sy", cp2046),), inc="s_pe")
    ev_ebb = S.op("dve", lambda e: (
        e.tensor_copy(ebb_sb[:, :], psc[:, 0:8]),
        e.tensor_copy(rv0b[:, :], psc[:, 512:576]),
        e.tensor_copy(rv2b[:, :], psc[:, 1024:1088]))[-1],
        waits=(("s_pe", pe_rv2), ("s_pe", pe_rv0), ("s_pe", pe_eb)), inc="s_dve")

    x_t = arf(XT_, 1024)
    sq_t = arf(SQ_, 1024)
    xn_tm = ar[:, XNTM_:XNTM_ + 1024]
    xb = ap_in["xb"]; xq = ap_in["xq"]

    xv8 = xn_fm[:, :].rearrange("p (c n) -> p c n", c=8)
    prev_tr, prev_ev = 0, 0
    for t in range(32):
        lx = sydma(lambda e, t=t: e.dma_start(out=x_t[:, :],
                                              in_=xb[t * 128:(t + 1) * 128, :]),
                   waits=(("s_pe", prev_tr),))
        l1 = S.op("dve", lambda e: (
            e.reduce_sum(st(T1), x_t[:, :], AX.X),
            e.tensor_tensor(sq_t[:, :], x_t[:, :], x_t[:, :], ALU.mult))[-1],
            waits=(("s_sy", lx),), inc="s_dve")
        l2 = S.op("dve", lambda e, t=t: (
            e.reduce_sum(st(SSQ), sq_t[:, :], AX.X),
            e.tensor_scalar_mul(st(SMEAN0 + t), st(T1), 1.0 / 1024.0))[-1],
            waits=(("s_dve", l1),), inc="s_dve")
        l3 = S.op("dve", lambda e, t=t: (
            e.tensor_scalar_mul(st(VAR), st(SSQ), 1.0 / 1024.0),
            e.tensor_tensor(st(T2), st(SMEAN0 + t), st(SMEAN0 + t), ALU.mult))[-1],
            waits=(("s_dve", l2),), inc="s_dve")
        lnc = S.op("dve", lambda e: e.tensor_sub(st(VAR), st(VAR), st(T2)),
                   waits=(("s_dve", l3),), inc="s_dve")
        S.op("act", lambda e: e.activation(st(T1), st(VAR), AF.Sqrt, bias=st(EPS)),
             waits=(("s_dve", lnc),))
        rstd = S.op("act", lambda e: e.activation(sq_t[:, 0:512], x_t[:, 0:512],
                                                  AF.Square), inc="s_act")
        rc1 = S.op("dve", lambda e, t=t: e.reciprocal(st(SRSTD0 + t), st(T1)),
                   waits=(("s_act", rstd), ("s_pe", prev_tr)), inc="s_dve")
        rcp = S.op("dve", lambda e, t=t: e.tensor_scalar(
            xn_tm[:, :], x_t[:, :], st(SMEAN0 + t), st(SRSTD0 + t),
            ALU.subtract, ALU.mult),
            waits=(("s_dve", rc1),), inc="s_dve")
        trp = S.op("pe", lambda e: [
            e.matmul(psc[:, c * 128:(c + 1) * 128], xn_tm[:, c * 128:(c + 1) * 128],
                     identb[:, :], start=True, stop=True) for c in range(8)][-1],
            waits=(("s_dve", rcp), ("s_gp", ld0), ("s_dve", ev_ebb),
                   ("s_dve", prev_ev)), inc="s_pe")
        prev_ev = S.op("dve", lambda e, t=t: e.tensor_copy(
            xv8[:, :, t * 128:(t + 1) * 128],
            psc[:, 0:1024].rearrange("p (c n) -> p c n", c=8)),
            waits=(("s_pe", trp),), inc="s_dve")
        prev_tr = trp
    ev_xn_done = prev_ev
    ev_b_pe = prev_tr
    dbg_ev = {}
    if DBG:
        dbg_ev["dxn"] = sydma(lambda e: e.dma_start(out=dbg["dxn"], in_=xn_fm[:, :]),
                              waits=(("s_dve", ev_xn_done),))

    xf32 = arf(XF_, 1024)
    xn32 = arf(XN32_, 1024)
    sC = gt8[:, 0:8]; cC = gt8[:, 8:16]; G4 = gt8[:, 16:20]
    cnt4 = gt8[:, 20:24]; gm4 = gt8[:, 24:28]
    A16 = gt8[:, 32:48]; C16 = gt8[:, 48:64]
    sm8 = gt8[:, 64:72]; tmp8 = gt8[:, 72:80]

    prev_c_pe = ev_b_pe
    ev_cw = 0
    prev_sig = 0
    prev_dgt = 0
    for t in range(NT):
        lx = sydma(lambda e, t=t: e.dma_start(out=x_t[:, :],
                                              in_=xq[t * 128:(t + 1) * 128, :]),
                   waits=(("s_pe", prev_c_pe), ("s_dve", ev_xn_done)))
        xsel = S.op("dve", lambda e, t=t: (
            e.tensor_scalar_mul(st(T1), st(SMEAN0 + 2 * t), m_lo),
            e.tensor_scalar_mul(st(T2), st(SRSTD0 + 2 * t), m_lo))[-1],
            waits=(("s_sy", lx), ("s_pe", prev_c_pe)), inc="s_dve")
        xsel = S.op("dve", lambda e, t=t: (
            e.scalar_tensor_tensor(st(T1), st(SMEAN0 + 2 * t + 1), m_hi, st(T1),
                                   ALU.mult, ALU.add),
            e.scalar_tensor_tensor(st(T2), st(SRSTD0 + 2 * t + 1), m_hi, st(T2),
                                   ALU.mult, ALU.add))[-1],
            waits=(("s_dve", xsel),), inc="s_dve")
        xnf = S.op("dve", lambda e: e.tensor_scalar(
            xn32[:, :], x_t[:, :], st(T1), st(T2), ALU.subtract, ALU.mult),
            waits=(("s_dve", xsel),), inc="s_dve")
        trp = S.op("pe", lambda e: [
            e.transpose(psc[:, c * 128:(c + 1) * 128], xn32[:, c * 128:(c + 1) * 128],
                        identf_sb[:, :]) for c in range(8)][-1],
            waits=(("s_dve", xnf),), inc="s_pe")
        evf = S.op("dve", lambda e: e.tensor_copy(xf32[:, :], psc[:, 0:1024]),
                   waits=(("s_pe", trp),), inc="s_dve")
        lg = S.op("pe", lambda e: [
            e.matmul(psc[:, 1024:1032], xf32[:, c * 128:(c + 1) * 128],
                     gw_sb[:, c * 8:(c + 1) * 8], start=(c == 0), stop=(c == 7))
            for c in range(8)][-1],
            waits=(("s_dve", evf), ("s_act", prev_sig)), inc="s_pe")
        S.op("act", lambda e: e.activation(sC, psc[:, 1024:1032], AF.Sigmoid),
             waits=(("s_pe", lg), ("s_dve", ev_cw), ("s_sy", prev_dgt)))
        sig = S.op("act", lambda e: e.activation(xf32[:, 0:512], xf32[:, 512:1024],
                                                 AF.Square), inc="s_act")
        prev_sig = sig
        cprev = S.op("dve", lambda e: e.tensor_add(cC, sC, ebb_sb[:, :]),
                     waits=(("s_act", sig), ("s_dve", ev_ebb)), inc="s_dve")
        cprev = S.op("dve", lambda e: e.tensor_add(G4, cC[:, 0:8:2], cC[:, 1:8:2]),
                     waits=(("s_dve", cprev),), inc="s_dve")
        cprev = S.op("dve", lambda e: (
            e.tensor_tensor(A16.rearrange("p (a b) -> p a b", a=4),
                            G4.unsqueeze(2).broadcast_to((128, 4, 4)),
                            G4.unsqueeze(1).broadcast_to((128, 4, 4)),
                            ALU.is_lt),
            e.tensor_tensor(C16.rearrange("p (a b) -> p a b", a=4),
                            G4.unsqueeze(2).broadcast_to((128, 4, 4)),
                            G4.unsqueeze(1).broadcast_to((128, 4, 4)),
                            ALU.is_equal))[-1],
            waits=(("s_dve", cprev),), inc="s_dve")
        cprev = S.op("dve", lambda e: e.tensor_tensor(C16, C16, hmask_sb[:, :],
                                                      ALU.mult),
                     waits=(("s_dve", cprev),), inc="s_dve")
        cprev = S.op("dve", lambda e: e.tensor_add(A16, A16, C16),
                     waits=(("s_dve", cprev),), inc="s_dve")
        cprev = S.op("dve", lambda e: e.tensor_reduce(
            cnt4, A16.rearrange("p (a b) -> p a b", a=4), AX.X, ALU.add),
            waits=(("s_dve", cprev),), inc="s_dve")
        cprev = S.op("dve", lambda e: e.tensor_single_scalar(gm4, cnt4, 2.0,
                                                             ALU.is_lt),
                     waits=(("s_dve", cprev),), inc="s_dve")
        cprev = S.op("dve", lambda e: e.tensor_copy(
            sm8.rearrange("p (g j) -> p g j", g=4),
            gm4.unsqueeze(2).broadcast_to((128, 4, 2))),
            waits=(("s_dve", cprev),), inc="s_dve")
        cprev = S.op("dve", lambda e: e.tensor_tensor(tmp8, cC, sm8, ALU.mult),
                     waits=(("s_dve", cprev),), inc="s_dve")
        cprev = S.op("dve", lambda e: (
            e.tensor_tensor(relv_f[:, 0:64].rearrange("p (a b) -> p a b", a=8),
                            tmp8.unsqueeze(2).broadcast_to((128, 8, 8)),
                            tmp8.unsqueeze(1).broadcast_to((128, 8, 8)),
                            ALU.is_lt),
            e.tensor_tensor(add1_f[:, 0:64].rearrange("p (a b) -> p a b", a=8),
                            tmp8.unsqueeze(2).broadcast_to((128, 8, 8)),
                            tmp8.unsqueeze(1).broadcast_to((128, 8, 8)),
                            ALU.is_equal))[-1],
            waits=(("s_dve", cprev),), inc="s_dve")
        cprev = S.op("dve", lambda e: e.tensor_tensor(
            add1_f[:, 0:64], add1_f[:, 0:64], fmask_sb[:, :], ALU.mult),
            waits=(("s_dve", cprev),), inc="s_dve")
        cprev = S.op("dve", lambda e: e.tensor_add(
            relv_f[:, 0:64], relv_f[:, 0:64], add1_f[:, 0:64]),
            waits=(("s_dve", cprev),), inc="s_dve")
        cprev = S.op("dve", lambda e: e.tensor_reduce(
            tmp8, relv_f[:, 0:64].rearrange("p (a b) -> p a b", a=8), AX.X, ALU.add),
            waits=(("s_dve", cprev),), inc="s_dve")
        cprev = S.op("dve", lambda e: e.tensor_single_scalar(tmp8, tmp8, 2.0,
                                                             ALU.is_lt),
                     waits=(("s_dve", cprev),), inc="s_dve")
        cprev = S.op("dve", lambda e: e.tensor_tensor(tmp8, tmp8, sC, ALU.mult),
                     waits=(("s_dve", cprev),), inc="s_dve")
        cprev = S.op("dve", lambda e: e.reduce_sum(st(WSUM), tmp8, AX.X),
                     waits=(("s_dve", cprev),), inc="s_dve")
        cprev = S.op("dve", lambda e: e.tensor_scalar_add(st(T1), st(WSUM), 1e-20),
                     waits=(("s_dve", cprev),), inc="s_dve")
        cprev = S.op("dve", lambda e: e.reciprocal(st(T2), st(T1)),
                     waits=(("s_dve", cprev),), inc="s_dve")
        ev_cw = S.op("dve", lambda e, t=t: e.tensor_scalar_mul(
            cw_sb[:, t * 8:(t + 1) * 8], tmp8, st(T2)),
            waits=(("s_dve", cprev),), inc="s_dve")
        if DBG:
            prev_dgt = sydma(lambda e, t=t: e.dma_start(
                out=dbg["dgt"][:, t * 80:(t + 1) * 80], in_=gt8[:, :]),
                waits=(("s_dve", ev_cw),))
        prev_c_pe = lg
    ev_c_dve = ev_cw
    if DBG:
        sydma(lambda e: e.dma_start(out=dbg["dcw"], in_=cw_sb[:, :]),
              waits=(("s_dve", ev_c_dve),))

    K_aug = [ar[:, KA_:KA_ + 4096], ar[:, KB_:KB_ + 4096]]
    Q_aug = [ar[:, QA_:QA_ + 2048], ar[:, QB_:QB_ + 2048]]
    v_pair = ar[:, VV_:VV_ + 4128]
    ww_raw = ar[:, WW_:WW_ + 1024]
    ww_lo = ar[:, WW_ + 1024:WW_ + 2048]
    ww_hi = ar[:, WW_ + 2048:WW_ + 3072]
    qr_s = ar[:, QR_:QR_ + IMG_W]
    bu = ar[:, BU_:BU_ + BU_W]
    p_b = ar[:, PBUF_:PBUF_ + PB_W]
    au = ar[:, AUF_:AUF_ + AUF_W]
    pt_s = ar[:, PT_:PT_ + 2048]
    apt_s = ar[:, APT_:APT_ + 2048]
    atr_s = ar[:, ATR_:ATR_ + 64]

    def geo(t):
        w0 = max(0, 256 * t - 2942)
        ce = min(BU_W, 256 * t + 1279)
        roff = w0 + 3197 - 256 * t
        wlo2 = max(0, 256 * t - 3070)
        whi2_a = max(0, 256 * t - 3070)
        whi2_b = max(0, 256 * t - 2942)
        return w0, ce, roff, wlo2, whi2_a, whi2_b

    ev = {
        "exp": S.cnt["s_act"],
        "epi_dve": S.cnt["s_dve"],
        "qk_pe": S.cnt["s_pe"],
        "band_pe": S.cnt["s_pe"],
        "aread_sy": [0, 0],
        "aread_gp": [0, 0],
        "pbuf_sy": 0, "pbuf_gp": 0, "pbuf_dve": 0, "pbuf_pe": 0,
        "ev_v": 0,
        "bu_ready": 0,
    }
    gtile = [0]
    pend = [None]

    def emit_front1(hp, hi, t, Ka, Qa):
        g = gtile[0]; gtile[0] += 1
        slot = g % 2
        w0, ce, roff, wlo2, whi2_a, whi2_b = geo(t)

        qr_mm = S.op("pe", lambda e: [
            e.matmul(psc[:, n0:n1], Qa[0:64, t * 128:(t + 1) * 128],
                     rkT_sb[0:64, n0:n1], start=True, stop=True)
            for (n0, n1) in ((0, 512), (512, 1024), (1024, 1536), (1536, 2047))][-1],
            waits=(("s_act", ev["exp"]), ("s_dve", ev["ev_v"]), ("s_gp", ld0),
                   ("s_sy", fl_ev)),
            inc="s_pe")
        qr_ev = S.op("dve", lambda e: (
            e.tensor_copy(stats[:, QK0:QK0 + 1], psc[:, 0:1]),
            e.tensor_copy(stats[:, QK2:QK2 + 1], psc[:, 2046:2047]),
            e.tensor_copy(qr_s[:, 127:127 + 512], psc[:, 0:512]),
            e.tensor_copy(qr_s[:, 639:639 + 512], psc[:, 512:1024]),
            e.tensor_copy(qr_s[:, 1151:1151 + 512], psc[:, 1024:1536]),
            e.tensor_copy(qr_s[:, 1663:1663 + 511], psc[:, 1536:2047]))[-1],
            waits=(("s_pe", qr_mm),), inc="s_dve")
        pads = S.op("dve", lambda e: (
            e.tensor_scalar(qr_s[:, 0:127], identb[:, 0:127], 0.0,
                            stats[:, QK0:QK0 + 1], ALU.mult, ALU.add),
            e.tensor_scalar(qr_s[:, 2174:2301], identb[:, 0:127], 0.0,
                            stats[:, QK2:QK2 + 1], ALU.mult, ALU.add))[-1],
            waits=(("s_dve", qr_ev),), inc="s_dve")
        qkT_mm = S.op("pe", lambda e: e.matmul(
            psc[0:2, 2304:2432], stats[:, QK0:QK0 + 2], identf_sb[:, :],
            start=True, stop=True),
            waits=(("s_dve", qr_ev), ("s_dve", ev.get("qkT_ev", 0))), inc="s_pe")
        qkT_ev = S.op("dve", lambda e: e.tensor_copy(qkT_row[0:2, :],
                                                     psc[0:2, 2304:2432]),
                      waits=(("s_pe", qkT_mm), ("s_sy", ev.get("qrow", 0))),
                      inc="s_dve")
        ev["qkT_ev"] = qkT_ev
        qrow_w = sydma(lambda e, t=t: e.dma_start(
            out=Qa[64:66, t * 128:(t + 1) * 128], in_=qkT_row[0:2, :]),
            waits=(("s_dve", qkT_ev),))
        ev["qrow"] = qrow_w
        step_ev = gpdma(lambda e, t=t: e.dma_start(
            out=Ka[64:66, :], in_=ap_in["steps"][2 * t:2 * t + 2, :]),
            waits=(("s_pe", ev["qk_pe"]),))
        img_w = sydma(lambda e, slot=slot: e.dma_start(
            out=dram_flat(qrimg, slot * 128 * IMG_RS + IMG_FL, IMG_RS, 128, 2174),
            in_=qr_s[:, 0:2174]),
            waits=(("s_dve", qr_ev), ("s_dve", pads)))
        img_w = sydma(lambda e, slot=slot: e.dma_start(
            out=dram_flat(qrimg, slot * 128 * IMG_RS + IMG_FL + 2174, IMG_RS,
                          128, 2174),
            in_=qr_s[:, 127:2301]),
            waits=(("s_dve", qr_ev), ("s_dve", pads)))
        if t == 0:
            bz = S.op("dve", lambda e: e.memset(bu[:, :], 0.0),
                      waits=(("s_pe", ev["band_pe"]),), inc="s_dve")
        else:
            pw0_ = geo(t - 1)[0]
            if w0 > pw0_:
                bz = S.op("dve", lambda e, pw0_=pw0_, w0=w0: e.memset(
                    bu[:, pw0_:w0], 0.0),
                    waits=(("s_pe", ev["band_pe"]),), inc="s_dve")
            else:
                bz = ev["bu_ready"]
        band_r = sydma(lambda e, slot=slot, w0=w0, ce=ce, roff=roff: e.dma_start(
            out=bu[:, w0:ce],
            in_=dram_flat(qrimg, slot * 128 * IMG_RS + roff, IMG_RS - 1,
                          128, ce - w0)),
            waits=(("s_sy", img_w), ("s_pe", ev["band_pe"]), ("s_dve", bz)))
        ev["bu_ready"] = bz
        return dict(hp=hp, hi=hi, t=t, Ka=Ka, Qa=Qa, slot=slot, w0=w0, ce=ce,
                    wlo2=wlo2, whi2_a=whi2_a, whi2_b=whi2_b, band_r=band_r,
                    qrow_w=qrow_w, step_ev=step_ev)

    def emit_front2(fs):
        hp, hi, t, Ka, Qa = fs["hp"], fs["hi"], fs["t"], fs["Ka"], fs["Qa"]
        w0, ce = fs["w0"], fs["ce"]
        band_r, qrow_w, step_ev = fs["band_r"], fs["qrow_w"], fs["step_ev"]
        exp_ps = []
        qk_ps = []
        for PS in range(4):
            off = 1024 * (PS % 2)
            lo_banks = [b for b in range(2)
                        if max(128 + 512 * (2 * PS + b), w0)
                        < min(128 + 512 * (2 * PS + b) + 512, ce)]
            hi_banks = [b for b in range(2)
                        if max(512 * (2 * PS + b), w0)
                        < min(512 * (2 * PS + b) + 512, ce)]
            def qk(e, lo_banks=lo_banks, hi_banks=hi_banks, t=t, PS=PS, off=off):
                last = None
                for b in range(2):
                    gb = 2 * PS + b
                    nadd = (b in lo_banks) + (b in hi_banks)
                    last = e.matmul(psc[:, off + b * 512:off + (b + 1) * 512],
                                    Qa[0:66, t * 128:(t + 1) * 128],
                                    Ka[0:66, gb * 512:(gb + 1) * 512],
                                    start=True, stop=(nadd == 0))
                    if b in lo_banks:
                        nadd -= 1
                        last = e.matmul(psc[:, off + b * 512:off + (b + 1) * 512],
                                        idlo_sb[:, :],
                                        bu[:, 128 + 512 * gb:640 + 512 * gb],
                                        start=False, stop=(nadd == 0))
                    if b in hi_banks:
                        last = e.matmul(psc[:, off + b * 512:off + (b + 1) * 512],
                                        idhi_sb[:, :],
                                        bu[:, 512 * gb:512 * gb + 512],
                                        start=False, stop=True)
                return last
            qk_mm = S.op("pe", qk, waits=(
                ("s_sy", band_r), ("s_sy", qrow_w), ("s_gp", step_ev),
                ("s_act", exp_ps[PS - 2] if PS >= 2 else ev["exp"])), inc="s_pe")
            qk_ps.append(qk_mm)
            dscr_ev = None
            if DBG and hp == 0 and hi == 0:
                dcp = S.op("dve", lambda e, off=off: e.tensor_copy(
                    ar[:, PT_:PT_ + 2048].bitcast(F32)[:, 0:1024],
                    psc[:, off:off + 1024]),
                    waits=(("s_pe", qk_mm),), inc="s_dve")
                dscr_ev = sydma(lambda e, t=t, PS=PS: e.dma_start(
                    out=dbg["dscr"][:, t * 4096 + 1024 * PS:
                                    t * 4096 + 1024 * PS + 1024],
                    in_=ar[:, PT_:PT_ + 2048].bitcast(F32)[:, 0:1024]),
                    waits=(("s_dve", dcp),))
            expp = S.op("act", lambda e, PS=PS, off=off: e.activation(
                p_b[:, PB_L + 1024 * PS:PB_L + 1024 * PS + 1024],
                psc[:, off:off + 1024],
                AF.Exp, bias=st(MNEG)),
                waits=(("s_pe", qk_mm), ("s_sy", ev["pbuf_sy"]),
                       ("s_gp", ev["pbuf_gp"]),
                       ("s_dve", ev["pbuf_dve"]), ("s_pe", ev["pbuf_pe"]),
                       ("s_sy", dscr_ev)),
                inc="s_act")
            exp_ps.append(expp)
            ev["band_pe"] = qk_mm
            ev["qk_pe"] = qk_mm
        exp_a, expv = exp_ps[1], exp_ps[3]
        ev["exp"] = expv
        if DBG and hp == 0 and hi == 0:
            ev["pbuf_sy"] = sydma(lambda e, t=t: e.dma_start(
                out=dbg["dpb"][:, t * 4096:(t + 1) * 4096],
                in_=p_b[:, PB_L:PB_L + 4096]),
                waits=(("s_act", expv),))
        fs["exp_a"] = exp_a
        fs["expv"] = expv

    def emit_back1(fs):
        hp, hi, t, Ka, Qa = fs["hp"], fs["hi"], fs["t"], fs["Ka"], fs["Qa"]
        slot, w0, ce = fs["slot"], fs["w0"], fs["ce"]
        wlo2, whi2_a, whi2_b = fs["wlo2"], fs["whi2_a"], fs["whi2_b"]
        exp_a, expv = fs["exp_a"], fs["expv"]
        pw0 = PBUF_ + PB_L + 256 * t - 3070
        pimg_w = gpdma(lambda e, slot=slot, pw0=pw0: e.dma_start(
            out=dram_flat(pimg, slot * 128 * PW2, PW2, 128, PW2),
            in_=ar[:, pw0:pw0 + PW2]),
            waits=(("s_act", expv), ("s_gp", ev["aread_gp"][slot]),
                   ("s_sy", ev["aread_sy"][slot])))
        auf = gpdma(lambda e, slot=slot: e.dma_start(
            out=au[:, :], in_=dram_flat(pimg, slot * 128 * PW2, PW2 + 1,
                                        128, AUF_W)),
            waits=(("s_gp", pimg_w), ("s_pe", ev.get("atf_pe", 0))))
        ev["aread_gp"][slot] = auf
        ev["pbuf_sy"] = 0
        ev["pbuf_gp"] = pimg_w
        def rs0fn(e, wlo2=wlo2, whi2_a=whi2_a, whi2_b=whi2_b):
            if wlo2 > 0:
                last = e.reduce_sum(st(RS0), p_b[:, PB_L:PB_L + wlo2], AX.X)
            else:
                last = e.memset(st(RS0), 0.0)
            if whi2_b > whi2_a:
                last = e.reduce_sum(st(T3), p_b[:, PB_L + whi2_a:PB_L + whi2_b],
                                    AX.X)
            return last
        rs0e = S.op("dve", rs0fn, waits=(("s_act", exp_a),), inc="s_dve")
        if whi2_b > whi2_a:
            rs0e = S.op("dve", lambda e: e.scalar_tensor_tensor(
                st(RS0), st(T3), m_hi, st(RS0), ALU.mult, ALU.add),
                waits=(("s_dve", rs0e),), inc="s_dve")
        axl0 = PBUF_ + PB_L + 256 * t - 3070
        axr0 = PBUF_ + PB_L + 256 * t + 1023
        cs1 = S.op("dve", lambda e, axl0=axl0, axr0=axr0: (
            e.tensor_tensor(apt_s[:, 0:128], ar[:, axl0:axl0 + 128],
                            masklo_sb[:, :], ALU.mult),
            e.tensor_tensor(apt_s[:, 128:256], ar[:, axl0 + 128:axl0 + 256],
                            masklo_sb[:, :], ALU.mult),
            e.tensor_tensor(apt_s[:, 256:384], ar[:, axr0:axr0 + 128],
                            maskup_sb[:, :], ALU.mult),
            e.tensor_tensor(apt_s[:, 384:512], ar[:, axr0 + 128:axr0 + 256],
                            maskup_sb[:, :], ALU.mult))[-1],
            waits=(("s_gp", auf), ("s_act", expv),
                   ("s_pe", ev.get("rv_pe", 0))), inc="s_dve")
        cs2 = S.op("dve", lambda e: (
            e.reduce_sum(st(CSLL), apt_s[:, 0:128], AX.X),
            e.reduce_sum(st(CSLH), apt_s[:, 128:256], AX.X),
            e.reduce_sum(st(CSRL), apt_s[:, 256:384], AX.X),
            e.reduce_sum(st(CSRH), apt_s[:, 384:512], AX.X))[-1],
            waits=(("s_dve", cs1),), inc="s_dve")
        cs3 = S.op("dve", lambda e: (
            e.tensor_tensor(st(T1), st(CSLL), m_lo, ALU.mult),
            e.tensor_tensor(st(T2), st(CSLH), m_hi, ALU.mult),
            e.tensor_tensor(st(T3), st(CSRL), m_lo, ALU.mult),
            e.tensor_tensor(st(T4), st(CSRH), m_hi, ALU.mult))[-1],
            waits=(("s_dve", cs2),), inc="s_dve")
        csum = S.op("dve", lambda e: (
            e.tensor_add(au[:, 0:1], au[:, 0:1], st(T1)),
            e.tensor_add(au[:, 128:129], au[:, 128:129], st(T2)),
            e.tensor_add(au[:, 4093:4094], au[:, 4093:4094], st(T3)),
            e.tensor_add(au[:, 4221:4222], au[:, 4221:4222], st(T4)))[-1],
            waits=(("s_dve", cs3),), inc="s_dve")
        ev["pbuf_dve"] = csum
        if DBG and hp == 0 and hi == 0:
            ev["aread_sy"][slot] = sydma(lambda e, t=t: e.dma_start(
                out=dbg["dauf"][:, t * AUF_W:(t + 1) * AUF_W], in_=au[:, :]),
                waits=(("s_dve", csum),))
        pt_ev = []
        av_mm = None
        for gq in range(8):
            bank_off = 2560 if gq % 2 == 0 else 3072
            ptm = S.op("pe", lambda e, gq=gq, bank_off=bank_off: [
                e.matmul(psc[:, bank_off + j * 128:bank_off + (j + 1) * 128],
                         p_b[:, PB_L + (gq * 4 + j) * 128:PB_L + (gq * 4 + j + 1) * 128],
                         identb[:, :], start=True, stop=True) for j in range(4)][-1],
                waits=(("s_act", exp_a if gq < 4 else expv),
                       ("s_dve", pt_ev[gq - 2] if gq >= 2
                        else ev.get("ate_last", 0))),
                inc="s_pe")
            pte = S.op("dve", lambda e, gq=gq, bank_off=bank_off: e.tensor_copy(
                pt_s[:, (gq % 4) * 512:(gq % 4 + 1) * 512],
                psc[:, bank_off:bank_off + 512]),
                waits=(("s_pe", ptm),), inc="s_dve")
            pt_ev.append(pte)
            if gq in (3, 7):
                k0 = 0 if gq == 3 else 16
                av_mm = S.op("pe", lambda e, hi=hi, k0=k0: [
                    e.matmul(psc[:, 2048:2113],
                             pt_s[:, (k - k0) * 128:(k - k0 + 1) * 128],
                             v_pair[:, k * 129 + hi * 64:k * 129 + hi * 64 + 65],
                             start=(k == 0), stop=(k == 31))
                    for k in range(k0, k0 + 16)][-1],
                    waits=(("s_dve", pte), ("s_dve", ev["ev_v"])), inc="s_pe")
        ev["pbuf_pe"] = av_mm
        ev["pte_last"] = pt_ev[-1]
        fs["csum"] = csum
        fs["rs0e"] = rs0e
        fs["av_mm"] = av_mm

    def emit_back2(fs):
        hp, hi, t = fs["hp"], fs["hi"], fs["t"]
        csum, rs0e, av_mm = fs["csum"], fs["rs0e"], fs["av_mm"]
        av0 = 2048 + (1 if hi == 1 else 0)
        rs_col = 2112 if hi == 0 else 2048
        apt_ev = []
        for gq, (k0g, nk) in enumerate(((0, 12), (12, 4))):
            def atf(e, k0g=k0g, nk=nk, t=t):
                last = None
                for j in range(nk):
                    k = k0g + j
                    w = 128 if k < 15 else 127
                    keep_wrap = (k + 2) * 128 > 3070 - 256 * t
                    reg = psc[0:w, 2560 + j * 128:2560 + (j + 1) * 128]
                    last = e.matmul(reg, au[:, 2047 + k * 128:2047 + k * 128 + w],
                                    idlo_sb[:, :], start=True, stop=False)
                    last = e.matmul(reg, au[:, 2175 + k * 128:2175 + k * 128 + w],
                                    idhi_sb[:, :], start=False,
                                    stop=not keep_wrap)
                    if keep_wrap:
                        last = e.matmul(reg, au[:, k * 128:k * 128 + w],
                                        idlo_sb[:, :], start=False, stop=False)
                        last = e.matmul(reg, au[:, 128 + k * 128:128 + k * 128 + w],
                                        idhi_sb[:, :], start=False, stop=True)
                return last
            atm = S.op("pe", atf,
                       waits=(("s_dve", csum),
                              ("s_dve", apt_ev[gq - 1] if gq >= 1
                               else ev.get("ate_last", 0))),
                       inc="s_pe")
            ate = S.op("dve", lambda e, k0g=k0g, nk=nk: e.tensor_copy(
                apt_s[:, k0g * 128:(k0g + nk) * 128],
                psc[:, 2560:2560 + nk * 128]),
                waits=(("s_pe", atm),), inc="s_dve")
            apt_ev.append(ate)
        rv_mm = S.op("pe", lambda e: [
            e.matmul(psc[:, 2113:2178],
                     apt_s[0:(128 if k < 15 else 127), k * 128:(k + 1) * 128],
                     rv_sb[0:(128 if k < 15 else 127), k * 65:(k + 1) * 65],
                     start=(k == 0), stop=(k == 15)) for k in range(16)][-1],
            waits=(("s_dve", apt_ev[-1]),), inc="s_pe")
        epiA = S.op("dve", lambda e, av0=av0, rs_col=rs_col: (
            e.tensor_copy(st(RSUM), psc[:, rs_col:rs_col + 1]),
            e.tensor_copy(relv_f[:, :], psc[:, 2113:2178]),
            e.tensor_copy(add1_f[:, :], psc[:, av0:av0 + 64]))[-1],
            waits=(("s_pe", rv_mm), ("s_pe", av_mm), ("s_dve", rs0e),
                   ("s_dve", ev_ebb)), inc="s_dve")
        epiB = S.op("dve", lambda e: (
            e.reciprocal(st(RINV), st(RSUM)),
            e.tensor_add(st(T3), relv_f[:, 64:65], st(RS0)))[-1],
            waits=(("s_dve", epiA),), inc="s_dve")
        epiC = S.op("dve", lambda e: (
            e.tensor_sub(st(RS2), st(RSUM), st(T3)),
            e.tensor_add(add1_f[:, :], add1_f[:, :], relv_f[:, 0:64]))[-1],
            waits=(("s_dve", epiB),), inc="s_dve")
        epiD = S.op("dve", lambda e: e.scalar_tensor_tensor(
            add1_f[:, :], rv0b[:, :], st(RS0), add1_f[:, :], ALU.mult, ALU.add),
            waits=(("s_dve", epiC),), inc="s_dve")
        epiE = S.op("dve", lambda e: e.scalar_tensor_tensor(
            add1_f[:, :], rv2b[:, :], st(RS2), add1_f[:, :], ALU.mult, ALU.add),
            waits=(("s_dve", epiD),), inc="s_dve")
        epi1 = S.op("dve", lambda e: e.tensor_scalar_mul(atr_s[:, :], add1_f[:, :],
                                                         st(RINV)),
                    waits=(("s_dve", epiE),), inc="s_dve")
        atr_mm = S.op("pe", lambda e: e.matmul(
            psc[0:64, 2432:2560], atr_s[:, :], identb[:, :], start=True, stop=True),
            waits=(("s_dve", epi1),), inc="s_pe")
        dep_ev = None
        if DBG and hp == 0 and hi == 0:
            sydma(lambda e, t=t: e.dma_start(
                out=dbg["dstats"][:, t * 32:t * 32 + 32], in_=stats[:, 0:32]),
                waits=(("s_dve", epi1),))
            sydma(lambda e, t=t: e.dma_start(
                out=dbg["drelv"][:, t * 129:t * 129 + 65], in_=relv_f[:, :]),
                waits=(("s_dve", epi1),))
            sydma(lambda e, t=t: e.dma_start(
                out=dbg["drelv"][:, t * 129 + 65:t * 129 + 129], in_=add1_f[:, :]),
                waits=(("s_dve", epi1),))
            dep_ev = sydma(lambda e, t=t: e.dma_start(
                out=dbg["datr"][:, t * 64:(t + 1) * 64], in_=atr_s[:, :]),
                waits=(("s_dve", epi1),))
        epi2 = S.op("dve", lambda e, t=t, hp=hp, hi=hi: e.tensor_copy(
            attn_fm[64 * hi:64 * hi + 64,
                    hp * 2048 + t * 128:hp * 2048 + (t + 1) * 128],
            psc[0:64, 2432:2560]),
            waits=(("s_pe", atr_mm), ("s_sy", dep_ev)), inc="s_dve")
        ev["epi_dve"] = epi2
        ev["ate_last"] = apt_ev[-1]
        ev["rv_pe"] = rv_mm
        ev["atf_pe"] = rv_mm

    def emit_pair(hp):
        if pend[0] is not None:
            emit_back1(pend[0])
            emit_back2(pend[0])
            pend[0] = None
        c0 = hp * 128
        ww3 = ww_raw.rearrange("p (c n) -> p c n", c=8)
        wq_in = ap_in["wqT"][:, c0:c0 + 128].rearrange("(c p) n -> p c n", p=128)
        wk_in = ap_in["wkT"][:, c0:c0 + 128].rearrange("(c p) n -> p c n", p=128)
        wv_in = ap_in["wvT"][:, c0:c0 + 128].rearrange("(c p) n -> p c n", p=128)

        wq_ev = gpdma(lambda e: e.dma_start(out=ww3, in_=wq_in),
                      waits=(("s_pe", ev["qk_pe"]),))
        wqm = S.op("dve", lambda e: (
            e.tensor_scalar_mul(ww_lo[:, :], ww_raw[:, :], m_lo),
            e.tensor_scalar_mul(ww_hi[:, :], ww_raw[:, :], m_hi))[-1],
            waits=(("s_gp", wq_ev), ("s_pe", ev["qk_pe"]), ("s_dve", ev_ebb)),
            inc="s_dve")

        def qproj(e):
            last = None
            for T in range(NT):
                reg = psc[:, T * 128:(T + 1) * 128]
                for c in range(8):
                    last = e.matmul(reg, ww_lo[:, c * 128:(c + 1) * 128],
                                    xn_fm[:, c * 4096 + 256 * T:c * 4096 + 256 * T + 128],
                                    start=(c == 0), stop=False)
                    last = e.matmul(reg, ww_hi[:, c * 128:(c + 1) * 128],
                                    xn_fm[:, c * 4096 + 256 * T + 128:c * 4096 + 256 * T + 256],
                                    start=False, stop=False)
                last = e.matmul(reg, bias_sb[0:1, c0:c0 + 128],
                                ones_row[0:1, 0:128], start=False, stop=True)
            return last
        qp = S.op("pe", qproj, waits=(("s_dve", wqm), ("s_dve", ev_xn_done),
                                      ("s_act", ev["exp"]), ("s_dve", ev["epi_dve"])),
                  inc="s_pe")
        qe = S.op("dve", lambda e: (
            e.tensor_copy(Q_aug[0][0:64, :], psc[0:64, 0:2048]),
            e.tensor_copy(Q_aug[1][0:64, :], psc[64:128, 0:2048]))[-1],
            waits=(("s_pe", qp),), inc="s_dve")

        wk_ev = gpdma(lambda e: e.dma_start(out=ww3, in_=wk_in),
                      waits=(("s_pe", qp),))
        def kproj(e):
            last = None
            for b in range(8):
                for c in range(8):
                    last = e.matmul(psc[:, b * 512:(b + 1) * 512],
                                    ww_raw[:, c * 128:(c + 1) * 128],
                                    xn_fm[:, c * 4096 + b * 512:c * 4096 + (b + 1) * 512],
                                    start=(c == 0), stop=False)
                last = e.matmul(psc[:, b * 512:(b + 1) * 512],
                                bias_sb[0:1, D + c0:D + c0 + 128],
                                ones_row[0:1, 0:512],
                                start=False, stop=True)
            return last
        kp = S.op("pe", kproj, waits=(("s_gp", wk_ev), ("s_dve", qe)), inc="s_pe")
        ke = S.op("dve", lambda e: (
            e.tensor_scalar_mul(K_aug[0][0:64, :], psc[0:64, :], SCALE),
            e.tensor_scalar_mul(K_aug[1][0:64, :], psc[64:128, :], SCALE))[-1],
            waits=(("s_pe", kp),), inc="s_dve")

        wv_ev = gpdma(lambda e: e.dma_start(out=ww3, in_=wv_in),
                      waits=(("s_pe", kp),))
        last_ve = ke
        for rc4 in range(8):
            def vgrp(e, rc4=rc4):
                last = None
                for j in range(4):
                    rcx = rc4 * 4 + j
                    for c in range(8):
                        last = e.matmul(psc[:, j * 512:j * 512 + 128],
                                        xn_fm[:, c * 4096 + rcx * 128:c * 4096 + (rcx + 1) * 128],
                                        ww_raw[:, c * 128:(c + 1) * 128],
                                        start=(c == 0), stop=False)
                    last = e.matmul(psc[:, j * 512:j * 512 + 128],
                                    ones_row[0:1, 0:128],
                                    bias_sb[0:1, 2 * D + c0:2 * D + c0 + 128],
                                    start=False, stop=True)
                return last
            vp = S.op("pe", vgrp, waits=(("s_gp", wv_ev), ("s_dve", last_ve)),
                      inc="s_pe")
            last_ve = S.op("dve", lambda e, rc4=rc4: (
                e.tensor_copy(
                    v_pair[:, rc4 * 516:rc4 * 516 + 516].rearrange(
                        "p (j a) -> p j a", j=4)[:, :, 0:64],
                    psc[:, 0:2048].rearrange("p (j a) -> p j a", j=4)[:, :, 0:64]),
                e.tensor_copy(
                    v_pair[:, rc4 * 516:rc4 * 516 + 516].rearrange(
                        "p (j a) -> p j a", j=4)[:, :, 65:129],
                    psc[:, 0:2048].rearrange("p (j a) -> p j a", j=4)[:, :, 64:128]))[-1],
                waits=(("s_pe", vp),), inc="s_dve")
        ev["ev_v"] = last_ve
        ev["qk_pe"] = S.cnt["s_pe"]

        for hi in (0, 1):
            for t in range(NT):
                fs = emit_front1(hp, hi, t, K_aug[hi], Q_aug[hi])
                if pend[0] is not None:
                    emit_back1(pend[0])
                emit_front2(fs)
                if pend[0] is not None:
                    emit_back2(pend[0])
                pend[0] = fs

    for hp in range(8):
        emit_pair(hp)
    emit_back1(pend[0])
    emit_back2(pend[0])
    pend[0] = None

    ev_attn_pe = S.cnt["s_pe"]
    ev_attn_dve = S.cnt["s_dve"]
    if DBG:
        dbg_ev["datn"] = sydma(lambda e: e.dma_start(out=dbg["datn"],
                                                     in_=attn_fm[:, :]),
                               waits=(("s_dve", ev_attn_dve),))

    oacc = arf(OACC_, 16384)
    wo_s = ar[:, WO_:WO_ + 8192]
    x_te = arf(XTE_, 1024)
    wo_in = ap_in["woT"].rearrange("(c p) n -> p c n", p=128)
    wo_ev = gpdma(lambda e: e.dma_start(
        out=wo_s.rearrange("p (c n) -> p c n", c=8), in_=wo_in),
        waits=(("s_pe", ev_attn_pe), ("s_dve", ev_attn_dve)))
    prev_ev2 = ev_attn_dve
    prev_pe2 = ev_attn_pe
    for t in range(NT):
        lx = sydma(lambda e, t=t: e.dma_start(out=x_te[:, :],
                                              in_=xq[t * 128:(t + 1) * 128, :]),
                   waits=(("s_dve", prev_ev2),))
        def oproj(e, t=t):
            last = None
            for b in range(2):
                for c in range(8):
                    last = e.matmul(psc[:, b * 512:(b + 1) * 512],
                                    attn_fm[:, c * 2048 + t * 128:c * 2048 + (t + 1) * 128],
                                    wo_s[:, c * 1024 + b * 512:c * 1024 + (b + 1) * 512],
                                    start=(c == 0), stop=False)
                last = e.matmul(psc[:, b * 512:(b + 1) * 512],
                                ones_row[0:1, 0:128],
                                bias_sb[0:1, 3 * D + b * 512:3 * D + (b + 1) * 512],
                                start=False, stop=True)
            return last
        om = S.op("pe", oproj, waits=(("s_gp", wo_ev), ("s_dve", prev_ev2)),
                  inc="s_pe")
        prev_ev2 = S.op("dve", lambda e, t=t: e.tensor_add(
            oacc[:, t * 1024:(t + 1) * 1024], psc[:, 0:1024], x_te[:, :]),
            waits=(("s_pe", om), ("s_sy", lx)), inc="s_dve")
        prev_pe2 = om

    fx_t = arf(FXT_, 1024)
    fxn_tm = ar[:, FXN_:FXN_ + 1024]
    xnq8 = xn_fm[:, 0:16384].rearrange("p (c n) -> p c n", c=8)
    prev_f_pe, prev_f_ev = prev_pe2, prev_ev2
    for t in range(NT):
        lx = sydma(lambda e, t=t: e.dma_start(out=fx_t[:, :],
                                              in_=xq[t * 128:(t + 1) * 128, :]),
                   waits=(("s_pe", prev_f_pe), ("s_dve", prev_f_ev)))
        fsel = S.op("dve", lambda e, t=t: (
            e.tensor_scalar_mul(st(T1), st(SMEAN0 + 2 * t), m_lo),
            e.tensor_scalar_mul(st(T2), st(SRSTD0 + 2 * t), m_lo))[-1],
            waits=(("s_sy", lx), ("s_pe", prev_f_pe)), inc="s_dve")
        fsel = S.op("dve", lambda e, t=t: (
            e.scalar_tensor_tensor(st(T1), st(SMEAN0 + 2 * t + 1), m_hi, st(T1),
                                   ALU.mult, ALU.add),
            e.scalar_tensor_tensor(st(T2), st(SRSTD0 + 2 * t + 1), m_hi, st(T2),
                                   ALU.mult, ALU.add))[-1],
            waits=(("s_dve", fsel),), inc="s_dve")
        rcp = S.op("dve", lambda e: e.tensor_scalar(
            fxn_tm[:, :], fx_t[:, :], st(T1), st(T2), ALU.subtract, ALU.mult),
            waits=(("s_dve", fsel),), inc="s_dve")
        trp = S.op("pe", lambda e: [
            e.matmul(psc[:, c * 128:(c + 1) * 128], fxn_tm[:, c * 128:(c + 1) * 128],
                     identb[:, :], start=True, stop=True) for c in range(8)][-1],
            waits=(("s_dve", rcp), ("s_dve", prev_f_ev),
                   ("s_sy", dbg_ev.get("dxn"))), inc="s_pe")
        prev_f_ev = S.op("dve", lambda e, t=t: e.tensor_copy(
            xnq8[:, :, t * 128:(t + 1) * 128],
            psc[:, 0:1024].rearrange("p (c n) -> p c n", c=8)),
            waits=(("s_pe", trp),), inc="s_dve")
        prev_f_pe = trp

    gpw = ar[:, GP_:GP_ + 4096]
    upw = ar[:, UP_:UP_ + 4096]
    dpw = attn_fm[:, 12288:16384]
    h_fm = attn_fm[:, 0:8192]
    silu_f = attn_fm[:, 8192:12288].bitcast(F32)
    last_comb = prev_f_ev
    last_pe = prev_f_pe
    for ex in range(8):
        gp_in = ap_in["gpT"][ex].rearrange("(c p) n -> p c n", p=128)
        up_in = ap_in["upT"][ex].rearrange("(c p) n -> p c n", p=128)
        dp_in = ap_in["dpT"][ex].rearrange("(c p) n -> p c n", p=128)
        wg = gpdma(lambda e, gp_in=gp_in: e.dma_start(
            out=gpw.rearrange("p (c n) -> p c n", c=8), in_=gp_in),
            waits=(("s_pe", last_pe),))
        wu = gpdma(lambda e, up_in=up_in: e.dma_start(
            out=upw.rearrange("p (c n) -> p c n", c=8), in_=up_in))
        wd = gpdma(lambda e, dp_in=dp_in: e.dma_start(
            out=dpw.rearrange("p (c n) -> p c n", c=4), in_=dp_in),
            waits=(("s_pe", last_pe), ("s_dve", last_comb),
                   ("s_sy", dbg_ev.get("datn"))))
        hev = last_comb
        for ib in range(4):
            def gu(e, ib=ib):
                last = None
                for half, wten in ((0, gpw), (1, upw)):
                    for b in range(4):
                        for c in range(8):
                            last = e.matmul(
                                psc[:, half * 2048 + b * 512:half * 2048 + (b + 1) * 512],
                                wten[:, c * 512 + ib * 128:c * 512 + ib * 128 + 128],
                                xn_fm[:, c * 2048 + b * 512:c * 2048 + (b + 1) * 512],
                                start=(c == 0), stop=(c == 7))
                return last
            gum = S.op("pe", gu, waits=(("s_gp", wu), ("s_dve", hev),
                                        ("s_sy", dbg_ev.get("datn"))), inc="s_pe")
            sil = S.op("act", lambda e: (
                [e.activation(silu_f[:, j * 512:(j + 1) * 512],
                              psc[:, j * 512:(j + 1) * 512], AF.Silu)
                 for j in range(4)],
                e.activation(psc[:, 0:256], silu_f[:, 0:256], AF.Square))[-1],
                waits=(("s_pe", gum),), inc="s_act")
            hev = S.op("dve", lambda e, ib=ib: e.tensor_mul(
                h_fm[:, ib * 2048:(ib + 1) * 2048], silu_f[:, 0:2048],
                psc[:, 2048:4096]),
                waits=(("s_act", sil),), inc="s_dve")
        for t in range(NT):
            def down(e, t=t):
                last = None
                for b in range(2):
                    for ic in range(4):
                        last = e.matmul(psc[:, b * 512:(b + 1) * 512],
                                        h_fm[:, ic * 2048 + t * 128:ic * 2048 + (t + 1) * 128],
                                        dpw[:, ic * 1024 + b * 512:ic * 1024 + (b + 1) * 512],
                                        start=(ic == 0), stop=(ic == 3))
                return last
            dm = S.op("pe", down, waits=(("s_gp", wd), ("s_dve", hev),
                                         ("s_dve", last_comb)), inc="s_pe")
            last_comb = S.op("dve", lambda e, t=t, ex=ex: e.scalar_tensor_tensor(
                oacc[:, t * 1024:(t + 1) * 1024], psc[:, 0:1024],
                cw_sb[:, t * 8 + ex:t * 8 + ex + 1],
                oacc[:, t * 1024:(t + 1) * 1024],
                ALU.mult, ALU.add),
                waits=(("s_pe", dm), ("s_dve", ev_c_dve)), inc="s_dve")
            last_pe = dm

    fin = sydma(lambda e: e.dma_start(
        out=out_d.rearrange("(t p) d -> p t d", p=128),
        in_=oacc.rearrange("p (t d) -> p t d", t=16)),
        waits=(("s_dve", last_comb),))
    S.plan["sy"].append(lambda e, fin=fin: e.wait_ge(S.sem["s_sy"], fin))


_NC_CACHE = {}


def _steps(par):
    s = np.zeros((2 * NT, L), dtype=np.float32)
    q = np.arange(L)
    for t in range(NT):
        l0 = 256 * t + 128 * par
        c_lo2 = max(0, 256 * t - 2942) - 128 * (1 - par)
        c_hi = l0 + 1151
        s[2 * t] = (q < c_lo2).astype(np.float32)
        s[2 * t + 1] = (q >= c_hi).astype(np.float32)
    return s


def kernel(L0_attn_x, L0_attn_Wq, L0_attn_bq, L0_attn_Wk, L0_attn_bk, L0_attn_Wv,
           L0_attn_bv, L0_attn_Wo, L0_attn_bo, L0_attn_relative_k, L0_attn_relative_v,
           L0_ffn_gate_weight, L0_ffn_e_score_correction_bias, L0_ffn_gate_proj,
           L0_ffn_up_proj, L0_ffn_down_proj):
    x = np.ascontiguousarray(np.asarray(L0_attn_x, dtype=np.float32))
    C = np.ascontiguousarray

    fm = np.zeros((8, 8), np.float32)
    for e_ in range(8):
        fm[e_, :e_] = 1.0
    hm = np.zeros((4, 4), np.float32)
    for g_ in range(4):
        hm[g_, :g_] = 1.0
    eye = np.eye(128, dtype=np.float32)

    shared = {
        "wqT": C(np.asarray(L0_attn_Wq, np.float32).T),
        "wkT": C(np.asarray(L0_attn_Wk, np.float32).T),
        "wvT": C(np.asarray(L0_attn_Wv, np.float32).T),
        "woT": C(np.asarray(L0_attn_Wo, np.float32).T),
        "bq": C(np.asarray(L0_attn_bq, np.float32).reshape(1, D)),
        "bk": C(np.asarray(L0_attn_bk, np.float32).reshape(1, D)),
        "bv": C(np.asarray(L0_attn_bv, np.float32).reshape(1, D)),
        "bo": C(np.asarray(L0_attn_bo, np.float32).reshape(1, D)),
        "rkT": C(np.asarray(L0_attn_relative_k, np.float32).T),
        "rv": C(np.asarray(L0_attn_relative_v, np.float32)),
        "gwT": C(np.asarray(L0_ffn_gate_weight, np.float32).T),
        "ebias": C(np.asarray(L0_ffn_e_score_correction_bias,
                              np.float32).reshape(1, 8)),
        "gpT": C(np.asarray(L0_ffn_gate_proj, np.float32).transpose(0, 2, 1)),
        "upT": C(np.asarray(L0_ffn_up_proj, np.float32).transpose(0, 2, 1)),
        "dpT": C(np.asarray(L0_ffn_down_proj, np.float32).transpose(0, 2, 1)),
        "identf": eye,
        "masklo": np.tril(np.ones((128, 128), np.float32), -1),
        "maskup": np.triu(np.ones((128, 128), np.float32), 1),
        "fmask64": np.tile(fm.reshape(1, 64), (128, 1)),
        "hmask16": np.tile(hm.reshape(1, 16), (128, 1)),
    }

    if "nc" not in _NC_CACHE:
        _NC_CACHE["nc"] = build_program()
    nc = _NC_CACHE["nc"]

    in_maps = []
    for c in range(8):
        b = c // 2
        par = c % 2
        m = dict(shared)
        m["xb"] = C(x[b])
        m["xq"] = C(x[b].reshape(32, 128, D)[par::2].reshape(NQ, D))
        m["steps"] = _steps(par)
        m["mcol"] = np.tile(np.array([[1.0 - par, float(par)]], np.float32),
                            (128, 1))
        m["idlo"] = C(eye * (1.0 - par))
        m["idhi"] = C(eye * par)
        in_maps.append(m)

    res = run_bass_kernel_spmd(nc, in_maps, core_ids=list(range(8)))
    out = np.empty((B, L, D), dtype=np.float32)
    for c in range(8):
        b = c // 2
        par = c % 2
        out[b].reshape(32, 128, D)[par::2] = res.results[c]["out"].reshape(NT, 128, D)
    return out


if __name__ == "__main__":
    build_program()
    print("program built ok")


# version 14 (final)
# speedup vs baseline: 1.6265x; 1.0094x over previous
import numpy as np
from contextlib import ExitStack

import concourse.bass as bass
import concourse.mybir as mybir
from concourse.bass_utils import run_bass_kernel_spmd

F32 = mybir.dt.float32
BF16 = mybir.dt.bfloat16
AF = mybir.ActivationFunctionType
ALU = mybir.AluOpType
AX = mybir.AxisListType

B, L, D = 4, 4096, 1024
NH, DK = 16, 64
NQ = 2048
NT = 16
NJ = 2047
SCALE = 0.125

IMG_W = 4348
IMG_FL = 128
IMG_RS = IMG_W + 2 * IMG_FL
PW2 = 4349
BU_W = 4224
AUF_W = 4222
PB_L, PB_R = 3072, 1024
PB_W = PB_L + L + PB_R

AR_W = 43008
KA_, KB_ = 0, 4096
QA_, QB_ = 8192, 10240
VV_ = 12288
WW_ = 16416
QR_ = 19488
BU_ = 21792
PBUF_ = 26016
AUF_ = 34208
PT_ = 38430
APT_ = 40478
ATR_ = 42526
XT_, SQ_, XN32_, XNTM_, XF_ = 0, 2048, 4096, 6144, 8192
OACC_, WO_, XTE_ = 0, 32768, 40960
FXT_, FSQ_, FXN_ = 32768, 34816, 36864
GP_, UP_ = 32768, 36864

ENGS = ("pe", "dve", "act", "sy", "gp")

DBG = False

(MNEG, RSA, RSB, RSUM, RINV, CSLL, CSLH, CSRL, CSRH, RS0, RS2, WSUM,
 MEAN, VAR, RSTD, SSQ, T1, T2, T3, T4, QK0_, QK2_, EPS) = range(23)
QK0, QK2 = 87, 88
SMEAN0 = 23
SRSTD0 = 55


class Sched:
    def __init__(self):
        self.plan = {k: [] for k in ENGS}
        self.cnt = {}
        self.sem = {}

    def add_sem(self, name, handle):
        self.sem[name] = handle
        self.cnt[name] = 0

    def op(self, eng, fn, waits=(), inc=None, n=1):
        val = None
        if inc is not None:
            self.cnt[inc] += n
            val = self.cnt[inc]
        wl = tuple((s, v) for (s, v) in waits if v is not None and v > 0)
        sems = self.sem

        def run(e, fn=fn, wl=wl, inc=inc, n=n):
            seen = {}
            for s, v in wl:
                if seen.get(s, -1) < v:
                    seen[s] = v
            for s, v in seen.items():
                e.wait_ge(sems[s], v)
            ins = fn(e)
            if inc is not None:
                ins.then_inc(sems[inc], n)

        self.plan[eng].append(run)
        return val


def build_program():
    nc = bass.Bass("TRN2", target_bir_lowering=False)

    ap_in = {}
    def din(name, shape, dt=F32):
        ap_in[name] = nc.dram_tensor(name, shape, dt, kind="ExternalInput").ap()
        return ap_in[name]

    din("xb", [L, D]); din("xq", [NQ, D])
    din("wqT", [D, D]); din("wkT", [D, D]); din("wvT", [D, D]); din("woT", [D, D])
    din("bq", [1, D]); din("bk", [1, D]); din("bv", [1, D]); din("bo", [1, D])
    din("rkT", [DK, NJ]); din("rv", [NJ, DK])
    din("gwT", [D, 8]); din("ebias", [1, 8])
    din("gpT", [8, D, 512]); din("upT", [8, D, 512]); din("dpT", [8, 512, D])
    din("identf", [128, 128])
    din("idlo", [128, 128]); din("idhi", [128, 128])
    din("masklo", [128, 128]); din("maskup", [128, 128])
    din("fmask64", [128, 64]); din("hmask16", [128, 16])
    din("steps", [2 * NT, L])
    din("mcol", [128, 2])

    out_d = nc.dram_tensor("out", [NQ, D], F32, kind="ExternalOutput").ap()
    qrimg = nc.dram_tensor("qrimg", [2 * 128 * IMG_RS], BF16, kind="Internal")
    pimg = nc.dram_tensor("pimg", [2 * 128 * PW2], BF16, kind="Internal")
    dbg = {}
    if DBG:
        dbg["dxn"] = nc.dram_tensor("dxn", [128, 8 * L], BF16,
                                    kind="ExternalOutput").ap()
        dbg["dcw"] = nc.dram_tensor("dcw", [128, 128], F32,
                                    kind="ExternalOutput").ap()
        dbg["datn"] = nc.dram_tensor("datn", [128, 8 * NQ], BF16,
                                     kind="ExternalOutput").ap()
        dbg["dscr"] = nc.dram_tensor("dscr", [128, NT * 4096], BF16,
                                     kind="ExternalOutput").ap()
        dbg["dpb"] = nc.dram_tensor("dpb", [128, NT * 4096], BF16,
                                    kind="ExternalOutput").ap()
        dbg["dstats"] = nc.dram_tensor("dstats", [128, NT * 32], F32,
                                       kind="ExternalOutput").ap()
        dbg["drelv"] = nc.dram_tensor("drelv", [128, NT * 129], F32,
                                      kind="ExternalOutput").ap()
        dbg["dauf"] = nc.dram_tensor("dauf", [128, NT * AUF_W], BF16,
                                     kind="ExternalOutput").ap()
        dbg["datr"] = nc.dram_tensor("datr", [128, NT * 64], BF16,
                                     kind="ExternalOutput").ap()
        dbg["dgt"] = nc.dram_tensor("dgt", [128, NT * 80], F32,
                                    kind="ExternalOutput").ap()

    es = ExitStack()
    with es:
        S = Sched()
        for s in ("s_pe", "s_dve", "s_act", "s_sy", "s_gp"):
            S.add_sem(s, es.enter_context(nc.semaphore(s)))

        t_ = {}
        def sb(name, shape, dt):
            t_[name] = es.enter_context(nc.sbuf_tensor(name, shape, dt))
            return t_[name]

        sb("arena", [128, AR_W], BF16)
        sb("xn_fm", [128, 8 * L], BF16)
        sb("attn_fm", [128, 8 * NQ], BF16)
        sb("stats", [128, 96], F32)
        sb("cw", [128, 128], F32)
        sb("identb", [128, 128], BF16)
        sb("identf_sb", [128, 128], F32)
        sb("idlo_sb", [128, 128], BF16)
        sb("idhi_sb", [128, 128], BF16)
        sb("masklo_sb", [128, 128], BF16)
        sb("maskup_sb", [128, 128], BF16)
        sb("fmask_sb", [128, 64], F32)
        sb("hmask_sb", [128, 16], F32)
        sb("mcol_sb", [128, 2], F32)
        sb("rkT_sb", [64, 2048], BF16)
        sb("rv_sb", [128, 16 * 65], BF16)
        sb("gw_sb", [128, 64], F32)
        sb("eb_sb", [1, 8], F32)
        sb("ebb_sb", [128, 8], F32)
        sb("rv0b", [128, 64], F32)
        sb("rv2b", [128, 64], F32)
        sb("bias_sb", [1, 4 * D], BF16)
        sb("ones_row", [1, 512], BF16)
        sb("onesf_row", [1, 128], F32)
        sb("qkT_row", [2, 128], BF16)
        sb("relv_f", [128, 65], F32)
        sb("add1_f", [128, 64], F32)
        sb("gt8", [128, 80], F32)
        t_["psc"] = es.enter_context(nc.psum_tensor("psc", [128, 4096], F32))

        with nc.Block() as block:
            build_plan(S, t_, ap_in, out_d, qrimg, pimg, dbg)
            for eng, attr in (("sy", "sync"), ("gp", "gpsimd"), ("pe", "tensor"),
                              ("dve", "vector"), ("act", "scalar")):
                lst = S.plan[eng]
                def runner(e, lst=lst, eng=eng):
                    for i, f in enumerate(lst):
                        try:
                            f(e)
                        except Exception:
                            print(f"FAILING OP: eng={eng} index={i}/{len(lst)}")
                            raise
                getattr(block, attr)(runner)
    return nc


def build_plan(S, t_, ap_in, out_d, qrimg, pimg, dbg):
    ar = t_["arena"]; xn_fm = t_["xn_fm"]; attn_fm = t_["attn_fm"]
    stats = t_["stats"]; cw_sb = t_["cw"]
    identb = t_["identb"]; identf_sb = t_["identf_sb"]
    idlo_sb = t_["idlo_sb"]; idhi_sb = t_["idhi_sb"]
    masklo_sb = t_["masklo_sb"]; maskup_sb = t_["maskup_sb"]
    fmask_sb = t_["fmask_sb"]; hmask_sb = t_["hmask_sb"]; mcol_sb = t_["mcol_sb"]
    rkT_sb = t_["rkT_sb"]; rv_sb = t_["rv_sb"]; gw_sb = t_["gw_sb"]
    eb_sb = t_["eb_sb"]; ebb_sb = t_["ebb_sb"]; rv0b = t_["rv0b"]; rv2b = t_["rv2b"]
    bias_sb = t_["bias_sb"]; ones_row = t_["ones_row"]; onesf_row = t_["onesf_row"]
    qkT_row = t_["qkT_row"]
    relv_f = t_["relv_f"]; add1_f = t_["add1_f"]
    gt8 = t_["gt8"]; psc = t_["psc"]

    def st(i):
        return stats[:, i:i + 1]

    def arf(c0, ncols):
        return ar[:, c0:c0 + 2 * ncols].bitcast(F32)

    def dram_flat(th, off, rowstride, p, width):
        return bass.AP(tensor=th, offset=off, ap=[[rowstride, p], [1, width]])

    m_lo = mcol_sb[:, 0:1]
    m_hi = mcol_sb[:, 1:2]

    def gpdma(fn, waits=()):
        return S.op("gp", fn, waits=waits, inc="s_gp", n=16)

    def sydma(fn, waits=()):
        return S.op("sy", fn, waits=waits, inc="s_sy", n=16)

    def sydma_split(fn2, waits=()):
        def g(e, fn2=fn2):
            fn2(e, 0).then_inc(S.sem["s_sy"], 16)
            return fn2(e, 64)
        S.cnt["s_sy"] += 16
        return S.op("sy", g, waits=waits, inc="s_sy", n=16)

    ld0 = gpdma(lambda e: e.dma_start(out=identb[:, :], in_=ap_in["identf"][:, :]))
    gpdma(lambda e: e.dma_start(out=identf_sb[:, :], in_=ap_in["identf"][:, :]))
    gpdma(lambda e: e.dma_start(out=idlo_sb[:, :], in_=ap_in["idlo"][:, :]))
    gpdma(lambda e: e.dma_start(out=idhi_sb[:, :], in_=ap_in["idhi"][:, :]))
    gpdma(lambda e: e.dma_start(out=masklo_sb[:, :], in_=ap_in["masklo"][:, :]))
    gpdma(lambda e: e.dma_start(out=maskup_sb[:, :], in_=ap_in["maskup"][:, :]))
    gpdma(lambda e: e.dma_start(out=fmask_sb[:, :], in_=ap_in["fmask64"][:, :]))
    gpdma(lambda e: e.dma_start(out=hmask_sb[:, :], in_=ap_in["hmask16"][:, :]))
    gpdma(lambda e: e.dma_start(out=mcol_sb[:, :], in_=ap_in["mcol"][:, :]))
    gpdma(lambda e: e.dma_start(out=rkT_sb[0:64, 0:NJ], in_=ap_in["rkT"][:, :]))
    gpdma(lambda e: e.dma_start(out=gw_sb[:, :].rearrange("p (c e) -> p c e", c=8),
                                in_=ap_in["gwT"].rearrange("(c p) e -> p c e", p=128)))
    gpdma(lambda e: e.dma_start(out=eb_sb[:, :], in_=ap_in["ebias"][:, :]))
    for i, nm in enumerate(("bq", "bk", "bv", "bo")):
        gpdma(lambda e, i=i, nm=nm: e.dma_start(
            out=bias_sb[0:1, i * D:(i + 1) * D], in_=ap_in[nm][:, :]))
    mrv = S.op("dve", lambda e: e.memset(rv_sb[:, :], 0.0), inc="s_dve")
    rv_last = None
    for k in range(16):
        r0, r1 = k * 128, min(NJ, (k + 1) * 128)
        rv_last = gpdma(lambda e, k=k, r0=r0, r1=r1: e.dma_start(
            out=rv_sb[0:r1 - r0, k * 65:k * 65 + 64], in_=ap_in["rv"][r0:r1, :]),
            waits=(("s_dve", mrv),))
    ld_consts = rv_last

    S.op("dve", lambda e: e.memset(ones_row[:, :], 1.0), inc="s_dve")
    S.op("dve", lambda e: e.memset(onesf_row[:, :], 1.0), inc="s_dve")
    ev_eps = S.op("dve", lambda e: (
        e.memset(st(EPS), 1e-5),
        e.memset(st(MNEG), 0.0))[-1], inc="s_dve")
    rv_ones = S.op("dve", lambda e: e.memset(
        rv_sb[:, :].rearrange("p (k c) -> p k c", c=65)[:, :, 64:65], 1.0),
        waits=(("s_gp", rv_last),), inc="s_dve")
    S.op("dve", lambda e: e.memset(
        ar[:, VV_:VV_ + 4128].rearrange("p (k c) -> p k c", c=129)[:, :, 64:65], 1.0),
        inc="s_dve")
    S.op("dve", lambda e: e.memset(ar[:, PBUF_:PBUF_ + PB_L], 0.0), inc="s_dve")
    S.op("dve", lambda e: e.memset(ar[:, PBUF_ + PB_L + L:PBUF_ + PB_W], 0.0),
         inc="s_dve")
    mzf = S.op("dve", lambda e: e.memset(ar[:, BU_:BU_ + 128], 0.0), inc="s_dve")
    fl_ev = None
    for slot in range(2):
        for off in (0, IMG_FL + IMG_W):
            fl_ev = sydma(lambda e, slot=slot, off=off: e.dma_start(
                out=dram_flat(qrimg, slot * 128 * IMG_RS + off, IMG_RS, 128, IMG_FL),
                in_=ar[:, BU_:BU_ + 128]),
                waits=(("s_dve", mzf),))

    pe_eb = S.op("pe", lambda e: e.matmul(psc[:, 0:8], onesf_row[0:1, :],
                                          eb_sb[0:1, :], start=True, stop=True),
                 waits=(("s_gp", ld_consts),), inc="s_pe")
    pe_rv0 = S.op("pe", lambda e: e.matmul(psc[:, 512:576], ones_row[0:1, 0:128],
                                           rv_sb[0:1, 0:64], start=True, stop=True),
                  waits=(("s_dve", rv_ones),), inc="s_pe")
    cp2046 = sydma(lambda e: e.dma_start(out=qkT_row[0:1, 0:64],
                                         in_=rv_sb[126:127, 15 * 65:15 * 65 + 64]),
                   waits=(("s_dve", rv_ones),))
    pe_rv2 = S.op("pe", lambda e: e.matmul(psc[:, 1024:1088], ones_row[0:1, 0:128],
                                           qkT_row[0:1, 0:64], start=True, stop=True),
                  waits=(("s_sy", cp2046),), inc="s_pe")
    ev_ebb = S.op("dve", lambda e: (
        e.tensor_copy(ebb_sb[:, :], psc[:, 0:8]),
        e.tensor_copy(rv0b[:, :], psc[:, 512:576]),
        e.tensor_copy(rv2b[:, :], psc[:, 1024:1088]))[-1],
        waits=(("s_pe", pe_rv2), ("s_pe", pe_rv0), ("s_pe", pe_eb)), inc="s_dve")

    x_t = arf(XT_, 1024)
    sq_t = arf(SQ_, 1024)
    xn_tm = ar[:, XNTM_:XNTM_ + 1024]
    xb = ap_in["xb"]; xq = ap_in["xq"]

    xv8 = xn_fm[:, :].rearrange("p (c n) -> p c n", c=8)
    prev_tr, prev_ev = 0, 0
    for t in range(32):
        lx = sydma(lambda e, t=t: e.dma_start(out=x_t[:, :],
                                              in_=xb[t * 128:(t + 1) * 128, :]),
                   waits=(("s_pe", prev_tr),))
        l1 = S.op("dve", lambda e: (
            e.reduce_sum(st(T1), x_t[:, :], AX.X),
            e.tensor_tensor(sq_t[:, :], x_t[:, :], x_t[:, :], ALU.mult))[-1],
            waits=(("s_sy", lx),), inc="s_dve")
        l2 = S.op("dve", lambda e, t=t: (
            e.reduce_sum(st(SSQ), sq_t[:, :], AX.X),
            e.tensor_scalar_mul(st(SMEAN0 + t), st(T1), 1.0 / 1024.0))[-1],
            waits=(("s_dve", l1),), inc="s_dve")
        l3 = S.op("dve", lambda e, t=t: (
            e.tensor_scalar_mul(st(VAR), st(SSQ), 1.0 / 1024.0),
            e.tensor_tensor(st(T2), st(SMEAN0 + t), st(SMEAN0 + t), ALU.mult))[-1],
            waits=(("s_dve", l2),), inc="s_dve")
        lnc = S.op("dve", lambda e: e.tensor_sub(st(VAR), st(VAR), st(T2)),
                   waits=(("s_dve", l3),), inc="s_dve")
        S.op("act", lambda e: e.activation(st(T1), st(VAR), AF.Sqrt, bias=st(EPS)),
             waits=(("s_dve", lnc),))
        rstd = S.op("act", lambda e: e.activation(sq_t[:, 0:512], x_t[:, 0:512],
                                                  AF.Square), inc="s_act")
        rc1 = S.op("dve", lambda e, t=t: e.reciprocal(st(SRSTD0 + t), st(T1)),
                   waits=(("s_act", rstd), ("s_pe", prev_tr)), inc="s_dve")
        rcp = S.op("dve", lambda e, t=t: e.tensor_scalar(
            xn_tm[:, :], x_t[:, :], st(SMEAN0 + t), st(SRSTD0 + t),
            ALU.subtract, ALU.mult),
            waits=(("s_dve", rc1),), inc="s_dve")
        trp = S.op("pe", lambda e: [
            e.matmul(psc[:, c * 128:(c + 1) * 128], xn_tm[:, c * 128:(c + 1) * 128],
                     identb[:, :], start=True, stop=True) for c in range(8)][-1],
            waits=(("s_dve", rcp), ("s_gp", ld0), ("s_dve", ev_ebb),
                   ("s_dve", prev_ev)), inc="s_pe")
        prev_ev = S.op("dve", lambda e, t=t: e.tensor_copy(
            xv8[:, :, t * 128:(t + 1) * 128],
            psc[:, 0:1024].rearrange("p (c n) -> p c n", c=8)),
            waits=(("s_pe", trp),), inc="s_dve")
        prev_tr = trp
    ev_xn_done = prev_ev
    ev_b_pe = prev_tr
    dbg_ev = {}
    if DBG:
        dbg_ev["dxn"] = sydma(lambda e: e.dma_start(out=dbg["dxn"], in_=xn_fm[:, :]),
                              waits=(("s_dve", ev_xn_done),))

    xf32 = arf(XF_, 1024)
    xn32 = arf(XN32_, 1024)
    sC = gt8[:, 0:8]; cC = gt8[:, 8:16]; G4 = gt8[:, 16:20]
    cnt4 = gt8[:, 20:24]; gm4 = gt8[:, 24:28]
    A16 = gt8[:, 32:48]; C16 = gt8[:, 48:64]
    sm8 = gt8[:, 64:72]; tmp8 = gt8[:, 72:80]

    prev_c_pe = ev_b_pe
    ev_cw = 0
    prev_sig = 0
    prev_dgt = 0
    for t in range(NT):
        lx = sydma(lambda e, t=t: e.dma_start(out=x_t[:, :],
                                              in_=xq[t * 128:(t + 1) * 128, :]),
                   waits=(("s_pe", prev_c_pe), ("s_dve", ev_xn_done)))
        xsel = S.op("dve", lambda e, t=t: (
            e.tensor_scalar_mul(st(T1), st(SMEAN0 + 2 * t), m_lo),
            e.tensor_scalar_mul(st(T2), st(SRSTD0 + 2 * t), m_lo))[-1],
            waits=(("s_sy", lx), ("s_pe", prev_c_pe)), inc="s_dve")
        xsel = S.op("dve", lambda e, t=t: (
            e.scalar_tensor_tensor(st(T1), st(SMEAN0 + 2 * t + 1), m_hi, st(T1),
                                   ALU.mult, ALU.add),
            e.scalar_tensor_tensor(st(T2), st(SRSTD0 + 2 * t + 1), m_hi, st(T2),
                                   ALU.mult, ALU.add))[-1],
            waits=(("s_dve", xsel),), inc="s_dve")
        xnf = S.op("dve", lambda e: e.tensor_scalar(
            xn32[:, :], x_t[:, :], st(T1), st(T2), ALU.subtract, ALU.mult),
            waits=(("s_dve", xsel),), inc="s_dve")
        trp = S.op("pe", lambda e: [
            e.transpose(psc[:, c * 128:(c + 1) * 128], xn32[:, c * 128:(c + 1) * 128],
                        identf_sb[:, :]) for c in range(8)][-1],
            waits=(("s_dve", xnf),), inc="s_pe")
        evf = S.op("dve", lambda e: e.tensor_copy(xf32[:, :], psc[:, 0:1024]),
                   waits=(("s_pe", trp),), inc="s_dve")
        lg = S.op("pe", lambda e: [
            e.matmul(psc[:, 1024:1032], xf32[:, c * 128:(c + 1) * 128],
                     gw_sb[:, c * 8:(c + 1) * 8], start=(c == 0), stop=(c == 7))
            for c in range(8)][-1],
            waits=(("s_dve", evf), ("s_act", prev_sig)), inc="s_pe")
        S.op("act", lambda e: e.activation(sC, psc[:, 1024:1032], AF.Sigmoid),
             waits=(("s_pe", lg), ("s_dve", ev_cw), ("s_sy", prev_dgt)))
        sig = S.op("act", lambda e: e.activation(xf32[:, 0:512], xf32[:, 512:1024],
                                                 AF.Square), inc="s_act")
        prev_sig = sig
        cprev = S.op("dve", lambda e: e.tensor_add(cC, sC, ebb_sb[:, :]),
                     waits=(("s_act", sig), ("s_dve", ev_ebb)), inc="s_dve")
        cprev = S.op("dve", lambda e: e.tensor_add(G4, cC[:, 0:8:2], cC[:, 1:8:2]),
                     waits=(("s_dve", cprev),), inc="s_dve")
        cprev = S.op("dve", lambda e: (
            e.tensor_tensor(A16.rearrange("p (a b) -> p a b", a=4),
                            G4.unsqueeze(2).broadcast_to((128, 4, 4)),
                            G4.unsqueeze(1).broadcast_to((128, 4, 4)),
                            ALU.is_lt),
            e.tensor_tensor(C16.rearrange("p (a b) -> p a b", a=4),
                            G4.unsqueeze(2).broadcast_to((128, 4, 4)),
                            G4.unsqueeze(1).broadcast_to((128, 4, 4)),
                            ALU.is_equal))[-1],
            waits=(("s_dve", cprev),), inc="s_dve")
        cprev = S.op("dve", lambda e: e.tensor_tensor(C16, C16, hmask_sb[:, :],
                                                      ALU.mult),
                     waits=(("s_dve", cprev),), inc="s_dve")
        cprev = S.op("dve", lambda e: e.tensor_add(A16, A16, C16),
                     waits=(("s_dve", cprev),), inc="s_dve")
        cprev = S.op("dve", lambda e: e.tensor_reduce(
            cnt4, A16.rearrange("p (a b) -> p a b", a=4), AX.X, ALU.add),
            waits=(("s_dve", cprev),), inc="s_dve")
        cprev = S.op("dve", lambda e: e.tensor_single_scalar(gm4, cnt4, 2.0,
                                                             ALU.is_lt),
                     waits=(("s_dve", cprev),), inc="s_dve")
        cprev = S.op("dve", lambda e: e.tensor_copy(
            sm8.rearrange("p (g j) -> p g j", g=4),
            gm4.unsqueeze(2).broadcast_to((128, 4, 2))),
            waits=(("s_dve", cprev),), inc="s_dve")
        cprev = S.op("dve", lambda e: e.tensor_tensor(tmp8, cC, sm8, ALU.mult),
                     waits=(("s_dve", cprev),), inc="s_dve")
        cprev = S.op("dve", lambda e: (
            e.tensor_tensor(relv_f[:, 0:64].rearrange("p (a b) -> p a b", a=8),
                            tmp8.unsqueeze(2).broadcast_to((128, 8, 8)),
                            tmp8.unsqueeze(1).broadcast_to((128, 8, 8)),
                            ALU.is_lt),
            e.tensor_tensor(add1_f[:, 0:64].rearrange("p (a b) -> p a b", a=8),
                            tmp8.unsqueeze(2).broadcast_to((128, 8, 8)),
                            tmp8.unsqueeze(1).broadcast_to((128, 8, 8)),
                            ALU.is_equal))[-1],
            waits=(("s_dve", cprev),), inc="s_dve")
        cprev = S.op("dve", lambda e: e.tensor_tensor(
            add1_f[:, 0:64], add1_f[:, 0:64], fmask_sb[:, :], ALU.mult),
            waits=(("s_dve", cprev),), inc="s_dve")
        cprev = S.op("dve", lambda e: e.tensor_add(
            relv_f[:, 0:64], relv_f[:, 0:64], add1_f[:, 0:64]),
            waits=(("s_dve", cprev),), inc="s_dve")
        cprev = S.op("dve", lambda e: e.tensor_reduce(
            tmp8, relv_f[:, 0:64].rearrange("p (a b) -> p a b", a=8), AX.X, ALU.add),
            waits=(("s_dve", cprev),), inc="s_dve")
        cprev = S.op("dve", lambda e: e.tensor_single_scalar(tmp8, tmp8, 2.0,
                                                             ALU.is_lt),
                     waits=(("s_dve", cprev),), inc="s_dve")
        cprev = S.op("dve", lambda e: e.tensor_tensor(tmp8, tmp8, sC, ALU.mult),
                     waits=(("s_dve", cprev),), inc="s_dve")
        cprev = S.op("dve", lambda e: e.reduce_sum(st(WSUM), tmp8, AX.X),
                     waits=(("s_dve", cprev),), inc="s_dve")
        cprev = S.op("dve", lambda e: e.tensor_scalar_add(st(T1), st(WSUM), 1e-20),
                     waits=(("s_dve", cprev),), inc="s_dve")
        cprev = S.op("dve", lambda e: e.reciprocal(st(T2), st(T1)),
                     waits=(("s_dve", cprev),), inc="s_dve")
        ev_cw = S.op("dve", lambda e, t=t: e.tensor_scalar_mul(
            cw_sb[:, t * 8:(t + 1) * 8], tmp8, st(T2)),
            waits=(("s_dve", cprev),), inc="s_dve")
        if DBG:
            prev_dgt = sydma(lambda e, t=t: e.dma_start(
                out=dbg["dgt"][:, t * 80:(t + 1) * 80], in_=gt8[:, :]),
                waits=(("s_dve", ev_cw),))
        prev_c_pe = lg
    ev_c_dve = ev_cw
    if DBG:
        sydma(lambda e: e.dma_start(out=dbg["dcw"], in_=cw_sb[:, :]),
              waits=(("s_dve", ev_c_dve),))

    K_aug = [ar[:, KA_:KA_ + 4096], ar[:, KB_:KB_ + 4096]]
    Q_aug = [ar[:, QA_:QA_ + 2048], ar[:, QB_:QB_ + 2048]]
    v_pair = ar[:, VV_:VV_ + 4128]
    ww_raw = ar[:, WW_:WW_ + 1024]
    ww_lo = ar[:, WW_ + 1024:WW_ + 2048]
    ww_hi = ar[:, WW_ + 2048:WW_ + 3072]
    qr_s = ar[:, QR_:QR_ + IMG_W]
    bu = ar[:, BU_:BU_ + BU_W]
    p_b = ar[:, PBUF_:PBUF_ + PB_W]
    au = ar[:, AUF_:AUF_ + AUF_W]
    pt_s = ar[:, PT_:PT_ + 2048]
    apt_s = ar[:, APT_:APT_ + 2048]
    atr_s = ar[:, ATR_:ATR_ + 64]

    def geo(t):
        w0 = max(0, 256 * t - 2942)
        ce = min(BU_W, 256 * t + 1279)
        roff = w0 + 3197 - 256 * t
        wlo2 = max(0, 256 * t - 3070)
        whi2_a = max(0, 256 * t - 3070)
        whi2_b = max(0, 256 * t - 2942)
        return w0, ce, roff, wlo2, whi2_a, whi2_b

    ev = {
        "exp": S.cnt["s_act"],
        "epi_dve": S.cnt["s_dve"],
        "qk_pe": S.cnt["s_pe"],
        "band_pe": S.cnt["s_pe"],
        "aread_sy": [0, 0],
        "aread_gp": [0, 0],
        "pbuf_sy": 0, "pbuf_gp": 0, "pbuf_dve": 0, "pbuf_pe": 0,
        "ev_v": 0,
        "bu_ready": 0,
    }
    gtile = [0]
    pend = [None]

    def emit_front1(hp, hi, t, Ka, Qa):
        g = gtile[0]; gtile[0] += 1
        slot = g % 2
        w0, ce, roff, wlo2, whi2_a, whi2_b = geo(t)

        qr_mm = S.op("pe", lambda e: [
            e.matmul(psc[:, n0:n1], Qa[0:64, t * 128:(t + 1) * 128],
                     rkT_sb[0:64, n0:n1], start=True, stop=True)
            for (n0, n1) in ((0, 512), (512, 1024), (1024, 1536), (1536, 2047))][-1],
            waits=(("s_act", ev["exp"]), ("s_dve", ev["ev_v"]), ("s_gp", ld0),
                   ("s_sy", fl_ev)),
            inc="s_pe")
        qr_ev = S.op("dve", lambda e: (
            e.tensor_copy(stats[:, QK0:QK0 + 1], psc[:, 0:1]),
            e.tensor_copy(stats[:, QK2:QK2 + 1], psc[:, 2046:2047]),
            e.tensor_copy(qr_s[:, 127:127 + 512], psc[:, 0:512]),
            e.tensor_copy(qr_s[:, 639:639 + 512], psc[:, 512:1024]),
            e.tensor_copy(qr_s[:, 1151:1151 + 512], psc[:, 1024:1536]),
            e.tensor_copy(qr_s[:, 1663:1663 + 511], psc[:, 1536:2047]))[-1],
            waits=(("s_pe", qr_mm),), inc="s_dve")
        pads = S.op("dve", lambda e: (
            e.tensor_scalar(qr_s[:, 0:127], identb[:, 0:127], 0.0,
                            stats[:, QK0:QK0 + 1], ALU.mult, ALU.add),
            e.tensor_scalar(qr_s[:, 2174:2301], identb[:, 0:127], 0.0,
                            stats[:, QK2:QK2 + 1], ALU.mult, ALU.add))[-1],
            waits=(("s_dve", qr_ev),), inc="s_dve")
        qkT_mm = S.op("pe", lambda e: e.matmul(
            psc[0:66, 2304:2432], stats[:, 23:89], identf_sb[:, :],
            start=True, stop=True),
            waits=(("s_dve", qr_ev), ("s_dve", ev.get("qkT_ev", 0))), inc="s_pe")
        qkT_ev = S.op("dve", lambda e, t=t: e.tensor_copy(
            Qa[64:66, t * 128:(t + 1) * 128], psc[64:66, 2304:2432]),
            waits=(("s_pe", qkT_mm),), inc="s_dve")
        ev["qkT_ev"] = qkT_ev
        step_ev = gpdma(lambda e, t=t: e.dma_start(
            out=Ka[64:66, :], in_=ap_in["steps"][2 * t:2 * t + 2, :]),
            waits=(("s_pe", ev["qk_pe"]),))
        img_w = sydma(lambda e, slot=slot: e.dma_start(
            out=dram_flat(qrimg, slot * 128 * IMG_RS + IMG_FL, IMG_RS, 128, 2174),
            in_=qr_s[:, 0:2174]),
            waits=(("s_dve", qr_ev), ("s_dve", pads)))
        img_w = sydma(lambda e, slot=slot: e.dma_start(
            out=dram_flat(qrimg, slot * 128 * IMG_RS + IMG_FL + 2174, IMG_RS,
                          128, 2174),
            in_=qr_s[:, 127:2301]),
            waits=(("s_dve", qr_ev), ("s_dve", pads)))
        if t == 0:
            bz = S.op("dve", lambda e: e.memset(bu[:, :], 0.0),
                      waits=(("s_pe", ev["band_pe"]),), inc="s_dve")
        else:
            pw0_ = geo(t - 1)[0]
            if w0 > pw0_:
                bz = S.op("dve", lambda e, pw0_=pw0_, w0=w0: e.memset(
                    bu[:, pw0_:w0], 0.0),
                    waits=(("s_pe", ev["band_pe"]),), inc="s_dve")
            else:
                bz = ev["bu_ready"]
        band_r = sydma(lambda e, slot=slot, w0=w0, ce=ce, roff=roff: e.dma_start(
            out=bu[:, w0:ce],
            in_=dram_flat(qrimg, slot * 128 * IMG_RS + roff, IMG_RS - 1,
                          128, ce - w0)),
            waits=(("s_sy", img_w), ("s_pe", ev["band_pe"]), ("s_dve", bz)))
        ev["bu_ready"] = bz
        return dict(hp=hp, hi=hi, t=t, Ka=Ka, Qa=Qa, slot=slot, w0=w0, ce=ce,
                    wlo2=wlo2, whi2_a=whi2_a, whi2_b=whi2_b, band_r=band_r,
                    qkT_ev=qkT_ev, step_ev=step_ev)

    def emit_front2(fs):
        hp, hi, t, Ka, Qa = fs["hp"], fs["hi"], fs["t"], fs["Ka"], fs["Qa"]
        w0, ce = fs["w0"], fs["ce"]
        band_r, qkT_ev, step_ev = fs["band_r"], fs["qkT_ev"], fs["step_ev"]
        exp_ps = []
        qk_ps = []
        for PS in range(4):
            off = 1024 * (PS % 2)
            lo_banks = [b for b in range(2)
                        if max(128 + 512 * (2 * PS + b), w0)
                        < min(128 + 512 * (2 * PS + b) + 512, ce)]
            hi_banks = [b for b in range(2)
                        if max(512 * (2 * PS + b), w0)
                        < min(512 * (2 * PS + b) + 512, ce)]
            def qk(e, lo_banks=lo_banks, hi_banks=hi_banks, t=t, PS=PS, off=off):
                last = None
                for b in range(2):
                    gb = 2 * PS + b
                    nadd = (b in lo_banks) + (b in hi_banks)
                    last = e.matmul(psc[:, off + b * 512:off + (b + 1) * 512],
                                    Qa[0:66, t * 128:(t + 1) * 128],
                                    Ka[0:66, gb * 512:(gb + 1) * 512],
                                    start=True, stop=(nadd == 0))
                    if b in lo_banks:
                        nadd -= 1
                        last = e.matmul(psc[:, off + b * 512:off + (b + 1) * 512],
                                        idlo_sb[:, :],
                                        bu[:, 128 + 512 * gb:640 + 512 * gb],
                                        start=False, stop=(nadd == 0))
                    if b in hi_banks:
                        last = e.matmul(psc[:, off + b * 512:off + (b + 1) * 512],
                                        idhi_sb[:, :],
                                        bu[:, 512 * gb:512 * gb + 512],
                                        start=False, stop=True)
                return last
            qk_mm = S.op("pe", qk, waits=(
                ("s_sy", band_r), ("s_dve", qkT_ev), ("s_gp", step_ev),
                ("s_act", exp_ps[PS - 2] if PS >= 2 else ev["exp"])), inc="s_pe")
            qk_ps.append(qk_mm)
            dscr_ev = None
            if DBG and hp == 0 and hi == 0:
                dcp = S.op("dve", lambda e, off=off: e.tensor_copy(
                    ar[:, PT_:PT_ + 2048].bitcast(F32)[:, 0:1024],
                    psc[:, off:off + 1024]),
                    waits=(("s_pe", qk_mm),), inc="s_dve")
                dscr_ev = sydma(lambda e, t=t, PS=PS: e.dma_start(
                    out=dbg["dscr"][:, t * 4096 + 1024 * PS:
                                    t * 4096 + 1024 * PS + 1024],
                    in_=ar[:, PT_:PT_ + 2048].bitcast(F32)[:, 0:1024]),
                    waits=(("s_dve", dcp),))
            expp = S.op("act", lambda e, PS=PS, off=off: e.activation(
                p_b[:, PB_L + 1024 * PS:PB_L + 1024 * PS + 1024],
                psc[:, off:off + 1024],
                AF.Exp, bias=st(MNEG)),
                waits=(("s_pe", qk_mm), ("s_sy", ev["pbuf_sy"]),
                       ("s_gp", ev["pbuf_gp"]),
                       ("s_dve", ev["pbuf_dve"]), ("s_pe", ev["pbuf_pe"]),
                       ("s_sy", dscr_ev)),
                inc="s_act")
            exp_ps.append(expp)
            ev["band_pe"] = qk_mm
            ev["qk_pe"] = qk_mm
        exp_a, expv = exp_ps[1], exp_ps[3]
        ev["exp"] = expv
        if DBG and hp == 0 and hi == 0:
            ev["pbuf_sy"] = sydma(lambda e, t=t: e.dma_start(
                out=dbg["dpb"][:, t * 4096:(t + 1) * 4096],
                in_=p_b[:, PB_L:PB_L + 4096]),
                waits=(("s_act", expv),))
        fs["exp_a"] = exp_a
        fs["expv"] = expv

    def emit_back1(fs):
        hp, hi, t, Ka, Qa = fs["hp"], fs["hi"], fs["t"], fs["Ka"], fs["Qa"]
        slot, w0, ce = fs["slot"], fs["w0"], fs["ce"]
        wlo2, whi2_a, whi2_b = fs["wlo2"], fs["whi2_a"], fs["whi2_b"]
        exp_a, expv = fs["exp_a"], fs["expv"]
        pw0 = PBUF_ + PB_L + 256 * t - 3070
        pimg_w = gpdma(lambda e, slot=slot, pw0=pw0: e.dma_start(
            out=dram_flat(pimg, slot * 128 * PW2, PW2, 128, PW2),
            in_=ar[:, pw0:pw0 + PW2]),
            waits=(("s_act", expv), ("s_gp", ev["aread_gp"][slot]),
                   ("s_sy", ev["aread_sy"][slot])))
        auf = gpdma(lambda e, slot=slot: e.dma_start(
            out=au[:, :], in_=dram_flat(pimg, slot * 128 * PW2, PW2 + 1,
                                        128, AUF_W)),
            waits=(("s_gp", pimg_w), ("s_pe", ev.get("atf_pe", 0))))
        ev["aread_gp"][slot] = auf
        ev["pbuf_sy"] = 0
        ev["pbuf_gp"] = pimg_w
        def rs0fn(e, wlo2=wlo2, whi2_a=whi2_a, whi2_b=whi2_b):
            if wlo2 > 0:
                last = e.reduce_sum(st(RS0), p_b[:, PB_L:PB_L + wlo2], AX.X)
            else:
                last = e.memset(st(RS0), 0.0)
            if whi2_b > whi2_a:
                last = e.reduce_sum(st(T3), p_b[:, PB_L + whi2_a:PB_L + whi2_b],
                                    AX.X)
            return last
        rs0e = S.op("dve", rs0fn, waits=(("s_act", exp_a),), inc="s_dve")
        if whi2_b > whi2_a:
            rs0e = S.op("dve", lambda e: e.scalar_tensor_tensor(
                st(RS0), st(T3), m_hi, st(RS0), ALU.mult, ALU.add),
                waits=(("s_dve", rs0e),), inc="s_dve")
        axl0 = PBUF_ + PB_L + 256 * t - 3070
        axr0 = PBUF_ + PB_L + 256 * t + 1023
        cs1 = S.op("dve", lambda e, axl0=axl0, axr0=axr0: (
            e.tensor_tensor(apt_s[:, 0:128], ar[:, axl0:axl0 + 128],
                            masklo_sb[:, :], ALU.mult),
            e.tensor_tensor(apt_s[:, 128:256], ar[:, axl0 + 128:axl0 + 256],
                            masklo_sb[:, :], ALU.mult),
            e.tensor_tensor(apt_s[:, 256:384], ar[:, axr0:axr0 + 128],
                            maskup_sb[:, :], ALU.mult),
            e.tensor_tensor(apt_s[:, 384:512], ar[:, axr0 + 128:axr0 + 256],
                            maskup_sb[:, :], ALU.mult))[-1],
            waits=(("s_gp", auf), ("s_act", expv),
                   ("s_pe", ev.get("rv_pe", 0))), inc="s_dve")
        cs2 = S.op("dve", lambda e: (
            e.reduce_sum(st(CSLL), apt_s[:, 0:128], AX.X),
            e.reduce_sum(st(CSLH), apt_s[:, 128:256], AX.X),
            e.reduce_sum(st(CSRL), apt_s[:, 256:384], AX.X),
            e.reduce_sum(st(CSRH), apt_s[:, 384:512], AX.X))[-1],
            waits=(("s_dve", cs1),), inc="s_dve")
        cs3 = S.op("dve", lambda e: (
            e.tensor_tensor(st(T1), st(CSLL), m_lo, ALU.mult),
            e.tensor_tensor(st(T2), st(CSLH), m_hi, ALU.mult),
            e.tensor_tensor(st(T3), st(CSRL), m_lo, ALU.mult),
            e.tensor_tensor(st(T4), st(CSRH), m_hi, ALU.mult))[-1],
            waits=(("s_dve", cs2),), inc="s_dve")
        csum = S.op("dve", lambda e: (
            e.tensor_add(au[:, 0:1], au[:, 0:1], st(T1)),
            e.tensor_add(au[:, 128:129], au[:, 128:129], st(T2)),
            e.tensor_add(au[:, 4093:4094], au[:, 4093:4094], st(T3)),
            e.tensor_add(au[:, 4221:4222], au[:, 4221:4222], st(T4)))[-1],
            waits=(("s_dve", cs3),), inc="s_dve")
        ev["pbuf_dve"] = csum
        if DBG and hp == 0 and hi == 0:
            ev["aread_sy"][slot] = sydma(lambda e, t=t: e.dma_start(
                out=dbg["dauf"][:, t * AUF_W:(t + 1) * AUF_W], in_=au[:, :]),
                waits=(("s_dve", csum),))
        pt_ev = []
        av_mm = None
        for gq in range(8):
            bank_off = 2560 if gq % 2 == 0 else 3072
            ptm = S.op("pe", lambda e, gq=gq, bank_off=bank_off: [
                e.matmul(psc[:, bank_off + j * 128:bank_off + (j + 1) * 128],
                         p_b[:, PB_L + (gq * 4 + j) * 128:PB_L + (gq * 4 + j + 1) * 128],
                         identb[:, :], start=True, stop=True) for j in range(4)][-1],
                waits=(("s_act", exp_a if gq < 4 else expv),
                       ("s_dve", pt_ev[gq - 2] if gq >= 2
                        else ev.get("ate_last", 0))),
                inc="s_pe")
            pte = S.op("dve", lambda e, gq=gq, bank_off=bank_off: e.tensor_copy(
                pt_s[:, (gq % 4) * 512:(gq % 4 + 1) * 512],
                psc[:, bank_off:bank_off + 512]),
                waits=(("s_pe", ptm),), inc="s_dve")
            pt_ev.append(pte)
            if gq in (3, 7):
                k0 = 0 if gq == 3 else 16
                av_mm = S.op("pe", lambda e, hi=hi, k0=k0: [
                    e.matmul(psc[:, 2048:2113],
                             pt_s[:, (k - k0) * 128:(k - k0 + 1) * 128],
                             v_pair[:, k * 129 + hi * 64:k * 129 + hi * 64 + 65],
                             start=(k == 0), stop=(k == 31))
                    for k in range(k0, k0 + 16)][-1],
                    waits=(("s_dve", pte), ("s_dve", ev["ev_v"])), inc="s_pe")
        ev["pbuf_pe"] = av_mm
        ev["pte_last"] = pt_ev[-1]
        fs["csum"] = csum
        fs["rs0e"] = rs0e
        fs["av_mm"] = av_mm

    def emit_back2(fs):
        hp, hi, t = fs["hp"], fs["hi"], fs["t"]
        csum, rs0e, av_mm = fs["csum"], fs["rs0e"], fs["av_mm"]
        av0 = 2048 + (1 if hi == 1 else 0)
        rs_col = 2112 if hi == 0 else 2048
        apt_ev = []
        for gq, (k0g, nk) in enumerate(((0, 12), (12, 4))):
            def atf(e, k0g=k0g, nk=nk, t=t):
                last = None
                for j in range(nk):
                    k = k0g + j
                    w = 128 if k < 15 else 127
                    keep_wrap = (k + 2) * 128 > 3070 - 256 * t
                    reg = psc[0:w, 2560 + j * 128:2560 + (j + 1) * 128]
                    last = e.matmul(reg, au[:, 2047 + k * 128:2047 + k * 128 + w],
                                    idlo_sb[:, :], start=True, stop=False)
                    last = e.matmul(reg, au[:, 2175 + k * 128:2175 + k * 128 + w],
                                    idhi_sb[:, :], start=False,
                                    stop=not keep_wrap)
                    if keep_wrap:
                        last = e.matmul(reg, au[:, k * 128:k * 128 + w],
                                        idlo_sb[:, :], start=False, stop=False)
                        last = e.matmul(reg, au[:, 128 + k * 128:128 + k * 128 + w],
                                        idhi_sb[:, :], start=False, stop=True)
                return last
            atm = S.op("pe", atf,
                       waits=(("s_dve", csum),
                              ("s_dve", apt_ev[gq - 1] if gq >= 1
                               else ev.get("ate_last", 0))),
                       inc="s_pe")
            ate = S.op("dve", lambda e, k0g=k0g, nk=nk: e.tensor_copy(
                apt_s[:, k0g * 128:(k0g + nk) * 128],
                psc[:, 2560:2560 + nk * 128]),
                waits=(("s_pe", atm),), inc="s_dve")
            apt_ev.append(ate)
        rv_mm = S.op("pe", lambda e: [
            e.matmul(psc[:, 2113:2178],
                     apt_s[0:(128 if k < 15 else 127), k * 128:(k + 1) * 128],
                     rv_sb[0:(128 if k < 15 else 127), k * 65:(k + 1) * 65],
                     start=(k == 0), stop=(k == 15)) for k in range(16)][-1],
            waits=(("s_dve", apt_ev[-1]),), inc="s_pe")
        epiA = S.op("dve", lambda e, av0=av0, rs_col=rs_col: (
            e.tensor_copy(st(RSUM), psc[:, rs_col:rs_col + 1]),
            e.tensor_copy(relv_f[:, :], psc[:, 2113:2178]),
            e.tensor_copy(add1_f[:, :], psc[:, av0:av0 + 64]))[-1],
            waits=(("s_pe", rv_mm), ("s_pe", av_mm), ("s_dve", rs0e),
                   ("s_dve", ev_ebb)), inc="s_dve")
        epiC = S.op("dve", lambda e: (
            e.reciprocal(st(RINV), st(RSUM)),
            e.scalar_tensor_tensor(st(RS2), st(RSUM), st(RS0), relv_f[:, 64:65],
                                   ALU.subtract, ALU.subtract),
            e.tensor_add(add1_f[:, :], add1_f[:, :], relv_f[:, 0:64]))[-1],
            waits=(("s_dve", epiA),), inc="s_dve")
        epiD = S.op("dve", lambda e: e.scalar_tensor_tensor(
            add1_f[:, :], rv0b[:, :], st(RS0), add1_f[:, :], ALU.mult, ALU.add),
            waits=(("s_dve", epiC),), inc="s_dve")
        epiE = S.op("dve", lambda e: e.scalar_tensor_tensor(
            add1_f[:, :], rv2b[:, :], st(RS2), add1_f[:, :], ALU.mult, ALU.add),
            waits=(("s_dve", epiD),), inc="s_dve")
        epi1 = S.op("dve", lambda e: e.tensor_scalar_mul(atr_s[:, :], add1_f[:, :],
                                                         st(RINV)),
                    waits=(("s_dve", epiE),), inc="s_dve")
        atr_mm = S.op("pe", lambda e: e.matmul(
            psc[0:64, 2432:2560], atr_s[:, :], identb[:, :], start=True, stop=True),
            waits=(("s_dve", epi1),), inc="s_pe")
        dep_ev = None
        if DBG and hp == 0 and hi == 0:
            sydma(lambda e, t=t: e.dma_start(
                out=dbg["dstats"][:, t * 32:t * 32 + 32], in_=stats[:, 0:32]),
                waits=(("s_dve", epi1),))
            sydma(lambda e, t=t: e.dma_start(
                out=dbg["drelv"][:, t * 129:t * 129 + 65], in_=relv_f[:, :]),
                waits=(("s_dve", epi1),))
            sydma(lambda e, t=t: e.dma_start(
                out=dbg["drelv"][:, t * 129 + 65:t * 129 + 129], in_=add1_f[:, :]),
                waits=(("s_dve", epi1),))
            dep_ev = sydma(lambda e, t=t: e.dma_start(
                out=dbg["datr"][:, t * 64:(t + 1) * 64], in_=atr_s[:, :]),
                waits=(("s_dve", epi1),))
        epi2 = S.op("dve", lambda e, t=t, hp=hp, hi=hi: e.tensor_copy(
            attn_fm[64 * hi:64 * hi + 64,
                    hp * 2048 + t * 128:hp * 2048 + (t + 1) * 128],
            psc[0:64, 2432:2560]),
            waits=(("s_pe", atr_mm), ("s_sy", dep_ev)), inc="s_dve")
        ev["epi_dve"] = epi2
        ev["ate_last"] = apt_ev[-1]
        ev["rv_pe"] = rv_mm
        ev["atf_pe"] = rv_mm

    def emit_pair(hp):
        if pend[0] is not None:
            emit_back1(pend[0])
            emit_back2(pend[0])
            pend[0] = None
        c0 = hp * 128
        ww3 = ww_raw.rearrange("p (c n) -> p c n", c=8)
        wq_in = ap_in["wqT"][:, c0:c0 + 128].rearrange("(c p) n -> p c n", p=128)
        wk_in = ap_in["wkT"][:, c0:c0 + 128].rearrange("(c p) n -> p c n", p=128)
        wv_in = ap_in["wvT"][:, c0:c0 + 128].rearrange("(c p) n -> p c n", p=128)

        wq_ev = gpdma(lambda e: e.dma_start(out=ww3, in_=wq_in),
                      waits=(("s_pe", ev["qk_pe"]),))
        wqm = S.op("dve", lambda e: (
            e.tensor_scalar_mul(ww_lo[:, :], ww_raw[:, :], m_lo),
            e.tensor_scalar_mul(ww_hi[:, :], ww_raw[:, :], m_hi))[-1],
            waits=(("s_gp", wq_ev), ("s_pe", ev["qk_pe"]), ("s_dve", ev_ebb)),
            inc="s_dve")

        def qproj(e):
            last = None
            for T in range(NT):
                reg = psc[:, T * 128:(T + 1) * 128]
                for c in range(8):
                    last = e.matmul(reg, ww_lo[:, c * 128:(c + 1) * 128],
                                    xn_fm[:, c * 4096 + 256 * T:c * 4096 + 256 * T + 128],
                                    start=(c == 0), stop=False)
                    last = e.matmul(reg, ww_hi[:, c * 128:(c + 1) * 128],
                                    xn_fm[:, c * 4096 + 256 * T + 128:c * 4096 + 256 * T + 256],
                                    start=False, stop=False)
                last = e.matmul(reg, bias_sb[0:1, c0:c0 + 128],
                                ones_row[0:1, 0:128], start=False, stop=True)
            return last
        qp = S.op("pe", qproj, waits=(("s_dve", wqm), ("s_dve", ev_xn_done),
                                      ("s_act", ev["exp"]), ("s_dve", ev["epi_dve"])),
                  inc="s_pe")
        qe = S.op("dve", lambda e: (
            e.tensor_copy(Q_aug[0][0:64, :], psc[0:64, 0:2048]),
            e.tensor_copy(Q_aug[1][0:64, :], psc[64:128, 0:2048]))[-1],
            waits=(("s_pe", qp),), inc="s_dve")

        wk_ev = gpdma(lambda e: e.dma_start(out=ww3, in_=wk_in),
                      waits=(("s_pe", qp),))
        def kproj(e):
            last = None
            for b in range(8):
                for c in range(8):
                    last = e.matmul(psc[:, b * 512:(b + 1) * 512],
                                    ww_raw[:, c * 128:(c + 1) * 128],
                                    xn_fm[:, c * 4096 + b * 512:c * 4096 + (b + 1) * 512],
                                    start=(c == 0), stop=False)
                last = e.matmul(psc[:, b * 512:(b + 1) * 512],
                                bias_sb[0:1, D + c0:D + c0 + 128],
                                ones_row[0:1, 0:512],
                                start=False, stop=True)
            return last
        kp = S.op("pe", kproj, waits=(("s_gp", wk_ev), ("s_dve", qe)), inc="s_pe")
        ke = S.op("dve", lambda e: (
            e.tensor_scalar_mul(K_aug[0][0:64, :], psc[0:64, :], SCALE),
            e.tensor_scalar_mul(K_aug[1][0:64, :], psc[64:128, :], SCALE))[-1],
            waits=(("s_pe", kp),), inc="s_dve")

        wv_ev = gpdma(lambda e: e.dma_start(out=ww3, in_=wv_in),
                      waits=(("s_pe", kp),))
        last_ve = ke
        for rc4 in range(8):
            def vgrp(e, rc4=rc4):
                last = None
                for j in range(4):
                    rcx = rc4 * 4 + j
                    for c in range(8):
                        last = e.matmul(psc[:, j * 512:j * 512 + 128],
                                        xn_fm[:, c * 4096 + rcx * 128:c * 4096 + (rcx + 1) * 128],
                                        ww_raw[:, c * 128:(c + 1) * 128],
                                        start=(c == 0), stop=False)
                    last = e.matmul(psc[:, j * 512:j * 512 + 128],
                                    ones_row[0:1, 0:128],
                                    bias_sb[0:1, 2 * D + c0:2 * D + c0 + 128],
                                    start=False, stop=True)
                return last
            vp = S.op("pe", vgrp, waits=(("s_gp", wv_ev), ("s_dve", last_ve)),
                      inc="s_pe")
            last_ve = S.op("dve", lambda e, rc4=rc4: (
                e.tensor_copy(
                    v_pair[:, rc4 * 516:rc4 * 516 + 516].rearrange(
                        "p (j a) -> p j a", j=4)[:, :, 0:64],
                    psc[:, 0:2048].rearrange("p (j a) -> p j a", j=4)[:, :, 0:64]),
                e.tensor_copy(
                    v_pair[:, rc4 * 516:rc4 * 516 + 516].rearrange(
                        "p (j a) -> p j a", j=4)[:, :, 65:129],
                    psc[:, 0:2048].rearrange("p (j a) -> p j a", j=4)[:, :, 64:128]))[-1],
                waits=(("s_pe", vp),), inc="s_dve")
        ev["ev_v"] = last_ve
        ev["qk_pe"] = S.cnt["s_pe"]

        for hi in (0, 1):
            for t in range(NT):
                fs = emit_front1(hp, hi, t, K_aug[hi], Q_aug[hi])
                if pend[0] is not None:
                    emit_back1(pend[0])
                emit_front2(fs)
                if pend[0] is not None:
                    emit_back2(pend[0])
                pend[0] = fs

    for hp in range(8):
        emit_pair(hp)
    emit_back1(pend[0])
    emit_back2(pend[0])
    pend[0] = None

    ev_attn_pe = S.cnt["s_pe"]
    ev_attn_dve = S.cnt["s_dve"]
    if DBG:
        dbg_ev["datn"] = sydma(lambda e: e.dma_start(out=dbg["datn"],
                                                     in_=attn_fm[:, :]),
                               waits=(("s_dve", ev_attn_dve),))

    oacc = arf(OACC_, 16384)
    wo_s = ar[:, WO_:WO_ + 8192]
    x_te = arf(XTE_, 1024)
    wo_in = ap_in["woT"].rearrange("(c p) n -> p c n", p=128)
    wo_ev = gpdma(lambda e: e.dma_start(
        out=wo_s.rearrange("p (c n) -> p c n", c=8), in_=wo_in),
        waits=(("s_pe", ev_attn_pe), ("s_dve", ev_attn_dve)))
    prev_ev2 = ev_attn_dve
    prev_pe2 = ev_attn_pe
    for t in range(NT):
        lx = sydma(lambda e, t=t: e.dma_start(out=x_te[:, :],
                                              in_=xq[t * 128:(t + 1) * 128, :]),
                   waits=(("s_dve", prev_ev2),))
        def oproj(e, t=t):
            last = None
            for b in range(2):
                for c in range(8):
                    last = e.matmul(psc[:, b * 512:(b + 1) * 512],
                                    attn_fm[:, c * 2048 + t * 128:c * 2048 + (t + 1) * 128],
                                    wo_s[:, c * 1024 + b * 512:c * 1024 + (b + 1) * 512],
                                    start=(c == 0), stop=False)
                last = e.matmul(psc[:, b * 512:(b + 1) * 512],
                                ones_row[0:1, 0:128],
                                bias_sb[0:1, 3 * D + b * 512:3 * D + (b + 1) * 512],
                                start=False, stop=True)
            return last
        om = S.op("pe", oproj, waits=(("s_gp", wo_ev), ("s_dve", prev_ev2)),
                  inc="s_pe")
        prev_ev2 = S.op("dve", lambda e, t=t: e.tensor_add(
            oacc[:, t * 1024:(t + 1) * 1024], psc[:, 0:1024], x_te[:, :]),
            waits=(("s_pe", om), ("s_sy", lx)), inc="s_dve")
        prev_pe2 = om

    fx_t = arf(FXT_, 1024)
    fxn_tm = ar[:, FXN_:FXN_ + 1024]
    xnq8 = xn_fm[:, 0:16384].rearrange("p (c n) -> p c n", c=8)
    prev_f_pe, prev_f_ev = prev_pe2, prev_ev2
    for t in range(NT):
        lx = sydma(lambda e, t=t: e.dma_start(out=fx_t[:, :],
                                              in_=xq[t * 128:(t + 1) * 128, :]),
                   waits=(("s_pe", prev_f_pe), ("s_dve", prev_f_ev)))
        fsel = S.op("dve", lambda e, t=t: (
            e.tensor_scalar_mul(st(T1), st(SMEAN0 + 2 * t), m_lo),
            e.tensor_scalar_mul(st(T2), st(SRSTD0 + 2 * t), m_lo))[-1],
            waits=(("s_sy", lx), ("s_pe", prev_f_pe)), inc="s_dve")
        fsel = S.op("dve", lambda e, t=t: (
            e.scalar_tensor_tensor(st(T1), st(SMEAN0 + 2 * t + 1), m_hi, st(T1),
                                   ALU.mult, ALU.add),
            e.scalar_tensor_tensor(st(T2), st(SRSTD0 + 2 * t + 1), m_hi, st(T2),
                                   ALU.mult, ALU.add))[-1],
            waits=(("s_dve", fsel),), inc="s_dve")
        rcp = S.op("dve", lambda e: e.tensor_scalar(
            fxn_tm[:, :], fx_t[:, :], st(T1), st(T2), ALU.subtract, ALU.mult),
            waits=(("s_dve", fsel),), inc="s_dve")
        trp = S.op("pe", lambda e: [
            e.matmul(psc[:, c * 128:(c + 1) * 128], fxn_tm[:, c * 128:(c + 1) * 128],
                     identb[:, :], start=True, stop=True) for c in range(8)][-1],
            waits=(("s_dve", rcp), ("s_dve", prev_f_ev),
                   ("s_sy", dbg_ev.get("dxn"))), inc="s_pe")
        prev_f_ev = S.op("dve", lambda e, t=t: e.tensor_copy(
            xnq8[:, :, t * 128:(t + 1) * 128],
            psc[:, 0:1024].rearrange("p (c n) -> p c n", c=8)),
            waits=(("s_pe", trp),), inc="s_dve")
        prev_f_pe = trp

    gpw = ar[:, GP_:GP_ + 4096]
    upw = ar[:, UP_:UP_ + 4096]
    dpw = attn_fm[:, 12288:16384]
    h_fm = attn_fm[:, 0:8192]
    silu_f = attn_fm[:, 8192:12288].bitcast(F32)
    last_comb = prev_f_ev
    last_pe = prev_f_pe
    for ex in range(8):
        gp_in = ap_in["gpT"][ex].rearrange("(c p) n -> p c n", p=128)
        up_in = ap_in["upT"][ex].rearrange("(c p) n -> p c n", p=128)
        dp_in = ap_in["dpT"][ex].rearrange("(c p) n -> p c n", p=128)
        wg = gpdma(lambda e, gp_in=gp_in: e.dma_start(
            out=gpw.rearrange("p (c n) -> p c n", c=8), in_=gp_in),
            waits=(("s_pe", last_pe),))
        wu = gpdma(lambda e, up_in=up_in: e.dma_start(
            out=upw.rearrange("p (c n) -> p c n", c=8), in_=up_in))
        wd = gpdma(lambda e, dp_in=dp_in: e.dma_start(
            out=dpw.rearrange("p (c n) -> p c n", c=4), in_=dp_in),
            waits=(("s_pe", last_pe), ("s_dve", last_comb),
                   ("s_sy", dbg_ev.get("datn"))))
        hev = last_comb
        for ib in range(4):
            def gu(e, ib=ib):
                last = None
                for half, wten in ((0, gpw), (1, upw)):
                    for b in range(4):
                        for c in range(8):
                            last = e.matmul(
                                psc[:, half * 2048 + b * 512:half * 2048 + (b + 1) * 512],
                                wten[:, c * 512 + ib * 128:c * 512 + ib * 128 + 128],
                                xn_fm[:, c * 2048 + b * 512:c * 2048 + (b + 1) * 512],
                                start=(c == 0), stop=(c == 7))
                return last
            gum = S.op("pe", gu, waits=(("s_gp", wu), ("s_dve", hev),
                                        ("s_sy", dbg_ev.get("datn"))), inc="s_pe")
            sil = S.op("act", lambda e: (
                [e.activation(silu_f[:, j * 512:(j + 1) * 512],
                              psc[:, j * 512:(j + 1) * 512], AF.Silu)
                 for j in range(4)],
                e.activation(psc[:, 0:256], silu_f[:, 0:256], AF.Square))[-1],
                waits=(("s_pe", gum),), inc="s_act")
            hev = S.op("dve", lambda e, ib=ib: e.tensor_mul(
                h_fm[:, ib * 2048:(ib + 1) * 2048], silu_f[:, 0:2048],
                psc[:, 2048:4096]),
                waits=(("s_act", sil),), inc="s_dve")
        for t in range(NT):
            def down(e, t=t):
                last = None
                for b in range(2):
                    for ic in range(4):
                        last = e.matmul(psc[:, b * 512:(b + 1) * 512],
                                        h_fm[:, ic * 2048 + t * 128:ic * 2048 + (t + 1) * 128],
                                        dpw[:, ic * 1024 + b * 512:ic * 1024 + (b + 1) * 512],
                                        start=(ic == 0), stop=(ic == 3))
                return last
            dm = S.op("pe", down, waits=(("s_gp", wd), ("s_dve", hev),
                                         ("s_dve", last_comb)), inc="s_pe")
            last_comb = S.op("dve", lambda e, t=t, ex=ex: e.scalar_tensor_tensor(
                oacc[:, t * 1024:(t + 1) * 1024], psc[:, 0:1024],
                cw_sb[:, t * 8 + ex:t * 8 + ex + 1],
                oacc[:, t * 1024:(t + 1) * 1024],
                ALU.mult, ALU.add),
                waits=(("s_pe", dm), ("s_dve", ev_c_dve)), inc="s_dve")
            last_pe = dm

    fin = sydma(lambda e: e.dma_start(
        out=out_d.rearrange("(t p) d -> p t d", p=128),
        in_=oacc.rearrange("p (t d) -> p t d", t=16)),
        waits=(("s_dve", last_comb),))
    S.plan["sy"].append(lambda e, fin=fin: e.wait_ge(S.sem["s_sy"], fin))


_NC_CACHE = {}


def _steps(par):
    s = np.zeros((2 * NT, L), dtype=np.float32)
    q = np.arange(L)
    for t in range(NT):
        l0 = 256 * t + 128 * par
        c_lo2 = max(0, 256 * t - 2942) - 128 * (1 - par)
        c_hi = l0 + 1151
        s[2 * t] = (q < c_lo2).astype(np.float32)
        s[2 * t + 1] = (q >= c_hi).astype(np.float32)
    return s


def kernel(L0_attn_x, L0_attn_Wq, L0_attn_bq, L0_attn_Wk, L0_attn_bk, L0_attn_Wv,
           L0_attn_bv, L0_attn_Wo, L0_attn_bo, L0_attn_relative_k, L0_attn_relative_v,
           L0_ffn_gate_weight, L0_ffn_e_score_correction_bias, L0_ffn_gate_proj,
           L0_ffn_up_proj, L0_ffn_down_proj):
    x = np.ascontiguousarray(np.asarray(L0_attn_x, dtype=np.float32))
    C = np.ascontiguousarray

    fm = np.zeros((8, 8), np.float32)
    for e_ in range(8):
        fm[e_, :e_] = 1.0
    hm = np.zeros((4, 4), np.float32)
    for g_ in range(4):
        hm[g_, :g_] = 1.0
    eye = np.eye(128, dtype=np.float32)

    shared = {
        "wqT": C(np.asarray(L0_attn_Wq, np.float32).T),
        "wkT": C(np.asarray(L0_attn_Wk, np.float32).T),
        "wvT": C(np.asarray(L0_attn_Wv, np.float32).T),
        "woT": C(np.asarray(L0_attn_Wo, np.float32).T),
        "bq": C(np.asarray(L0_attn_bq, np.float32).reshape(1, D)),
        "bk": C(np.asarray(L0_attn_bk, np.float32).reshape(1, D)),
        "bv": C(np.asarray(L0_attn_bv, np.float32).reshape(1, D)),
        "bo": C(np.asarray(L0_attn_bo, np.float32).reshape(1, D)),
        "rkT": C(np.asarray(L0_attn_relative_k, np.float32).T),
        "rv": C(np.asarray(L0_attn_relative_v, np.float32)),
        "gwT": C(np.asarray(L0_ffn_gate_weight, np.float32).T),
        "ebias": C(np.asarray(L0_ffn_e_score_correction_bias,
                              np.float32).reshape(1, 8)),
        "gpT": C(np.asarray(L0_ffn_gate_proj, np.float32).transpose(0, 2, 1)),
        "upT": C(np.asarray(L0_ffn_up_proj, np.float32).transpose(0, 2, 1)),
        "dpT": C(np.asarray(L0_ffn_down_proj, np.float32).transpose(0, 2, 1)),
        "identf": eye,
        "masklo": np.tril(np.ones((128, 128), np.float32), -1),
        "maskup": np.triu(np.ones((128, 128), np.float32), 1),
        "fmask64": np.tile(fm.reshape(1, 64), (128, 1)),
        "hmask16": np.tile(hm.reshape(1, 16), (128, 1)),
    }

    if "nc" not in _NC_CACHE:
        _NC_CACHE["nc"] = build_program()
    nc = _NC_CACHE["nc"]

    in_maps = []
    for c in range(8):
        b = c // 2
        par = c % 2
        m = dict(shared)
        m["xb"] = C(x[b])
        m["xq"] = C(x[b].reshape(32, 128, D)[par::2].reshape(NQ, D))
        m["steps"] = _steps(par)
        m["mcol"] = np.tile(np.array([[1.0 - par, float(par)]], np.float32),
                            (128, 1))
        m["idlo"] = C(eye * (1.0 - par))
        m["idhi"] = C(eye * par)
        in_maps.append(m)

    res = run_bass_kernel_spmd(nc, in_maps, core_ids=list(range(8)))
    out = np.empty((B, L, D), dtype=np.float32)
    for c in range(8):
        b = c // 2
        par = c % 2
        out[b].reshape(32, 128, D)[par::2] = res.results[c]["out"].reshape(NT, 128, D)
    return out


if __name__ == "__main__":
    build_program()
    print("program built ok")
